# version 08
# baseline (speedup 1.0000x reference)
import numpy as np
import ml_dtypes
from contextlib import ExitStack

import concourse.bass as bass
import concourse.mybir as mybir
from concourse.bass_utils import run_bass_kernel_spmd
from concourse.bass_core_rust import IndirectOffsetOnAxis

F32 = mybir.dt.float32
BF16 = mybir.dt.bfloat16
I32 = mybir.dt.int32
AF = mybir.ActivationFunctionType
ALU = mybir.AluOpType
AX = mybir.AxisListType

SEQ = 4096
D = 1024
NH = 16
DHD = 64
NDC = 8
DFF = 4096
NFC = 32
NEXP = 8
KEXP = 1024
NST = 16
LN_EPS = 1e-5
RMS_EPS = 1e-6
MAXLOG = float(np.log(1.0 / 0.01))
BIGIDX = 1 << 28

bf = ml_dtypes.bfloat16


class SemC:

    def __init__(self, ctx, nc, name):
        self.h = ctx.enter_context(nc.semaphore(name))
        self.n = 0

    def inc(self, instr, k=1):
        instr.then_inc(self.h, k)
        self.n += k
        return self.n


class Env:
    pass


def _wait(eng, sem, val=None):
    eng.wait_ge(sem.h, sem.n if val is None else val)



def host_constants(seq=SEQ):
    c = {}
    c["ident_bf"] = np.eye(128, dtype=bf)
    c["ident_f32"] = np.eye(128, dtype=np.float32)
    c["ones1"] = np.ones((1, 128), dtype=np.float32)
    sel2 = np.zeros((2, 128), dtype=np.float32)
    sel2[0, 0:64] = 1.0
    sel2[1, 64:128] = 1.0
    c["sel2"] = sel2
    s16 = np.zeros((NST, NST, 128), dtype=np.float32)
    for n in range(NST):
        s16[n, n, :] = 1.0
    c["sel16"] = s16
    hm = np.zeros((NDC, 128, NH), dtype=np.float32)
    for ch in range(NDC):
        hm[ch, 0:64, 2 * ch] = 1.0
        hm[ch, 64:128, 2 * ch + 1] = 1.0
    c["hmask"] = hm
    m2 = np.zeros((128, 2), dtype=np.float32)
    m2[0:64, 0] = 1.0
    m2[64:128, 1] = 1.0
    c["mask2"] = m2
    c["tri"] = np.triu(np.ones((128, 128), dtype=np.float32), 1)
    c["tokid"] = (np.arange(seq // 128)[None, :] * 128 + np.arange(128)[:, None]).astype(np.int32)
    c["jcol"] = (np.arange(8)[None, :] * 128 + np.arange(128)[:, None]).astype(np.int32)
    return c


def host_weights(I):
    w = {}
    w["w_qkv"] = I["L0_attn_w_qkv"]
    w["w_out"] = I["L0_attn_w_out"]
    w["b_out"] = I["L0_attn_b_out"].reshape(NDC, 128).T.copy()
    w["logit_scale"] = I["L0_attn_logit_scale"].reshape(NH)
    def och_pm(a, no):
        ic = a.shape[1] // 128
        return np.ascontiguousarray(
            a.T.reshape(ic, 128, no, 128).transpose(2, 1, 0, 3).reshape(no, 128, ic * 128))
    w["fw1"] = och_pm(I["L0_ffn_w1_w"], 32)
    w["fw2"] = och_pm(I["L0_ffn_w2_w"], 32)
    w["fw3"] = och_pm(I["L0_ffn_w3_w"], 8)
    w["fh1"] = och_pm(I["L0_ffn_h1_w"], 32)
    w["fh2"] = och_pm(I["L0_ffn_h2_w"], 32)
    w["s1"] = I["L0_ffn_w1_s"].reshape(NFC, 128).T.copy()
    w["s2"] = I["L0_ffn_w2_s"].reshape(NFC, 128).T.copy()
    w["h1s"] = I["L0_ffn_h1_s"].reshape(NFC, 128).T.copy()
    w["h2s"] = I["L0_ffn_h2_s"].reshape(NFC, 128).T.copy()
    w["s3"] = I["L0_ffn_w3_s"].reshape(NDC, 128).T.copy()
    w["norm1_w"] = I["L1_attn_norm1_w"].reshape(NDC, 128).T.copy()
    w["norm2_w"] = I["L1_attn_norm2_w"].reshape(NDC, 128).T.copy()
    w["A_log"] = I["L1_attn_A_log"]
    w["D_param"] = I["L1_attn_D_param"].reshape(NDC, 128).T.copy()
    w["mscale"] = I["L1_attn_scale"].reshape(NDC, 128).T.copy()
    w["Wdt"] = np.ascontiguousarray(I["L1_attn_Wdt"].T)
    w["bdt"] = I["L1_attn_bdt"].reshape(NDC, 128).T.copy()
    wbc = np.zeros((1024, 64), dtype=np.float32)
    wbc[:, 0:16] = I["L1_attn_WB"].T
    wbc[:, 32:48] = I["L1_attn_WC"].T
    w["WBC"] = wbc
    bbc = np.zeros((64, 1), dtype=np.float32)
    bbc[0:16, 0] = I["L1_attn_bB"]
    bbc[32:48, 0] = I["L1_attn_bC"]
    w["bBC"] = bbc
    w["W1"] = och_pm(I["L1_attn_W1"], 32)
    w["b1"] = I["L1_attn_b1"].reshape(NFC, 128).T.copy()
    w["W2"] = och_pm(I["L1_attn_W2"], 8)
    w["b2"] = I["L1_attn_b2"].reshape(NDC, 128).T.copy()
    w["gate_w"] = I["L1_ffn_gate_w"]
    w["ew1"] = np.ascontiguousarray(
        I["L1_ffn_w1"].reshape(NEXP, 8, 128, DFF).transpose(0, 2, 1, 3).reshape(NEXP, 128, 8 * DFF))
    w["ew2"] = np.ascontiguousarray(
        I["L1_ffn_w2"].reshape(NEXP, 32, 128, D).transpose(0, 2, 1, 3).reshape(NEXP, 128, 32 * D))
    return w



def declare_io(nc, seq, dbg=False):
    d = Env()
    T = lambda name, shape, dt, kind="ExternalInput": nc.dram_tensor(name, list(shape), dt, kind=kind).ap()
    scr = "ExternalOutput" if dbg else "Internal"
    d.x0 = T("x0", (seq, D), F32)
    d.w_qkv = T("w_qkv", (D, 3 * D), F32)
    d.w_out = T("w_out", (D, D), F32)
    d.b_out = T("b_out", (128, NDC), F32)
    d.logit_scale = T("logit_scale", (NH,), F32)
    d.fw1 = T("fw1", (NFC, 128, D), F32)
    d.fw2 = T("fw2", (NFC, 128, D), F32)
    d.fw3 = T("fw3", (NDC, 128, DFF), F32)
    d.fh1 = T("fh1", (NFC, 128, DFF), F32)
    d.fh2 = T("fh2", (NFC, 128, DFF), F32)
    d.fw1b = T("fw1b", (NFC, 128, D), BF16, "Internal")
    d.fw2b = T("fw2b", (NFC, 128, D), BF16, "Internal")
    d.fw3b = T("fw3b", (NDC, 128, DFF), BF16, "Internal")
    d.fh1b = T("fh1b", (NFC, 128, DFF), BF16, "Internal")
    d.fh2b = T("fh2b", (NFC, 128, DFF), BF16, "Internal")
    d.W1b = T("W1b", (NFC, 128, D), BF16, "Internal")
    d.W2b = T("W2b", (NDC, 128, DFF), BF16, "Internal")
    d.ew1b = T("ew1b", (NEXP, 128, 8 * DFF), BF16, "Internal")
    d.ew2b = T("ew2b", (NEXP, 128, 32 * D), BF16, "Internal")
    d.s1 = T("s1", (128, NFC), F32)
    d.s2 = T("s2", (128, NFC), F32)
    d.h1s = T("h1s", (128, NFC), F32)
    d.h2s = T("h2s", (128, NFC), F32)
    d.s3 = T("s3", (128, NDC), F32)
    d.norm1_w = T("norm1_w", (128, NDC), F32)
    d.norm2_w = T("norm2_w", (128, NDC), F32)
    d.A_log = T("A_log", (NST,), F32)
    d.D_param = T("D_param", (128, NDC), F32)
    d.mscale = T("mscale", (128, NDC), F32)
    d.Wdt = T("Wdt", (D, D), F32)
    d.bdt = T("bdt", (128, NDC), F32)
    d.WBC = T("WBC", (D, 64), F32)
    d.bBC = T("bBC", (64, 1), F32)
    d.W1 = T("W1", (NFC, 128, D), F32)
    d.b1 = T("b1", (128, NFC), F32)
    d.W2 = T("W2", (NDC, 128, DFF), F32)
    d.b2 = T("b2", (128, NDC), F32)
    d.gate_w = T("gate_w", (D, NEXP), F32)
    d.ew1 = T("ew1", (NEXP, 128, 8 * DFF), F32)
    d.ew2 = T("ew2", (NEXP, 128, 32 * D), F32)
    d.ident_bf = T("ident_bf", (128, 128), BF16)
    d.ident_f32 = T("ident_f32", (128, 128), F32)
    d.ones1 = T("ones1", (1, 128), F32)
    d.sel2 = T("sel2", (2, 128), F32)
    d.hmask = T("hmask", (NDC, 128, NH), F32)
    d.mask2 = T("mask2", (128, 2), F32)
    d.tri = T("tri", (128, 128), F32)
    d.tokid = T("tokid", (128, seq // 128), I32)
    d.sel16 = T("sel16", (NST, NST, 128), F32)
    d.jcol = T("jcol", (128, 8), I32)
    d.xn0_d = T("xn0_d", (D, seq), BF16, scr)
    d.vaug_d = T("vaug_d", (NDC, 128, (seq // 128) * 130), BF16, scr)
    d.s0_d = T("s0_d", (D, seq), F32, scr)
    d.out1_d = T("out1_d", (seq, D), F32, scr)
    d.xn1fm_d = T("xn1fm_d", (D, seq), BF16, scr)
    d.xn1tm_d = T("xn1tm_d", (seq, D), BF16, scr)
    d.s1tm_d = T("s1tm_d", (seq, D), F32, scr)
    d.x1_d = T("x1_d", (D, seq), BF16, scr)
    d.probs_d = T("probs_d", (seq, NEXP), F32, scr)
    d.expout_ds = [T(f"expout_d{ex}", (seq // 4 + 1, D), BF16, scr) for ex in range(NEXP)]
    d.inv_ds = [T(f"inv_d{ex}", (seq,), I32, scr) for ex in range(NEXP)]
    d.m_ds = [T(f"m_d{ex}", (seq // 4,), I32, scr) for ex in range(NEXP)]
    if dbg:
        d.qn_d = T("qn_d", (128, NDC * 1024), BF16, "ExternalOutput")
        d.ps_d = T("ps_d", (128, 2 * 1024), F32, "ExternalOutput")
        d.pt_d = T("pt_d", (128, 2 * 1024), BF16, "ExternalOutput")
        d.rk_d = T("rk_d", (128, 64 * NH), F32, "ExternalOutput")
        d.rk1b_d = T("rk1b_d", (NST, 512), F32, "ExternalOutput")
        d.ss_d = T("ss_d", (NST, 512), F32, "ExternalOutput")
        d.sq_d = T("sq_d", (128, 512), F32, "ExternalOutput")
        d.k_d = T("k_d", (128, NDC * 512), BF16, "ExternalOutput")
        d.dt_d = T("dt_d", (128, NDC * 512), F32, "ExternalOutput")
        d.bvc_d = T("bvc_d", (NST, 2 * 512), F32, "ExternalOutput")
        d.yac_d = T("yac_d", (128, NDC * 512), F32, "ExternalOutput")
        d.xnn_d = T("xnn_d", (128, NDC * 512), BF16, "ExternalOutput")
        d.wc_d = T("wc_d", (128, NDC * 512), BF16, "ExternalOutput")
        d.thr_d = T("thr_d", (128, NEXP), F32, "ExternalOutput")
        d.gacc_d = T("gacc_d", (128, 1024), F32, "ExternalOutput")
        d.gtmp_d = T("gtmp_d", (128, 1024), BF16, "ExternalOutput")
        d.xinfm_d = T("xinfm_d", (128, NDC * (seq // 4)), BF16, "ExternalOutput")
        d.hgel_d = T("hgel_d", (128, 2 * (seq // 4)), BF16, "ExternalOutput")
        d.inv0_d = T("inv0_d", (128, NEXP * (seq // 128)), I32, "ExternalOutput")
    d.out = T("out", (seq, D), F32, "ExternalOutput")
    return d


def make_in_map(xb, W, C):
    m = dict(x0=np.ascontiguousarray(xb))
    m.update({k: np.ascontiguousarray(v) for k, v in W.items()})
    m.update({k: np.ascontiguousarray(v) for k, v in C.items()})
    return m




ACT_SET = {AF.Exp: 6, AF.Ln: 6, AF.Gelu: 10, AF.Gelu_apprx_tanh: 11, AF.Silu: 18}

ATTN_LA = 3
ATTN_PTM = 8


class Eng:
    def __init__(self, env, raw, sem, is_dma=False):
        self.env, self.raw, self.sem = env, raw, sem
        self.is_dma = is_dma

    def wait(self, *toks):
        for tok in toks:
            if tok is None:
                continue
            sem, val = tok
            if val <= 0:
                continue
            if sem is self.sem and not self.is_dma:
                continue
            assert val <= sem.n, f"wait on future token {val} > {sem.n}"
            key = (id(self.sem), id(sem))
            if self.env.waited.get(key, 0) >= val:
                continue
            self.raw.wait_ge(sem.h, val)
            self.env.waited[key] = val

    def emit(self, instr, dma=False):
        self.sem.inc(instr, 16 if dma else 1)
        return (self.sem, self.sem.n)

    def drain(self):
        self.raw.wait_ge(self.sem.h, self.sem.n)

    def activation(self, *a, **kw):
        sid = ACT_SET.get(kw.get("func"))
        if sid is None and self.env.cur_tset is None:
            sid = 6
        if sid is not None and self.env.cur_tset != sid:
            self.raw.add_instruction(mybir.InstLoadActFuncSet(
                name=self.raw.bass.get_next_instruction_name(),
                act_func_set_id=sid, ins=[], outs=[]))
            self.env.cur_tset = sid
        return self.raw.activation(*a, **kw)

    def __getattr__(self, name):
        return getattr(self.raw, name)


ENGS = ["tensor", "vector", "scalar", "gpsimd", "sync"]


def mk_env(nc, ctx):
    env = Env()
    env.nc = nc
    env.ctx = ctx
    env.waited = {}
    env.cur_tset = None
    env.sems = {k: SemC(ctx, nc, f"es_{k}") for k in ENGS}
    env.ppsem = SemC(ctx, nc, "es_pp")
    env.tok_pp = {}
    return env


def _mk_section(env, block, engname, body):
    getattr(block, engname)(lambda raw: body(
        Eng(env, raw, env.sems[engname], is_dma=engname in ("sync", "gpsimd"))))



CONSTS = ["r1024", "ln_eps", "rms_eps", "mhalf", "half", "mone", "one"]
CONST_VALS = {"r1024": 1.0 / 1024.0, "ln_eps": LN_EPS, "rms_eps": RMS_EPS,
              "mhalf": -0.5, "half": 0.5, "mone": -1.0, "one": 1.0}


def setup_stage(nc, env, d, block):
    ctx = env.ctx
    e = ctx.enter_context
    env.cpool = e(nc.sbuf_tensor("cpool", [128, len(CONSTS)], F32))
    env.cap = {}
    for i, k in enumerate(CONSTS):
        ap = env.cpool[:, i:i + 1]
        env.cap[k] = ap
        nc.const_aps.register(F32, float(CONST_VALS[k]), ap)
        if float(np.float32(CONST_VALS[k])) != float(CONST_VALS[k]):
            nc.const_aps.register(F32, float(np.float32(CONST_VALS[k])), ap)
    env.ident_bf = e(nc.sbuf_tensor("identbf", [128, 128], BF16))
    env.ident_f32 = e(nc.sbuf_tensor("identf", [128, 128], F32))
    env.ones1 = e(nc.sbuf_tensor("ones1s", [1, 128], F32))
    env.onescol = e(nc.sbuf_tensor("onescol", [128, 1], F32))
    env.sel2 = e(nc.sbuf_tensor("sel2s", [2, 128], F32))
    env.hmask = e(nc.sbuf_tensor("hmasks", [128, NDC * NH], F32))
    env.mask2 = e(nc.sbuf_tensor("mask2s", [128, 2], F32))
    env.tri = e(nc.sbuf_tensor("tris", [128, 128], F32))
    env.bout_sb = e(nc.sbuf_tensor("bouts", [128, NDC], F32))
    env.s1_sb = e(nc.sbuf_tensor("s1s", [128, NFC], F32))
    env.s2_sb = e(nc.sbuf_tensor("s2s", [128, NFC], F32))
    env.h1s_sb = e(nc.sbuf_tensor("h1ss", [128, NFC], F32))
    env.h2s_sb = e(nc.sbuf_tensor("h2ss", [128, NFC], F32))
    env.s3_sb = e(nc.sbuf_tensor("s3s", [128, NDC], F32))

    def ysec(Y):
        toks = []
        for src, dst in [(d.ident_bf, env.ident_bf), (d.ident_f32, env.ident_f32),
                         (d.ones1, env.ones1), (d.sel2, env.sel2),
                         (d.tri, env.tri), (d.b_out, env.bout_sb), (d.mask2, env.mask2),
                         (d.s1, env.s1_sb), (d.s2, env.s2_sb),
                         (d.h1s, env.h1s_sb), (d.h2s, env.h2s_sb), (d.s3, env.s3_sb)]:
            toks.append(Y.emit(Y.dma_start(out=dst[:], in_=src[:]), dma=True))
        toks.append(Y.emit(Y.dma_start(
            out=env.hmask[:].rearrange("p (c h) -> p c h", c=NDC),
            in_=d.hmask.transpose([1, 0, 2])), dma=True))
        env.tok_consts_loaded = toks[-1]
    _mk_section(env, block, "sync", ysec)

    def vsec(V):
        for i, k in enumerate(CONSTS):
            t = V.emit(V.memset(env.cpool[:, i:i + 1], float(np.float32(CONST_VALS[k]))))
        t = V.emit(V.memset(env.onescol[:], 1.0))
        env.tok_cpool = t
    _mk_section(env, block, "vector", vsec)


def prepass_weights(nc, env, d, block, group):
    pp = env.ppsem

    def ssec(SS):
        if group == "p1":
            pairs = [("fw1", d.fw1, d.fw1b, NFC), ("fw2", d.fw2, d.fw2b, NFC),
                     ("fh1", d.fh1, d.fh1b, NFC), ("fh2", d.fh2, d.fh2b, NFC),
                     ("fw3", d.fw3, d.fw3b, NDC), ("W1", d.W1, d.W1b, NFC),
                     ("W2", d.W2, d.W2b, NDC)]
        else:
            pairs = [("ew1", d.ew1, d.ew1b, NEXP), ("ew2", d.ew2, d.ew2b, NEXP)]
        for name, src, dst, n in pairs:
            for oc in range(n):
                ins = SS.raw.dma_start(out=dst[oc], in_=src[oc])
                pp.inc(ins, 16)
                env.tok_pp[(name, oc)] = (pp, pp.n)
    _mk_section(env, block, "gpsimd", ssec)



def phase1(nc, env, d, block, cfg):
    ctx = env.ctx
    e = ctx.enter_context
    seq = cfg.seq
    NT = seq // 128
    NT5 = seq // 512
    NG = seq // 1024
    sec = lambda name, body: _mk_section(env, block, name, body)

    pstack = ExitStack()
    PB = [pstack.enter_context(nc.psum_tensor(f"p1PB{i}", [128, 1024], F32)) for i in range(4)]

    kbstack = ExitStack()
    ekb = kbstack.enter_context
    K_sb = ekb(nc.sbuf_tensor("K_sb", [128, NDC * seq], BF16))
    rkT = ekb(nc.sbuf_tensor("rkT", [128, NT * NH], F32))
    negsh = ekb(nc.sbuf_tensor("negsh", [128, NH], F32))
    shv16 = ekb(nc.sbuf_tensor("shv16", [NST, 1], F32))
    lsrow = ekb(nc.sbuf_tensor("lsrow", [1, NH], F32))

    astack = ExitStack()
    ea = astack.enter_context
    kqw = ea(nc.sbuf_tensor("kqw", [128, NDC * 3 * D], BF16))
    kqw3 = kqw[:].rearrange("p (dd f) -> p dd f", dd=NDC)
    xin = ea(nc.sbuf_tensor("xin", [128, 2 * 1024], F32))
    junk = ea(nc.sbuf_tensor("junk1", [128, 1024], F32))
    mvr = ea(nc.sbuf_tensor("mvr", [128, 8], F32))
    xn_tm = ea(nc.sbuf_tensor("xn_tm", [128, 2 * 1024], BF16))
    xnfm = ea(nc.sbuf_tensor("xnfm", [128, 2 * NDC * 512], BF16))
    vtile = ea(nc.sbuf_tensor("vtile", [128, 2 * NH * 65], BF16))

    trps = [PB[0].bitcast(BF16)[:, 0:128], PB[0].bitcast(BF16)[:, 1024:1152]]
    kps = [PB[1][:, 0:512], PB[1][:, 512:1024]]
    vps = [PB[2][:, 0:512], PB[2][:, 512:1024]]

    def gsec_w(G):
        G.wait(env.tok_cpool)
        env.tok_kqw = G.emit(G.dma_start(
            out=kqw3, in_=d.w_qkv.rearrange("(dd p) f -> p dd f", p=128)), dma=True)
    sec("gpsimd", gsec_w)

    tok_ln = {}
    tok_xin_free = {}
    tok_tr = {}
    tok_trev = {}
    tok_kmm = {}
    tok_kev = {}
    tok_vmm = {}
    tok_vev = {}
    tok_vst = {}
    tok_xn0st = {}
    tok_slab_done = {}

    for ts in range(NT5):
        fmb = ts % 2
        xnv = xnfm[:, fmb * NDC * 512:(fmb + 1) * NDC * 512]
        for sub in range(4):
            tt = ts * 4 + sub
            bx = tt % 2

            def ysec(Y, tt=tt, bx=bx):
                Y.wait(tok_xin_free.get(tt - 2))
                env.tok_x0 = Y.emit(Y.dma_start(
                    out=xin[:, bx * 1024:(bx + 1) * 1024],
                    in_=d.x0[tt * 128:(tt + 1) * 128, :]), dma=True)
            sec("sync", ysec)
            tok_x0 = env.tok_x0

            def vsec(V, tt=tt, bx=bx, tok_x0=tok_x0):
                V.wait(tok_x0)
                xv = xin[:, bx * 1024:(bx + 1) * 1024]
                V.emit(V.tensor_reduce(mvr[:, 0:1], xv, AX.X, ALU.add))
                V.drain()
                V.emit(V.tensor_scalar_mul(mvr[:, 1:2], mvr[:, 0:1], 1.0 / 1024.0))
                V.drain()
                env.tok_v = V.emit(V.scalar_tensor_tensor(
                    junk[:], xv, mvr[:, 1:2], xv, ALU.subtract, ALU.mult,
                    accum_out=mvr[:, 2:3]))
            sec("vector", vsec)
            tok_stats = env.tok_v

            def ssec(SS, tok_stats=tok_stats):
                SS.wait(env.tok_cpool, tok_stats)
                SS.emit(SS.activation(out=mvr[:, 3:4], in_=mvr[:, 2:3], func=AF.Ln,
                                      scale=1.0 / 1024.0, bias=LN_EPS))
                SS.drain()
                env.tok_s = SS.emit(SS.activation(out=mvr[:, 4:5], in_=mvr[:, 3:4],
                                                  func=AF.Exp, scale=-0.5))
            sec("scalar", ssec)
            tok_rstd = env.tok_s

            def vsec2(V, tt=tt, bx=bx, tok_rstd=tok_rstd):
                V.wait(tok_rstd)
                V.wait(tok_tr.get((tt - 2, NDC - 1)))
                xv = xin[:, bx * 1024:(bx + 1) * 1024]
                env.tok_v = V.emit(V.tensor_scalar(
                    xn_tm[:, bx * 1024:(bx + 1) * 1024], xv, mvr[:, 1:2], mvr[:, 4:5],
                    ALU.subtract, ALU.mult))
                tok_xin_free[tt] = env.tok_v
            sec("vector", vsec2)
            tok_ln[tt] = env.tok_v

            for dc in range(NDC):
                def tsec(T, tt=tt, bx=bx, dc=dc, ts=ts):
                    T.wait(tok_ln[tt])
                    T.wait(tok_trev.get((tt, dc - 2), tok_trev.get((tt - 1, NDC - 2 + dc), None) if dc < 2 else None))
                    T.wait(tok_slab_done.get(ts - 2))
                    tok_tr[(tt, dc)] = T.emit(T.transpose(
                        trps[dc % 2], xn_tm[:, bx * 1024 + dc * 128: bx * 1024 + (dc + 1) * 128],
                        env.ident_bf[:]))
                sec("tensor", tsec)

                def vsec3(V, tt=tt, dc=dc, sub=sub, xnv=xnv):
                    V.wait(tok_tr[(tt, dc)])
                    dst = xnv[:, dc * 512 + sub * 128: dc * 512 + sub * 128 + 128]
                    tok_trev[(tt, dc)] = V.emit(V.tensor_copy(dst, trps[dc % 2]))
                sec("vector", vsec3)

        for c in range(NDC):
            def tsec(T, ts=ts, c=c, xnv=xnv):
                T.wait(env.tok_kqw)
                T.wait(tok_trev[(ts * 4 + 3, NDC - 1)])
                T.wait(tok_kev.get((ts, c - 2)) if c >= 2 else tok_kev.get((ts - 1, NDC - 2 + c)))
                for dd in range(NDC):
                    ins = T.matmul(kps[c % 2],
                                   kqw3[:, dd, D + c * 128: D + (c + 1) * 128],
                                   xnv[:, dd * 512:(dd + 1) * 512],
                                   start=(dd == 0), stop=(dd == NDC - 1))
                tok_kmm[(ts, c)] = T.emit(ins)
            sec("tensor", tsec)

            def vsec(V, ts=ts, c=c):
                V.wait(tok_kmm[(ts, c)])
                tok_kev[(ts, c)] = V.emit(V.tensor_copy(
                    K_sb[:, c * seq + ts * 512: c * seq + (ts + 1) * 512], kps[c % 2]))
            sec("vector", vsec)

        for sub in range(4):
            tt = ts * 4 + sub
            vb = tt % 2
            vt = vtile[:, vb * NH * 65:(vb + 1) * NH * 65]
            vt3 = vt.rearrange("p (h c) -> p h c", h=NH)
            for fc in range(2):
                def tsec(T, ts=ts, sub=sub, fc=fc, xnv=xnv):
                    T.wait(tok_vev.get((ts * 4 + sub - 1, fc)))
                    for dd in range(NDC):
                        ins = T.matmul(vps[fc],
                                       xnv[:, dd * 512 + sub * 128: dd * 512 + sub * 128 + 128],
                                       kqw3[:, dd, 2 * D + fc * 512: 2 * D + (fc + 1) * 512],
                                       start=(dd == 0), stop=(dd == NDC - 1))
                    tok_vmm[(tt, fc)] = T.emit(ins)
                sec("tensor", tsec)

            def vsec(V, tt=tt, vt3=vt3):
                V.wait(tok_vst.get(tt - 2))
                V.emit(V.memset(vt3[:, :, 64:65], 1.0))
                for fc in range(2):
                    V.wait(tok_vmm[(tt, fc)])
                    src = vps[fc].rearrange("p (h dh) -> p h dh", h=8)
                    tok_vev[(tt, fc)] = V.emit(V.tensor_copy(vt3[:, fc * 8:(fc + 1) * 8, 0:64], src))
            sec("vector", vsec)

            def ysec(Y, tt=tt, vt=vt):
                Y.wait(tok_vev[(tt, 1)])
                tok_vst[tt] = Y.emit(Y.dma_start(
                    out=d.vaug_d.transpose([1, 0, 2])[:, :, tt * 130:(tt + 1) * 130],
                    in_=vt.rearrange("p (pr c) -> p pr c", pr=NDC)), dma=True)
            sec("sync", ysec)

        def ysec2(Y, ts=ts, xnv=xnv):
            Y.wait(tok_kev[(ts, NDC - 1)])
            Y.wait(tok_vmm[(ts * 4 + 3, 1)])
            tok_xn0st[ts] = Y.emit(Y.dma_start(
                out=d.xn0_d.rearrange("(dc p) q -> p dc q", p=128)[:, :, ts * 512:(ts + 1) * 512],
                in_=xnv.rearrange("p (dc q) -> p dc q", dc=NDC)), dma=True)
            tok_slab_done[ts] = tok_xn0st[ts]
        sec("sync", ysec2)

    sq1b = ea(nc.sbuf_tensor("sq1b", [128, 2 * 512], F32))
    nrm1b = ea(nc.sbuf_tensor("nrm1b", [NST, 512], F32))
    rk1b = ea(nc.sbuf_tensor("rk1b", [NST, 512], F32))

    ssps = PB[3][0:16, 0:512]
    trp16 = PB[3][:, 512:640]
    nshps = PB[3][0:128, 512 + 128:512 + 144]

    def ysec(Y):
        Y.emit(Y.dma_start(out=shv16[:], in_=d.logit_scale.rearrange("(a b) -> a b", b=1)), dma=True)
        env.tok_ls = Y.emit(Y.dma_start(out=lsrow[:], in_=d.logit_scale.rearrange("(b a) -> b a", b=1)), dma=True)
    sec("sync", ysec)
    tok_ls = env.tok_ls

    def vsec(V, tok_ls=tok_ls):
        V.wait(tok_ls)
        V.emit(V.tensor_scalar_min(shv16[:], shv16[:], MAXLOG))
        env.tok_v = V.emit(V.tensor_scalar_min(lsrow[:], lsrow[:], MAXLOG))
        V.drain()
    sec("vector", vsec)
    tok_min = env.tok_v

    def ssec(SS, tok_min=tok_min):
        SS.wait(tok_min)
        SS.emit(SS.activation(out=shv16[:], in_=shv16[:], func=AF.Exp))
        env.tok_s = SS.emit(SS.activation(out=lsrow[:], in_=lsrow[:], func=AF.Exp))
        SS.drain()
    sec("scalar", ssec)
    tok_exp = env.tok_s

    def vsec2(V, tok_exp=tok_exp):
        V.wait(tok_exp)
        env.tok_v = V.emit(V.tensor_scalar_mul(lsrow[:], lsrow[:], -1.0))
        V.drain()
    sec("vector", vsec2)
    tok_neg = env.tok_v

    def tsec(T, tok_neg=tok_neg):
        T.wait(tok_neg, env.tok_consts_loaded)
        env.tok_t = T.emit(T.matmul(nshps, env.ones1[:], lsrow[:], start=True, stop=True))
    sec("tensor", tsec)
    tok_nshmm = env.tok_t

    def vsec3(V, tok_nshmm=tok_nshmm):
        V.wait(tok_nshmm)
        env.tok_v = V.emit(V.tensor_copy(negsh[:], nshps))
    sec("vector", vsec3)
    env.tok_negsh = env.tok_v

    hm3 = env.hmask[:].rearrange("p (c h) -> p c h", c=NDC)
    tok_sq = {}
    tok_ssmm = {}
    env.tok_trk_ev = {}
    env.tok_nrm_hist = {}
    for ts in range(NT5):
        for c in range(NDC):
            def ssec(SS, ts=ts, c=c):
                SS.wait(tok_kev[(ts, c)])
                SS.wait(tok_ssmm.get((ts, c - 2)) if c >= 2 else tok_ssmm.get((ts - 1, NDC - 2 + c)))
                tok_sq[(ts, c)] = SS.emit(SS.activation(
                    out=sq1b[:, (c % 2) * 512:(c % 2 + 1) * 512],
                    in_=K_sb[:, c * seq + ts * 512: c * seq + (ts + 1) * 512],
                    func=AF.Square))
            sec("scalar", ssec)

            def tsec(T, ts=ts, c=c):
                T.wait(tok_sq[(ts, c)])
                if c == 0 and ts > 0:
                    T.wait(env.tok_nrm_hist[ts - 1])
                tok_ssmm[(ts, c)] = T.emit(T.matmul(
                    ssps, hm3[:, c, :], sq1b[:, (c % 2) * 512:(c % 2 + 1) * 512],
                    start=(c == 0), stop=(c == NDC - 1)))
            sec("tensor", tsec)

        def ssec2(SS, ts=ts):
            SS.wait(tok_ssmm[(ts, NDC - 1)])
            if cfg.dbg and ts == 0:
                SS.emit(SS.activation(out=rk1b[:], in_=ssps, func=AF.Identity))
            SS.emit(SS.activation(out=nrm1b[:], in_=ssps, func=AF.Ln))
            SS.drain()
            env.tok_s = SS.emit(SS.activation(out=nrm1b[:], in_=nrm1b[:], func=AF.Exp, scale=0.5))
        sec("scalar", ssec2)
        if cfg.dbg and ts == 0:
            def ydbg1b(Y):
                Y.wait(env.tok_s)
                for cc in range(NDC):
                    Y.emit(Y.dma_start(out=d.k_d[:, cc * 512:(cc + 1) * 512],
                                       in_=K_sb[:, cc * seq + 0:cc * seq + 512]), dma=True)
                Y.emit(Y.dma_start(out=d.ss_d[:, :], in_=rk1b[:]), dma=True)
                env.tok_y1b = Y.emit(Y.dma_start(out=d.sq_d[:, :], in_=sq1b[:, 512:1024]), dma=True)
            sec("sync", ydbg1b)
        tok_nrm = env.tok_s
        env.tok_nrm_hist[ts] = tok_nrm

        def vsec4(V, ts=ts, tok_nrm=tok_nrm):
            V.wait(tok_nrm)
            if cfg.dbg and ts == 0:
                V.wait(getattr(env, "tok_y1b", None))
            V.emit(V.tensor_scalar_max(nrm1b[:], nrm1b[:], 1e-12))
            V.drain()
            V.emit(V.reciprocal(rk1b[:], nrm1b[:]))
            V.drain()
            env.tok_v = V.emit(V.tensor_scalar(rk1b[:], rk1b[:], shv16[:, 0:1], None, ALU.mult))
        sec("vector", vsec4)
        if cfg.dbg and ts == 0:
            def ydbg1c(Y):
                Y.wait(env.tok_v)
                Y.emit(Y.dma_start(out=d.rk1b_d[:, :], in_=rk1b[:]), dma=True)
            sec("sync", ydbg1c)
        tok_rk = env.tok_v

        for i in range(4):
            def tsec2(T, ts=ts, i=i, tok_rk=tok_rk):
                T.wait(tok_rk)
                T.wait(env.tok_trk_ev.get((ts, i - 1)) if i >= 1 else env.tok_trk_ev.get((ts - 1, 3)))
                if ts == 0 and i == 0:
                    T.wait(env.tok_negsh)
                env.tok_t = T.emit(T.transpose(trp16[0:128, 0:16],
                                               rk1b[0:16, i * 128:(i + 1) * 128], env.ident_f32[0:16, 0:16]))
            sec("tensor", tsec2)
            tok_trk = env.tok_t

            def vsec5(V, ts=ts, i=i, tok_trk=tok_trk):
                V.wait(tok_trk)
                kt = ts * 4 + i
                env.tok_v = V.emit(V.tensor_copy(rkT[:, kt * NH:(kt + 1) * NH], trp16[0:128, 0:16]))
                env.tok_trk_ev[(ts, i)] = env.tok_v
            sec("vector", vsec5)
    env.tok_rkT_done = env.tok_v


    barrier(env, block)
    astack.close()
    NKT = seq // 128
    cstack = ExitStack()
    ec = cstack.enter_context
    wq = ec(nc.sbuf_tensor("wq", [128, NDC * D], BF16))
    wq3 = wq[:].rearrange("p (dd f) -> p dd f", dd=NDC)
    wow = ec(nc.sbuf_tensor("wow", [128, NDC * D], BF16))
    wow3 = wow[:].rearrange("p (dd f) -> p dd f", dd=NDC)
    xnsl = ec(nc.sbuf_tensor("xnsl", [128, NDC * 1024], BF16))
    Qn = ec(nc.sbuf_tensor("Qn", [128, NDC * 1024], BF16))
    sqq = ec(nc.sbuf_tensor("sqq", [128, 2 * 1024], F32))
    rq2 = ec(nc.sbuf_tensor("rq2", [2, 2 * 1024], F32))
    Pt = ec(nc.sbuf_tensor("Pt", [128, 2 * 2 * 1024], BF16))
    vp = ec(nc.sbuf_tensor("vp", [128, 2 * NKT * 130], BF16))
    tmppv = ec(nc.sbuf_tensor("tmppv", [128, 1024], F32))
    rsumP = sqq[0:1, 0:2048]
    attnfm = ec(nc.sbuf_tensor("attnfm", [128, NDC * 1024], BF16))
    s0st = ec(nc.sbuf_tensor("s0st", [128, 2 * 1024], F32))

    def gsec_wq(G):
        G.emit(G.dma_start(out=wq3, in_=d.w_qkv.rearrange("(dd p) f -> p dd f", p=128)[:, :, 0:D]), dma=True)
        env.tok_wq = G.emit(G.dma_start(out=wow3, in_=d.w_out.rearrange("(dd p) f -> p dd f", p=128)), dma=True)
    sec("gpsimd", gsec_wq)
    prepass_weights(nc, env, d, block, "p1")

    tok_xnsl = {}
    tok_q = {}
    tok_qsq = {}
    tok_qev = {}
    tok_qssmm = {}
    tok_qnrm = {}
    tok_bc_hist = {}
    tok_qapply = {}
    tok_S = {}
    tok_E = {}
    tok_PV = {}
    tok_tail = {}
    tok_sums = {}
    tok_rln = {}
    tok_dmm = {}
    tok_vld = {}
    tok_proj = {}
    tok_projev = {}
    tok_projst = {}
    tok_div = {}

    psQs = {0: PB[0], 1: PB[2]}

    for g in range(NG):
        def ysec(Y, g=g):
            Y.wait(tok_xn0st[NT5 - 1])
            if g >= 1:
                Y.wait(tok_qapply.get((g - 1, NDC - 1)))
            tok_xnsl[g] = Y.emit(Y.dma_start(
                out=xnsl[:].rearrange("p (dc q) -> p dc q", dc=NDC),
                in_=d.xn0_d.rearrange("(dc p) q -> p dc q", p=128)[:, :, g * 1024:(g + 1) * 1024]), dma=True)
        sec("sync", ysec)

        for c in range(NDC):
            def tsec(T, g=g, c=c):
                T.wait(tok_xnsl[g], env.tok_wq)
                if c == 0:
                    if g == 0:
                        T.wait(env.tok_rkT_done)
                    else:
                        T.wait(tok_div[(g - 1, NH - 1)])
                        T.wait(tok_projev[(g - 1, NDC - 1)])
                T.wait(tok_qev.get((g, c - 2)) if c >= 2 else tok_qev.get((g - 1, NDC - 2 + c)))
                T.wait(tok_qsq.get((g, c - 2)) if c >= 2 else tok_qsq.get((g - 1, NDC - 2 + c)))
                for hf in range(2):
                    for dd in range(NDC):
                        ins = T.matmul(psQs[c % 2][:, hf * 512:(hf + 1) * 512],
                                       wq3[:, dd, c * 128:(c + 1) * 128],
                                       xnsl[:, dd * 1024 + hf * 512: dd * 1024 + (hf + 1) * 512],
                                       start=(dd == 0), stop=(dd == NDC - 1))
                tok_q[(g, c)] = T.emit(ins)
            sec("tensor", tsec)

            def ssec(SS, g=g, c=c):
                SS.wait(tok_q[(g, c)])
                tok_qsq[(g, c)] = SS.emit(SS.activation(
                    out=sqq[:, (c % 2) * 1024:(c % 2 + 1) * 1024], in_=psQs[c % 2][:], func=AF.Square))
            sec("scalar", ssec)

            def vsec(V, g=g, c=c):
                V.wait(tok_qsq[(g, c)])
                tok_qev[(g, c)] = V.emit(V.tensor_copy(Qn[:, c * 1024:(c + 1) * 1024], psQs[c % 2][:]))
            sec("vector", vsec)

            def tsec2(T, g=g, c=c):
                T.wait(tok_qsq[(g, c)])
                T.wait(tok_qnrm.get((g, c - 1), tok_qnrm.get((g - 1, NDC - 1)) if c == 0 else None))
                for hf in range(2):
                    ins = T.matmul(PB[3][0:2, hf * 512:(hf + 1) * 512], env.mask2[:],
                                   sqq[:, (c % 2) * 1024 + hf * 512:(c % 2) * 1024 + (hf + 1) * 512],
                                   start=True, stop=True)
                tok_qssmm[(g, c)] = T.emit(ins)
            sec("tensor", tsec2)

            def ssec2(SS, g=g, c=c):
                SS.wait(tok_qssmm[(g, c)])
                SS.wait(tok_bc_hist.get((g, c - 2)) if c >= 2 else tok_bc_hist.get((g - 1, NDC - 2 + c)))
                rv = rq2[0:2, (c % 2) * 1024:(c % 2 + 1) * 1024]
                SS.emit(SS.activation(out=rv, in_=PB[3][0:2, 0:1024], func=AF.Ln))
                SS.drain()
                env.tok_s = SS.emit(SS.activation(out=rv, in_=rv, func=AF.Exp, scale=0.5))
            sec("scalar", ssec2)
            tok_qn = env.tok_s

            def vsec2(V, g=g, c=c, tok_qn=tok_qn):
                V.wait(tok_qn)
                rv = rq2[0:2, (c % 2) * 1024:(c % 2 + 1) * 1024]
                V.emit(V.tensor_scalar_max(rv, rv, 1e-12))
                V.drain()
                env.tok_v = V.emit(V.reciprocal(rv, rv))
                V.drain()
            sec("vector", vsec2)
            tok_qnrm[(g, c)] = env.tok_v

            def tsec3(T, g=g, c=c):
                T.wait(tok_qnrm[(g, c)])
                T.wait(tok_qapply.get((g, c - 1), tok_qapply.get((g - 1, NDC - 1)) if c == 0 else None))
                for hf in range(2):
                    ins = T.matmul(PB[1][:, hf * 512:(hf + 1) * 512], env.sel2[:],
                                   rq2[0:2, (c % 2) * 1024 + hf * 512: (c % 2) * 1024 + (hf + 1) * 512],
                                   start=True, stop=True)
                env.tok_t = T.emit(ins)
            sec("tensor", tsec3)
            tok_bc = env.tok_t
            tok_bc_hist[(g, c)] = tok_bc

            def vsec3(V, g=g, c=c, tok_bc=tok_bc):
                V.wait(tok_bc)
                qv = Qn[:, c * 1024:(c + 1) * 1024]
                tok_qapply[(g, c)] = V.emit(V.tensor_tensor(qv, qv, PB[1][:], ALU.mult))
            sec("vector", vsec3)

        if cfg.dbg and g == 0:
            def ydbg(Y):
                Y.wait(tok_qapply[(0, NDC - 1)], env.tok_rkT_done)
                Y.emit(Y.dma_start(out=d.qn_d[:, :], in_=Qn[:]), dma=True)
                Y.emit(Y.dma_start(out=d.rk_d[:, 0:NT * NH], in_=rkT[:]), dma=True)
            sec("sync", ydbg)

        PS = [PB[0], PB[1], PB[2]]
        psPV = PB[3][0:65, :]
        LAGP = 3
        NPASS = NDC * 2

        def emit_divide(i, erd, g=g):
            ul = (i + 1) * NKT - 1
            t = PS[ul % 3] if i != NPASS - 1 else PS[(ul - 1) % 3]
            p_, qh_ = divmod(i, 2)

            def ssecR(SS, g=g, i=i):
                SS.wait(tok_sums[(g, i)])
                rs = rsumP[0:1, (i % 2) * 1024:(i % 2 + 1) * 1024]
                SS.emit(SS.activation(out=rs, in_=rs, func=AF.Ln))
                tok_rln[(g, i)] = SS.emit(SS.activation(out=rs, in_=rs, func=AF.Exp, scale=-1.0))
            sec("scalar", ssecR)

            def tsecD(T, g=g, i=i, t=t, erd=erd):
                T.wait(tok_rln[(g, i)])
                T.wait(tok_E[(g, erd)])
                for hh in range(2):
                    ins = T.matmul(t[:, hh * 512:(hh + 1) * 512], env.ones1[:],
                                   rsumP[0:1, (i % 2) * 1024 + hh * 512:
                                         (i % 2) * 1024 + (hh + 1) * 512],
                                   start=True, stop=True)
                tok_dmm[(g, i)] = T.emit(ins)
            sec("tensor", tsecD)

            def vsecD(V, g=g, i=i, t=t, p_=p_, qh_=qh_):
                V.wait(tok_dmm[(g, i)])
                for hh in range(2):
                    ins = V.tensor_tensor(
                        attnfm[hh * 64:(hh + 1) * 64,
                               p_ * 1024 + qh_ * 512: p_ * 1024 + qh_ * 512 + 512],
                        tmppv[0:64, hh * 512:(hh + 1) * 512],
                        t[hh * 64:(hh + 1) * 64, hh * 512:(hh + 1) * 512], ALU.mult)
                tok_div[(g, i)] = V.emit(ins)
            sec("vector", vsecD)

        for p in range(NDC):
            vslab = vp[:, (p % 2) * NKT * 130:(p % 2 + 1) * NKT * 130]

            def ysec2(Y, g=g, p=p, vslab=vslab):
                prev = tok_PV.get((g, (2 * (p - 2) + 2) * NKT - 1),
                                  tok_PV.get((g - 1, (2 * (NDC - 2 + p) + 2) * NKT - 1)))
                Y.wait(prev)
                if g == 0:
                    Y.wait(tok_vst[NT - 1])
                tok_vld[(g, p)] = Y.emit(Y.dma_start(out=vslab, in_=d.vaug_d[p, :, :]), dma=True)
            sec("sync", ysec2)

            for qh in range(2):
                i = 2 * p + qh
                ub = i * NKT

                def emit_pv(u, g=g, p=p, i=i, vslab=vslab, ub=ub):
                    kt = u - ub

                    def tsecPV(T, g=g, p=p, i=i, u=u, kt=kt, vslab=vslab):
                        T.wait(tok_E[(g, u)])
                        if kt == 0:
                            T.wait(tok_vld[(g, p)])
                            T.wait(tok_tail.get((g, i - 1),
                                                tok_tail.get((g - 1, NPASS - 1)) if i == 0 else None))
                        for hh in range(2):
                            ins = T.matmul(
                                psPV[:, hh * 512:(hh + 1) * 512],
                                vslab[:, kt * 130 + hh * 65: kt * 130 + (hh + 1) * 65],
                                Pt[:, ((u % 4) * 2 + hh) * 512:((u % 4) * 2 + hh + 1) * 512],
                                start=(kt == 0), stop=(kt == NKT - 1))
                        tok_PV[(g, u)] = T.emit(ins)
                    sec("tensor", tsecPV)

                for kt in range(NKT):
                    u = ub + kt
                    if kt == 1 and i >= 1:
                        emit_divide(i - 1, ub)

                    def tsecS(T, g=g, p=p, qh=qh, i=i, u=u, kt=kt):
                        if u == 0:
                            T.wait(tok_qapply[(g, NDC - 1)])
                        if u >= 2:
                            T.wait(tok_E[(g, u - 2)])
                        if kt == 2 and i >= 1:
                            T.wait(tok_div[(g, i - 1)])
                        for hh in range(2):
                            ins = T.matmul(
                                PS[u % 3][:, hh * 512:(hh + 1) * 512],
                                K_sb[hh * 64:(hh + 1) * 64, p * seq + kt * 128:p * seq + (kt + 1) * 128],
                                Qn[hh * 64:(hh + 1) * 64, p * 1024 + qh * 512: p * 1024 + qh * 512 + 512],
                                start=True, stop=True)
                        tok_S[(g, u)] = T.emit(ins)
                    sec("tensor", tsecS)

                    if kt >= LAGP:
                        emit_pv(u - LAGP)

                    def ssecE(SS, g=g, p=p, u=u, kt=kt):
                        SS.wait(tok_S[(g, u)])
                        if u >= 4:
                            SS.wait(tok_PV[(g, u - 4)])
                        for hh in range(2):
                            hx = 2 * p + hh
                            ins = SS.activation(
                                out=Pt[:, ((u % 4) * 2 + hh) * 512:((u % 4) * 2 + hh + 1) * 512],
                                in_=PS[u % 3][:, hh * 512:(hh + 1) * 512],
                                func=AF.Exp,
                                scale=rkT[:, kt * NH + hx: kt * NH + hx + 1],
                                bias=negsh[:, hx:hx + 1])
                        tok_E[(g, u)] = SS.emit(ins)
                    sec("scalar", ssecE)

                for u in range(ub + NKT - LAGP, ub + NKT):
                    emit_pv(u)

                def vsecT(V, g=g, i=i, ub=ub):
                    V.wait(tok_PV[(g, ub + NKT - 1)])
                    tok_sums[(g, i)] = V.emit(V.tensor_copy(
                        rsumP[0:1, (i % 2) * 1024:(i % 2 + 1) * 1024], psPV[64:65, :]))
                    tok_tail[(g, i)] = V.emit(V.tensor_copy(
                        tmppv[0:64, :], psPV[0:64, :]))
                sec("vector", vsecT)

        emit_divide(NPASS - 1, NPASS * NKT - 1)

        for oc in range(NDC):
            def tsecP(T, g=g, oc=oc):
                T.wait(tok_div[(g, NH - 1)], env.tok_wq)
                T.wait(tok_projev.get((g, oc - 2)))
                for hf in range(2):
                    for dd in range(NDC):
                        ins = T.matmul(psQs[oc % 2][:, hf * 512:(hf + 1) * 512],
                                       wow3[:, dd, oc * 128:(oc + 1) * 128],
                                       attnfm[:, dd * 1024 + hf * 512: dd * 1024 + (hf + 1) * 512],
                                       start=(dd == 0), stop=(dd == NDC - 1))
                tok_proj[(g, oc)] = T.emit(ins)
            sec("tensor", tsecP)

            def ssecP(SS, g=g, oc=oc):
                SS.wait(tok_proj[(g, oc)])
                SS.wait(tok_projst.get((g, oc - 2)))
                tok_projev[(g, oc)] = SS.emit(SS.activation(
                    out=s0st[:, (oc % 2) * 1024:(oc % 2 + 1) * 1024], in_=psQs[oc % 2][:],
                    func=AF.Identity, bias=env.bout_sb[:, oc:oc + 1]))
            sec("scalar", ssecP)

            def ysecP(Y, g=g, oc=oc):
                Y.wait(tok_projev[(g, oc)])
                tok_projst[(g, oc)] = Y.emit(Y.dma_start(
                    out=d.s0_d[oc * 128:(oc + 1) * 128, g * 1024:(g + 1) * 1024],
                    in_=s0st[:, (oc % 2) * 1024:(oc % 2 + 1) * 1024]), dma=True)
            sec("sync", ysecP)
    env.tok_attn_done = tok_projst[(NG - 1, NDC - 1)]

    barrier(env, block)
    cstack.close()
    kbstack.close()
    dstack = ExitStack()
    e = dstack.enter_context
    fw12st = e(nc.sbuf_tensor("fw12st", [128, 2 * 2 * NDC * 128], BF16))
    fw3st = e(nc.sbuf_tensor("fw3st", [128, 2 * NFC * 128], BF16))
    fhst = e(nc.sbuf_tensor("fhst", [128, 2 * NFC * 128], BF16))
    xnsl2 = e(nc.sbuf_tensor("xnsl2", [128, 2 * NDC * 512], BF16))
    ab_sb = e(nc.sbuf_tensor("ab_sb", [128, 4 * 512], BF16))
    h_sb = e(nc.sbuf_tensor("h_sb", [128, NFC * 512], BF16))
    h1o_sb = e(nc.sbuf_tensor("h1o_sb", [128, NFC * 512], BF16))
    g0_sb = e(nc.sbuf_tensor("g0_sb", [128, 2 * 512], F32))
    s0sl = e(nc.sbuf_tensor("s0sl", [128, 2 * 512], F32))
    x0tm = e(nc.sbuf_tensor("x0tm", [128, 2 * 512], F32))
    x0fm = e(nc.sbuf_tensor("x0fm", [128, 2 * 512], F32))
    out1_sb = e(nc.sbuf_tensor("out1_sb", [128, NDC * 512], F32))
    sqd = e(nc.sbuf_tensor("sqd", [128, 2 * 512], F32))
    lnrow = e(nc.sbuf_tensor("lnrow", [1, 6 * 512], F32))
    xn1c = e(nc.sbuf_tensor("xn1c", [128, 2 * 512], BF16))
    xn1tm_st = e(nc.sbuf_tensor("xn1tm_st", [128, 4 * 1024], BF16))
    out1tm_st = e(nc.sbuf_tensor("out1tm_st", [128, 4 * 1024], F32))

    tok_fw12 = {}
    tok_fw3 = {}
    tok_xn2 = {}
    tok_ab = {}
    tok_abev = {}
    tok_h = {}
    tok_fh = {}
    tok_h1mm = {}
    tok_h1ev = {}
    tok_h2mm = {}
    tok_h2ev = {}
    tok_w3 = {}
    tok_g0 = {}
    tok_s0l = {}
    tok_x0l = {}
    tok_x0tr = {}
    tok_x0ev = {}
    tok_cmb = {}
    tok_lnm = {}
    tok_sqf = {}
    tok_lns = {}
    tok_xn1 = {}
    tok_tr1 = {}
    tok_trev1 = {}
    tok_st1d = {}
    env.tok_lbc_hist = {}
    env.tok_ffn_last = None

    for ft in range(NT5):
        fb = ft % 2
        xnv = xnsl2[:, fb * NDC * 512:(fb + 1) * NDC * 512]

        def ysec(Y, ft=ft, xnv=xnv):
            Y.wait(tok_xn0st[NT5 - 1])
            Y.wait(tok_ab.get((ft - 2, NFC - 1)))
            tok_xn2[ft] = Y.emit(Y.dma_start(
                out=xnv.rearrange("p (dc q) -> p dc q", dc=NDC),
                in_=d.xn0_d.rearrange("(dc p) q -> p dc q", p=128)[:, :, ft * 512:(ft + 1) * 512]), dma=True)
        sec("sync", ysec)

        for oc in range(NFC):
            pb = PB[oc % 2]

            def gsecA(G, ft=ft, oc=oc):
                G.wait(tok_ab.get((ft, oc - 2)) if oc >= 2 else tok_ab.get((ft - 1, NFC - 2 + oc)))
                G.wait(env.tok_pp[("fw1", oc)], env.tok_pp[("fw2", oc)])
                sl = fw12st[:, (oc % 2) * 2 * NDC * 128:((oc % 2) + 1) * 2 * NDC * 128]
                G.emit(G.dma_start(out=sl[:, 0:NDC * 128], in_=d.fw1b[oc]), dma=True)
                tok_fw12[(ft, oc)] = G.emit(G.dma_start(
                    out=sl[:, NDC * 128:2 * NDC * 128], in_=d.fw2b[oc]), dma=True)
            sec("gpsimd", gsecA)

            def tsec(T, ft=ft, oc=oc, pb=pb, xnv=xnv):
                T.wait(tok_xn2[ft], tok_fw12[(ft, oc)])
                if ft == 0 and oc == 0:
                    T.wait(tok_div[(NG - 1, NH - 1)], tok_projev[(NG - 1, NDC - 1)])
                if oc == 0 and ft >= 1:
                    T.wait(tok_xn1[(ft - 1, NDC - 1)])
                T.wait(tok_abev.get((ft, oc - 2)) if oc >= 2 else tok_abev.get((ft - 1, NFC - 2 + oc)))
                sl = fw12st[:, (oc % 2) * 2 * NDC * 128:((oc % 2) + 1) * 2 * NDC * 128]
                for dd in range(NDC):
                    T.matmul(pb[:, 0:512], sl[:, dd * 128:(dd + 1) * 128],
                             xnv[:, dd * 512:(dd + 1) * 512],
                             start=(dd == 0), stop=(dd == NDC - 1))
                for dd in range(NDC):
                    ins = T.matmul(pb[:, 512:1024], sl[:, NDC * 128 + dd * 128:NDC * 128 + (dd + 1) * 128],
                                   xnv[:, dd * 512:(dd + 1) * 512],
                                   start=(dd == 0), stop=(dd == NDC - 1))
                tok_ab[(ft, oc)] = T.emit(ins)
            sec("tensor", tsec)

            def ssec(SS, ft=ft, oc=oc, pb=pb):
                SS.wait(tok_ab[(ft, oc)])
                SS.wait(tok_h.get((ft, oc - 2)) if oc >= 2 else tok_h.get((ft - 1, NFC - 2 + oc)))
                SS.emit(SS.activation(out=ab_sb[:, (oc % 2) * 1024:(oc % 2) * 1024 + 512],
                                      in_=pb[:, 0:512], func=AF.Silu,
                                      scale=env.s1_sb[:, oc:oc + 1]))
                tok_abev[(ft, oc)] = SS.emit(SS.activation(
                    out=ab_sb[:, (oc % 2) * 1024 + 512:(oc % 2) * 1024 + 1024],
                    in_=pb[:, 512:1024], func=AF.Identity, scale=env.s2_sb[:, oc:oc + 1]))
            sec("scalar", ssec)

            def vsec(V, ft=ft, oc=oc):
                V.wait(tok_abev[(ft, oc)])
                tok_h[(ft, oc)] = V.emit(V.tensor_tensor(
                    h_sb[:, oc * 512:(oc + 1) * 512],
                    ab_sb[:, (oc % 2) * 1024:(oc % 2) * 1024 + 512],
                    ab_sb[:, (oc % 2) * 1024 + 512:(oc % 2) * 1024 + 1024], ALU.mult))
            sec("vector", vsec)

        for oc in range(NFC):
            def gsec(G, ft=ft, oc=oc):
                G.wait(tok_h1mm.get((ft, oc - 2)) if oc >= 2 else tok_h2mm.get((ft - 1, NFC - 2 + oc)))
                G.wait(env.tok_pp[("fh1", oc)])
                tok_fh[(ft, 1, oc)] = G.emit(G.dma_start(
                    out=fhst[:, (oc % 2) * NFC * 128:((oc % 2) + 1) * NFC * 128],
                    in_=d.fh1b[oc]), dma=True)
            sec("gpsimd", gsec)

            def tsec(T, ft=ft, oc=oc):
                T.wait(tok_fh[(ft, 1, oc)])
                T.wait(tok_h[(ft, NFC - 1)])
                T.wait(tok_h1ev.get((ft, oc - 2)))
                fv = fhst[:, (oc % 2) * NFC * 128:((oc % 2) + 1) * NFC * 128]
                for kc in range(NFC):
                    ins = T.matmul(PB[2][:, (oc % 2) * 512:((oc % 2) + 1) * 512],
                                   fv[:, kc * 128:(kc + 1) * 128],
                                   h_sb[:, kc * 512:(kc + 1) * 512],
                                   start=(kc == 0), stop=(kc == NFC - 1))
                tok_h1mm[(ft, oc)] = T.emit(ins)
            sec("tensor", tsec)

            def ssec(SS, ft=ft, oc=oc):
                SS.wait(tok_h1mm[(ft, oc)])
                tok_h1ev[(ft, oc)] = SS.emit(SS.activation(
                    out=h1o_sb[:, oc * 512:(oc + 1) * 512],
                    in_=PB[2][:, (oc % 2) * 512:((oc % 2) + 1) * 512],
                    func=AF.Gelu, scale=env.h1s_sb[:, oc:oc + 1]))
            sec("scalar", ssec)

        for oc in range(NFC):
            def gsec(G, ft=ft, oc=oc):
                G.wait(tok_h1mm.get((ft, NFC - 2 + oc)) if oc < 2 else tok_h2mm.get((ft, oc - 2)))
                G.wait(env.tok_pp[("fh2", oc)])
                tok_fh[(ft, 2, oc)] = G.emit(G.dma_start(
                    out=fhst[:, (oc % 2) * NFC * 128:((oc % 2) + 1) * NFC * 128],
                    in_=d.fh2b[oc]), dma=True)
            sec("gpsimd", gsec)

            def tsec(T, ft=ft, oc=oc):
                T.wait(tok_fh[(ft, 2, oc)])
                T.wait(tok_h1ev[(ft, NFC - 1)])
                T.wait(tok_h2ev.get((ft, oc - 2)))
                fv = fhst[:, (oc % 2) * NFC * 128:((oc % 2) + 1) * NFC * 128]
                for kc in range(NFC):
                    ins = T.matmul(PB[2][:, (oc % 2) * 512:((oc % 2) + 1) * 512],
                                   fv[:, kc * 128:(kc + 1) * 128],
                                   h1o_sb[:, kc * 512:(kc + 1) * 512],
                                   start=(kc == 0), stop=(kc == NFC - 1))
                tok_h2mm[(ft, oc)] = T.emit(ins)
            sec("tensor", tsec)

            def ssec(SS, ft=ft, oc=oc):
                SS.wait(tok_h2mm[(ft, oc)])
                tok_h2ev[(ft, oc)] = SS.emit(SS.activation(
                    out=h_sb[:, oc * 512:(oc + 1) * 512],
                    in_=PB[2][:, (oc % 2) * 512:((oc % 2) + 1) * 512],
                    func=AF.Gelu, scale=env.h2s_sb[:, oc:oc + 1]))
            sec("scalar", ssec)

        for oc in range(NDC):
            def gsecD(G, ft=ft, oc=oc):
                G.wait(tok_w3.get((ft, oc - 2)) if oc >= 2 else tok_w3.get((ft - 1, NDC - 2 + oc)))
                G.wait(env.tok_pp[("fw3", oc)])
                tok_fw3[(ft, oc)] = G.emit(G.dma_start(
                    out=fw3st[:, (oc % 2) * NFC * 128:((oc % 2) + 1) * NFC * 128],
                    in_=d.fw3b[oc]), dma=True)
            sec("gpsimd", gsecD)

            def tsec(T, ft=ft, oc=oc):
                T.wait(tok_h2ev[(ft, NFC - 1)], tok_fw3[(ft, oc)])
                T.wait(tok_g0.get((ft, oc - 2)) if oc >= 2 else tok_g0.get((ft - 1, NDC - 2 + oc)))
                fv = fw3st[:, (oc % 2) * NFC * 128:((oc % 2) + 1) * NFC * 128]
                for kc in range(NFC):
                    ins = T.matmul(PB[3][:, (oc % 2) * 512:((oc % 2) + 1) * 512],
                                   fv[:, kc * 128:(kc + 1) * 128],
                                   h_sb[:, kc * 512:(kc + 1) * 512],
                                   start=(kc == 0), stop=(kc == NFC - 1))
                tok_w3[(ft, oc)] = T.emit(ins)
            sec("tensor", tsec)

            def ssec(SS, ft=ft, oc=oc):
                SS.wait(tok_w3[(ft, oc)])
                SS.wait(tok_cmb.get((ft, oc - 2)))
                tok_g0[(ft, oc)] = SS.emit(SS.activation(
                    out=g0_sb[:, (oc % 2) * 512:((oc % 2) + 1) * 512],
                    in_=PB[3][:, (oc % 2) * 512:((oc % 2) + 1) * 512],
                    func=AF.Silu, scale=env.s3_sb[:, oc:oc + 1]))
            sec("scalar", ssec)

            def ysec2(Y, ft=ft, oc=oc):
                Y.wait(tok_cmb.get((ft, oc - 2)) if oc >= 2 else tok_cmb.get((ft - 1, NDC - 2 + oc)))
                Y.wait(tok_projst[(ft // 2, oc)])
                tok_s0l[(ft, oc)] = Y.emit(Y.dma_start(
                    out=s0sl[:, (oc % 2) * 512:((oc % 2) + 1) * 512],
                    in_=d.s0_d[oc * 128:(oc + 1) * 128, ft * 512:(ft + 1) * 512]), dma=True)
                tok_x0l[(ft, oc)] = Y.emit(Y.dma_start(
                    out=x0tm[:, (oc % 2) * 512:((oc % 2) + 1) * 512].rearrange("p (s q) -> p s q", s=4),
                    in_=d.x0.rearrange("(s p) dd -> p s dd", p=128)[:, 4 * ft:4 * ft + 4,
                                                                   oc * 128:(oc + 1) * 128]), dma=True)
            sec("sync", ysec2)

            for j in range(4):
                def tsec2(T, ft=ft, oc=oc, j=j):
                    T.wait(tok_x0l[(ft, oc)])
                    T.wait(tok_x0ev.get((ft, oc, j - 2)) if j >= 2 else
                           (tok_x0ev.get((ft, oc - 1, 2 + j)) if oc >= 1 else tok_x0ev.get((ft - 1, NDC - 1, 2 + j))))
                    tok_x0tr[(ft, oc, j)] = T.emit(T.transpose(
                        PB[0].bitcast(F32)[:, (j % 2) * 512:(j % 2) * 512 + 128],
                        x0tm[:, (oc % 2) * 512 + j * 128:(oc % 2) * 512 + (j + 1) * 128],
                        env.ident_f32[:]))
                sec("tensor", tsec2)

                def vsec2(V, ft=ft, oc=oc, j=j):
                    V.wait(tok_x0tr[(ft, oc, j)])
                    tok_x0ev[(ft, oc, j)] = V.emit(V.tensor_copy(
                        x0fm[:, (oc % 2) * 512 + j * 128:(oc % 2) * 512 + (j + 1) * 128],
                        PB[0].bitcast(F32)[:, (j % 2) * 512:(j % 2) * 512 + 128]))
                sec("vector", vsec2)

            def vsec3(V, ft=ft, oc=oc):
                V.wait(tok_g0[(ft, oc)], tok_s0l[(ft, oc)], tok_x0ev[(ft, oc, 3)])
                sl = slice((oc % 2) * 512, ((oc % 2) + 1) * 512)
                V.emit(V.tensor_tensor(sqd[:, 0:512], s0sl[:, sl], g0_sb[:, sl], ALU.mult))
                tok_cmb[(ft, oc)] = V.emit(V.tensor_tensor(
                    out1_sb[:, oc * 512:(oc + 1) * 512], sqd[:, 0:512], x0fm[:, sl], ALU.add))
            sec("vector", vsec3)

            def tsec3(T, ft=ft, oc=oc):
                T.wait(tok_cmb[(ft, oc)])
                tok_lnm[(ft, oc)] = T.emit(T.matmul(
                    PB[1][0:1, 0:512], env.onescol[:], out1_sb[:, oc * 512:(oc + 1) * 512],
                    start=(oc == 0), stop=(oc == NDC - 1)))
            sec("tensor", tsec3)

            def ssec2(SS, ft=ft, oc=oc):
                SS.wait(tok_cmb[(ft, oc)])
                SS.wait(tok_lnm.get((ft, oc - 1, "ss")) if oc >= 1 else tok_lnm.get((ft - 1, NDC - 1, "ss")))
                tok_sqf[(ft, oc)] = SS.emit(SS.activation(
                    out=sqd[:, 512:1024], in_=out1_sb[:, oc * 512:(oc + 1) * 512], func=AF.Square))
            sec("scalar", ssec2)

            def tsec4(T, ft=ft, oc=oc):
                T.wait(tok_sqf[(ft, oc)])
                tok_lnm[(ft, oc, "ss")] = T.emit(T.matmul(
                    PB[1][0:1, 512:1024], env.onescol[:], sqd[:, 512:1024],
                    start=(oc == 0), stop=(oc == NDC - 1)))
            sec("tensor", tsec4)

        def ssec3(SS, ft=ft):
            SS.wait(tok_lnm[(ft, NDC - 1)], tok_lnm[(ft, NDC - 1, "ss")])
            SS.wait(env.tok_lbc_hist.get(ft - 1))
            SS.emit(SS.activation(out=lnrow[0:1, 0:512], in_=PB[1][0:1, 0:512],
                                  func=AF.Identity, scale=1.0 / 1024.0))
            SS.drain()
            SS.emit(SS.activation(out=lnrow[0:1, 512:1024], in_=lnrow[0:1, 0:512], func=AF.Square))
            env.tok_s = SS.emit(SS.activation(out=lnrow[0:1, 1024:1536], in_=PB[1][0:1, 512:1024],
                                              func=AF.Identity, scale=1.0 / 1024.0))
            SS.drain()
        sec("scalar", ssec3)
        tok_l1 = env.tok_s

        def vsec4(V, ft=ft, tok_l1=tok_l1):
            V.wait(tok_l1)
            env.tok_v = V.emit(V.tensor_sub(lnrow[0:1, 1024:1536], lnrow[0:1, 1024:1536],
                                            lnrow[0:1, 512:1024]))
            V.drain()
        sec("vector", vsec4)
        tok_l2 = env.tok_v

        def ssec4(SS, ft=ft, tok_l2=tok_l2):
            SS.wait(tok_l2)
            SS.emit(SS.activation(out=lnrow[0:1, 1536:2048], in_=lnrow[0:1, 1024:1536],
                                  func=AF.Ln, bias=LN_EPS))
            SS.drain()
            env.tok_s = SS.emit(SS.activation(out=lnrow[0:1, 1536:2048], in_=lnrow[0:1, 1536:2048],
                                              func=AF.Exp, scale=-0.5))
            SS.drain()
        sec("scalar", ssec4)
        tok_l3 = env.tok_s

        def vsec5(V, ft=ft, tok_l3=tok_l3):
            V.wait(tok_l3)
            V.emit(V.tensor_mul(lnrow[0:1, 2048:2560], lnrow[0:1, 0:512], lnrow[0:1, 1536:2048]))
            V.drain()
            env.tok_v = V.emit(V.tensor_scalar_mul(lnrow[0:1, 2048:2560], lnrow[0:1, 2048:2560], -1.0))
            V.drain()
        sec("vector", vsec5)
        tok_l4 = env.tok_v

        def tsec5(T, ft=ft, tok_l4=tok_l4):
            T.wait(tok_l4)
            T.emit(T.matmul(PB[0][:, 0:512], env.ones1[:], lnrow[0:1, 1536:2048], start=True, stop=True))
            env.tok_t = T.emit(T.matmul(PB[0][:, 512:1024], env.ones1[:], lnrow[0:1, 2048:2560],
                                        start=True, stop=True))
        sec("tensor", tsec5)
        tok_lbc = env.tok_t
        env.tok_lbc_hist[ft] = tok_lbc

        for oc in range(NDC):
            def vsec6(V, ft=ft, oc=oc, tok_lbc=tok_lbc):
                V.wait(tok_lbc)
                if oc >= 2:
                    V.wait(tok_st1d.get((ft, oc - 2, "fm")), tok_tr1.get((ft, oc - 2, "x", 3)))
                else:
                    V.wait(tok_st1d.get((ft - 1, NDC - 2 + oc, "fm")), tok_tr1.get((ft - 1, NDC - 2 + oc, "x", 3)))
                sl = slice((oc % 2) * 512, ((oc % 2) + 1) * 512)
                V.emit(V.tensor_tensor(sqd[:, 0:512], out1_sb[:, oc * 512:(oc + 1) * 512],
                                       PB[0][:, 0:512], ALU.mult))
                tok_xn1[(ft, oc)] = V.emit(V.tensor_tensor(
                    xn1c[:, sl], sqd[:, 0:512], PB[0][:, 512:1024], ALU.add))
            sec("vector", vsec6)

            def ysec3(Y, ft=ft, oc=oc):
                Y.wait(tok_xn1[(ft, oc)])
                tok_st1d[(ft, oc, "fm")] = Y.emit(Y.dma_start(
                    out=d.xn1fm_d[oc * 128:(oc + 1) * 128, ft * 512:(ft + 1) * 512],
                    in_=xn1c[:, (oc % 2) * 512:((oc % 2) + 1) * 512]), dma=True)
            sec("sync", ysec3)

            for j in range(4):
                def tsec6(T, ft=ft, oc=oc, j=j):
                    T.wait(tok_xn1[(ft, oc)])
                    T.wait(tok_trev1.get((ft, oc, "x", j - 1)) if j >= 1 else
                           tok_trev1.get((ft, oc - 1, "x", 3), tok_trev1.get((ft - 1, NDC - 1, "x", 3)) if oc == 0 else None))
                    tok_tr1[(ft, oc, "x", j)] = T.emit(T.transpose(
                        PB[2].bitcast(BF16)[:, 0:128],
                        xn1c[:, (oc % 2) * 512 + j * 128:(oc % 2) * 512 + (j + 1) * 128],
                        env.ident_bf[:]))
                sec("tensor", tsec6)

                def vsec7(V, ft=ft, oc=oc, j=j):
                    V.wait(tok_tr1[(ft, oc, "x", j)])
                    V.wait(tok_st1d.get((ft - 1, "xn1tm")))
                    tok_trev1[(ft, oc, "x", j)] = V.emit(V.tensor_copy(
                        xn1tm_st[:, j * 1024 + oc * 128: j * 1024 + (oc + 1) * 128],
                        PB[2].bitcast(BF16)[:, 0:128]))
                sec("vector", vsec7)

                def tsec7(T, ft=ft, oc=oc, j=j):
                    T.wait(tok_trev1.get((ft, oc, "o", j - 1)) if j >= 1 else
                           tok_trev1.get((ft, oc - 1, "o", 3), tok_trev1.get((ft - 1, NDC - 1, "o", 3)) if oc == 0 else None))
                    tok_tr1[(ft, oc, "o", j)] = T.emit(T.transpose(
                        PB[2].bitcast(F32)[:, 512:640],
                        out1_sb[:, oc * 512 + j * 128: oc * 512 + (j + 1) * 128],
                        env.ident_f32[:]))
                sec("tensor", tsec7)

                def vsec8(V, ft=ft, oc=oc, j=j):
                    V.wait(tok_tr1[(ft, oc, "o", j)])
                    V.wait(tok_st1d.get((ft - 1, "tmstores")))
                    tok_trev1[(ft, oc, "o", j)] = V.emit(V.tensor_copy(
                        out1tm_st[:, j * 1024 + oc * 128: j * 1024 + (oc + 1) * 128],
                        PB[2].bitcast(F32)[:, 512:640]))
                sec("vector", vsec8)

        def ysec4(Y, ft=ft):
            Y.wait(tok_trev1[(ft, NDC - 1, "x", 3)], tok_trev1[(ft, NDC - 1, "o", 3)])
            tok_st1d[(ft, "xn1tm")] = Y.emit(Y.dma_start(
                out=d.xn1tm_d.rearrange("(s p) dd -> p s dd", p=128)[:, 4 * ft:4 * ft + 4, :],
                in_=xn1tm_st[:].rearrange("p (s dd) -> p s dd", s=4)), dma=True)
            tok_st1d[(ft, "tmstores")] = Y.emit(Y.dma_start(
                out=d.out1_d.rearrange("(s p) dd -> p s dd", p=128)[:, 4 * ft:4 * ft + 4, :],
                in_=out1tm_st[:].rearrange("p (s dd) -> p s dd", s=4)), dma=True)
        sec("sync", ysec4)
        env.tok_ffn_last = tok_st1d[(ft, "tmstores")]
    env.tok_phase1_done = env.tok_ffn_last
    dstack.close()
    pstack.close()




def phase2(nc, env, d, block, cfg):
    ctx = env.ctx
    e = ctx.enter_context
    seq = cfg.seq
    NT5 = seq // 512
    sec = lambda name, body: _mk_section(env, block, name, body)

    pstack2 = ExitStack()
    PB = [pstack2.enter_context(nc.psum_tensor(f"p2PB{i}", [128, 1024], F32)) for i in range(4)]

    n2w = e(nc.sbuf_tensor("n2w", [128, NDC], F32))

    sstack = ExitStack()
    es = sstack.enter_context
    xsl = es(nc.sbuf_tensor("xsl", [128, 2 * NDC * 512], BF16))
    xnn = es(nc.sbuf_tensor("xnn", [128, NDC * 512], BF16))
    wdt = es(nc.sbuf_tensor("wdt", [128, NDC * D], BF16))
    wdt3 = wdt[:].rearrange("p (dd f) -> p dd f", dd=NDC)
    wbc_sb = es(nc.sbuf_tensor("wbc_sb", [128, NDC * 64], BF16))
    wbc3 = wbc_sb[:].rearrange("p (dd f) -> p dd f", dd=NDC)
    bbc_sb = es(nc.sbuf_tensor("bbc_sb", [64, 1], F32))
    n1w = es(nc.sbuf_tensor("n1w", [128, NDC], F32))
    env.bdt_sb = es(nc.sbuf_tensor("bdt_sb", [128, NDC], F32))
    dpar = es(nc.sbuf_tensor("dpar", [128, NDC], F32))
    mscl = es(nc.sbuf_tensor("mscl", [128, NDC], F32))
    sel16 = es(nc.sbuf_tensor("sel16s", [NST, NST * 128], F32))
    sel163 = sel16[:].rearrange("p (n m) -> p n m", n=NST)
    alog = es(nc.sbuf_tensor("alog", [NST, 2], F32))
    alogr = es(nc.sbuf_tensor("alogr", [1, NST], F32))
    abc = es(nc.sbuf_tensor("abc", [128, NST], F32))
    dtc = es(nc.sbuf_tensor("dtc", [128, NDC * 512], F32))
    wc_ = es(nc.sbuf_tensor("wc_", [128, NDC * 512], BF16))
    rdt = es(nc.sbuf_tensor("rdt", [128, 512], F32))
    sp1 = es(nc.sbuf_tensor("sp1", [128, 3 * 512], F32))
    bvc = es(nc.sbuf_tensor("bvc", [NST, 2 * 512], F32))
    dA_s = es(nc.sbuf_tensor("dA_s", [128, 2 * 512], F32))
    b3_s = es(nc.sbuf_tensor("b3_s", [128, 2 * 512], F32))
    h_s = es(nc.sbuf_tensor("h_s", [128, 4 * 512], F32))
    cv_sb = es(nc.sbuf_tensor("cv_sb", [128, 2 * 512], F32))
    ybp = es(nc.sbuf_tensor("ybp", [128, 512], F32))
    yeps = es(nc.sbuf_tensor("yeps", [128, 512], F32))
    yac = es(nc.sbuf_tensor("yac", [128, NDC * 512], F32))
    stt_s = es(nc.sbuf_tensor("stt_s", [128, NST * NDC], F32))
    lrow = es(nc.sbuf_tensor("lrow", [1, 2 * 512], F32))
    x1c = es(nc.sbuf_tensor("x1c", [128, 2 * 512], BF16))

    tok_ld2 = {}
    tok_rs1 = {}
    tok_sq2 = {}
    tok_dt = {}
    tok_dtev = {}
    tok_w = {}
    tok_bvmm = {}
    tok_bv = {}
    tok_bc2 = {}
    tok_scan = {}
    tok_y = {}
    tok_cv = {}
    tok_yep = {}
    tok_x1st = {}
    tok_exp2 = {}
    tok_bvevh = {}
    env.tok_rs1row = {}

    def gsec0(G):
        G.emit(G.dma_start(out=wdt3, in_=d.Wdt.rearrange("(dd p) f -> p dd f", p=128)), dma=True)
        env.tok_g = G.emit(G.dma_start(out=wbc3, in_=d.WBC.rearrange("(dd p) f -> p dd f", p=128)), dma=True)
    sec("gpsimd", gsec0)
    tok_wdt = env.tok_g
    if 3 in cfg.phases:
        prepass_weights(nc, env, d, block, "p3")

    def ysec0(Y):
        for src_, dst in [(d.norm1_w, n1w), (d.norm2_w, n2w), (d.D_param, dpar),
                          (d.mscale, mscl), (d.bBC, bbc_sb), (d.bdt, env.bdt_sb)]:
            Y.emit(Y.dma_start(out=dst[:], in_=src_[:]), dma=True)
        Y.emit(Y.dma_start(out=sel163, in_=d.sel16[:]), dma=True)
        Y.emit(Y.dma_start(out=alog[:, 0:1], in_=d.A_log.rearrange("(a b) -> a b", b=1)), dma=True)
        env.tok_y = Y.emit(Y.dma_start(out=alogr[:], in_=d.A_log.rearrange("(b a) -> b a", b=1)), dma=True)
    sec("sync", ysec0)
    tok_cst2 = env.tok_y

    def ssec0(SS, tok_cst2=tok_cst2):
        SS.wait(tok_cst2)
        SS.emit(SS.activation(out=alog[:, 0:1], in_=alog[:, 0:1], func=AF.Exp))
        env.tok_s = SS.emit(SS.activation(out=alogr[:], in_=alogr[:], func=AF.Exp))
        SS.drain()
    sec("scalar", ssec0)
    tok_aexp = env.tok_s

    def vsec0(V, tok_aexp=tok_aexp):
        V.wait(tok_aexp)
        V.emit(V.tensor_scalar_mul(alog[:, 0:1], alog[:, 0:1], -1.0))
        V.drain()
        V.emit(V.reciprocal(alog[:, 1:2], alog[:, 0:1]))
        env.tok_v = V.emit(V.tensor_scalar_mul(alogr[:], alogr[:], -1.0))
        V.drain()
    sec("vector", vsec0)
    tok_arow = env.tok_v

    def tsec0(T, tok_arow=tok_arow):
        T.wait(tok_arow, env.tok_consts_loaded)
        env.tok_t = T.emit(T.matmul(PB[3][:, 0:NST], env.ones1[:], alogr[:], start=True, stop=True))
    sec("tensor", tsec0)
    tok_abcmm = env.tok_t

    def vsec0b(V, tok_abcmm=tok_abcmm):
        V.wait(tok_abcmm)
        env.tok_v = V.emit(V.tensor_copy(abc[:], PB[3][:, 0:NST]))
        V.drain()
    sec("vector", vsec0b)
    tok_abc = env.tok_v

    for ct in range(NT5):
        sb = ct % 2
        xv = xsl[:, sb * NDC * 512:(sb + 1) * NDC * 512]

        def yld(Y, ct=ct, xv=xv):
            Y.wait(env.tok_phase1_done)
            Y.wait(tok_yep.get((ct - 2, NDC - 1)))
            tok_ld2[ct] = Y.emit(Y.dma_start(
                out=xv.rearrange("p (dc q) -> p dc q", dc=NDC),
                in_=d.xn1fm_d.rearrange("(dc p) q -> p dc q", p=128)[:, :, ct * 512:(ct + 1) * 512]), dma=True)
        sec("sync", yld)

        for dc in range(NDC):
            def ssq(SS, ct=ct, dc=dc, xv=xv):
                SS.wait(tok_ld2[ct])
                SS.wait(tok_rs1.get((ct, dc - 2)) if dc >= 2 else tok_rs1.get((ct - 1, NDC - 2 + dc)))
                tok_sq2[(ct, dc)] = SS.emit(SS.activation(
                    out=sp1[:, (dc % 2) * 512:(dc % 2 + 1) * 512],
                    in_=xv[:, dc * 512:(dc + 1) * 512], func=AF.Square))
            sec("scalar", ssq)

            def tsq(T, ct=ct, dc=dc):
                T.wait(tok_sq2[(ct, dc)])
                if dc == 0:
                    T.wait(env.tok_rs1row.get(ct - 1))
                    T.wait(tok_scan.get((ct * NST - 1) * NDC + NDC - 1))
                    if ct >= 1:
                        T.wait(tok_exp2[(ct - 1, NST - 1, 0)])
                tok_rs1[(ct, dc)] = T.emit(T.matmul(
                    PB[3][0:1, 0:512], env.onescol[:], sp1[:, (dc % 2) * 512:(dc % 2 + 1) * 512],
                    start=(dc == 0), stop=(dc == NDC - 1)))
            sec("tensor", tsq)

        def srow(SS, ct=ct):
            SS.wait(tok_rs1[(ct, NDC - 1)])
            SS.emit(SS.activation(out=lrow[0:1, 0:512], in_=PB[3][0:1, 0:512],
                                  func=AF.Ln, scale=1.0 / 1024.0, bias=RMS_EPS))
            SS.drain()
            env.tok_s = SS.emit(SS.activation(out=lrow[0:1, 0:512], in_=lrow[0:1, 0:512],
                                              func=AF.Exp, scale=-0.5))
            SS.drain()
        sec("scalar", srow)
        env.tok_rs1row[ct] = env.tok_s
        tok_rrow = env.tok_s

        def tbc(T, ct=ct, tok_rrow=tok_rrow):
            T.wait(tok_rrow)
            T.wait(tok_w.get((ct - 1, NDC - 1, "nn")))
            env.tok_t = T.emit(T.matmul(PB[3][:, 512:1024], env.ones1[:], lrow[0:1, 0:512],
                                        start=True, stop=True))
        sec("tensor", tbc)
        tok_rbc = env.tok_t

        for dc in range(NDC):
            def vnn(V, ct=ct, dc=dc, xv=xv, tok_rbc=tok_rbc):
                V.wait(tok_rbc)
                V.wait(tok_dt.get((ct - 1, NDC - 1)))
                V.wait(tok_bvmm.get(ct - 1))
                tok_w[(ct, dc, "nn")] = V.emit(V.scalar_tensor_tensor(
                    xnn[:, dc * 512:(dc + 1) * 512], xv[:, dc * 512:(dc + 1) * 512],
                    n1w[:, dc:dc + 1], PB[3][:, 512:1024], ALU.mult, ALU.mult))
            sec("vector", vnn)

        for oc in range(NDC):
            def tdt(T, ct=ct, oc=oc):
                T.wait(tok_w[(ct, NDC - 1, "nn")], tok_wdt)
                T.wait(tok_dtev.get((ct, oc - 2)) if oc >= 2 else tok_dtev.get((ct - 1, NDC - 2 + oc)))
                pb = PB[oc % 2]
                for dd in range(NDC):
                    ins = T.matmul(pb[:, 0:512], wdt3[:, dd, oc * 128:(oc + 1) * 128],
                                   xnn[:, dd * 512:(dd + 1) * 512],
                                   start=(dd == 0), stop=(dd == NDC - 1))
                tok_dt[(ct, oc)] = T.emit(ins)
            sec("tensor", tdt)

            def sdt(SS, ct=ct, oc=oc):
                SS.wait(tok_dt[(ct, oc)])
                SS.wait(tok_dtev.get((ct, oc - 1)) if oc >= 1 else tok_dtev.get((ct - 1, NDC - 1)))
                pb = PB[oc % 2]
                SS.emit(SS.activation(out=sp1[:, 1024:1536], in_=pb[:, 0:512], func=AF.Abs,
                                      bias=d_bdt_col(env, oc)))
                SS.drain()
                SS.emit(SS.activation(out=sp1[:, 1024:1536], in_=sp1[:, 1024:1536],
                                      func=AF.Exp, scale=-1.0))
                SS.drain()
                SS.emit(SS.activation(out=sp1[:, 1024:1536], in_=sp1[:, 1024:1536],
                                      func=AF.Ln, bias=1.0))
                SS.drain()
                tok_dtev[(ct, oc, "a")] = SS.emit(SS.activation(
                    out=dtc[:, oc * 512:(oc + 1) * 512],
                    in_=pb[:, 0:512], func=AF.Relu, bias=d_bdt_col(env, oc)))
            sec("scalar", sdt)

            def vdt(V, ct=ct, oc=oc):
                V.wait(tok_dtev[(ct, oc, "a")])
                tok_dtev[(ct, oc)] = V.emit(V.tensor_add(
                    dtc[:, oc * 512:(oc + 1) * 512], dtc[:, oc * 512:(oc + 1) * 512],
                    sp1[:, 1024:1536]))
                V.drain()
                V.emit(V.reciprocal(rdt[:], dtc[:, oc * 512:(oc + 1) * 512]))
                V.drain()
                tok_w[(ct, oc)] = V.emit(V.tensor_tensor(
                    wc_[:, oc * 512:(oc + 1) * 512], xnn[:, oc * 512:(oc + 1) * 512],
                    rdt[:], ALU.mult))
            sec("vector", vdt)

        def tbv(T, ct=ct):
            T.wait(tok_w[(ct, NDC - 1, "nn")], tok_wdt)
            T.wait(tok_bvevh.get(ct - 1))
            for dd in range(NDC):
                ins = T.matmul(PB[2][0:64, 0:512], wbc3[:, dd, :],
                               xnn[:, dd * 512:(dd + 1) * 512],
                               start=(dd == 0), stop=(dd == NDC - 1))
            tok_bvmm[ct] = T.emit(ins)
        sec("tensor", tbv)

        def sbv(SS, ct=ct):
            SS.wait(tok_bvmm[ct])
            SS.emit(SS.activation(out=bvc[:, 0:512], in_=PB[2][0:16, 0:512],
                                  func=AF.Identity, bias=bbc_sb[0:16, 0:1]))
            env.tok_s = SS.emit(SS.activation(out=bvc[:, 512:1024], in_=PB[2][32:48, 0:512],
                                              func=AF.Identity, bias=bbc_sb[32:48, 0:1]))
            SS.drain()
        sec("scalar", sbv)
        tok_bvev = env.tok_s
        tok_bvevh[ct] = tok_bvev

        def vbv(V, ct=ct, tok_bvev=tok_bvev):
            V.wait(tok_bvev, tok_abc)
            tok_bv[ct] = V.emit(V.tensor_scalar(bvc[:, 0:512], bvc[:, 0:512],
                                                alog[:, 1:2], None, ALU.mult))
            V.drain()
        sec("vector", vbv)

        for n in range(NST):
            def tbc2(T, ct=ct, n=n):
                T.wait(tok_bv[ct])
                T.wait(tok_scan.get((ct * NST + n - 1) * NDC + NDC - 1))
                if n >= 1:
                    T.wait(tok_exp2[(ct, n - 1, 0)])
                elif ct >= 1:
                    T.wait(tok_exp2[(ct - 1, NST - 1, 0)])
                T.matmul(PB[2][:, 512:1024], sel163[:, n, :], bvc[:, 0:512], start=True, stop=True)
                env.tok_t = T.emit(T.matmul(PB[3][:, 0:512], sel163[:, n, :], bvc[:, 512:1024],
                                            start=True, stop=True))
            sec("tensor", tbc2)
            tok_bc2[(ct, n)] = env.tok_t

            def sevac(SS, ct=ct, n=n):
                SS.wait(tok_bc2[(ct, n)])
                prev = tok_y.get((ct * NST + n - 1) * NDC + 0)
                SS.wait(prev)
                tok_cv[(ct, n)] = SS.emit(SS.activation(
                    out=cv_sb[:, (n % 2) * 512:(n % 2 + 1) * 512], in_=PB[3][:, 0:512],
                    func=AF.Identity))
            sec("scalar", sevac)

            for dc in range(NDC):
                sl = ((n * NDC + dc) % 2)
                G = (ct * NST + n) * NDC + dc

                def sexp(SS, ct=ct, n=n, dc=dc, sl=sl, G=G):
                    SS.wait(tok_dtev[(ct, NDC - 1)])
                    SS.wait(tok_scan.get(G - 2))
                    tok_exp2[(ct, n, dc)] = SS.emit(SS.activation(
                        out=dA_s[:, sl * 512:(sl + 1) * 512], in_=dtc[:, dc * 512:(dc + 1) * 512],
                        func=AF.Exp, scale=abc[:, n:n + 1]))
                sec("scalar", sexp)

                def vscan(V, ct=ct, n=n, dc=dc, sl=sl, G=G):
                    V.wait(tok_exp2[(ct, n, dc)], tok_bc2[(ct, n)])
                    V.wait(tok_y.get(G - 3))
                    dav = dA_s[:, sl * 512:(sl + 1) * 512]
                    b3v = b3_s[:, sl * 512:(sl + 1) * 512]
                    hv = h_s[:, (G % 4) * 512:(G % 4 + 1) * 512]
                    V.emit(V.scalar_tensor_tensor(b3v, dav, -1.0, wc_[:, dc * 512:(dc + 1) * 512],
                                                  ALU.add, ALU.mult))
                    V.emit(V.tensor_tensor(b3v, b3v, PB[2][:, 512:1024], ALU.mult))
                    stcol = stt_s[:, (n * NDC + dc):(n * NDC + dc) + 1]
                    init = stcol if ct > 0 else 0.0
                    V.emit(V.tensor_tensor_scan(hv, dav, b3v, init, ALU.mult, ALU.add))
                    tok_scan[G] = V.emit(V.tensor_copy(stcol, hv[:, 511:512]))
                sec("vector", vscan)

                def gacc(G_, ct=ct, n=n, dc=dc, G=G):
                    G_.wait(tok_scan[G], tok_cv[(ct, n)])
                    hv = h_s[:, (G % 4) * 512:(G % 4 + 1) * 512]
                    cvv = cv_sb[:, (n % 2) * 512:(n % 2 + 1) * 512]
                    acc = yac[:, dc * 512:(dc + 1) * 512]
                    if n == 0:
                        if ct >= 1:
                            G_.wait(tok_yep[(ct - 1, dc)])
                        tok_y[G] = G_.emit(G_.tensor_tensor(acc, hv, cvv, ALU.mult))
                    else:
                        G_.emit(G_.tensor_tensor(ybp[:], hv, cvv, ALU.mult))
                        tok_y[G] = G_.emit(G_.tensor_add(acc, acc, ybp[:]))
                sec("gpsimd", gacc)

        if cfg.dbg and ct == 0:
            def ydbg2a(Y, ct=ct):
                Y.wait(tok_y[(ct * NST + NST - 1) * NDC + NDC - 1])
                Y.wait(tok_dtev[(ct, NDC - 1)], tok_bv[ct], tok_w[(ct, NDC - 1, "nn")])
                Y.emit(Y.dma_start(out=d.dt_d[:, :], in_=dtc[:, 0:NDC * 512]), dma=True)
                Y.emit(Y.dma_start(out=d.bvc_d[:, :], in_=bvc[:, 0:1024]), dma=True)
                Y.emit(Y.dma_start(out=d.yac_d[:, :], in_=yac[:, 0:NDC * 512]), dma=True)
                Y.emit(Y.dma_start(out=d.xnn_d[:, :], in_=xnn[:, 0:NDC * 512]), dma=True)
                env.tok_dbg2 = Y.emit(Y.dma_start(out=d.wc_d[:, :], in_=wc_[:, 0:NDC * 512]), dma=True)
            sec("sync", ydbg2a)

            def vdbg2(V):
                V.wait(env.tok_dbg2)
            sec("vector", vdbg2)

        for dc in range(NDC):
            def vyep(V, ct=ct, dc=dc, xv=xv):
                V.wait(tok_y[(ct * NST + NST - 1) * NDC + dc])
                V.wait(tok_x1st.get((ct, dc - 2)) if dc >= 2 else tok_x1st.get((ct - 1, NDC - 2 + dc)))
                V.emit(V.scalar_tensor_tensor(
                    yeps[:], xnn[:, dc * 512:(dc + 1) * 512], dpar[:, dc:dc + 1],
                    yac[:, dc * 512:(dc + 1) * 512], ALU.mult, ALU.add))
                tok_yep[(ct, dc)] = V.emit(V.scalar_tensor_tensor(
                    x1c[:, (dc % 2) * 512:(dc % 2 + 1) * 512], yeps[:],
                    mscl[:, dc:dc + 1], xv[:, dc * 512:(dc + 1) * 512], ALU.mult, ALU.add))
            sec("vector", vyep)

            def yx1(Y, ct=ct, dc=dc):
                Y.wait(tok_yep[(ct, dc)])
                tok_x1st[(ct, dc)] = Y.emit(Y.dma_start(
                    out=d.x1_d[dc * 128:(dc + 1) * 128, ct * 512:(ct + 1) * 512],
                    in_=x1c[:, (dc % 2) * 512:(dc % 2 + 1) * 512]), dma=True)
            sec("sync", yx1)

    barrier(env, block)
    sstack.close()

    fstack = ExitStack()
    ef = fstack.enter_context
    x1sl = ef(nc.sbuf_tensor("x1sl", [128, 2 * NDC * 512], BF16))
    xn2c = ef(nc.sbuf_tensor("xn2c", [128, NDC * 512], BF16))
    sqf2 = ef(nc.sbuf_tensor("sqf2", [128, 2 * 512], F32))
    hf2 = ef(nc.sbuf_tensor("hf2", [128, NFC * 512], BF16))
    w1st = ef(nc.sbuf_tensor("w1st", [128, 2 * NDC * 128], BF16))
    w2st = ef(nc.sbuf_tensor("w2st", [128, 2 * NFC * 128], BF16))
    b1sb = ef(nc.sbuf_tensor("b1sb", [128, NFC], F32))
    b2sb = ef(nc.sbuf_tensor("b2sb", [128, NDC], F32))
    sch = ef(nc.sbuf_tensor("sch", [128, 2 * 512], F32))
    stm = ef(nc.sbuf_tensor("stm", [128, 4 * 1024], F32))
    lrow2 = ef(nc.sbuf_tensor("lrow2", [1, 512], F32))

    def yldb(Y):
        Y.emit(Y.dma_start(out=b1sb[:], in_=d.b1[:]), dma=True)
        env.tok_y = Y.emit(Y.dma_start(out=b2sb[:], in_=d.b2[:]), dma=True)
    sec("sync", yldb)
    tok_b12 = env.tok_y

    tok_lx1 = {}
    tok_r2 = {}
    tok_sq3 = {}
    tok_n2 = {}
    tok_w1l = {}
    tok_w1m = {}
    tok_hf = {}
    tok_w2l = {}
    tok_w2m = {}
    tok_sev = {}
    tok_str = {}
    tok_strv = {}
    tok_stst = {}
    env.tok_r2row = {}

    for ft in range(NT5):
        fb = ft % 2
        xv = x1sl[:, fb * NDC * 512:(fb + 1) * NDC * 512]

        def yld2(Y, ft=ft, xv=xv):
            Y.wait(tok_x1st.get((ft, NDC - 1)))
            Y.wait(tok_w1m.get((ft - 2, NFC - 1)))
            tok_lx1[ft] = Y.emit(Y.dma_start(
                out=xv.rearrange("p (dc q) -> p dc q", dc=NDC),
                in_=d.x1_d.rearrange("(dc p) q -> p dc q", p=128)[:, :, ft * 512:(ft + 1) * 512]), dma=True)
        sec("sync", yld2)

        for dc in range(NDC):
            def ssq2(SS, ft=ft, dc=dc, xv=xv):
                SS.wait(tok_lx1[ft])
                SS.wait(tok_r2.get((ft, dc - 2)) if dc >= 2 else tok_r2.get((ft - 1, NDC - 2 + dc)))
                tok_sq3[(ft, dc)] = SS.emit(SS.activation(
                    out=sqf2[:, (dc % 2) * 512:(dc % 2 + 1) * 512],
                    in_=xv[:, dc * 512:(dc + 1) * 512], func=AF.Square))
            sec("scalar", ssq2)

            def tsq2(T, ft=ft, dc=dc):
                T.wait(tok_sq3[(ft, dc)])
                if dc == 0:
                    T.wait(env.tok_r2row.get(ft - 1))
                    T.wait(tok_strv.get((ft - 1, NDC - 1, 3)))
                tok_r2[(ft, dc)] = T.emit(T.matmul(
                    PB[3][0:1, 0:512], env.onescol[:], sqf2[:, (dc % 2) * 512:(dc % 2 + 1) * 512],
                    start=(dc == 0), stop=(dc == NDC - 1)))
            sec("tensor", tsq2)

        def srow2(SS, ft=ft):
            SS.wait(tok_r2[(ft, NDC - 1)])
            SS.emit(SS.activation(out=lrow2[0:1, 0:512], in_=PB[3][0:1, 0:512],
                                  func=AF.Ln, scale=1.0 / 1024.0, bias=RMS_EPS))
            SS.drain()
            env.tok_s = SS.emit(SS.activation(out=lrow2[0:1, 0:512], in_=lrow2[0:1, 0:512],
                                              func=AF.Exp, scale=-0.5))
            SS.drain()
        sec("scalar", srow2)
        env.tok_r2row[ft] = env.tok_s
        tok_rr2 = env.tok_s

        def tbc3(T, ft=ft, tok_rr2=tok_rr2):
            T.wait(tok_rr2)
            T.wait(tok_n2.get((ft - 1, NDC - 1)))
            env.tok_t = T.emit(T.matmul(PB[3][:, 512:1024], env.ones1[:], lrow2[0:1, 0:512],
                                        start=True, stop=True))
        sec("tensor", tbc3)
        tok_rbc2 = env.tok_t

        for dc in range(NDC):
            def vn2(V, ft=ft, dc=dc, xv=xv, tok_rbc2=tok_rbc2):
                V.wait(tok_rbc2)
                V.wait(tok_w1m.get((ft - 1, NFC - 1)))
                tok_n2[(ft, dc)] = V.emit(V.scalar_tensor_tensor(
                    xn2c[:, dc * 512:(dc + 1) * 512], xv[:, dc * 512:(dc + 1) * 512],
                    n2w[:, dc:dc + 1], PB[3][:, 512:1024], ALU.mult, ALU.mult))
            sec("vector", vn2)

        for oc in range(NFC):
            def gw1(G, ft=ft, oc=oc):
                G.wait(tok_w1m.get((ft, oc - 2)) if oc >= 2 else tok_w1m.get((ft - 1, NFC - 2 + oc)))
                G.wait(env.tok_pp[("W1", oc)])
                tok_w1l[(ft, oc)] = G.emit(G.dma_start(
                    out=w1st[:, (oc % 2) * NDC * 128:((oc % 2) + 1) * NDC * 128],
                    in_=d.W1b[oc]), dma=True)
            sec("gpsimd", gw1)

            def tw1(T, ft=ft, oc=oc):
                T.wait(tok_n2[(ft, NDC - 1)], tok_w1l[(ft, oc)])
                T.wait(tok_hf.get((ft, oc - 2)) if oc >= 2 else tok_hf.get((ft - 1, NFC - 2 + oc)))
                sl = w1st[:, (oc % 2) * NDC * 128:((oc % 2) + 1) * NDC * 128]
                pb = PB[oc % 2]
                for dd in range(NDC):
                    ins = T.matmul(pb[:, 0:512], sl[:, dd * 128:(dd + 1) * 128],
                                   xn2c[:, dd * 512:(dd + 1) * 512],
                                   start=(dd == 0), stop=(dd == NDC - 1))
                tok_w1m[(ft, oc)] = T.emit(ins)
            sec("tensor", tw1)

            def sg1(SS, ft=ft, oc=oc):
                SS.wait(tok_w1m[(ft, oc)])
                tok_hf[(ft, oc)] = SS.emit(SS.activation(
                    out=hf2[:, oc * 512:(oc + 1) * 512], in_=PB[oc % 2][:, 0:512],
                    func=AF.Gelu_apprx_tanh, bias=b1sb[:, oc:oc + 1]))
            sec("scalar", sg1)

        for oc in range(NDC):
            def gw2(G, ft=ft, oc=oc):
                G.wait(tok_w2m.get((ft, oc - 2)) if oc >= 2 else tok_w2m.get((ft - 1, NDC - 2 + oc)))
                G.wait(env.tok_pp[("W2", oc)])
                tok_w2l[(ft, oc)] = G.emit(G.dma_start(
                    out=w2st[:, (oc % 2) * NFC * 128:((oc % 2) + 1) * NFC * 128],
                    in_=d.W2b[oc]), dma=True)
            sec("gpsimd", gw2)

            def tw2(T, ft=ft, oc=oc):
                T.wait(tok_hf[(ft, NFC - 1)], tok_w2l[(ft, oc)])
                T.wait(tok_sev.get((ft, oc - 2)) if oc >= 2 else tok_sev.get((ft - 1, NDC - 2 + oc)))
                sl = w2st[:, (oc % 2) * NFC * 128:((oc % 2) + 1) * NFC * 128]
                pb = PB[2]
                for kc in range(NFC):
                    ins = T.matmul(pb[:, (oc % 2) * 512:((oc % 2) + 1) * 512],
                                   sl[:, kc * 128:(kc + 1) * 128],
                                   hf2[:, kc * 512:(kc + 1) * 512],
                                   start=(kc == 0), stop=(kc == NFC - 1))
                tok_w2m[(ft, oc)] = T.emit(ins)
            sec("tensor", tw2)

            def vs2(V, ft=ft, oc=oc, xv=xv):
                V.wait(tok_w2m[(ft, oc)])
                V.wait(tok_str.get((ft, oc - 2, 3)) if oc >= 2 else tok_str.get((ft - 1, NDC - 2 + oc, 3)))
                tok_sev[(ft, oc)] = V.emit(V.scalar_tensor_tensor(
                    sch[:, (oc % 2) * 512:((oc % 2) + 1) * 512],
                    PB[2][:, (oc % 2) * 512:((oc % 2) + 1) * 512],
                    b2sb[:, oc:oc + 1], xv[:, oc * 512:(oc + 1) * 512], ALU.add, ALU.add))
            sec("vector", vs2)

            for j in range(4):
                def ts3(T, ft=ft, oc=oc, j=j):
                    T.wait(tok_sev[(ft, oc)])
                    T.wait(tok_strv.get((ft, oc, j - 1)) if j >= 1 else
                           tok_strv.get((ft, oc - 1, 3), tok_strv.get((ft - 1, NDC - 1, 3)) if oc == 0 else None))
                    tok_str[(ft, oc, j)] = T.emit(T.transpose(
                        PB[3].bitcast(F32)[:, 0:128],
                        sch[:, (oc % 2) * 512 + j * 128:(oc % 2) * 512 + (j + 1) * 128],
                        env.ident_f32[:]))
                sec("tensor", ts3)

                def vs3(V, ft=ft, oc=oc, j=j):
                    V.wait(tok_str[(ft, oc, j)])
                    V.wait(tok_stst.get(ft - 1))
                    tok_strv[(ft, oc, j)] = V.emit(V.tensor_copy(
                        stm[:, j * 1024 + oc * 128: j * 1024 + (oc + 1) * 128],
                        PB[3].bitcast(F32)[:, 0:128]))
                sec("vector", vs3)

        def yst(Y, ft=ft):
            Y.wait(tok_strv[(ft, NDC - 1, 3)])
            tok_stst[ft] = Y.emit(Y.dma_start(
                out=d.s1tm_d.rearrange("(s p) dd -> p s dd", p=128)[:, 4 * ft:4 * ft + 4, :],
                in_=stm[:].rearrange("p (s dd) -> p s dd", s=4)), dma=True)
        sec("sync", yst)
    env.tok_phase2_done = tok_stst[NT5 - 1]
    barrier(env, block)
    fstack.close()
    pstack2.close()


def d_bdt_col(env, oc):
    return env.bdt_sb[:, oc:oc + 1]



def phase3(nc, env, d, block, cfg):
    ctx = env.ctx
    seq = cfg.seq
    NT = seq // 128
    NT5 = seq // 512
    KE = seq // 4
    NJT = KE // 128
    JW = min(512, KE)
    NJH = KE // JW
    sec = lambda name, body: _mk_section(env, block, name, body)

    pstack3 = ExitStack()
    PB = [pstack3.enter_context(nc.psum_tensor(f"p3PB{i}", [128, 1024], F32)) for i in range(4)]
    st3 = ExitStack()
    e3 = st3.enter_context
    probs_sb = e3(nc.sbuf_tensor("probs_sb", [128, NT * NEXP], F32))
    prow = e3(nc.sbuf_tensor("prow", [128, 8], F32))
    tokid_sb = e3(nc.sbuf_tensor("tokid_sb", [128, NT], I32))
    jcol_sb = e3(nc.sbuf_tensor("jcol_sb", [128, NJT], I32))
    m_sb = e3(nc.sbuf_tensor("m_sb", [128, NEXP * NJT], I32))
    g_sb = e3(nc.sbuf_tensor("g_sb", [128, NEXP * NJT], F32))
    ptmp = e3(nc.sbuf_tensor("ptmp", [128, NEXP], F32))
    inv_sb = e3(nc.sbuf_tensor("inv_sb", [128, NEXP * NT], I32))
    bigi = e3(nc.sbuf_tensor("bigi", [128, NT], I32))

    tok_gld = {}
    tok_gmm = {}
    tok_gtr = {}
    tok_sm = {}
    tok_pst = {}
    tok_invsc = {}
    tok_gv = {}
    tok_mld = {}
    tok_w1l = {}
    tok_m1 = {}
    tok_hg = {}
    tok_w2l = {}
    tok_m2 = {}
    tok_eev = {}
    tok_est = {}
    tok_xg = {}
    tok_xtr = {}
    tok_xtrv = {}
    tok_gat = {}
    tok_cmb3 = {}
    tok_ost = {}

    def yc3(Y):
        Y.emit(Y.dma_start(out=tokid_sb[:], in_=d.tokid[:]), dma=True)
        env.tok_y = Y.emit(Y.dma_start(out=jcol_sb[:], in_=d.jcol[:, 0:NJT]), dma=True)
    sec("sync", yc3)
    tok_c3 = env.tok_y

    sta = ExitStack()
    ea = sta.enter_context
    gate_sb = ea(nc.sbuf_tensor("gate_sb", [128, NDC * NEXP], BF16))
    gate3 = gate_sb[:].rearrange("p (dd f) -> p dd f", dd=NDC)
    xsl3 = ea(nc.sbuf_tensor("xsl3", [128, 2 * NDC * 512], BF16))
    sc_st = ea(nc.sbuf_tensor("sc_st", [NEXP, 512], F32))
    lo_t = ea(nc.sbuf_tensor("lo_t", [128, NEXP], F32))
    hi_t = ea(nc.sbuf_tensor("hi_t", [128, NEXP], F32))
    mid_t = ea(nc.sbuf_tensor("mid_t", [128, NEXP], F32))
    cm_t = ea(nc.sbuf_tensor("cm_t", [128, NEXP], F32))
    cmi_t = ea(nc.sbuf_tensor("cmi_t", [128, NEXP], I32))
    cnt_t = ea(nc.sbuf_tensor("cnt_t", [128, NEXP], F32))
    crow = ea(nc.sbuf_tensor("crow", [1, NEXP], F32))
    msk = ea(nc.sbuf_tensor("msk", [128, NT], F32))
    onesNT = ea(nc.sbuf_tensor("onesNT", [128, NT], F32))
    pref = ea(nc.sbuf_tensor("pref", [128, NT], F32))
    excl = ea(nc.sbuf_tensor("excl", [128, 1], F32))
    jf_t = ea(nc.sbuf_tensor("jf_t", [128, NT], F32))
    jself = ea(nc.sbuf_tensor("jself", [128, NT], F32))
    bigf = ea(nc.sbuf_tensor("bigf", [128, NT], F32))
    jsel = ea(nc.sbuf_tensor("jsel", [128, NT], I32))
    mski = ea(nc.sbuf_tensor("mski", [128, NT], I32))

    def gg3(G):
        env.breg_ke = G.raw.alloc_register("bke")
        G.raw.reg_mov(env.breg_ke, KE - 1)
        env.breg_seq = G.raw.alloc_register("bseq")
        G.raw.reg_mov(env.breg_seq, seq - 1)
        env.tok_g = G.emit(G.dma_start(out=gate3, in_=d.gate_w.rearrange("(dd p) f -> p dd f", p=128)), dma=True)
    sec("gpsimd", gg3)
    tok_gw = env.tok_g

    zrow = ea(nc.sbuf_tensor("zrow", [1, 1024], BF16))

    def vc3(V):
        env.tok_zr = V.emit(V.memset(zrow[:], 0.0))
        V.emit(V.memset(onesNT[:], 1.0))
        V.emit(V.memset(bigf[:], float(BIGIDX)))
        V.emit(V.memset(bigi[:], KE))
        V.emit(V.memset(lo_t[:], 0.0))
        env.tok_v = V.emit(V.memset(hi_t[:], 1.0))
        V.drain()
    sec("vector", vc3)

    def yzr(Y):
        Y.wait(env.tok_zr)
        for ex in range(NEXP):
            env.tok_y = Y.emit(Y.dma_start(out=d.expout_ds[ex][KE:KE + 1, :], in_=zrow[:]), dma=True)
    sec("sync", yzr)

    for ts in range(NT5):
        sb = ts % 2
        xv = xsl3[:, sb * NDC * 512:(sb + 1) * NDC * 512]

        def yg(Y, ts=ts, xv=xv):
            Y.wait(env.tok_phase2_done)
            Y.wait(tok_gmm.get(ts - 2))
            tok_gld[ts] = Y.emit(Y.dma_start(
                out=xv.rearrange("p (dc q) -> p dc q", dc=NDC),
                in_=d.xn1fm_d.rearrange("(dc p) q -> p dc q", p=128)[:, :, ts * 512:(ts + 1) * 512]), dma=True)
        sec("sync", yg)

        def tg(T, ts=ts, xv=xv):
            T.wait(tok_gld[ts], tok_gw)
            T.wait(tok_sm.get((ts - 1, 3)))
            for dd in range(NDC):
                ins = T.matmul(PB[0][0:NEXP, 0:512], gate3[:, dd, :],
                               xv[:, dd * 512:(dd + 1) * 512],
                               start=(dd == 0), stop=(dd == NDC - 1))
            tok_gmm[ts] = T.emit(ins)
        sec("tensor", tg)

        def vgev(V, ts=ts):
            V.wait(tok_gmm[ts])
            env.tok_v = V.emit(V.tensor_copy(sc_st[:], PB[0][0:NEXP, 0:512]))
            V.drain()
        sec("vector", vgev)
        tok_scst = env.tok_v

        for j in range(4):
            def tgt2(T, ts=ts, j=j, tok_scst=tok_scst):
                T.wait(tok_scst)
                T.wait(tok_sm.get((ts, j - 2)) if j >= 2 else tok_sm.get((ts - 1, 2 + j)))
                tok_gtr[(ts, j)] = T.emit(T.transpose(
                    PB[1][:, (j % 2) * 512:(j % 2) * 512 + NEXP],
                    sc_st[:, j * 128:(j + 1) * 128], env.ident_f32[0:NEXP, 0:NEXP]))
            sec("tensor", tgt2)

            def vsm(V, ts=ts, j=j):
                V.wait(tok_gtr[(ts, j)])
                ps = PB[1][:, (j % 2) * 512:(j % 2) * 512 + NEXP]
                V.emit(V.tensor_reduce(prow[:, 0:1], ps, AX.X, ALU.max))
                V.drain()
                env.tok_v = V.emit(V.tensor_scalar_mul(prow[:, 1:2], prow[:, 0:1], -1.0))
                V.drain()
            sec("vector", vsm)
            tok_nm = env.tok_v

            def ssm(SS, ts=ts, j=j, tok_nm=tok_nm):
                SS.wait(tok_nm)
                tt = ts * 4 + j
                env.tok_s = SS.emit(SS.activation(
                    out=probs_sb[:, tt * NEXP:(tt + 1) * NEXP],
                    in_=PB[1][:, (j % 2) * 512:(j % 2) * 512 + NEXP],
                    func=AF.Exp, bias=prow[:, 1:2]))
                SS.drain()
            sec("scalar", ssm)
            tok_ex3 = env.tok_s

            def vsm2(V, ts=ts, j=j, tok_ex3=tok_ex3):
                V.wait(tok_ex3)
                tt = ts * 4 + j
                pv = probs_sb[:, tt * NEXP:(tt + 1) * NEXP]
                V.emit(V.tensor_reduce(prow[:, 3:4], pv, AX.X, ALU.add))
                V.drain()
                V.emit(V.reciprocal(prow[:, 4:5], prow[:, 3:4]))
                V.drain()
                tok_sm[(ts, j)] = V.emit(V.tensor_scalar(pv, pv, prow[:, 4:5], None, ALU.mult))
                V.drain()
            sec("vector", vsm2)

            def yps(Y, ts=ts, j=j):
                tt = ts * 4 + j
                Y.wait(tok_sm[(ts, j)])
                tok_pst[tt] = Y.emit(Y.dma_start(
                    out=d.probs_d[tt * 128:(tt + 1) * 128, :],
                    in_=probs_sb[:, tt * NEXP:(tt + 1) * NEXP]), dma=True)
            sec("sync", yps)

    NITER = 50
    for it in range(NITER):
        def vbi(V, it=it):
            if it == 0:
                V.wait(tok_sm[(NT5 - 1, 3)])
            V.emit(V.tensor_add(mid_t[:], lo_t[:], hi_t[:]))
            V.drain()
            V.emit(V.tensor_scalar_mul(mid_t[:], mid_t[:], 0.5))
            V.drain()
            ins = None
            for ex in range(NEXP):
                pv = probs_sb[:].rearrange("p (t e) -> p e t", e=NEXP)[:, ex, :]
                V.emit(V.tensor_single_scalar(msk[:], pv, mid_t[:, ex:ex + 1], ALU.is_ge))
                V.drain()
                ins = V.emit(V.tensor_reduce(cnt_t[:, ex:ex + 1], msk[:], AX.X, ALU.add))
                V.drain()
            env.tok_v = ins
        sec("vector", vbi)
        tok_ct = env.tok_v

        def tbi(T, it=it, tok_ct=tok_ct):
            T.wait(tok_ct)
            env.tok_t = T.emit(T.matmul(PB[2][0:1, 0:NEXP], env.onescol[:], cnt_t[:],
                                        start=True, stop=True))
        sec("tensor", tbi)
        tok_cr = env.tok_t

        def vbi2(V, it=it, tok_cr=tok_cr):
            V.wait(tok_cr)
            env.tok_v = V.emit(V.tensor_copy(crow[:], PB[2][0:1, 0:NEXP]))
            V.drain()
        sec("vector", vbi2)
        tok_crv = env.tok_v

        def tbi2(T, it=it, tok_crv=tok_crv):
            T.wait(tok_crv)
            env.tok_t = T.emit(T.matmul(PB[2][:, 512:512 + NEXP], env.ones1[:], crow[:],
                                        start=True, stop=True))
        sec("tensor", tbi2)
        tok_cb2 = env.tok_t

        def vbi3(V, it=it, tok_cb2=tok_cb2):
            V.wait(tok_cb2)
            V.emit(V.tensor_single_scalar(cm_t[:], PB[2][:, 512:512 + NEXP], float(KE), ALU.is_ge))
            V.drain()
            V.emit(V.tensor_copy(cmi_t[:], cm_t[:]))
            V.drain()
            V.emit(V.select(lo_t[:], cmi_t[:], mid_t[:], lo_t[:]))
            V.drain()
            V.emit(V.select(cm_t[:], cmi_t[:], hi_t[:], mid_t[:]))
            V.drain()
            env.tok_v = V.emit(V.tensor_copy(hi_t[:], cm_t[:]))
            V.drain()
        sec("vector", vbi3)
    tok_thr = env.tok_v

    if cfg.dbg:
        def ythr(Y, tok_thr=tok_thr):
            Y.wait(tok_thr)
            env.tok_y = Y.emit(Y.dma_start(out=d.thr_d[:, :], in_=lo_t[:]), dma=True)
        sec("sync", ythr)

    for ex in range(NEXP):
        def vcp(V, ex=ex, tok_thr=tok_thr):
            V.wait(tok_thr)
            V.wait(tok_invsc.get(ex - 1), tok_gv.get((ex - 1, NJT - 1)))
            pv = probs_sb[:].rearrange("p (t e) -> p e t", e=NEXP)[:, ex, :]
            V.emit(V.tensor_single_scalar(msk[:], pv, lo_t[:, ex:ex + 1], ALU.is_ge))
            V.drain()
            V.emit(V.tensor_tensor_scan(pref[:], onesNT[:], msk[:], 0.0, ALU.mult, ALU.add))
            V.drain()
            env.tok_v = V.emit(V.tensor_copy(prow[:, 5:6], pref[:, NT - 1:NT]))
            V.drain()
        sec("vector", vcp)
        tok_pf = env.tok_v

        def tcp(T, ex=ex, tok_pf=tok_pf):
            T.wait(tok_pf)
            env.tok_t = T.emit(T.matmul(PB[2][:, 0:1], env.tri[:], prow[:, 5:6], start=True, stop=True))
        sec("tensor", tcp)
        tok_exm = env.tok_t

        def vcp2(V, ex=ex, tok_exm=tok_exm):
            V.wait(tok_exm)
            V.emit(V.tensor_copy(excl[:], PB[2][:, 0:1]))
            V.drain()
            V.emit(V.tensor_scalar(jf_t[:], pref[:], excl[:, 0:1], -1.0, ALU.add, ALU.add))
            V.drain()
            V.emit(V.tensor_copy(mski[:], msk[:]))
            V.drain()
            V.emit(V.select(jself[:], mski[:], jf_t[:], bigf[:]))
            V.drain()
            env.tok_v = V.emit(V.tensor_copy(jsel[:], jself[:]))
            V.drain()
        sec("vector", vcp2)
        tok_js = env.tok_v

        def gsc(G, ex=ex, tok_js=tok_js):
            G.wait(tok_js, tok_c3)
            if ex == 0:
                G.wait(tok_pst[NT - 1])
            for tt in range(NT):
                ins = G.indirect_dma_start(
                    out=d.m_ds[ex].rearrange("(a b) -> a b", b=1),
                    out_offset=IndirectOffsetOnAxis(jsel[:, tt:tt + 1], 0),
                    in_=tokid_sb[:, tt:tt + 1], in_offset=None,
                    bounds_check=env.breg_ke, oob_is_err=False)
                G.emit(ins, dma=True)
            env.tok_g = G.emit(G.dma_start(
                out=d.inv_ds[ex].rearrange("(p t) -> p t", t=NT), in_=bigi[:, 0:NT]), dma=True)
        sec("gpsimd", gsc)
        tok_msc = env.tok_g

        def ym(Y, ex=ex, tok_msc=tok_msc):
            Y.wait(tok_msc)
            tok_mld[ex] = Y.emit(Y.dma_start(
                out=m_sb[:, ex * NJT:(ex + 1) * NJT],
                in_=d.m_ds[ex].rearrange("(p j) -> p j", j=NJT)), dma=True)
        sec("sync", ym)

        def gsc2(G, ex=ex):
            G.wait(tok_mld[ex])
            ins = None
            for jt in range(NJT):
                ins = G.indirect_dma_start(
                    out=d.inv_ds[ex].rearrange("(a b) -> a b", b=1),
                    out_offset=IndirectOffsetOnAxis(m_sb[:, ex * NJT + jt:ex * NJT + jt + 1], 0),
                    in_=jcol_sb[:, jt:jt + 1], in_offset=None,
                    bounds_check=env.breg_seq, oob_is_err=False)
                G.emit(ins, dma=True)
            tok_invsc[ex] = (env.sems["gpsimd"], env.sems["gpsimd"].n)
        sec("gpsimd", gsc2)

        for jt in range(NJT):
            def gpv(G, ex=ex, jt=jt):
                if jt > 0:
                    G.wait(tok_gv[(ex, jt - 1)])
                ins = G.indirect_dma_start(
                    out=ptmp[:], out_offset=None,
                    in_=d.probs_d[:],
                    in_offset=IndirectOffsetOnAxis(m_sb[:, ex * NJT + jt:ex * NJT + jt + 1], 0))
                tok_xg[(ex, jt, "p")] = G.emit(ins, dma=True)
            sec("gpsimd", gpv)

            def vgv(V, ex=ex, jt=jt):
                V.wait(tok_xg[(ex, jt, "p")])
                tok_gv[(ex, jt)] = V.emit(V.tensor_copy(
                    g_sb[:, ex * NJT + jt:ex * NJT + jt + 1], ptmp[:, ex:ex + 1]))
            sec("vector", vgv)

    barrier(env, block)
    sta.close()

    stb = ExitStack()
    eb = stb.enter_context
    w1h = eb(nc.sbuf_tensor("w1h", [128, NDC * DFF], BF16))
    w1h3 = w1h[:].rearrange("p (dd f) -> p dd f", dd=NDC)
    w2h = eb(nc.sbuf_tensor("w2h", [128, NFC * 512], BF16))
    w2h3 = w2h[:].rearrange("p (kc f) -> p kc f", kc=NFC)
    xin_tm = eb(nc.sbuf_tensor("xin_tm", [128, 2 * 1024], BF16))
    xin_fm = eb(nc.sbuf_tensor("xin_fm", [128, NDC * KE], BF16))
    hgel = eb(nc.sbuf_tensor("hgel", [128, NFC * KE], BF16))
    eo_tm = eb(nc.sbuf_tensor("eo_tm", [128, 2 * 512], BF16))

    for ex in range(NEXP):
        for jt in range(NJT):
            def gx(G, ex=ex, jt=jt):
                G.wait(tok_xtr.get((ex, jt - 2, NDC - 1)) if jt >= 2 else
                       tok_xtr.get((ex - 1, NJT - 2 + jt, NDC - 1)))
                ins = G.indirect_dma_start(
                    out=xin_tm[:, (jt % 2) * 1024:(jt % 2 + 1) * 1024], out_offset=None,
                    in_=d.xn1tm_d[:],
                    in_offset=IndirectOffsetOnAxis(m_sb[:, ex * NJT + jt:ex * NJT + jt + 1], 0))
                tok_xg[(ex, jt)] = G.emit(ins, dma=True)
            sec("gpsimd", gx)

            for dc in range(NDC):
                def tx(T, ex=ex, jt=jt, dc=dc):
                    T.wait(tok_xg[(ex, jt)])
                    T.wait(tok_xtrv.get((ex, jt, dc - 2)) if dc >= 2 else
                           tok_xtrv.get((ex, jt - 1, NDC - 2 + dc),
                                        tok_xtrv.get((ex - 1, NJT - 1, NDC - 2 + dc)) if jt == 0 else None))
                    if ex == 0 and jt == 0 and dc == 0:
                        T.wait(tok_gv.get((NEXP - 1, NJT - 1)))
                    tok_xtr[(ex, jt, dc)] = T.emit(T.transpose(
                        PB[3].bitcast(BF16)[:, (dc % 2) * 1024:(dc % 2) * 1024 + 128],
                        xin_tm[:, (jt % 2) * 1024 + dc * 128:(jt % 2) * 1024 + (dc + 1) * 128],
                        env.ident_bf[:]))
                sec("tensor", tx)

                def vx(V, ex=ex, jt=jt, dc=dc):
                    V.wait(tok_xtr[(ex, jt, dc)])
                    tok_xtrv[(ex, jt, dc)] = V.emit(V.tensor_copy(
                        xin_fm[:, dc * KE + jt * 128:dc * KE + (jt + 1) * 128],
                        PB[3].bitcast(BF16)[:, (dc % 2) * 1024:(dc % 2) * 1024 + 128]))
                sec("vector", vx)

        for hh in range(2):
            def gw1(G, ex=ex, hh=hh):
                G.wait(tok_m1.get((ex - 1, NFC - 1)))
                G.wait(env.tok_pp[("ew1", ex)])
                tok_w1l[(ex, hh)] = G.emit(G.dma_start(
                    out=w1h3[:, :, hh * 2048:(hh + 1) * 2048],
                    in_=d.ew1b[ex].rearrange("p (dd f) -> p dd f", dd=NDC)[:, :, hh * 2048:(hh + 1) * 2048]), dma=True)
            sec("gpsimd", gw1)

        for oc in range(NFC):
            hh = oc // (NFC // 2)

            def tm1(T, ex=ex, oc=oc, hh=hh):
                T.wait(tok_w1l[(ex, hh)], tok_xtrv[(ex, NJT - 1, NDC - 1)])
                T.wait(tok_hg.get((ex, oc - 2)) if oc >= 2 else tok_hg.get((ex - 1, NFC - 2 + oc)))
                pb = PB[oc % 2]
                for jh in range(NJH):
                    for dd in range(NDC):
                        ins = T.matmul(pb[:, jh * JW:(jh + 1) * JW],
                                       w1h3[:, dd, oc * 128:(oc + 1) * 128],
                                       xin_fm[:, dd * KE + jh * JW:dd * KE + (jh + 1) * JW],
                                       start=(dd == 0), stop=(dd == NDC - 1))
                tok_m1[(ex, oc)] = T.emit(ins)
            sec("tensor", tm1)

            def sg1(SS, ex=ex, oc=oc):
                SS.wait(tok_m1[(ex, oc)])
                tok_hg[(ex, oc)] = SS.emit(SS.activation(
                    out=hgel[:, oc * KE:(oc + 1) * KE], in_=PB[oc % 2][:, 0:KE],
                    func=AF.Gelu_apprx_tanh))
            sec("scalar", sg1)

        if cfg.dbg and ex == 0:
            def ydbg4(Y, ex=ex):
                Y.wait(tok_hg[(0, NFC - 1)], tok_xtrv[(0, NJT - 1, NDC - 1)])
                Y.emit(Y.dma_start(out=d.xinfm_d[:, :], in_=xin_fm[:]), dma=True)
                env.tok_y = Y.emit(Y.dma_start(out=d.hgel_d[:, :], in_=hgel[:, 0:2 * KE]), dma=True)
            sec("sync", ydbg4)

            def vdbg4(V):
                V.wait(env.tok_y)
            sec("vector", vdbg4)

        for dh in range(2):
            def gw2(G, ex=ex, dh=dh):
                G.wait(tok_m2.get((ex, dh - 1, NJT - 1), tok_m2.get((ex - 1, 1, NJT - 1)) if dh == 0 else None))
                G.wait(env.tok_pp[("ew2", ex)])
                tok_w2l[(ex, dh)] = G.emit(G.dma_start(
                    out=w2h3,
                    in_=d.ew2b[ex].rearrange("p (kc f) -> p kc f", kc=NFC)[:, :, dh * 512:(dh + 1) * 512]), dma=True)
            sec("gpsimd", gw2)

            for jb in range(NJT):
                def tm2(T, ex=ex, dh=dh, jb=jb):
                    T.wait(tok_w2l[(ex, dh)], tok_hg[(ex, NFC - 1)])
                    T.wait(tok_eev.get((ex, dh, jb - 2)) if jb >= 2 else
                           (tok_eev.get((ex, dh - 1, NJT - 2 + jb)) if dh == 1 else
                            tok_eev.get((ex - 1, 1, NJT - 2 + jb))))
                    pb2 = PB[2][:, (jb % 2) * 512:((jb % 2) + 1) * 512]
                    for kc in range(NFC):
                        ins = T.matmul(pb2,
                                       hgel[:, kc * KE + jb * 128:kc * KE + (jb + 1) * 128],
                                       w2h3[:, kc, :],
                                       start=(kc == 0), stop=(kc == NFC - 1))
                    tok_m2[(ex, dh, jb)] = T.emit(ins)
                sec("tensor", tm2)

                def ve2(V, ex=ex, dh=dh, jb=jb):
                    V.wait(tok_m2[(ex, dh, jb)])
                    V.wait(tok_est.get((ex, dh, jb - 2)) if jb >= 2 else
                           (tok_est.get((ex, dh - 1, NJT - 2 + jb)) if dh == 1 else
                            tok_est.get((ex - 1, 1, NJT - 2 + jb))))
                    tok_eev[(ex, dh, jb)] = V.emit(V.tensor_scalar(
                        eo_tm[:, (jb % 2) * 512:((jb % 2) + 1) * 512],
                        PB[2][:, (jb % 2) * 512:((jb % 2) + 1) * 512],
                        g_sb[:, ex * NJT + jb:ex * NJT + jb + 1], None, ALU.mult))
                sec("vector", ve2)

                def ye2(Y, ex=ex, dh=dh, jb=jb):
                    Y.wait(tok_eev[(ex, dh, jb)])
                    tok_est[(ex, dh, jb)] = Y.emit(Y.dma_start(
                        out=d.expout_ds[ex][jb * 128:(jb + 1) * 128, dh * 512:(dh + 1) * 512],
                        in_=eo_tm[:, (jb % 2) * 512:((jb % 2) + 1) * 512]), dma=True)
                sec("sync", ye2)

    barrier(env, block)
    stb.close()

    stc = ExitStack()
    ec3 = stc.enter_context
    gacc = ec3(nc.sbuf_tensor("gacc", [128, 1024], F32))
    gtmp = ec3(nc.sbuf_tensor("gtmp", [128, 1024], BF16))
    gbf = ec3(nc.sbuf_tensor("gbf", [128, 1024], BF16))
    gsil = ec3(nc.sbuf_tensor("gsil", [128, 1024], F32))
    o1t = ec3(nc.sbuf_tensor("o1t", [128, 2 * 1024], F32))
    s1t = ec3(nc.sbuf_tensor("s1t", [128, 2 * 1024], F32))
    outt = ec3(nc.sbuf_tensor("outt", [128, 2 * 1024], F32))

    def yinv(Y):
        ins = None
        for ex in range(NEXP):
            ins = Y.dma_start(out=inv_sb[:, ex * NT:(ex + 1) * NT],
                              in_=d.inv_ds[ex].rearrange("(p t) -> p t", t=NT))
            env.tok_y = Y.emit(ins, dma=True)
    sec("sync", yinv)
    tok_invld = env.tok_y

    tok_ld3 = {}
    for tt in range(NT):
        def yl3(Y, tt=tt):
            Y.wait(tok_cmb3.get(tt - 2))
            tok_ld3[tt] = Y.emit(Y.dma_start(
                out=o1t[:, (tt % 2) * 1024:(tt % 2 + 1) * 1024],
                in_=d.out1_d.rearrange("(p t) dd -> p t dd", t=NT)[:, tt, :]), dma=True)
            tok_ld3[(tt, "s")] = Y.emit(Y.dma_start(
                out=s1t[:, (tt % 2) * 1024:(tt % 2 + 1) * 1024],
                in_=d.s1tm_d.rearrange("(p t) dd -> p t dd", t=NT)[:, tt, :]), dma=True)
        sec("sync", yl3)

        def vz0(V, tt=tt):
            V.wait(tok_cmb3.get(tt - 1))
            V.emit(V.memset(gacc[:], 0.0))
            tok_gat[(tt, 0, "z")] = V.emit(V.memset(gtmp[:], 0.0))
        sec("vector", vz0)

        for ex in range(NEXP):
            def gga(G, tt=tt, ex=ex):
                G.wait(tok_invld)
                G.wait(tok_gat[(tt, ex, "z")])
                ins = G.indirect_dma_start(
                    out=gtmp[:], out_offset=None,
                    in_=d.expout_ds[ex][:],
                    in_offset=IndirectOffsetOnAxis(inv_sb[:, ex * NT + tt:ex * NT + tt + 1], 0))
                tok_gat[(tt, ex)] = G.emit(ins, dma=True)
            sec("gpsimd", gga)

            def vga(V, tt=tt, ex=ex):
                V.wait(tok_gat[(tt, ex)])
                if cfg.dbg and tt == 0 and ex == 0:
                    V.emit(V.tensor_copy(gbf[:], gtmp[:]))
                tok_gat[(tt, ex, "v")] = V.emit(V.tensor_tensor(gacc[:], gacc[:], gtmp[:], ALU.add))
                if ex < NEXP - 1:
                    tok_gat[(tt, ex + 1, "z")] = V.emit(V.memset(gtmp[:], 0.0))
                else:
                    env.tok_v = V.emit(V.tensor_copy(gbf[:], gacc[:]))
            sec("vector", vga)
        tok_gsum = env.tok_v

        if cfg.dbg and tt == 0:
            def ydbg3(Y, tok_gsum=tok_gsum):
                Y.wait(tok_gsum, tok_invld)
                Y.emit(Y.dma_start(out=d.gtmp_d[:, :], in_=gbf[:]), dma=True)
                Y.emit(Y.dma_start(out=d.gacc_d[:, :], in_=gacc[:]), dma=True)
                env.tok_y = Y.emit(Y.dma_start(out=d.inv0_d[:, :], in_=inv_sb[:]), dma=True)
            sec("sync", ydbg3)

            def vdbg3(V):
                V.wait(env.tok_y)
            sec("vector", vdbg3)

        def sga(SS, tt=tt, tok_gsum=tok_gsum):
            SS.wait(tok_gsum)
            env.tok_s = SS.emit(SS.activation(out=gsil[:], in_=gbf[:], func=AF.Silu))
        sec("scalar", sga)
        tok_sil = env.tok_s

        def vcb(V, tt=tt, tok_sil=tok_sil):
            V.wait(tok_sil, tok_ld3[(tt, "s")])
            V.wait(tok_ost.get(tt - 2))
            sl = slice((tt % 2) * 1024, (tt % 2 + 1) * 1024)
            V.emit(V.tensor_tensor(gsil[:], s1t[:, sl], gsil[:], ALU.mult))
            tok_cmb3[tt] = V.emit(V.tensor_add(outt[:, sl], o1t[:, sl], gsil[:]))
        sec("vector", vcb)

        def yo(Y, tt=tt):
            Y.wait(tok_cmb3[tt])
            tok_ost[tt] = Y.emit(Y.dma_start(
                out=d.out.rearrange("(p t) dd -> p t dd", t=NT)[:, tt, :],
                in_=outt[:, (tt % 2) * 1024:(tt % 2 + 1) * 1024]), dma=True)
        sec("sync", yo)

    env.tok_out_done = tok_ost[NT - 1]
    barrier(env, block)
    stc.close()
    st3.close()
    pstack3.close()

def barrier(env, block):
    toks = [(env.sems[k], env.sems[k].n) for k in ENGS]
    for k in ENGS:
        def body(E, toks=toks):
            E.wait(*toks)
        _mk_section(env, block, k, body)


class Cfg:
    def __init__(self, seq=SEQ, dbg=False, phases=(1, 2, 3)):
        self.seq = seq
        self.dbg = dbg
        self.phases = phases


def build_program(cfg):
    nc = bass.Bass("TRN2", target_bir_lowering=False, detect_race_conditions=False)
    with ExitStack() as ctx:
        env = mk_env(nc, ctx)
        d = declare_io(nc, cfg.seq, dbg=cfg.dbg)
        with nc.Block() as block:
            setup_stage(nc, env, d, block)
            if 1 in cfg.phases:
                phase1(nc, env, d, block, cfg)
            if 2 in cfg.phases:
                barrier(env, block)
                phase2(nc, env, d, block, cfg)
            if 3 in cfg.phases:
                barrier(env, block)
                phase3(nc, env, d, block, cfg)
            barrier(env, block)
    return nc


def kernel(**inputs):
    W = host_weights(inputs)
    x = np.asarray(inputs["L0_attn_x"])
    C = host_constants(x.shape[1])
    nb = x.shape[0]
    cfg = Cfg(seq=x.shape[1], dbg=False, phases=(1, 2, 3))
    nc = build_program(cfg)
    in_maps = [make_in_map(x[b], W, C) for b in range(nb)]
    res = run_bass_kernel_spmd(nc, in_maps, core_ids=list(range(nb)))
    return np.stack([r["out"] for r in res.results], axis=0).astype(np.float32)


# version 09
# speedup vs baseline: 1.0392x; 1.0392x over previous
import numpy as np
import ml_dtypes
from contextlib import ExitStack

import concourse.bass as bass
import concourse.mybir as mybir
from concourse.bass_utils import run_bass_kernel_spmd
from concourse.bass_core_rust import IndirectOffsetOnAxis

F32 = mybir.dt.float32
BF16 = mybir.dt.bfloat16
I32 = mybir.dt.int32
AF = mybir.ActivationFunctionType
ALU = mybir.AluOpType
AX = mybir.AxisListType

SEQ = 4096
D = 1024
NH = 16
DHD = 64
NDC = 8
DFF = 4096
NFC = 32
NEXP = 8
KEXP = 1024
NST = 16
LN_EPS = 1e-5
RMS_EPS = 1e-6
MAXLOG = float(np.log(1.0 / 0.01))
BIGIDX = 1 << 28

bf = ml_dtypes.bfloat16


class SemC:

    def __init__(self, ctx, nc, name):
        self.h = ctx.enter_context(nc.semaphore(name))
        self.n = 0

    def inc(self, instr, k=1):
        instr.then_inc(self.h, k)
        self.n += k
        return self.n


class Env:
    pass


def _wait(eng, sem, val=None):
    eng.wait_ge(sem.h, sem.n if val is None else val)



def host_constants(seq=SEQ):
    c = {}
    c["ident_bf"] = np.eye(128, dtype=bf)
    c["ident_f32"] = np.eye(128, dtype=np.float32)
    c["ones1"] = np.ones((1, 128), dtype=np.float32)
    sel2 = np.zeros((2, 128), dtype=np.float32)
    sel2[0, 0:64] = 1.0
    sel2[1, 64:128] = 1.0
    c["sel2"] = sel2
    s16 = np.zeros((NST, NST, 128), dtype=np.float32)
    for n in range(NST):
        s16[n, n, :] = 1.0
    c["sel16"] = s16
    hm = np.zeros((NDC, 128, NH), dtype=np.float32)
    for ch in range(NDC):
        hm[ch, 0:64, 2 * ch] = 1.0
        hm[ch, 64:128, 2 * ch + 1] = 1.0
    c["hmask"] = hm
    m2 = np.zeros((128, 2), dtype=np.float32)
    m2[0:64, 0] = 1.0
    m2[64:128, 1] = 1.0
    c["mask2"] = m2
    c["tri"] = np.triu(np.ones((128, 128), dtype=np.float32), 1)
    c["tokid"] = (np.arange(seq // 128)[None, :] * 128 + np.arange(128)[:, None]).astype(np.int32)
    c["jcol"] = (np.arange(8)[None, :] * 128 + np.arange(128)[:, None]).astype(np.int32)
    return c


def host_weights(I):
    w = {}
    w["w_qkv"] = I["L0_attn_w_qkv"]
    w["w_out"] = I["L0_attn_w_out"]
    w["b_out"] = I["L0_attn_b_out"].reshape(NDC, 128).T.copy()
    w["logit_scale"] = I["L0_attn_logit_scale"].reshape(NH)
    def och_pm(a, no):
        ic = a.shape[1] // 128
        return np.ascontiguousarray(
            a.T.reshape(ic, 128, no, 128).transpose(2, 1, 0, 3).reshape(no, 128, ic * 128))
    w["fw1"] = och_pm(I["L0_ffn_w1_w"], 32)
    w["fw2"] = och_pm(I["L0_ffn_w2_w"], 32)
    w["fw3"] = och_pm(I["L0_ffn_w3_w"], 8)
    w["fh1"] = och_pm(I["L0_ffn_h1_w"], 32)
    w["fh2"] = och_pm(I["L0_ffn_h2_w"], 32)
    w["s1"] = I["L0_ffn_w1_s"].reshape(NFC, 128).T.copy()
    w["s2"] = I["L0_ffn_w2_s"].reshape(NFC, 128).T.copy()
    w["h1s"] = I["L0_ffn_h1_s"].reshape(NFC, 128).T.copy()
    w["h2s"] = I["L0_ffn_h2_s"].reshape(NFC, 128).T.copy()
    w["s3"] = I["L0_ffn_w3_s"].reshape(NDC, 128).T.copy()
    w["norm1_w"] = I["L1_attn_norm1_w"].reshape(NDC, 128).T.copy()
    w["norm2_w"] = I["L1_attn_norm2_w"].reshape(NDC, 128).T.copy()
    w["A_log"] = I["L1_attn_A_log"]
    w["D_param"] = I["L1_attn_D_param"].reshape(NDC, 128).T.copy()
    w["mscale"] = I["L1_attn_scale"].reshape(NDC, 128).T.copy()
    w["Wdt"] = np.ascontiguousarray(I["L1_attn_Wdt"].T)
    w["bdt"] = I["L1_attn_bdt"].reshape(NDC, 128).T.copy()
    wbc = np.zeros((1024, 64), dtype=np.float32)
    wbc[:, 0:16] = I["L1_attn_WB"].T
    wbc[:, 32:48] = I["L1_attn_WC"].T
    w["WBC"] = wbc
    bbc = np.zeros((64, 1), dtype=np.float32)
    bbc[0:16, 0] = I["L1_attn_bB"]
    bbc[32:48, 0] = I["L1_attn_bC"]
    w["bBC"] = bbc
    w["W1"] = och_pm(I["L1_attn_W1"], 32)
    w["b1"] = I["L1_attn_b1"].reshape(NFC, 128).T.copy()
    w["W2"] = och_pm(I["L1_attn_W2"], 8)
    w["b2"] = I["L1_attn_b2"].reshape(NDC, 128).T.copy()
    w["gate_w"] = I["L1_ffn_gate_w"]
    w["ew1"] = np.ascontiguousarray(
        I["L1_ffn_w1"].reshape(NEXP, 8, 128, DFF).transpose(0, 2, 1, 3).reshape(NEXP, 128, 8 * DFF))
    w["ew2"] = np.ascontiguousarray(
        I["L1_ffn_w2"].reshape(NEXP, 32, 128, D).transpose(0, 2, 1, 3).reshape(NEXP, 128, 32 * D))
    return w



def declare_io(nc, seq, dbg=False):
    d = Env()
    T = lambda name, shape, dt, kind="ExternalInput": nc.dram_tensor(name, list(shape), dt, kind=kind).ap()
    scr = "ExternalOutput" if dbg else "Internal"
    d.x0 = T("x0", (seq, D), F32)
    d.w_qkv = T("w_qkv", (D, 3 * D), F32)
    d.w_out = T("w_out", (D, D), F32)
    d.b_out = T("b_out", (128, NDC), F32)
    d.logit_scale = T("logit_scale", (NH,), F32)
    d.fw1 = T("fw1", (NFC, 128, D), F32)
    d.fw2 = T("fw2", (NFC, 128, D), F32)
    d.fw3 = T("fw3", (NDC, 128, DFF), F32)
    d.fh1 = T("fh1", (NFC, 128, DFF), F32)
    d.fh2 = T("fh2", (NFC, 128, DFF), F32)
    d.fw1b = T("fw1b", (NFC, 128, D), BF16, "Internal")
    d.fw2b = T("fw2b", (NFC, 128, D), BF16, "Internal")
    d.fw3b = T("fw3b", (NDC, 128, DFF), BF16, "Internal")
    d.fh1b = T("fh1b", (NFC, 128, DFF), BF16, "Internal")
    d.fh2b = T("fh2b", (NFC, 128, DFF), BF16, "Internal")
    d.W1b = T("W1b", (NFC, 128, D), BF16, "Internal")
    d.W2b = T("W2b", (NDC, 128, DFF), BF16, "Internal")
    d.ew1b = T("ew1b", (NEXP, 128, 8 * DFF), BF16, "Internal")
    d.ew2b = T("ew2b", (NEXP, 128, 32 * D), BF16, "Internal")
    d.s1 = T("s1", (128, NFC), F32)
    d.s2 = T("s2", (128, NFC), F32)
    d.h1s = T("h1s", (128, NFC), F32)
    d.h2s = T("h2s", (128, NFC), F32)
    d.s3 = T("s3", (128, NDC), F32)
    d.norm1_w = T("norm1_w", (128, NDC), F32)
    d.norm2_w = T("norm2_w", (128, NDC), F32)
    d.A_log = T("A_log", (NST,), F32)
    d.D_param = T("D_param", (128, NDC), F32)
    d.mscale = T("mscale", (128, NDC), F32)
    d.Wdt = T("Wdt", (D, D), F32)
    d.bdt = T("bdt", (128, NDC), F32)
    d.WBC = T("WBC", (D, 64), F32)
    d.bBC = T("bBC", (64, 1), F32)
    d.W1 = T("W1", (NFC, 128, D), F32)
    d.b1 = T("b1", (128, NFC), F32)
    d.W2 = T("W2", (NDC, 128, DFF), F32)
    d.b2 = T("b2", (128, NDC), F32)
    d.gate_w = T("gate_w", (D, NEXP), F32)
    d.ew1 = T("ew1", (NEXP, 128, 8 * DFF), F32)
    d.ew2 = T("ew2", (NEXP, 128, 32 * D), F32)
    d.ident_bf = T("ident_bf", (128, 128), BF16)
    d.ident_f32 = T("ident_f32", (128, 128), F32)
    d.ones1 = T("ones1", (1, 128), F32)
    d.sel2 = T("sel2", (2, 128), F32)
    d.hmask = T("hmask", (NDC, 128, NH), F32)
    d.mask2 = T("mask2", (128, 2), F32)
    d.tri = T("tri", (128, 128), F32)
    d.tokid = T("tokid", (128, seq // 128), I32)
    d.sel16 = T("sel16", (NST, NST, 128), F32)
    d.jcol = T("jcol", (128, 8), I32)
    d.xn0_d = T("xn0_d", (D, seq), BF16, scr)
    d.vaug_d = T("vaug_d", (NDC, 128, (seq // 128) * 130), BF16, scr)
    d.s0_d = T("s0_d", (D, seq), F32, scr)
    d.out1_d = T("out1_d", (seq, D), F32, scr)
    d.xn1fm_d = T("xn1fm_d", (D, seq), BF16, scr)
    d.xn1tm_d = T("xn1tm_d", (seq, D), BF16, scr)
    d.s1tm_d = T("s1tm_d", (seq, D), F32, scr)
    d.x1_d = T("x1_d", (D, seq), BF16, scr)
    d.probs_d = T("probs_d", (seq, NEXP), F32, scr)
    d.expout_ds = [T(f"expout_d{ex}", (seq // 4 + 1, D), F32, scr) for ex in range(NEXP)]
    d.inv_ds = [T(f"inv_d{ex}", (seq,), I32, scr) for ex in range(NEXP)]
    d.m_ds = [T(f"m_d{ex}", (seq // 4,), I32, scr) for ex in range(NEXP)]
    if dbg:
        d.qn_d = T("qn_d", (128, NDC * 1024), BF16, "ExternalOutput")
        d.ps_d = T("ps_d", (128, 2 * 1024), F32, "ExternalOutput")
        d.pt_d = T("pt_d", (128, 2 * 1024), BF16, "ExternalOutput")
        d.rk_d = T("rk_d", (128, 64 * NH), F32, "ExternalOutput")
        d.rk1b_d = T("rk1b_d", (NST, 512), F32, "ExternalOutput")
        d.ss_d = T("ss_d", (NST, 512), F32, "ExternalOutput")
        d.sq_d = T("sq_d", (128, 512), F32, "ExternalOutput")
        d.k_d = T("k_d", (128, NDC * 512), BF16, "ExternalOutput")
        d.dt_d = T("dt_d", (128, NDC * 512), F32, "ExternalOutput")
        d.bvc_d = T("bvc_d", (NST, 2 * 512), F32, "ExternalOutput")
        d.yac_d = T("yac_d", (128, NDC * 512), F32, "ExternalOutput")
        d.xnn_d = T("xnn_d", (128, NDC * 512), BF16, "ExternalOutput")
        d.wc_d = T("wc_d", (128, NDC * 512), BF16, "ExternalOutput")
        d.thr_d = T("thr_d", (128, NEXP), F32, "ExternalOutput")
        d.gacc_d = T("gacc_d", (128, 1024), F32, "ExternalOutput")
        d.gtmp_d = T("gtmp_d", (128, 1024), BF16, "ExternalOutput")
        d.xinfm_d = T("xinfm_d", (128, NDC * (seq // 4)), BF16, "ExternalOutput")
        d.hgel_d = T("hgel_d", (128, 2 * (seq // 4)), BF16, "ExternalOutput")
        d.inv0_d = T("inv0_d", (128, NEXP * (seq // 128)), I32, "ExternalOutput")
    d.out = T("out", (seq, D), F32, "ExternalOutput")
    return d


def make_in_map(xb, W, C):
    m = dict(x0=np.ascontiguousarray(xb))
    m.update({k: np.ascontiguousarray(v) for k, v in W.items()})
    m.update({k: np.ascontiguousarray(v) for k, v in C.items()})
    return m




ACT_SET = {AF.Exp: 6, AF.Ln: 6, AF.Gelu: 10, AF.Gelu_apprx_tanh: 11, AF.Silu: 18}

ATTN_LA = 3
ATTN_PTM = 8


class Eng:
    def __init__(self, env, raw, sem, is_dma=False):
        self.env, self.raw, self.sem = env, raw, sem
        self.is_dma = is_dma

    def wait(self, *toks):
        for tok in toks:
            if tok is None:
                continue
            sem, val = tok
            if val <= 0:
                continue
            if sem is self.sem and not self.is_dma:
                continue
            assert val <= sem.n, f"wait on future token {val} > {sem.n}"
            key = (id(self.sem), id(sem))
            if self.env.waited.get(key, 0) >= val:
                continue
            self.raw.wait_ge(sem.h, val)
            self.env.waited[key] = val

    def emit(self, instr, dma=False):
        self.sem.inc(instr, 16 if dma else 1)
        return (self.sem, self.sem.n)

    def drain(self):
        self.raw.wait_ge(self.sem.h, self.sem.n)

    def activation(self, *a, **kw):
        sid = ACT_SET.get(kw.get("func"))
        if sid is None and self.env.cur_tset is None:
            sid = 6
        if sid is not None and self.env.cur_tset != sid:
            self.raw.add_instruction(mybir.InstLoadActFuncSet(
                name=self.raw.bass.get_next_instruction_name(),
                act_func_set_id=sid, ins=[], outs=[]))
            self.env.cur_tset = sid
        return self.raw.activation(*a, **kw)

    def __getattr__(self, name):
        return getattr(self.raw, name)


ENGS = ["tensor", "vector", "scalar", "gpsimd", "sync"]


def mk_env(nc, ctx):
    env = Env()
    env.nc = nc
    env.ctx = ctx
    env.waited = {}
    env.cur_tset = None
    env.sems = {k: SemC(ctx, nc, f"es_{k}") for k in ENGS}
    env.ppsem = SemC(ctx, nc, "es_pp")
    env.tok_pp = {}
    return env


def _mk_section(env, block, engname, body):
    getattr(block, engname)(lambda raw: body(
        Eng(env, raw, env.sems[engname], is_dma=engname in ("sync", "gpsimd"))))



CONSTS = ["r1024", "ln_eps", "rms_eps", "mhalf", "half", "mone", "one"]
CONST_VALS = {"r1024": 1.0 / 1024.0, "ln_eps": LN_EPS, "rms_eps": RMS_EPS,
              "mhalf": -0.5, "half": 0.5, "mone": -1.0, "one": 1.0}


def setup_stage(nc, env, d, block):
    ctx = env.ctx
    e = ctx.enter_context
    env.cpool = e(nc.sbuf_tensor("cpool", [128, len(CONSTS)], F32))
    env.cap = {}
    for i, k in enumerate(CONSTS):
        ap = env.cpool[:, i:i + 1]
        env.cap[k] = ap
        nc.const_aps.register(F32, float(CONST_VALS[k]), ap)
        if float(np.float32(CONST_VALS[k])) != float(CONST_VALS[k]):
            nc.const_aps.register(F32, float(np.float32(CONST_VALS[k])), ap)
    env.ident_bf = e(nc.sbuf_tensor("identbf", [128, 128], BF16))
    env.ident_f32 = e(nc.sbuf_tensor("identf", [128, 128], F32))
    env.ones1 = e(nc.sbuf_tensor("ones1s", [1, 128], F32))
    env.onescol = e(nc.sbuf_tensor("onescol", [128, 1], F32))
    env.sel2 = e(nc.sbuf_tensor("sel2s", [2, 128], F32))
    env.hmask = e(nc.sbuf_tensor("hmasks", [128, NDC * NH], F32))
    env.mask2 = e(nc.sbuf_tensor("mask2s", [128, 2], F32))
    env.tri = e(nc.sbuf_tensor("tris", [128, 128], F32))
    env.bout_sb = e(nc.sbuf_tensor("bouts", [128, NDC], F32))
    env.s1_sb = e(nc.sbuf_tensor("s1s", [128, NFC], F32))
    env.s2_sb = e(nc.sbuf_tensor("s2s", [128, NFC], F32))
    env.h1s_sb = e(nc.sbuf_tensor("h1ss", [128, NFC], F32))
    env.h2s_sb = e(nc.sbuf_tensor("h2ss", [128, NFC], F32))
    env.s3_sb = e(nc.sbuf_tensor("s3s", [128, NDC], F32))

    def ysec(Y):
        toks = []
        for src, dst in [(d.ident_bf, env.ident_bf), (d.ident_f32, env.ident_f32),
                         (d.ones1, env.ones1), (d.sel2, env.sel2),
                         (d.tri, env.tri), (d.b_out, env.bout_sb), (d.mask2, env.mask2),
                         (d.s1, env.s1_sb), (d.s2, env.s2_sb),
                         (d.h1s, env.h1s_sb), (d.h2s, env.h2s_sb), (d.s3, env.s3_sb)]:
            toks.append(Y.emit(Y.dma_start(out=dst[:], in_=src[:]), dma=True))
        toks.append(Y.emit(Y.dma_start(
            out=env.hmask[:].rearrange("p (c h) -> p c h", c=NDC),
            in_=d.hmask.transpose([1, 0, 2])), dma=True))
        env.tok_consts_loaded = toks[-1]
    _mk_section(env, block, "sync", ysec)

    def vsec(V):
        for i, k in enumerate(CONSTS):
            t = V.emit(V.memset(env.cpool[:, i:i + 1], float(np.float32(CONST_VALS[k]))))
        t = V.emit(V.memset(env.onescol[:], 1.0))
        env.tok_cpool = t
    _mk_section(env, block, "vector", vsec)


def prepass_weights(nc, env, d, block, group):
    pp = env.ppsem

    def ssec(SS):
        if group == "p1":
            pairs = [("fw1", d.fw1, d.fw1b, NFC), ("fw2", d.fw2, d.fw2b, NFC),
                     ("fh1", d.fh1, d.fh1b, NFC), ("fh2", d.fh2, d.fh2b, NFC),
                     ("fw3", d.fw3, d.fw3b, NDC), ("W1", d.W1, d.W1b, NFC),
                     ("W2", d.W2, d.W2b, NDC)]
        else:
            pairs = [("ew1", d.ew1, d.ew1b, NEXP), ("ew2", d.ew2, d.ew2b, NEXP)]
        for name, src, dst, n in pairs:
            for oc in range(n):
                ins = SS.raw.dma_start(out=dst[oc], in_=src[oc])
                pp.inc(ins, 16)
                env.tok_pp[(name, oc)] = (pp, pp.n)
    _mk_section(env, block, "gpsimd", ssec)



def phase1(nc, env, d, block, cfg):
    ctx = env.ctx
    e = ctx.enter_context
    seq = cfg.seq
    NT = seq // 128
    NT5 = seq // 512
    NG = seq // 1024
    sec = lambda name, body: _mk_section(env, block, name, body)

    pstack = ExitStack()
    PB = [pstack.enter_context(nc.psum_tensor(f"p1PB{i}", [128, 1024], F32)) for i in range(4)]

    kbstack = ExitStack()
    ekb = kbstack.enter_context
    K_sb = ekb(nc.sbuf_tensor("K_sb", [128, NDC * seq], BF16))
    rkT = ekb(nc.sbuf_tensor("rkT", [128, NT * NH], F32))
    negsh = ekb(nc.sbuf_tensor("negsh", [128, NH], F32))
    shv16 = ekb(nc.sbuf_tensor("shv16", [NST, 1], F32))
    lsrow = ekb(nc.sbuf_tensor("lsrow", [1, NH], F32))

    astack = ExitStack()
    ea = astack.enter_context
    kqw = ea(nc.sbuf_tensor("kqw", [128, NDC * 3 * D], BF16))
    kqw3 = kqw[:].rearrange("p (dd f) -> p dd f", dd=NDC)
    xin = ea(nc.sbuf_tensor("xin", [128, 2 * 1024], F32))
    junk = ea(nc.sbuf_tensor("junk1", [128, 1024], F32))
    mvr = ea(nc.sbuf_tensor("mvr", [128, 8], F32))
    xn_tm = ea(nc.sbuf_tensor("xn_tm", [128, 2 * 1024], BF16))
    xnfm = ea(nc.sbuf_tensor("xnfm", [128, 2 * NDC * 512], BF16))
    vtile = ea(nc.sbuf_tensor("vtile", [128, 2 * NH * 65], BF16))

    trps = [PB[0].bitcast(BF16)[:, 0:128], PB[0].bitcast(BF16)[:, 1024:1152]]
    kps = [PB[1][:, 0:512], PB[1][:, 512:1024]]
    vps = [PB[2][:, 0:512], PB[2][:, 512:1024]]

    def gsec_w(G):
        G.wait(env.tok_cpool)
        env.tok_kqw = G.emit(G.dma_start(
            out=kqw3, in_=d.w_qkv.rearrange("(dd p) f -> p dd f", p=128)), dma=True)
    sec("gpsimd", gsec_w)

    tok_ln = {}
    tok_xin_free = {}
    tok_tr = {}
    tok_trev = {}
    tok_kmm = {}
    tok_kev = {}
    tok_vmm = {}
    tok_vev = {}
    tok_vst = {}
    tok_xn0st = {}
    tok_slab_done = {}

    for ts in range(NT5):
        fmb = ts % 2
        xnv = xnfm[:, fmb * NDC * 512:(fmb + 1) * NDC * 512]
        for sub in range(4):
            tt = ts * 4 + sub
            bx = tt % 2

            def ysec(Y, tt=tt, bx=bx):
                Y.wait(tok_xin_free.get(tt - 2))
                env.tok_x0 = Y.emit(Y.dma_start(
                    out=xin[:, bx * 1024:(bx + 1) * 1024],
                    in_=d.x0[tt * 128:(tt + 1) * 128, :]), dma=True)
            sec("sync", ysec)
            tok_x0 = env.tok_x0

            def vsec(V, tt=tt, bx=bx, tok_x0=tok_x0):
                V.wait(tok_x0)
                xv = xin[:, bx * 1024:(bx + 1) * 1024]
                V.emit(V.tensor_reduce(mvr[:, 0:1], xv, AX.X, ALU.add))
                V.drain()
                V.emit(V.tensor_scalar_mul(mvr[:, 1:2], mvr[:, 0:1], 1.0 / 1024.0))
                V.drain()
                env.tok_v = V.emit(V.scalar_tensor_tensor(
                    junk[:], xv, mvr[:, 1:2], xv, ALU.subtract, ALU.mult,
                    accum_out=mvr[:, 2:3]))
            sec("vector", vsec)
            tok_stats = env.tok_v

            def ssec(SS, tok_stats=tok_stats):
                SS.wait(env.tok_cpool, tok_stats)
                SS.emit(SS.activation(out=mvr[:, 3:4], in_=mvr[:, 2:3], func=AF.Ln,
                                      scale=1.0 / 1024.0, bias=LN_EPS))
                SS.drain()
                env.tok_s = SS.emit(SS.activation(out=mvr[:, 4:5], in_=mvr[:, 3:4],
                                                  func=AF.Exp, scale=-0.5))
            sec("scalar", ssec)
            tok_rstd = env.tok_s

            def vsec2(V, tt=tt, bx=bx, tok_rstd=tok_rstd):
                V.wait(tok_rstd)
                V.wait(tok_tr.get((tt - 2, NDC - 1)))
                xv = xin[:, bx * 1024:(bx + 1) * 1024]
                env.tok_v = V.emit(V.tensor_scalar(
                    xn_tm[:, bx * 1024:(bx + 1) * 1024], xv, mvr[:, 1:2], mvr[:, 4:5],
                    ALU.subtract, ALU.mult))
                tok_xin_free[tt] = env.tok_v
            sec("vector", vsec2)
            tok_ln[tt] = env.tok_v

            for dc in range(NDC):
                def tsec(T, tt=tt, bx=bx, dc=dc, ts=ts):
                    T.wait(tok_ln[tt])
                    T.wait(tok_trev.get((tt, dc - 2), tok_trev.get((tt - 1, NDC - 2 + dc), None) if dc < 2 else None))
                    T.wait(tok_slab_done.get(ts - 2))
                    tok_tr[(tt, dc)] = T.emit(T.transpose(
                        trps[dc % 2], xn_tm[:, bx * 1024 + dc * 128: bx * 1024 + (dc + 1) * 128],
                        env.ident_bf[:]))
                sec("tensor", tsec)

                def vsec3(V, tt=tt, dc=dc, sub=sub, xnv=xnv):
                    V.wait(tok_tr[(tt, dc)])
                    dst = xnv[:, dc * 512 + sub * 128: dc * 512 + sub * 128 + 128]
                    tok_trev[(tt, dc)] = V.emit(V.tensor_copy(dst, trps[dc % 2]))
                sec("vector", vsec3)

        for c in range(NDC):
            def tsec(T, ts=ts, c=c, xnv=xnv):
                T.wait(env.tok_kqw)
                T.wait(tok_trev[(ts * 4 + 3, NDC - 1)])
                T.wait(tok_kev.get((ts, c - 2)) if c >= 2 else tok_kev.get((ts - 1, NDC - 2 + c)))
                for dd in range(NDC):
                    ins = T.matmul(kps[c % 2],
                                   kqw3[:, dd, D + c * 128: D + (c + 1) * 128],
                                   xnv[:, dd * 512:(dd + 1) * 512],
                                   start=(dd == 0), stop=(dd == NDC - 1))
                tok_kmm[(ts, c)] = T.emit(ins)
            sec("tensor", tsec)

            def vsec(V, ts=ts, c=c):
                V.wait(tok_kmm[(ts, c)])
                tok_kev[(ts, c)] = V.emit(V.tensor_copy(
                    K_sb[:, c * seq + ts * 512: c * seq + (ts + 1) * 512], kps[c % 2]))
            sec("vector", vsec)

        for sub in range(4):
            tt = ts * 4 + sub
            vb = tt % 2
            vt = vtile[:, vb * NH * 65:(vb + 1) * NH * 65]
            vt3 = vt.rearrange("p (h c) -> p h c", h=NH)
            for fc in range(2):
                def tsec(T, ts=ts, sub=sub, fc=fc, xnv=xnv):
                    T.wait(tok_vev.get((ts * 4 + sub - 1, fc)))
                    for dd in range(NDC):
                        ins = T.matmul(vps[fc],
                                       xnv[:, dd * 512 + sub * 128: dd * 512 + sub * 128 + 128],
                                       kqw3[:, dd, 2 * D + fc * 512: 2 * D + (fc + 1) * 512],
                                       start=(dd == 0), stop=(dd == NDC - 1))
                    tok_vmm[(tt, fc)] = T.emit(ins)
                sec("tensor", tsec)

            def vsec(V, tt=tt, vt3=vt3):
                V.wait(tok_vst.get(tt - 2))
                V.emit(V.memset(vt3[:, :, 64:65], 1.0))
                for fc in range(2):
                    V.wait(tok_vmm[(tt, fc)])
                    src = vps[fc].rearrange("p (h dh) -> p h dh", h=8)
                    tok_vev[(tt, fc)] = V.emit(V.tensor_copy(vt3[:, fc * 8:(fc + 1) * 8, 0:64], src))
            sec("vector", vsec)

            def ysec(Y, tt=tt, vt=vt):
                Y.wait(tok_vev[(tt, 1)])
                tok_vst[tt] = Y.emit(Y.dma_start(
                    out=d.vaug_d.transpose([1, 0, 2])[:, :, tt * 130:(tt + 1) * 130],
                    in_=vt.rearrange("p (pr c) -> p pr c", pr=NDC)), dma=True)
            sec("sync", ysec)

        def ysec2(Y, ts=ts, xnv=xnv):
            Y.wait(tok_kev[(ts, NDC - 1)])
            Y.wait(tok_vmm[(ts * 4 + 3, 1)])
            tok_xn0st[ts] = Y.emit(Y.dma_start(
                out=d.xn0_d.rearrange("(dc p) q -> p dc q", p=128)[:, :, ts * 512:(ts + 1) * 512],
                in_=xnv.rearrange("p (dc q) -> p dc q", dc=NDC)), dma=True)
            tok_slab_done[ts] = tok_xn0st[ts]
        sec("sync", ysec2)

    sq1b = ea(nc.sbuf_tensor("sq1b", [128, 2 * 512], F32))
    nrm1b = ea(nc.sbuf_tensor("nrm1b", [NST, 512], F32))
    rk1b = ea(nc.sbuf_tensor("rk1b", [NST, 512], F32))

    ssps = PB[3][0:16, 0:512]
    trp16 = PB[3][:, 512:640]
    nshps = PB[3][0:128, 512 + 128:512 + 144]

    def ysec(Y):
        Y.emit(Y.dma_start(out=shv16[:], in_=d.logit_scale.rearrange("(a b) -> a b", b=1)), dma=True)
        env.tok_ls = Y.emit(Y.dma_start(out=lsrow[:], in_=d.logit_scale.rearrange("(b a) -> b a", b=1)), dma=True)
    sec("sync", ysec)
    tok_ls = env.tok_ls

    def vsec(V, tok_ls=tok_ls):
        V.wait(tok_ls)
        V.emit(V.tensor_scalar_min(shv16[:], shv16[:], MAXLOG))
        env.tok_v = V.emit(V.tensor_scalar_min(lsrow[:], lsrow[:], MAXLOG))
        V.drain()
    sec("vector", vsec)
    tok_min = env.tok_v

    def ssec(SS, tok_min=tok_min):
        SS.wait(tok_min)
        SS.emit(SS.activation(out=shv16[:], in_=shv16[:], func=AF.Exp))
        env.tok_s = SS.emit(SS.activation(out=lsrow[:], in_=lsrow[:], func=AF.Exp))
        SS.drain()
    sec("scalar", ssec)
    tok_exp = env.tok_s

    def vsec2(V, tok_exp=tok_exp):
        V.wait(tok_exp)
        env.tok_v = V.emit(V.tensor_scalar_mul(lsrow[:], lsrow[:], -1.0))
        V.drain()
    sec("vector", vsec2)
    tok_neg = env.tok_v

    def tsec(T, tok_neg=tok_neg):
        T.wait(tok_neg, env.tok_consts_loaded)
        env.tok_t = T.emit(T.matmul(nshps, env.ones1[:], lsrow[:], start=True, stop=True))
    sec("tensor", tsec)
    tok_nshmm = env.tok_t

    def vsec3(V, tok_nshmm=tok_nshmm):
        V.wait(tok_nshmm)
        env.tok_v = V.emit(V.tensor_copy(negsh[:], nshps))
    sec("vector", vsec3)
    env.tok_negsh = env.tok_v

    hm3 = env.hmask[:].rearrange("p (c h) -> p c h", c=NDC)
    tok_sq = {}
    tok_ssmm = {}
    env.tok_trk_ev = {}
    env.tok_nrm_hist = {}
    for ts in range(NT5):
        for c in range(NDC):
            def ssec(SS, ts=ts, c=c):
                SS.wait(tok_kev[(ts, c)])
                SS.wait(tok_ssmm.get((ts, c - 2)) if c >= 2 else tok_ssmm.get((ts - 1, NDC - 2 + c)))
                tok_sq[(ts, c)] = SS.emit(SS.activation(
                    out=sq1b[:, (c % 2) * 512:(c % 2 + 1) * 512],
                    in_=K_sb[:, c * seq + ts * 512: c * seq + (ts + 1) * 512],
                    func=AF.Square))
            sec("scalar", ssec)

            def tsec(T, ts=ts, c=c):
                T.wait(tok_sq[(ts, c)])
                if c == 0 and ts > 0:
                    T.wait(env.tok_nrm_hist[ts - 1])
                tok_ssmm[(ts, c)] = T.emit(T.matmul(
                    ssps, hm3[:, c, :], sq1b[:, (c % 2) * 512:(c % 2 + 1) * 512],
                    start=(c == 0), stop=(c == NDC - 1)))
            sec("tensor", tsec)

        def ssec2(SS, ts=ts):
            SS.wait(tok_ssmm[(ts, NDC - 1)])
            if cfg.dbg and ts == 0:
                SS.emit(SS.activation(out=rk1b[:], in_=ssps, func=AF.Identity))
            SS.emit(SS.activation(out=nrm1b[:], in_=ssps, func=AF.Ln))
            SS.drain()
            env.tok_s = SS.emit(SS.activation(out=nrm1b[:], in_=nrm1b[:], func=AF.Exp, scale=0.5))
        sec("scalar", ssec2)
        if cfg.dbg and ts == 0:
            def ydbg1b(Y):
                Y.wait(env.tok_s)
                for cc in range(NDC):
                    Y.emit(Y.dma_start(out=d.k_d[:, cc * 512:(cc + 1) * 512],
                                       in_=K_sb[:, cc * seq + 0:cc * seq + 512]), dma=True)
                Y.emit(Y.dma_start(out=d.ss_d[:, :], in_=rk1b[:]), dma=True)
                env.tok_y1b = Y.emit(Y.dma_start(out=d.sq_d[:, :], in_=sq1b[:, 512:1024]), dma=True)
            sec("sync", ydbg1b)
        tok_nrm = env.tok_s
        env.tok_nrm_hist[ts] = tok_nrm

        def vsec4(V, ts=ts, tok_nrm=tok_nrm):
            V.wait(tok_nrm)
            if cfg.dbg and ts == 0:
                V.wait(getattr(env, "tok_y1b", None))
            V.emit(V.tensor_scalar_max(nrm1b[:], nrm1b[:], 1e-12))
            V.drain()
            V.emit(V.reciprocal(rk1b[:], nrm1b[:]))
            V.drain()
            env.tok_v = V.emit(V.tensor_scalar(rk1b[:], rk1b[:], shv16[:, 0:1], None, ALU.mult))
        sec("vector", vsec4)
        if cfg.dbg and ts == 0:
            def ydbg1c(Y):
                Y.wait(env.tok_v)
                Y.emit(Y.dma_start(out=d.rk1b_d[:, :], in_=rk1b[:]), dma=True)
            sec("sync", ydbg1c)
        tok_rk = env.tok_v

        for i in range(4):
            def tsec2(T, ts=ts, i=i, tok_rk=tok_rk):
                T.wait(tok_rk)
                T.wait(env.tok_trk_ev.get((ts, i - 1)) if i >= 1 else env.tok_trk_ev.get((ts - 1, 3)))
                if ts == 0 and i == 0:
                    T.wait(env.tok_negsh)
                env.tok_t = T.emit(T.transpose(trp16[0:128, 0:16],
                                               rk1b[0:16, i * 128:(i + 1) * 128], env.ident_f32[0:16, 0:16]))
            sec("tensor", tsec2)
            tok_trk = env.tok_t

            def vsec5(V, ts=ts, i=i, tok_trk=tok_trk):
                V.wait(tok_trk)
                kt = ts * 4 + i
                env.tok_v = V.emit(V.tensor_copy(rkT[:, kt * NH:(kt + 1) * NH], trp16[0:128, 0:16]))
                env.tok_trk_ev[(ts, i)] = env.tok_v
            sec("vector", vsec5)
    env.tok_rkT_done = env.tok_v


    barrier(env, block)
    astack.close()
    NKT = seq // 128
    cstack = ExitStack()
    ec = cstack.enter_context
    wq = ec(nc.sbuf_tensor("wq", [128, NDC * D], BF16))
    wq3 = wq[:].rearrange("p (dd f) -> p dd f", dd=NDC)
    wow = ec(nc.sbuf_tensor("wow", [128, NDC * D], BF16))
    wow3 = wow[:].rearrange("p (dd f) -> p dd f", dd=NDC)
    xnsl = ec(nc.sbuf_tensor("xnsl", [128, NDC * 1024], BF16))
    Qn = ec(nc.sbuf_tensor("Qn", [128, NDC * 1024], BF16))
    sqq = ec(nc.sbuf_tensor("sqq", [128, 2 * 1024], F32))
    rq2 = ec(nc.sbuf_tensor("rq2", [2, 2 * 1024], F32))
    Pt = ec(nc.sbuf_tensor("Pt", [128, 2 * 2 * 1024], BF16))
    vp = ec(nc.sbuf_tensor("vp", [128, 2 * NKT * 130], BF16))
    tmppv = ec(nc.sbuf_tensor("tmppv", [128, 1024], F32))
    rsumP = sqq[0:1, 0:2048]
    attnfm = ec(nc.sbuf_tensor("attnfm", [128, NDC * 1024], BF16))
    s0st = ec(nc.sbuf_tensor("s0st", [128, 2 * 1024], F32))

    def gsec_wq(G):
        G.emit(G.dma_start(out=wq3, in_=d.w_qkv.rearrange("(dd p) f -> p dd f", p=128)[:, :, 0:D]), dma=True)
        env.tok_wq = G.emit(G.dma_start(out=wow3, in_=d.w_out.rearrange("(dd p) f -> p dd f", p=128)), dma=True)
    sec("gpsimd", gsec_wq)
    prepass_weights(nc, env, d, block, "p1")

    tok_xnsl = {}
    tok_q = {}
    tok_qsq = {}
    tok_qev = {}
    tok_qssmm = {}
    tok_qnrm = {}
    tok_bc_hist = {}
    tok_qapply = {}
    tok_S = {}
    tok_E = {}
    tok_PV = {}
    tok_tail = {}
    tok_sums = {}
    tok_rln = {}
    tok_dmm = {}
    tok_vld = {}
    tok_proj = {}
    tok_projev = {}
    tok_projst = {}
    tok_div = {}

    psQs = {0: PB[0], 1: PB[2]}

    for g in range(NG):
        def ysec(Y, g=g):
            Y.wait(tok_xn0st[NT5 - 1])
            if g >= 1:
                Y.wait(tok_qapply.get((g - 1, NDC - 1)))
            tok_xnsl[g] = Y.emit(Y.dma_start(
                out=xnsl[:].rearrange("p (dc q) -> p dc q", dc=NDC),
                in_=d.xn0_d.rearrange("(dc p) q -> p dc q", p=128)[:, :, g * 1024:(g + 1) * 1024]), dma=True)
        sec("sync", ysec)

        for c in range(NDC):
            def tsec(T, g=g, c=c):
                T.wait(tok_xnsl[g], env.tok_wq)
                if c == 0:
                    if g == 0:
                        T.wait(env.tok_rkT_done)
                    else:
                        T.wait(tok_div[(g - 1, NH - 1)])
                        T.wait(tok_projev[(g - 1, NDC - 1)])
                T.wait(tok_qev.get((g, c - 2)) if c >= 2 else tok_qev.get((g - 1, NDC - 2 + c)))
                T.wait(tok_qsq.get((g, c - 2)) if c >= 2 else tok_qsq.get((g - 1, NDC - 2 + c)))
                for hf in range(2):
                    for dd in range(NDC):
                        ins = T.matmul(psQs[c % 2][:, hf * 512:(hf + 1) * 512],
                                       wq3[:, dd, c * 128:(c + 1) * 128],
                                       xnsl[:, dd * 1024 + hf * 512: dd * 1024 + (hf + 1) * 512],
                                       start=(dd == 0), stop=(dd == NDC - 1))
                tok_q[(g, c)] = T.emit(ins)
            sec("tensor", tsec)

            def ssec(SS, g=g, c=c):
                SS.wait(tok_q[(g, c)])
                tok_qsq[(g, c)] = SS.emit(SS.activation(
                    out=sqq[:, (c % 2) * 1024:(c % 2 + 1) * 1024], in_=psQs[c % 2][:], func=AF.Square))
            sec("scalar", ssec)

            def vsec(V, g=g, c=c):
                V.wait(tok_qsq[(g, c)])
                tok_qev[(g, c)] = V.emit(V.tensor_copy(Qn[:, c * 1024:(c + 1) * 1024], psQs[c % 2][:]))
            sec("vector", vsec)

            def tsec2(T, g=g, c=c):
                T.wait(tok_qsq[(g, c)])
                T.wait(tok_qnrm.get((g, c - 1), tok_qnrm.get((g - 1, NDC - 1)) if c == 0 else None))
                for hf in range(2):
                    ins = T.matmul(PB[3][0:2, hf * 512:(hf + 1) * 512], env.mask2[:],
                                   sqq[:, (c % 2) * 1024 + hf * 512:(c % 2) * 1024 + (hf + 1) * 512],
                                   start=True, stop=True)
                tok_qssmm[(g, c)] = T.emit(ins)
            sec("tensor", tsec2)

            def ssec2(SS, g=g, c=c):
                SS.wait(tok_qssmm[(g, c)])
                SS.wait(tok_bc_hist.get((g, c - 2)) if c >= 2 else tok_bc_hist.get((g - 1, NDC - 2 + c)))
                rv = rq2[0:2, (c % 2) * 1024:(c % 2 + 1) * 1024]
                SS.emit(SS.activation(out=rv, in_=PB[3][0:2, 0:1024], func=AF.Ln))
                SS.drain()
                env.tok_s = SS.emit(SS.activation(out=rv, in_=rv, func=AF.Exp, scale=0.5))
            sec("scalar", ssec2)
            tok_qn = env.tok_s

            def vsec2(V, g=g, c=c, tok_qn=tok_qn):
                V.wait(tok_qn)
                rv = rq2[0:2, (c % 2) * 1024:(c % 2 + 1) * 1024]
                V.emit(V.tensor_scalar_max(rv, rv, 1e-12))
                V.drain()
                env.tok_v = V.emit(V.reciprocal(rv, rv))
                V.drain()
            sec("vector", vsec2)
            tok_qnrm[(g, c)] = env.tok_v

            def tsec3(T, g=g, c=c):
                T.wait(tok_qnrm[(g, c)])
                T.wait(tok_qapply.get((g, c - 1), tok_qapply.get((g - 1, NDC - 1)) if c == 0 else None))
                for hf in range(2):
                    ins = T.matmul(PB[1][:, hf * 512:(hf + 1) * 512], env.sel2[:],
                                   rq2[0:2, (c % 2) * 1024 + hf * 512: (c % 2) * 1024 + (hf + 1) * 512],
                                   start=True, stop=True)
                env.tok_t = T.emit(ins)
            sec("tensor", tsec3)
            tok_bc = env.tok_t
            tok_bc_hist[(g, c)] = tok_bc

            def vsec3(V, g=g, c=c, tok_bc=tok_bc):
                V.wait(tok_bc)
                qv = Qn[:, c * 1024:(c + 1) * 1024]
                tok_qapply[(g, c)] = V.emit(V.tensor_tensor(qv, qv, PB[1][:], ALU.mult))
            sec("vector", vsec3)

        if cfg.dbg and g == 0:
            def ydbg(Y):
                Y.wait(tok_qapply[(0, NDC - 1)], env.tok_rkT_done)
                Y.emit(Y.dma_start(out=d.qn_d[:, :], in_=Qn[:]), dma=True)
                Y.emit(Y.dma_start(out=d.rk_d[:, 0:NT * NH], in_=rkT[:]), dma=True)
            sec("sync", ydbg)

        PS = [PB[0], PB[1], PB[2]]
        psPV = PB[3][0:65, :]
        LAGP = 3
        NPASS = NDC * 2

        def emit_divide(i, erd, g=g):
            ul = (i + 1) * NKT - 1
            t = PS[ul % 3] if i != NPASS - 1 else PS[(ul - 1) % 3]
            p_, qh_ = divmod(i, 2)

            def ssecR(SS, g=g, i=i):
                SS.wait(tok_sums[(g, i)])
                rs = rsumP[0:1, (i % 2) * 1024:(i % 2 + 1) * 1024]
                SS.emit(SS.activation(out=rs, in_=rs, func=AF.Ln))
                tok_rln[(g, i)] = SS.emit(SS.activation(out=rs, in_=rs, func=AF.Exp, scale=-1.0))
            sec("scalar", ssecR)

            def tsecD(T, g=g, i=i, t=t, erd=erd):
                T.wait(tok_rln[(g, i)])
                T.wait(tok_E[(g, erd)])
                for hh in range(2):
                    ins = T.matmul(t[:, hh * 512:(hh + 1) * 512], env.ones1[:],
                                   rsumP[0:1, (i % 2) * 1024 + hh * 512:
                                         (i % 2) * 1024 + (hh + 1) * 512],
                                   start=True, stop=True)
                tok_dmm[(g, i)] = T.emit(ins)
            sec("tensor", tsecD)

            def vsecD(V, g=g, i=i, t=t, p_=p_, qh_=qh_):
                V.wait(tok_dmm[(g, i)])
                for hh in range(2):
                    ins = V.tensor_tensor(
                        attnfm[hh * 64:(hh + 1) * 64,
                               p_ * 1024 + qh_ * 512: p_ * 1024 + qh_ * 512 + 512],
                        tmppv[0:64, hh * 512:(hh + 1) * 512],
                        t[hh * 64:(hh + 1) * 64, hh * 512:(hh + 1) * 512], ALU.mult)
                tok_div[(g, i)] = V.emit(ins)
            sec("vector", vsecD)

        for p in range(NDC):
            vslab = vp[:, (p % 2) * NKT * 130:(p % 2 + 1) * NKT * 130]

            def ysec2(Y, g=g, p=p, vslab=vslab):
                prev = tok_PV.get((g, (2 * (p - 2) + 2) * NKT - 1),
                                  tok_PV.get((g - 1, (2 * (NDC - 2 + p) + 2) * NKT - 1)))
                Y.wait(prev)
                if g == 0:
                    Y.wait(tok_vst[NT - 1])
                tok_vld[(g, p)] = Y.emit(Y.dma_start(out=vslab, in_=d.vaug_d[p, :, :]), dma=True)
            sec("sync", ysec2)

            for qh in range(2):
                i = 2 * p + qh
                ub = i * NKT

                def emit_pv(u, g=g, p=p, i=i, vslab=vslab, ub=ub):
                    kt = u - ub

                    def tsecPV(T, g=g, p=p, i=i, u=u, kt=kt, vslab=vslab):
                        T.wait(tok_E[(g, u)])
                        if kt == 0:
                            T.wait(tok_vld[(g, p)])
                            T.wait(tok_tail.get((g, i - 1),
                                                tok_tail.get((g - 1, NPASS - 1)) if i == 0 else None))
                        for hh in range(2):
                            ins = T.matmul(
                                psPV[:, hh * 512:(hh + 1) * 512],
                                vslab[:, kt * 130 + hh * 65: kt * 130 + (hh + 1) * 65],
                                Pt[:, ((u % 4) * 2 + hh) * 512:((u % 4) * 2 + hh + 1) * 512],
                                start=(kt == 0), stop=(kt == NKT - 1))
                        tok_PV[(g, u)] = T.emit(ins)
                    sec("tensor", tsecPV)

                for kt in range(NKT):
                    u = ub + kt
                    if kt == 1 and i >= 1:
                        emit_divide(i - 1, ub)

                    def tsecS(T, g=g, p=p, qh=qh, i=i, u=u, kt=kt):
                        if u == 0:
                            T.wait(tok_qapply[(g, NDC - 1)])
                        if u >= 2:
                            T.wait(tok_E[(g, u - 2)])
                        if kt == 2 and i >= 1:
                            T.wait(tok_div[(g, i - 1)])
                        for hh in range(2):
                            ins = T.matmul(
                                PS[u % 3][:, hh * 512:(hh + 1) * 512],
                                K_sb[hh * 64:(hh + 1) * 64, p * seq + kt * 128:p * seq + (kt + 1) * 128],
                                Qn[hh * 64:(hh + 1) * 64, p * 1024 + qh * 512: p * 1024 + qh * 512 + 512],
                                start=True, stop=True)
                        tok_S[(g, u)] = T.emit(ins)
                    sec("tensor", tsecS)

                    if kt >= LAGP:
                        emit_pv(u - LAGP)

                    def ssecE(SS, g=g, p=p, u=u, kt=kt):
                        SS.wait(tok_S[(g, u)])
                        if u >= 4:
                            SS.wait(tok_PV[(g, u - 4)])
                        for hh in range(2):
                            hx = 2 * p + hh
                            ins = SS.activation(
                                out=Pt[:, ((u % 4) * 2 + hh) * 512:((u % 4) * 2 + hh + 1) * 512],
                                in_=PS[u % 3][:, hh * 512:(hh + 1) * 512],
                                func=AF.Exp,
                                scale=rkT[:, kt * NH + hx: kt * NH + hx + 1],
                                bias=negsh[:, hx:hx + 1])
                        tok_E[(g, u)] = SS.emit(ins)
                    sec("scalar", ssecE)

                for u in range(ub + NKT - LAGP, ub + NKT):
                    emit_pv(u)

                def vsecT(V, g=g, i=i, ub=ub):
                    V.wait(tok_PV[(g, ub + NKT - 1)])
                    tok_sums[(g, i)] = V.emit(V.tensor_copy(
                        rsumP[0:1, (i % 2) * 1024:(i % 2 + 1) * 1024], psPV[64:65, :]))
                    tok_tail[(g, i)] = V.emit(V.tensor_copy(
                        tmppv[0:64, :], psPV[0:64, :]))
                sec("vector", vsecT)

        emit_divide(NPASS - 1, NPASS * NKT - 1)

        for oc in range(NDC):
            def tsecP(T, g=g, oc=oc):
                T.wait(tok_div[(g, NH - 1)], env.tok_wq)
                T.wait(tok_projev.get((g, oc - 2)))
                for hf in range(2):
                    for dd in range(NDC):
                        ins = T.matmul(psQs[oc % 2][:, hf * 512:(hf + 1) * 512],
                                       wow3[:, dd, oc * 128:(oc + 1) * 128],
                                       attnfm[:, dd * 1024 + hf * 512: dd * 1024 + (hf + 1) * 512],
                                       start=(dd == 0), stop=(dd == NDC - 1))
                tok_proj[(g, oc)] = T.emit(ins)
            sec("tensor", tsecP)

            def ssecP(SS, g=g, oc=oc):
                SS.wait(tok_proj[(g, oc)])
                SS.wait(tok_projst.get((g, oc - 2)))
                tok_projev[(g, oc)] = SS.emit(SS.activation(
                    out=s0st[:, (oc % 2) * 1024:(oc % 2 + 1) * 1024], in_=psQs[oc % 2][:],
                    func=AF.Identity, bias=env.bout_sb[:, oc:oc + 1]))
            sec("scalar", ssecP)

            def ysecP(Y, g=g, oc=oc):
                Y.wait(tok_projev[(g, oc)])
                tok_projst[(g, oc)] = Y.emit(Y.dma_start(
                    out=d.s0_d[oc * 128:(oc + 1) * 128, g * 1024:(g + 1) * 1024],
                    in_=s0st[:, (oc % 2) * 1024:(oc % 2 + 1) * 1024]), dma=True)
            sec("sync", ysecP)
    env.tok_attn_done = tok_projst[(NG - 1, NDC - 1)]

    barrier(env, block)
    cstack.close()
    kbstack.close()
    dstack = ExitStack()
    e = dstack.enter_context
    fw12st = e(nc.sbuf_tensor("fw12st", [128, 2 * 2 * NDC * 128], BF16))
    fw3st = e(nc.sbuf_tensor("fw3st", [128, 2 * NFC * 128], BF16))
    fhst = e(nc.sbuf_tensor("fhst", [128, 2 * NFC * 128], BF16))
    xnsl2 = e(nc.sbuf_tensor("xnsl2", [128, 2 * NDC * 512], BF16))
    ab_sb = e(nc.sbuf_tensor("ab_sb", [128, 4 * 512], BF16))
    h_sb = e(nc.sbuf_tensor("h_sb", [128, NFC * 512], BF16))
    h1o_sb = e(nc.sbuf_tensor("h1o_sb", [128, NFC * 512], BF16))
    g0_sb = e(nc.sbuf_tensor("g0_sb", [128, 2 * 512], F32))
    s0sl = e(nc.sbuf_tensor("s0sl", [128, 2 * 512], F32))
    x0tm = e(nc.sbuf_tensor("x0tm", [128, 2 * 512], F32))
    x0fm = e(nc.sbuf_tensor("x0fm", [128, 2 * 512], F32))
    out1_sb = e(nc.sbuf_tensor("out1_sb", [128, NDC * 512], F32))
    sqd = e(nc.sbuf_tensor("sqd", [128, 2 * 512], F32))
    lnrow = e(nc.sbuf_tensor("lnrow", [1, 6 * 512], F32))
    xn1c = e(nc.sbuf_tensor("xn1c", [128, 2 * 512], BF16))
    xn1tm_st = e(nc.sbuf_tensor("xn1tm_st", [128, 4 * 1024], BF16))
    out1tm_st = e(nc.sbuf_tensor("out1tm_st", [128, 4 * 1024], F32))

    tok_fw12 = {}
    tok_fw3 = {}
    tok_xn2 = {}
    tok_ab = {}
    tok_abev = {}
    tok_h = {}
    tok_fh = {}
    tok_h1mm = {}
    tok_h1ev = {}
    tok_h2mm = {}
    tok_h2ev = {}
    tok_w3 = {}
    tok_g0 = {}
    tok_s0l = {}
    tok_x0l = {}
    tok_x0tr = {}
    tok_x0ev = {}
    tok_cmb = {}
    tok_lnm = {}
    tok_sqf = {}
    tok_lns = {}
    tok_xn1 = {}
    tok_tr1 = {}
    tok_trev1 = {}
    tok_st1d = {}
    env.tok_lbc_hist = {}
    env.tok_ffn_last = None

    for ft in range(NT5):
        fb = ft % 2
        xnv = xnsl2[:, fb * NDC * 512:(fb + 1) * NDC * 512]

        def ysec(Y, ft=ft, xnv=xnv):
            Y.wait(tok_xn0st[NT5 - 1])
            Y.wait(tok_ab.get((ft - 2, NFC - 1)))
            tok_xn2[ft] = Y.emit(Y.dma_start(
                out=xnv.rearrange("p (dc q) -> p dc q", dc=NDC),
                in_=d.xn0_d.rearrange("(dc p) q -> p dc q", p=128)[:, :, ft * 512:(ft + 1) * 512]), dma=True)
        sec("sync", ysec)

        for oc in range(NFC):
            pb = PB[oc % 2]

            def gsecA(G, ft=ft, oc=oc):
                G.wait(tok_ab.get((ft, oc - 2)) if oc >= 2 else tok_ab.get((ft - 1, NFC - 2 + oc)))
                G.wait(env.tok_pp[("fw1", oc)], env.tok_pp[("fw2", oc)])
                sl = fw12st[:, (oc % 2) * 2 * NDC * 128:((oc % 2) + 1) * 2 * NDC * 128]
                G.emit(G.dma_start(out=sl[:, 0:NDC * 128], in_=d.fw1b[oc]), dma=True)
                tok_fw12[(ft, oc)] = G.emit(G.dma_start(
                    out=sl[:, NDC * 128:2 * NDC * 128], in_=d.fw2b[oc]), dma=True)
            sec("gpsimd", gsecA)

            def tsec(T, ft=ft, oc=oc, pb=pb, xnv=xnv):
                T.wait(tok_xn2[ft], tok_fw12[(ft, oc)])
                if ft == 0 and oc == 0:
                    T.wait(tok_div[(NG - 1, NH - 1)], tok_projev[(NG - 1, NDC - 1)])
                if oc == 0 and ft >= 1:
                    T.wait(tok_xn1[(ft - 1, NDC - 1)])
                T.wait(tok_abev.get((ft, oc - 2)) if oc >= 2 else tok_abev.get((ft - 1, NFC - 2 + oc)))
                sl = fw12st[:, (oc % 2) * 2 * NDC * 128:((oc % 2) + 1) * 2 * NDC * 128]
                for dd in range(NDC):
                    T.matmul(pb[:, 0:512], sl[:, dd * 128:(dd + 1) * 128],
                             xnv[:, dd * 512:(dd + 1) * 512],
                             start=(dd == 0), stop=(dd == NDC - 1))
                for dd in range(NDC):
                    ins = T.matmul(pb[:, 512:1024], sl[:, NDC * 128 + dd * 128:NDC * 128 + (dd + 1) * 128],
                                   xnv[:, dd * 512:(dd + 1) * 512],
                                   start=(dd == 0), stop=(dd == NDC - 1))
                tok_ab[(ft, oc)] = T.emit(ins)
            sec("tensor", tsec)

            def ssec(SS, ft=ft, oc=oc, pb=pb):
                SS.wait(tok_ab[(ft, oc)])
                SS.wait(tok_h.get((ft, oc - 2)) if oc >= 2 else tok_h.get((ft - 1, NFC - 2 + oc)))
                SS.emit(SS.activation(out=ab_sb[:, (oc % 2) * 1024:(oc % 2) * 1024 + 512],
                                      in_=pb[:, 0:512], func=AF.Silu,
                                      scale=env.s1_sb[:, oc:oc + 1]))
                tok_abev[(ft, oc)] = SS.emit(SS.activation(
                    out=ab_sb[:, (oc % 2) * 1024 + 512:(oc % 2) * 1024 + 1024],
                    in_=pb[:, 512:1024], func=AF.Identity, scale=env.s2_sb[:, oc:oc + 1]))
            sec("scalar", ssec)

            def vsec(V, ft=ft, oc=oc):
                V.wait(tok_abev[(ft, oc)])
                tok_h[(ft, oc)] = V.emit(V.tensor_tensor(
                    h_sb[:, oc * 512:(oc + 1) * 512],
                    ab_sb[:, (oc % 2) * 1024:(oc % 2) * 1024 + 512],
                    ab_sb[:, (oc % 2) * 1024 + 512:(oc % 2) * 1024 + 1024], ALU.mult))
            sec("vector", vsec)

        for oc in range(NFC):
            def gsec(G, ft=ft, oc=oc):
                G.wait(tok_h1mm.get((ft, oc - 2)) if oc >= 2 else tok_h2mm.get((ft - 1, NFC - 2 + oc)))
                G.wait(env.tok_pp[("fh1", oc)])
                tok_fh[(ft, 1, oc)] = G.emit(G.dma_start(
                    out=fhst[:, (oc % 2) * NFC * 128:((oc % 2) + 1) * NFC * 128],
                    in_=d.fh1b[oc]), dma=True)
            sec("gpsimd", gsec)

            def tsec(T, ft=ft, oc=oc):
                T.wait(tok_fh[(ft, 1, oc)])
                T.wait(tok_h[(ft, NFC - 1)])
                T.wait(tok_h1ev.get((ft, oc - 2)))
                fv = fhst[:, (oc % 2) * NFC * 128:((oc % 2) + 1) * NFC * 128]
                for kc in range(NFC):
                    ins = T.matmul(PB[2][:, (oc % 2) * 512:((oc % 2) + 1) * 512],
                                   fv[:, kc * 128:(kc + 1) * 128],
                                   h_sb[:, kc * 512:(kc + 1) * 512],
                                   start=(kc == 0), stop=(kc == NFC - 1))
                tok_h1mm[(ft, oc)] = T.emit(ins)
            sec("tensor", tsec)

            def ssec(SS, ft=ft, oc=oc):
                SS.wait(tok_h1mm[(ft, oc)])
                tok_h1ev[(ft, oc)] = SS.emit(SS.activation(
                    out=h1o_sb[:, oc * 512:(oc + 1) * 512],
                    in_=PB[2][:, (oc % 2) * 512:((oc % 2) + 1) * 512],
                    func=AF.Gelu, scale=env.h1s_sb[:, oc:oc + 1]))
            sec("scalar", ssec)

        for oc in range(NFC):
            def gsec(G, ft=ft, oc=oc):
                G.wait(tok_h1mm.get((ft, NFC - 2 + oc)) if oc < 2 else tok_h2mm.get((ft, oc - 2)))
                G.wait(env.tok_pp[("fh2", oc)])
                tok_fh[(ft, 2, oc)] = G.emit(G.dma_start(
                    out=fhst[:, (oc % 2) * NFC * 128:((oc % 2) + 1) * NFC * 128],
                    in_=d.fh2b[oc]), dma=True)
            sec("gpsimd", gsec)

            def tsec(T, ft=ft, oc=oc):
                T.wait(tok_fh[(ft, 2, oc)])
                T.wait(tok_h1ev[(ft, NFC - 1)])
                T.wait(tok_h2ev.get((ft, oc - 2)))
                fv = fhst[:, (oc % 2) * NFC * 128:((oc % 2) + 1) * NFC * 128]
                for kc in range(NFC):
                    ins = T.matmul(PB[2][:, (oc % 2) * 512:((oc % 2) + 1) * 512],
                                   fv[:, kc * 128:(kc + 1) * 128],
                                   h1o_sb[:, kc * 512:(kc + 1) * 512],
                                   start=(kc == 0), stop=(kc == NFC - 1))
                tok_h2mm[(ft, oc)] = T.emit(ins)
            sec("tensor", tsec)

            def ssec(SS, ft=ft, oc=oc):
                SS.wait(tok_h2mm[(ft, oc)])
                tok_h2ev[(ft, oc)] = SS.emit(SS.activation(
                    out=h_sb[:, oc * 512:(oc + 1) * 512],
                    in_=PB[2][:, (oc % 2) * 512:((oc % 2) + 1) * 512],
                    func=AF.Gelu, scale=env.h2s_sb[:, oc:oc + 1]))
            sec("scalar", ssec)

        for oc in range(NDC):
            def gsecD(G, ft=ft, oc=oc):
                G.wait(tok_w3.get((ft, oc - 2)) if oc >= 2 else tok_w3.get((ft - 1, NDC - 2 + oc)))
                G.wait(env.tok_pp[("fw3", oc)])
                tok_fw3[(ft, oc)] = G.emit(G.dma_start(
                    out=fw3st[:, (oc % 2) * NFC * 128:((oc % 2) + 1) * NFC * 128],
                    in_=d.fw3b[oc]), dma=True)
            sec("gpsimd", gsecD)

            def tsec(T, ft=ft, oc=oc):
                T.wait(tok_h2ev[(ft, NFC - 1)], tok_fw3[(ft, oc)])
                T.wait(tok_g0.get((ft, oc - 2)) if oc >= 2 else tok_g0.get((ft - 1, NDC - 2 + oc)))
                fv = fw3st[:, (oc % 2) * NFC * 128:((oc % 2) + 1) * NFC * 128]
                for kc in range(NFC):
                    ins = T.matmul(PB[3][:, (oc % 2) * 512:((oc % 2) + 1) * 512],
                                   fv[:, kc * 128:(kc + 1) * 128],
                                   h_sb[:, kc * 512:(kc + 1) * 512],
                                   start=(kc == 0), stop=(kc == NFC - 1))
                tok_w3[(ft, oc)] = T.emit(ins)
            sec("tensor", tsec)

            def ssec(SS, ft=ft, oc=oc):
                SS.wait(tok_w3[(ft, oc)])
                SS.wait(tok_cmb.get((ft, oc - 2)))
                tok_g0[(ft, oc)] = SS.emit(SS.activation(
                    out=g0_sb[:, (oc % 2) * 512:((oc % 2) + 1) * 512],
                    in_=PB[3][:, (oc % 2) * 512:((oc % 2) + 1) * 512],
                    func=AF.Silu, scale=env.s3_sb[:, oc:oc + 1]))
            sec("scalar", ssec)

            def ysec2(Y, ft=ft, oc=oc):
                Y.wait(tok_cmb.get((ft, oc - 2)) if oc >= 2 else tok_cmb.get((ft - 1, NDC - 2 + oc)))
                Y.wait(tok_projst[(ft // 2, oc)])
                tok_s0l[(ft, oc)] = Y.emit(Y.dma_start(
                    out=s0sl[:, (oc % 2) * 512:((oc % 2) + 1) * 512],
                    in_=d.s0_d[oc * 128:(oc + 1) * 128, ft * 512:(ft + 1) * 512]), dma=True)
                tok_x0l[(ft, oc)] = Y.emit(Y.dma_start(
                    out=x0tm[:, (oc % 2) * 512:((oc % 2) + 1) * 512].rearrange("p (s q) -> p s q", s=4),
                    in_=d.x0.rearrange("(s p) dd -> p s dd", p=128)[:, 4 * ft:4 * ft + 4,
                                                                   oc * 128:(oc + 1) * 128]), dma=True)
            sec("sync", ysec2)

            for j in range(4):
                def tsec2(T, ft=ft, oc=oc, j=j):
                    T.wait(tok_x0l[(ft, oc)])
                    T.wait(tok_x0ev.get((ft, oc, j - 2)) if j >= 2 else
                           (tok_x0ev.get((ft, oc - 1, 2 + j)) if oc >= 1 else tok_x0ev.get((ft - 1, NDC - 1, 2 + j))))
                    tok_x0tr[(ft, oc, j)] = T.emit(T.transpose(
                        PB[0].bitcast(F32)[:, (j % 2) * 512:(j % 2) * 512 + 128],
                        x0tm[:, (oc % 2) * 512 + j * 128:(oc % 2) * 512 + (j + 1) * 128],
                        env.ident_f32[:]))
                sec("tensor", tsec2)

                def vsec2(V, ft=ft, oc=oc, j=j):
                    V.wait(tok_x0tr[(ft, oc, j)])
                    tok_x0ev[(ft, oc, j)] = V.emit(V.tensor_copy(
                        x0fm[:, (oc % 2) * 512 + j * 128:(oc % 2) * 512 + (j + 1) * 128],
                        PB[0].bitcast(F32)[:, (j % 2) * 512:(j % 2) * 512 + 128]))
                sec("vector", vsec2)

            def vsec3(V, ft=ft, oc=oc):
                V.wait(tok_g0[(ft, oc)], tok_s0l[(ft, oc)], tok_x0ev[(ft, oc, 3)])
                sl = slice((oc % 2) * 512, ((oc % 2) + 1) * 512)
                V.emit(V.tensor_tensor(sqd[:, 0:512], s0sl[:, sl], g0_sb[:, sl], ALU.mult))
                tok_cmb[(ft, oc)] = V.emit(V.tensor_tensor(
                    out1_sb[:, oc * 512:(oc + 1) * 512], sqd[:, 0:512], x0fm[:, sl], ALU.add))
            sec("vector", vsec3)

            def tsec3(T, ft=ft, oc=oc):
                T.wait(tok_cmb[(ft, oc)])
                tok_lnm[(ft, oc)] = T.emit(T.matmul(
                    PB[1][0:1, 0:512], env.onescol[:], out1_sb[:, oc * 512:(oc + 1) * 512],
                    start=(oc == 0), stop=(oc == NDC - 1)))
            sec("tensor", tsec3)

            def ssec2(SS, ft=ft, oc=oc):
                SS.wait(tok_cmb[(ft, oc)])
                SS.wait(tok_lnm.get((ft, oc - 1, "ss")) if oc >= 1 else tok_lnm.get((ft - 1, NDC - 1, "ss")))
                tok_sqf[(ft, oc)] = SS.emit(SS.activation(
                    out=sqd[:, 512:1024], in_=out1_sb[:, oc * 512:(oc + 1) * 512], func=AF.Square))
            sec("scalar", ssec2)

            def tsec4(T, ft=ft, oc=oc):
                T.wait(tok_sqf[(ft, oc)])
                tok_lnm[(ft, oc, "ss")] = T.emit(T.matmul(
                    PB[1][0:1, 512:1024], env.onescol[:], sqd[:, 512:1024],
                    start=(oc == 0), stop=(oc == NDC - 1)))
            sec("tensor", tsec4)

        def ssec3(SS, ft=ft):
            SS.wait(tok_lnm[(ft, NDC - 1)], tok_lnm[(ft, NDC - 1, "ss")])
            SS.wait(env.tok_lbc_hist.get(ft - 1))
            SS.emit(SS.activation(out=lnrow[0:1, 0:512], in_=PB[1][0:1, 0:512],
                                  func=AF.Identity, scale=1.0 / 1024.0))
            SS.drain()
            SS.emit(SS.activation(out=lnrow[0:1, 512:1024], in_=lnrow[0:1, 0:512], func=AF.Square))
            env.tok_s = SS.emit(SS.activation(out=lnrow[0:1, 1024:1536], in_=PB[1][0:1, 512:1024],
                                              func=AF.Identity, scale=1.0 / 1024.0))
            SS.drain()
        sec("scalar", ssec3)
        tok_l1 = env.tok_s

        def vsec4(V, ft=ft, tok_l1=tok_l1):
            V.wait(tok_l1)
            env.tok_v = V.emit(V.tensor_sub(lnrow[0:1, 1024:1536], lnrow[0:1, 1024:1536],
                                            lnrow[0:1, 512:1024]))
            V.drain()
        sec("vector", vsec4)
        tok_l2 = env.tok_v

        def ssec4(SS, ft=ft, tok_l2=tok_l2):
            SS.wait(tok_l2)
            SS.emit(SS.activation(out=lnrow[0:1, 1536:2048], in_=lnrow[0:1, 1024:1536],
                                  func=AF.Ln, bias=LN_EPS))
            SS.drain()
            env.tok_s = SS.emit(SS.activation(out=lnrow[0:1, 1536:2048], in_=lnrow[0:1, 1536:2048],
                                              func=AF.Exp, scale=-0.5))
            SS.drain()
        sec("scalar", ssec4)
        tok_l3 = env.tok_s

        def vsec5(V, ft=ft, tok_l3=tok_l3):
            V.wait(tok_l3)
            V.emit(V.tensor_mul(lnrow[0:1, 2048:2560], lnrow[0:1, 0:512], lnrow[0:1, 1536:2048]))
            V.drain()
            env.tok_v = V.emit(V.tensor_scalar_mul(lnrow[0:1, 2048:2560], lnrow[0:1, 2048:2560], -1.0))
            V.drain()
        sec("vector", vsec5)
        tok_l4 = env.tok_v

        def tsec5(T, ft=ft, tok_l4=tok_l4):
            T.wait(tok_l4)
            T.emit(T.matmul(PB[0][:, 0:512], env.ones1[:], lnrow[0:1, 1536:2048], start=True, stop=True))
            env.tok_t = T.emit(T.matmul(PB[0][:, 512:1024], env.ones1[:], lnrow[0:1, 2048:2560],
                                        start=True, stop=True))
        sec("tensor", tsec5)
        tok_lbc = env.tok_t
        env.tok_lbc_hist[ft] = tok_lbc

        for oc in range(NDC):
            def vsec6(V, ft=ft, oc=oc, tok_lbc=tok_lbc):
                V.wait(tok_lbc)
                if oc >= 2:
                    V.wait(tok_st1d.get((ft, oc - 2, "fm")), tok_tr1.get((ft, oc - 2, "x", 3)))
                else:
                    V.wait(tok_st1d.get((ft - 1, NDC - 2 + oc, "fm")), tok_tr1.get((ft - 1, NDC - 2 + oc, "x", 3)))
                sl = slice((oc % 2) * 512, ((oc % 2) + 1) * 512)
                V.emit(V.tensor_tensor(sqd[:, 0:512], out1_sb[:, oc * 512:(oc + 1) * 512],
                                       PB[0][:, 0:512], ALU.mult))
                tok_xn1[(ft, oc)] = V.emit(V.tensor_tensor(
                    xn1c[:, sl], sqd[:, 0:512], PB[0][:, 512:1024], ALU.add))
            sec("vector", vsec6)

            def ysec3(Y, ft=ft, oc=oc):
                Y.wait(tok_xn1[(ft, oc)])
                tok_st1d[(ft, oc, "fm")] = Y.emit(Y.dma_start(
                    out=d.xn1fm_d[oc * 128:(oc + 1) * 128, ft * 512:(ft + 1) * 512],
                    in_=xn1c[:, (oc % 2) * 512:((oc % 2) + 1) * 512]), dma=True)
            sec("sync", ysec3)

            for j in range(4):
                def tsec6(T, ft=ft, oc=oc, j=j):
                    T.wait(tok_xn1[(ft, oc)])
                    T.wait(tok_trev1.get((ft, oc, "x", j - 1)) if j >= 1 else
                           tok_trev1.get((ft, oc - 1, "x", 3), tok_trev1.get((ft - 1, NDC - 1, "x", 3)) if oc == 0 else None))
                    tok_tr1[(ft, oc, "x", j)] = T.emit(T.transpose(
                        PB[2].bitcast(BF16)[:, 0:128],
                        xn1c[:, (oc % 2) * 512 + j * 128:(oc % 2) * 512 + (j + 1) * 128],
                        env.ident_bf[:]))
                sec("tensor", tsec6)

                def vsec7(V, ft=ft, oc=oc, j=j):
                    V.wait(tok_tr1[(ft, oc, "x", j)])
                    V.wait(tok_st1d.get((ft - 1, "xn1tm")))
                    tok_trev1[(ft, oc, "x", j)] = V.emit(V.tensor_copy(
                        xn1tm_st[:, j * 1024 + oc * 128: j * 1024 + (oc + 1) * 128],
                        PB[2].bitcast(BF16)[:, 0:128]))
                sec("vector", vsec7)

                def tsec7(T, ft=ft, oc=oc, j=j):
                    T.wait(tok_trev1.get((ft, oc, "o", j - 1)) if j >= 1 else
                           tok_trev1.get((ft, oc - 1, "o", 3), tok_trev1.get((ft - 1, NDC - 1, "o", 3)) if oc == 0 else None))
                    tok_tr1[(ft, oc, "o", j)] = T.emit(T.transpose(
                        PB[2].bitcast(F32)[:, 512:640],
                        out1_sb[:, oc * 512 + j * 128: oc * 512 + (j + 1) * 128],
                        env.ident_f32[:]))
                sec("tensor", tsec7)

                def vsec8(V, ft=ft, oc=oc, j=j):
                    V.wait(tok_tr1[(ft, oc, "o", j)])
                    V.wait(tok_st1d.get((ft - 1, "tmstores")))
                    tok_trev1[(ft, oc, "o", j)] = V.emit(V.tensor_copy(
                        out1tm_st[:, j * 1024 + oc * 128: j * 1024 + (oc + 1) * 128],
                        PB[2].bitcast(F32)[:, 512:640]))
                sec("vector", vsec8)

        def ysec4(Y, ft=ft):
            Y.wait(tok_trev1[(ft, NDC - 1, "x", 3)], tok_trev1[(ft, NDC - 1, "o", 3)])
            tok_st1d[(ft, "xn1tm")] = Y.emit(Y.dma_start(
                out=d.xn1tm_d.rearrange("(s p) dd -> p s dd", p=128)[:, 4 * ft:4 * ft + 4, :],
                in_=xn1tm_st[:].rearrange("p (s dd) -> p s dd", s=4)), dma=True)
            tok_st1d[(ft, "tmstores")] = Y.emit(Y.dma_start(
                out=d.out1_d.rearrange("(s p) dd -> p s dd", p=128)[:, 4 * ft:4 * ft + 4, :],
                in_=out1tm_st[:].rearrange("p (s dd) -> p s dd", s=4)), dma=True)
        sec("sync", ysec4)
        env.tok_ffn_last = tok_st1d[(ft, "tmstores")]
    env.tok_phase1_done = env.tok_ffn_last
    dstack.close()
    pstack.close()




def phase2(nc, env, d, block, cfg):
    ctx = env.ctx
    e = ctx.enter_context
    seq = cfg.seq
    NT5 = seq // 512
    sec = lambda name, body: _mk_section(env, block, name, body)

    pstack2 = ExitStack()
    PB = [pstack2.enter_context(nc.psum_tensor(f"p2PB{i}", [128, 1024], F32)) for i in range(4)]

    n2w = e(nc.sbuf_tensor("n2w", [128, NDC], F32))

    sstack = ExitStack()
    es = sstack.enter_context
    xsl = es(nc.sbuf_tensor("xsl", [128, 2 * NDC * 512], BF16))
    xnn = es(nc.sbuf_tensor("xnn", [128, NDC * 512], BF16))
    wdt = es(nc.sbuf_tensor("wdt", [128, NDC * D], BF16))
    wdt3 = wdt[:].rearrange("p (dd f) -> p dd f", dd=NDC)
    wbc_sb = es(nc.sbuf_tensor("wbc_sb", [128, NDC * 64], BF16))
    wbc3 = wbc_sb[:].rearrange("p (dd f) -> p dd f", dd=NDC)
    bbc_sb = es(nc.sbuf_tensor("bbc_sb", [64, 1], F32))
    n1w = es(nc.sbuf_tensor("n1w", [128, NDC], F32))
    env.bdt_sb = es(nc.sbuf_tensor("bdt_sb", [128, NDC], F32))
    dpar = es(nc.sbuf_tensor("dpar", [128, NDC], F32))
    mscl = es(nc.sbuf_tensor("mscl", [128, NDC], F32))
    sel16 = es(nc.sbuf_tensor("sel16s", [NST, NST * 128], F32))
    sel163 = sel16[:].rearrange("p (n m) -> p n m", n=NST)
    alog = es(nc.sbuf_tensor("alog", [NST, 2], F32))
    alogr = es(nc.sbuf_tensor("alogr", [1, NST], F32))
    abc = es(nc.sbuf_tensor("abc", [128, NST], F32))
    dtc = es(nc.sbuf_tensor("dtc", [128, NDC * 512], F32))
    wc_ = es(nc.sbuf_tensor("wc_", [128, NDC * 512], BF16))
    rdt = es(nc.sbuf_tensor("rdt", [128, 512], F32))
    sp1 = es(nc.sbuf_tensor("sp1", [128, 3 * 512], F32))
    bvc = es(nc.sbuf_tensor("bvc", [NST, 2 * 512], F32))
    dA_s = es(nc.sbuf_tensor("dA_s", [128, 2 * 512], F32))
    b3_s = es(nc.sbuf_tensor("b3_s", [128, 2 * 512], F32))
    h_s = es(nc.sbuf_tensor("h_s", [128, 4 * 512], F32))
    cv_sb = es(nc.sbuf_tensor("cv_sb", [128, 2 * 512], F32))
    ybp = es(nc.sbuf_tensor("ybp", [128, 512], F32))
    yeps = es(nc.sbuf_tensor("yeps", [128, 512], F32))
    yac = es(nc.sbuf_tensor("yac", [128, NDC * 512], F32))
    stt_s = es(nc.sbuf_tensor("stt_s", [128, NST * NDC], F32))
    lrow = es(nc.sbuf_tensor("lrow", [1, 2 * 512], F32))
    x1c = es(nc.sbuf_tensor("x1c", [128, 2 * 512], BF16))

    tok_ld2 = {}
    tok_rs1 = {}
    tok_sq2 = {}
    tok_dt = {}
    tok_dtev = {}
    tok_w = {}
    tok_bvmm = {}
    tok_bv = {}
    tok_bc2 = {}
    tok_scan = {}
    tok_y = {}
    tok_cv = {}
    tok_yep = {}
    tok_x1st = {}
    tok_exp2 = {}
    tok_bvevh = {}
    env.tok_rs1row = {}

    def gsec0(G):
        G.emit(G.dma_start(out=wdt3, in_=d.Wdt.rearrange("(dd p) f -> p dd f", p=128)), dma=True)
        env.tok_g = G.emit(G.dma_start(out=wbc3, in_=d.WBC.rearrange("(dd p) f -> p dd f", p=128)), dma=True)
    sec("gpsimd", gsec0)
    tok_wdt = env.tok_g
    if 3 in cfg.phases:
        prepass_weights(nc, env, d, block, "p3")

    def ysec0(Y):
        for src_, dst in [(d.norm1_w, n1w), (d.norm2_w, n2w), (d.D_param, dpar),
                          (d.mscale, mscl), (d.bBC, bbc_sb), (d.bdt, env.bdt_sb)]:
            Y.emit(Y.dma_start(out=dst[:], in_=src_[:]), dma=True)
        Y.emit(Y.dma_start(out=sel163, in_=d.sel16[:]), dma=True)
        Y.emit(Y.dma_start(out=alog[:, 0:1], in_=d.A_log.rearrange("(a b) -> a b", b=1)), dma=True)
        env.tok_y = Y.emit(Y.dma_start(out=alogr[:], in_=d.A_log.rearrange("(b a) -> b a", b=1)), dma=True)
    sec("sync", ysec0)
    tok_cst2 = env.tok_y

    def ssec0(SS, tok_cst2=tok_cst2):
        SS.wait(tok_cst2)
        SS.emit(SS.activation(out=alog[:, 0:1], in_=alog[:, 0:1], func=AF.Exp))
        env.tok_s = SS.emit(SS.activation(out=alogr[:], in_=alogr[:], func=AF.Exp))
        SS.drain()
    sec("scalar", ssec0)
    tok_aexp = env.tok_s

    def vsec0(V, tok_aexp=tok_aexp):
        V.wait(tok_aexp)
        V.emit(V.tensor_scalar_mul(alog[:, 0:1], alog[:, 0:1], -1.0))
        V.drain()
        V.emit(V.reciprocal(alog[:, 1:2], alog[:, 0:1]))
        env.tok_v = V.emit(V.tensor_scalar_mul(alogr[:], alogr[:], -1.0))
        V.drain()
    sec("vector", vsec0)
    tok_arow = env.tok_v

    def tsec0(T, tok_arow=tok_arow):
        T.wait(tok_arow, env.tok_consts_loaded)
        env.tok_t = T.emit(T.matmul(PB[3][:, 0:NST], env.ones1[:], alogr[:], start=True, stop=True))
    sec("tensor", tsec0)
    tok_abcmm = env.tok_t

    def vsec0b(V, tok_abcmm=tok_abcmm):
        V.wait(tok_abcmm)
        env.tok_v = V.emit(V.tensor_copy(abc[:], PB[3][:, 0:NST]))
        V.drain()
    sec("vector", vsec0b)
    tok_abc = env.tok_v

    for ct in range(NT5):
        sb = ct % 2
        xv = xsl[:, sb * NDC * 512:(sb + 1) * NDC * 512]

        def yld(Y, ct=ct, xv=xv):
            Y.wait(env.tok_phase1_done)
            Y.wait(tok_yep.get((ct - 2, NDC - 1)))
            tok_ld2[ct] = Y.emit(Y.dma_start(
                out=xv.rearrange("p (dc q) -> p dc q", dc=NDC),
                in_=d.xn1fm_d.rearrange("(dc p) q -> p dc q", p=128)[:, :, ct * 512:(ct + 1) * 512]), dma=True)
        sec("sync", yld)

        for dc in range(NDC):
            def ssq(SS, ct=ct, dc=dc, xv=xv):
                SS.wait(tok_ld2[ct])
                SS.wait(tok_rs1.get((ct, dc - 2)) if dc >= 2 else tok_rs1.get((ct - 1, NDC - 2 + dc)))
                tok_sq2[(ct, dc)] = SS.emit(SS.activation(
                    out=sp1[:, (dc % 2) * 512:(dc % 2 + 1) * 512],
                    in_=xv[:, dc * 512:(dc + 1) * 512], func=AF.Square))
            sec("scalar", ssq)

            def tsq(T, ct=ct, dc=dc):
                T.wait(tok_sq2[(ct, dc)])
                if dc == 0:
                    T.wait(env.tok_rs1row.get(ct - 1))
                    T.wait(tok_scan.get((ct * NST - 1) * NDC + NDC - 1))
                    if ct >= 1:
                        T.wait(tok_exp2[(ct - 1, NST - 1, 0)])
                tok_rs1[(ct, dc)] = T.emit(T.matmul(
                    PB[3][0:1, 0:512], env.onescol[:], sp1[:, (dc % 2) * 512:(dc % 2 + 1) * 512],
                    start=(dc == 0), stop=(dc == NDC - 1)))
            sec("tensor", tsq)

        def srow(SS, ct=ct):
            SS.wait(tok_rs1[(ct, NDC - 1)])
            SS.emit(SS.activation(out=lrow[0:1, 0:512], in_=PB[3][0:1, 0:512],
                                  func=AF.Ln, scale=1.0 / 1024.0, bias=RMS_EPS))
            SS.drain()
            env.tok_s = SS.emit(SS.activation(out=lrow[0:1, 0:512], in_=lrow[0:1, 0:512],
                                              func=AF.Exp, scale=-0.5))
            SS.drain()
        sec("scalar", srow)
        env.tok_rs1row[ct] = env.tok_s
        tok_rrow = env.tok_s

        def tbc(T, ct=ct, tok_rrow=tok_rrow):
            T.wait(tok_rrow)
            T.wait(tok_w.get((ct - 1, NDC - 1, "nn")))
            env.tok_t = T.emit(T.matmul(PB[3][:, 512:1024], env.ones1[:], lrow[0:1, 0:512],
                                        start=True, stop=True))
        sec("tensor", tbc)
        tok_rbc = env.tok_t

        for dc in range(NDC):
            def vnn(V, ct=ct, dc=dc, xv=xv, tok_rbc=tok_rbc):
                V.wait(tok_rbc)
                V.wait(tok_dt.get((ct - 1, NDC - 1)))
                V.wait(tok_bvmm.get(ct - 1))
                tok_w[(ct, dc, "nn")] = V.emit(V.scalar_tensor_tensor(
                    xnn[:, dc * 512:(dc + 1) * 512], xv[:, dc * 512:(dc + 1) * 512],
                    n1w[:, dc:dc + 1], PB[3][:, 512:1024], ALU.mult, ALU.mult))
            sec("vector", vnn)

        for oc in range(NDC):
            def tdt(T, ct=ct, oc=oc):
                T.wait(tok_w[(ct, NDC - 1, "nn")], tok_wdt)
                T.wait(tok_dtev.get((ct, oc - 2)) if oc >= 2 else tok_dtev.get((ct - 1, NDC - 2 + oc)))
                pb = PB[oc % 2]
                for dd in range(NDC):
                    ins = T.matmul(pb[:, 0:512], wdt3[:, dd, oc * 128:(oc + 1) * 128],
                                   xnn[:, dd * 512:(dd + 1) * 512],
                                   start=(dd == 0), stop=(dd == NDC - 1))
                tok_dt[(ct, oc)] = T.emit(ins)
            sec("tensor", tdt)

            def sdt(SS, ct=ct, oc=oc):
                SS.wait(tok_dt[(ct, oc)])
                SS.wait(tok_dtev.get((ct, oc - 1)) if oc >= 1 else tok_dtev.get((ct - 1, NDC - 1)))
                pb = PB[oc % 2]
                SS.emit(SS.activation(out=sp1[:, 1024:1536], in_=pb[:, 0:512], func=AF.Abs,
                                      bias=d_bdt_col(env, oc)))
                SS.drain()
                SS.emit(SS.activation(out=sp1[:, 1024:1536], in_=sp1[:, 1024:1536],
                                      func=AF.Exp, scale=-1.0))
                SS.drain()
                SS.emit(SS.activation(out=sp1[:, 1024:1536], in_=sp1[:, 1024:1536],
                                      func=AF.Ln, bias=1.0))
                SS.drain()
                tok_dtev[(ct, oc, "a")] = SS.emit(SS.activation(
                    out=dtc[:, oc * 512:(oc + 1) * 512],
                    in_=pb[:, 0:512], func=AF.Relu, bias=d_bdt_col(env, oc)))
            sec("scalar", sdt)

            def vdt(V, ct=ct, oc=oc):
                V.wait(tok_dtev[(ct, oc, "a")])
                tok_dtev[(ct, oc)] = V.emit(V.tensor_add(
                    dtc[:, oc * 512:(oc + 1) * 512], dtc[:, oc * 512:(oc + 1) * 512],
                    sp1[:, 1024:1536]))
                V.drain()
                V.emit(V.reciprocal(rdt[:], dtc[:, oc * 512:(oc + 1) * 512]))
                V.drain()
                tok_w[(ct, oc)] = V.emit(V.tensor_tensor(
                    wc_[:, oc * 512:(oc + 1) * 512], xnn[:, oc * 512:(oc + 1) * 512],
                    rdt[:], ALU.mult))
            sec("vector", vdt)

        def tbv(T, ct=ct):
            T.wait(tok_w[(ct, NDC - 1, "nn")], tok_wdt)
            T.wait(tok_bvevh.get(ct - 1))
            for dd in range(NDC):
                ins = T.matmul(PB[2][0:64, 0:512], wbc3[:, dd, :],
                               xnn[:, dd * 512:(dd + 1) * 512],
                               start=(dd == 0), stop=(dd == NDC - 1))
            tok_bvmm[ct] = T.emit(ins)
        sec("tensor", tbv)

        def sbv(SS, ct=ct):
            SS.wait(tok_bvmm[ct])
            SS.emit(SS.activation(out=bvc[:, 0:512], in_=PB[2][0:16, 0:512],
                                  func=AF.Identity, bias=bbc_sb[0:16, 0:1]))
            env.tok_s = SS.emit(SS.activation(out=bvc[:, 512:1024], in_=PB[2][32:48, 0:512],
                                              func=AF.Identity, bias=bbc_sb[32:48, 0:1]))
            SS.drain()
        sec("scalar", sbv)
        tok_bvev = env.tok_s
        tok_bvevh[ct] = tok_bvev

        def vbv(V, ct=ct, tok_bvev=tok_bvev):
            V.wait(tok_bvev, tok_abc)
            tok_bv[ct] = V.emit(V.tensor_scalar(bvc[:, 0:512], bvc[:, 0:512],
                                                alog[:, 1:2], None, ALU.mult))
            V.drain()
        sec("vector", vbv)

        for n in range(NST):
            def tbc2(T, ct=ct, n=n):
                T.wait(tok_bv[ct])
                T.wait(tok_scan.get((ct * NST + n - 1) * NDC + NDC - 1))
                if n >= 1:
                    T.wait(tok_exp2[(ct, n - 1, 0)])
                elif ct >= 1:
                    T.wait(tok_exp2[(ct - 1, NST - 1, 0)])
                T.matmul(PB[2][:, 512:1024], sel163[:, n, :], bvc[:, 0:512], start=True, stop=True)
                env.tok_t = T.emit(T.matmul(PB[3][:, 0:512], sel163[:, n, :], bvc[:, 512:1024],
                                            start=True, stop=True))
            sec("tensor", tbc2)
            tok_bc2[(ct, n)] = env.tok_t

            def sevac(SS, ct=ct, n=n):
                SS.wait(tok_bc2[(ct, n)])
                prev = tok_y.get((ct * NST + n - 1) * NDC + 0)
                SS.wait(prev)
                tok_cv[(ct, n)] = SS.emit(SS.activation(
                    out=cv_sb[:, (n % 2) * 512:(n % 2 + 1) * 512], in_=PB[3][:, 0:512],
                    func=AF.Identity))
            sec("scalar", sevac)

            for dc in range(NDC):
                sl = ((n * NDC + dc) % 2)
                G = (ct * NST + n) * NDC + dc

                def sexp(SS, ct=ct, n=n, dc=dc, sl=sl, G=G):
                    SS.wait(tok_dtev[(ct, NDC - 1)])
                    SS.wait(tok_scan.get(G - 2))
                    tok_exp2[(ct, n, dc)] = SS.emit(SS.activation(
                        out=dA_s[:, sl * 512:(sl + 1) * 512], in_=dtc[:, dc * 512:(dc + 1) * 512],
                        func=AF.Exp, scale=abc[:, n:n + 1]))
                sec("scalar", sexp)

                def vscan(V, ct=ct, n=n, dc=dc, sl=sl, G=G):
                    V.wait(tok_exp2[(ct, n, dc)], tok_bc2[(ct, n)])
                    V.wait(tok_y.get(G - 3))
                    dav = dA_s[:, sl * 512:(sl + 1) * 512]
                    b3v = b3_s[:, sl * 512:(sl + 1) * 512]
                    hv = h_s[:, (G % 4) * 512:(G % 4 + 1) * 512]
                    V.emit(V.scalar_tensor_tensor(b3v, dav, -1.0, wc_[:, dc * 512:(dc + 1) * 512],
                                                  ALU.add, ALU.mult))
                    V.emit(V.tensor_tensor(b3v, b3v, PB[2][:, 512:1024], ALU.mult))
                    stcol = stt_s[:, (n * NDC + dc):(n * NDC + dc) + 1]
                    init = stcol if ct > 0 else 0.0
                    V.emit(V.tensor_tensor_scan(hv, dav, b3v, init, ALU.mult, ALU.add))
                    tok_scan[G] = V.emit(V.tensor_copy(stcol, hv[:, 511:512]))
                sec("vector", vscan)

                def gacc(G_, ct=ct, n=n, dc=dc, G=G):
                    G_.wait(tok_scan[G], tok_cv[(ct, n)])
                    hv = h_s[:, (G % 4) * 512:(G % 4 + 1) * 512]
                    cvv = cv_sb[:, (n % 2) * 512:(n % 2 + 1) * 512]
                    acc = yac[:, dc * 512:(dc + 1) * 512]
                    if n == 0:
                        if ct >= 1:
                            G_.wait(tok_yep[(ct - 1, dc)])
                        tok_y[G] = G_.emit(G_.tensor_tensor(acc, hv, cvv, ALU.mult))
                    else:
                        G_.emit(G_.tensor_tensor(ybp[:], hv, cvv, ALU.mult))
                        tok_y[G] = G_.emit(G_.tensor_add(acc, acc, ybp[:]))
                sec("gpsimd", gacc)

        if cfg.dbg and ct == 0:
            def ydbg2a(Y, ct=ct):
                Y.wait(tok_y[(ct * NST + NST - 1) * NDC + NDC - 1])
                Y.wait(tok_dtev[(ct, NDC - 1)], tok_bv[ct], tok_w[(ct, NDC - 1, "nn")])
                Y.emit(Y.dma_start(out=d.dt_d[:, :], in_=dtc[:, 0:NDC * 512]), dma=True)
                Y.emit(Y.dma_start(out=d.bvc_d[:, :], in_=bvc[:, 0:1024]), dma=True)
                Y.emit(Y.dma_start(out=d.yac_d[:, :], in_=yac[:, 0:NDC * 512]), dma=True)
                Y.emit(Y.dma_start(out=d.xnn_d[:, :], in_=xnn[:, 0:NDC * 512]), dma=True)
                env.tok_dbg2 = Y.emit(Y.dma_start(out=d.wc_d[:, :], in_=wc_[:, 0:NDC * 512]), dma=True)
            sec("sync", ydbg2a)

            def vdbg2(V):
                V.wait(env.tok_dbg2)
            sec("vector", vdbg2)

        for dc in range(NDC):
            def vyep(V, ct=ct, dc=dc, xv=xv):
                V.wait(tok_y[(ct * NST + NST - 1) * NDC + dc])
                V.wait(tok_x1st.get((ct, dc - 2)) if dc >= 2 else tok_x1st.get((ct - 1, NDC - 2 + dc)))
                V.emit(V.scalar_tensor_tensor(
                    yeps[:], xnn[:, dc * 512:(dc + 1) * 512], dpar[:, dc:dc + 1],
                    yac[:, dc * 512:(dc + 1) * 512], ALU.mult, ALU.add))
                tok_yep[(ct, dc)] = V.emit(V.scalar_tensor_tensor(
                    x1c[:, (dc % 2) * 512:(dc % 2 + 1) * 512], yeps[:],
                    mscl[:, dc:dc + 1], xv[:, dc * 512:(dc + 1) * 512], ALU.mult, ALU.add))
            sec("vector", vyep)

            def yx1(Y, ct=ct, dc=dc):
                Y.wait(tok_yep[(ct, dc)])
                tok_x1st[(ct, dc)] = Y.emit(Y.dma_start(
                    out=d.x1_d[dc * 128:(dc + 1) * 128, ct * 512:(ct + 1) * 512],
                    in_=x1c[:, (dc % 2) * 512:(dc % 2 + 1) * 512]), dma=True)
            sec("sync", yx1)

    barrier(env, block)
    sstack.close()

    fstack = ExitStack()
    ef = fstack.enter_context
    x1sl = ef(nc.sbuf_tensor("x1sl", [128, 2 * NDC * 512], BF16))
    xn2c = ef(nc.sbuf_tensor("xn2c", [128, NDC * 512], BF16))
    sqf2 = ef(nc.sbuf_tensor("sqf2", [128, 2 * 512], F32))
    hf2 = ef(nc.sbuf_tensor("hf2", [128, NFC * 512], BF16))
    w1st = ef(nc.sbuf_tensor("w1st", [128, 2 * NDC * 128], BF16))
    w2st = ef(nc.sbuf_tensor("w2st", [128, 2 * NFC * 128], BF16))
    b1sb = ef(nc.sbuf_tensor("b1sb", [128, NFC], F32))
    b2sb = ef(nc.sbuf_tensor("b2sb", [128, NDC], F32))
    sch = ef(nc.sbuf_tensor("sch", [128, 2 * 512], F32))
    stm = ef(nc.sbuf_tensor("stm", [128, 4 * 1024], F32))
    lrow2 = ef(nc.sbuf_tensor("lrow2", [1, 512], F32))

    def yldb(Y):
        Y.emit(Y.dma_start(out=b1sb[:], in_=d.b1[:]), dma=True)
        env.tok_y = Y.emit(Y.dma_start(out=b2sb[:], in_=d.b2[:]), dma=True)
    sec("sync", yldb)
    tok_b12 = env.tok_y

    tok_lx1 = {}
    tok_r2 = {}
    tok_sq3 = {}
    tok_n2 = {}
    tok_w1l = {}
    tok_w1m = {}
    tok_hf = {}
    tok_w2l = {}
    tok_w2m = {}
    tok_sev = {}
    tok_str = {}
    tok_strv = {}
    tok_stst = {}
    env.tok_r2row = {}

    for ft in range(NT5):
        fb = ft % 2
        xv = x1sl[:, fb * NDC * 512:(fb + 1) * NDC * 512]

        def yld2(Y, ft=ft, xv=xv):
            Y.wait(tok_x1st.get((ft, NDC - 1)))
            Y.wait(tok_w1m.get((ft - 2, NFC - 1)))
            tok_lx1[ft] = Y.emit(Y.dma_start(
                out=xv.rearrange("p (dc q) -> p dc q", dc=NDC),
                in_=d.x1_d.rearrange("(dc p) q -> p dc q", p=128)[:, :, ft * 512:(ft + 1) * 512]), dma=True)
        sec("sync", yld2)

        for dc in range(NDC):
            def ssq2(SS, ft=ft, dc=dc, xv=xv):
                SS.wait(tok_lx1[ft])
                SS.wait(tok_r2.get((ft, dc - 2)) if dc >= 2 else tok_r2.get((ft - 1, NDC - 2 + dc)))
                tok_sq3[(ft, dc)] = SS.emit(SS.activation(
                    out=sqf2[:, (dc % 2) * 512:(dc % 2 + 1) * 512],
                    in_=xv[:, dc * 512:(dc + 1) * 512], func=AF.Square))
            sec("scalar", ssq2)

            def tsq2(T, ft=ft, dc=dc):
                T.wait(tok_sq3[(ft, dc)])
                if dc == 0:
                    T.wait(env.tok_r2row.get(ft - 1))
                    T.wait(tok_strv.get((ft - 1, NDC - 1, 3)))
                tok_r2[(ft, dc)] = T.emit(T.matmul(
                    PB[3][0:1, 0:512], env.onescol[:], sqf2[:, (dc % 2) * 512:(dc % 2 + 1) * 512],
                    start=(dc == 0), stop=(dc == NDC - 1)))
            sec("tensor", tsq2)

        def srow2(SS, ft=ft):
            SS.wait(tok_r2[(ft, NDC - 1)])
            SS.emit(SS.activation(out=lrow2[0:1, 0:512], in_=PB[3][0:1, 0:512],
                                  func=AF.Ln, scale=1.0 / 1024.0, bias=RMS_EPS))
            SS.drain()
            env.tok_s = SS.emit(SS.activation(out=lrow2[0:1, 0:512], in_=lrow2[0:1, 0:512],
                                              func=AF.Exp, scale=-0.5))
            SS.drain()
        sec("scalar", srow2)
        env.tok_r2row[ft] = env.tok_s
        tok_rr2 = env.tok_s

        def tbc3(T, ft=ft, tok_rr2=tok_rr2):
            T.wait(tok_rr2)
            T.wait(tok_n2.get((ft - 1, NDC - 1)))
            env.tok_t = T.emit(T.matmul(PB[3][:, 512:1024], env.ones1[:], lrow2[0:1, 0:512],
                                        start=True, stop=True))
        sec("tensor", tbc3)
        tok_rbc2 = env.tok_t

        for dc in range(NDC):
            def vn2(V, ft=ft, dc=dc, xv=xv, tok_rbc2=tok_rbc2):
                V.wait(tok_rbc2)
                V.wait(tok_w1m.get((ft - 1, NFC - 1)))
                tok_n2[(ft, dc)] = V.emit(V.scalar_tensor_tensor(
                    xn2c[:, dc * 512:(dc + 1) * 512], xv[:, dc * 512:(dc + 1) * 512],
                    n2w[:, dc:dc + 1], PB[3][:, 512:1024], ALU.mult, ALU.mult))
            sec("vector", vn2)

        for oc in range(NFC):
            def gw1(G, ft=ft, oc=oc):
                G.wait(tok_w1m.get((ft, oc - 2)) if oc >= 2 else tok_w1m.get((ft - 1, NFC - 2 + oc)))
                G.wait(env.tok_pp[("W1", oc)])
                tok_w1l[(ft, oc)] = G.emit(G.dma_start(
                    out=w1st[:, (oc % 2) * NDC * 128:((oc % 2) + 1) * NDC * 128],
                    in_=d.W1b[oc]), dma=True)
            sec("gpsimd", gw1)

            def tw1(T, ft=ft, oc=oc):
                T.wait(tok_n2[(ft, NDC - 1)], tok_w1l[(ft, oc)])
                T.wait(tok_hf.get((ft, oc - 2)) if oc >= 2 else tok_hf.get((ft - 1, NFC - 2 + oc)))
                sl = w1st[:, (oc % 2) * NDC * 128:((oc % 2) + 1) * NDC * 128]
                pb = PB[oc % 2]
                for dd in range(NDC):
                    ins = T.matmul(pb[:, 0:512], sl[:, dd * 128:(dd + 1) * 128],
                                   xn2c[:, dd * 512:(dd + 1) * 512],
                                   start=(dd == 0), stop=(dd == NDC - 1))
                tok_w1m[(ft, oc)] = T.emit(ins)
            sec("tensor", tw1)

            def sg1(SS, ft=ft, oc=oc):
                SS.wait(tok_w1m[(ft, oc)])
                tok_hf[(ft, oc)] = SS.emit(SS.activation(
                    out=hf2[:, oc * 512:(oc + 1) * 512], in_=PB[oc % 2][:, 0:512],
                    func=AF.Gelu_apprx_tanh, bias=b1sb[:, oc:oc + 1]))
            sec("scalar", sg1)

        for oc in range(NDC):
            def gw2(G, ft=ft, oc=oc):
                G.wait(tok_w2m.get((ft, oc - 2)) if oc >= 2 else tok_w2m.get((ft - 1, NDC - 2 + oc)))
                G.wait(env.tok_pp[("W2", oc)])
                tok_w2l[(ft, oc)] = G.emit(G.dma_start(
                    out=w2st[:, (oc % 2) * NFC * 128:((oc % 2) + 1) * NFC * 128],
                    in_=d.W2b[oc]), dma=True)
            sec("gpsimd", gw2)

            def tw2(T, ft=ft, oc=oc):
                T.wait(tok_hf[(ft, NFC - 1)], tok_w2l[(ft, oc)])
                T.wait(tok_sev.get((ft, oc - 2)) if oc >= 2 else tok_sev.get((ft - 1, NDC - 2 + oc)))
                sl = w2st[:, (oc % 2) * NFC * 128:((oc % 2) + 1) * NFC * 128]
                pb = PB[2]
                for kc in range(NFC):
                    ins = T.matmul(pb[:, (oc % 2) * 512:((oc % 2) + 1) * 512],
                                   sl[:, kc * 128:(kc + 1) * 128],
                                   hf2[:, kc * 512:(kc + 1) * 512],
                                   start=(kc == 0), stop=(kc == NFC - 1))
                tok_w2m[(ft, oc)] = T.emit(ins)
            sec("tensor", tw2)

            def vs2(V, ft=ft, oc=oc, xv=xv):
                V.wait(tok_w2m[(ft, oc)])
                V.wait(tok_str.get((ft, oc - 2, 3)) if oc >= 2 else tok_str.get((ft - 1, NDC - 2 + oc, 3)))
                tok_sev[(ft, oc)] = V.emit(V.scalar_tensor_tensor(
                    sch[:, (oc % 2) * 512:((oc % 2) + 1) * 512],
                    PB[2][:, (oc % 2) * 512:((oc % 2) + 1) * 512],
                    b2sb[:, oc:oc + 1], xv[:, oc * 512:(oc + 1) * 512], ALU.add, ALU.add))
            sec("vector", vs2)

            for j in range(4):
                def ts3(T, ft=ft, oc=oc, j=j):
                    T.wait(tok_sev[(ft, oc)])
                    T.wait(tok_strv.get((ft, oc, j - 1)) if j >= 1 else
                           tok_strv.get((ft, oc - 1, 3), tok_strv.get((ft - 1, NDC - 1, 3)) if oc == 0 else None))
                    tok_str[(ft, oc, j)] = T.emit(T.transpose(
                        PB[3].bitcast(F32)[:, 0:128],
                        sch[:, (oc % 2) * 512 + j * 128:(oc % 2) * 512 + (j + 1) * 128],
                        env.ident_f32[:]))
                sec("tensor", ts3)

                def vs3(V, ft=ft, oc=oc, j=j):
                    V.wait(tok_str[(ft, oc, j)])
                    V.wait(tok_stst.get(ft - 1))
                    tok_strv[(ft, oc, j)] = V.emit(V.tensor_copy(
                        stm[:, j * 1024 + oc * 128: j * 1024 + (oc + 1) * 128],
                        PB[3].bitcast(F32)[:, 0:128]))
                sec("vector", vs3)

        def yst(Y, ft=ft):
            Y.wait(tok_strv[(ft, NDC - 1, 3)])
            tok_stst[ft] = Y.emit(Y.dma_start(
                out=d.s1tm_d.rearrange("(s p) dd -> p s dd", p=128)[:, 4 * ft:4 * ft + 4, :],
                in_=stm[:].rearrange("p (s dd) -> p s dd", s=4)), dma=True)
        sec("sync", yst)
    env.tok_phase2_done = tok_stst[NT5 - 1]
    barrier(env, block)
    fstack.close()
    pstack2.close()


def d_bdt_col(env, oc):
    return env.bdt_sb[:, oc:oc + 1]



def phase3(nc, env, d, block, cfg):
    ctx = env.ctx
    seq = cfg.seq
    NT = seq // 128
    NT5 = seq // 512
    KE = seq // 4
    NJT = KE // 128
    JW = min(512, KE)
    NJH = KE // JW
    sec = lambda name, body: _mk_section(env, block, name, body)

    pstack3 = ExitStack()
    PB = [pstack3.enter_context(nc.psum_tensor(f"p3PB{i}", [128, 1024], F32)) for i in range(4)]
    st3 = ExitStack()
    e3 = st3.enter_context
    probs_sb = e3(nc.sbuf_tensor("probs_sb", [128, NT * NEXP], F32))
    prow = e3(nc.sbuf_tensor("prow", [128, 8], F32))
    tokid_sb = e3(nc.sbuf_tensor("tokid_sb", [128, NT], I32))
    jcol_sb = e3(nc.sbuf_tensor("jcol_sb", [128, NJT], I32))
    m_sb = e3(nc.sbuf_tensor("m_sb", [128, NEXP * NJT], I32))
    g_sb = e3(nc.sbuf_tensor("g_sb", [128, NEXP * NJT], F32))
    ptmp = e3(nc.sbuf_tensor("ptmp", [128, NEXP], F32))
    inv_sb = e3(nc.sbuf_tensor("inv_sb", [128, NEXP * NT], I32))
    bigi = e3(nc.sbuf_tensor("bigi", [128, NT], I32))

    tok_gld = {}
    tok_gmm = {}
    tok_gtr = {}
    tok_sm = {}
    tok_pst = {}
    tok_invsc = {}
    tok_gv = {}
    tok_mld = {}
    tok_w1l = {}
    tok_m1 = {}
    tok_hg = {}
    tok_w2l = {}
    tok_m2 = {}
    tok_eev = {}
    tok_est = {}
    tok_xg = {}
    tok_xtr = {}
    tok_xtrv = {}
    tok_gat = {}
    tok_acc = {}
    tok_cmb3 = {}
    tok_ost = {}

    def yc3(Y):
        Y.emit(Y.dma_start(out=tokid_sb[:], in_=d.tokid[:]), dma=True)
        env.tok_y = Y.emit(Y.dma_start(out=jcol_sb[:], in_=d.jcol[:, 0:NJT]), dma=True)
    sec("sync", yc3)
    tok_c3 = env.tok_y

    sta = ExitStack()
    ea = sta.enter_context
    gate_sb = ea(nc.sbuf_tensor("gate_sb", [128, NDC * NEXP], BF16))
    gate3 = gate_sb[:].rearrange("p (dd f) -> p dd f", dd=NDC)
    xsl3 = ea(nc.sbuf_tensor("xsl3", [128, 2 * NDC * 512], BF16))
    sc_st = ea(nc.sbuf_tensor("sc_st", [NEXP, 512], F32))
    lo_t = ea(nc.sbuf_tensor("lo_t", [128, NEXP], F32))
    hi_t = ea(nc.sbuf_tensor("hi_t", [128, NEXP], F32))
    mid_t = ea(nc.sbuf_tensor("mid_t", [128, NEXP], F32))
    cm_t = ea(nc.sbuf_tensor("cm_t", [128, NEXP], F32))
    cmi_t = ea(nc.sbuf_tensor("cmi_t", [128, NEXP], I32))
    cnt_t = ea(nc.sbuf_tensor("cnt_t", [128, NEXP], F32))
    crow = ea(nc.sbuf_tensor("crow", [1, NEXP], F32))
    msk = ea(nc.sbuf_tensor("msk", [128, NT], F32))
    onesNT = ea(nc.sbuf_tensor("onesNT", [128, NT], F32))
    pref = ea(nc.sbuf_tensor("pref", [128, NT], F32))
    excl = ea(nc.sbuf_tensor("excl", [128, 1], F32))
    jf_t = ea(nc.sbuf_tensor("jf_t", [128, NT], F32))
    jself = ea(nc.sbuf_tensor("jself", [128, NT], F32))
    bigf = ea(nc.sbuf_tensor("bigf", [128, NT], F32))
    jsel = ea(nc.sbuf_tensor("jsel", [128, NT], I32))
    mski = ea(nc.sbuf_tensor("mski", [128, NT], I32))

    def gg3(G):
        env.breg_ke = G.raw.alloc_register("bke")
        G.raw.reg_mov(env.breg_ke, KE - 1)
        env.breg_seq = G.raw.alloc_register("bseq")
        G.raw.reg_mov(env.breg_seq, seq - 1)
        env.tok_g = G.emit(G.dma_start(out=gate3, in_=d.gate_w.rearrange("(dd p) f -> p dd f", p=128)), dma=True)
    sec("gpsimd", gg3)
    tok_gw = env.tok_g

    zrow = ea(nc.sbuf_tensor("zrow", [1, 1024], F32))

    def vc3(V):
        env.tok_zr = V.emit(V.memset(zrow[:], 0.0))
        V.emit(V.memset(onesNT[:], 1.0))
        V.emit(V.memset(bigf[:], float(BIGIDX)))
        V.emit(V.memset(bigi[:], KE))
        V.emit(V.memset(lo_t[:], 0.0))
        env.tok_v = V.emit(V.memset(hi_t[:], 1.0))
        V.drain()
    sec("vector", vc3)

    def yzr(Y):
        Y.wait(env.tok_zr)
        for ex in range(NEXP):
            env.tok_y = Y.emit(Y.dma_start(out=d.expout_ds[ex][KE:KE + 1, :], in_=zrow[:]), dma=True)
    sec("sync", yzr)

    for ts in range(NT5):
        sb = ts % 2
        xv = xsl3[:, sb * NDC * 512:(sb + 1) * NDC * 512]

        def yg(Y, ts=ts, xv=xv):
            Y.wait(env.tok_phase2_done)
            Y.wait(tok_gmm.get(ts - 2))
            tok_gld[ts] = Y.emit(Y.dma_start(
                out=xv.rearrange("p (dc q) -> p dc q", dc=NDC),
                in_=d.xn1fm_d.rearrange("(dc p) q -> p dc q", p=128)[:, :, ts * 512:(ts + 1) * 512]), dma=True)
        sec("sync", yg)

        def tg(T, ts=ts, xv=xv):
            T.wait(tok_gld[ts], tok_gw)
            T.wait(tok_sm.get((ts - 1, 3)))
            for dd in range(NDC):
                ins = T.matmul(PB[0][0:NEXP, 0:512], gate3[:, dd, :],
                               xv[:, dd * 512:(dd + 1) * 512],
                               start=(dd == 0), stop=(dd == NDC - 1))
            tok_gmm[ts] = T.emit(ins)
        sec("tensor", tg)

        def vgev(V, ts=ts):
            V.wait(tok_gmm[ts])
            env.tok_v = V.emit(V.tensor_copy(sc_st[:], PB[0][0:NEXP, 0:512]))
            V.drain()
        sec("vector", vgev)
        tok_scst = env.tok_v

        for j in range(4):
            def tgt2(T, ts=ts, j=j, tok_scst=tok_scst):
                T.wait(tok_scst)
                T.wait(tok_sm.get((ts, j - 2)) if j >= 2 else tok_sm.get((ts - 1, 2 + j)))
                tok_gtr[(ts, j)] = T.emit(T.transpose(
                    PB[1][:, (j % 2) * 512:(j % 2) * 512 + NEXP],
                    sc_st[:, j * 128:(j + 1) * 128], env.ident_f32[0:NEXP, 0:NEXP]))
            sec("tensor", tgt2)

            def vsm(V, ts=ts, j=j):
                V.wait(tok_gtr[(ts, j)])
                ps = PB[1][:, (j % 2) * 512:(j % 2) * 512 + NEXP]
                V.emit(V.tensor_reduce(prow[:, 0:1], ps, AX.X, ALU.max))
                V.drain()
                env.tok_v = V.emit(V.tensor_scalar_mul(prow[:, 1:2], prow[:, 0:1], -1.0))
                V.drain()
            sec("vector", vsm)
            tok_nm = env.tok_v

            def ssm(SS, ts=ts, j=j, tok_nm=tok_nm):
                SS.wait(tok_nm)
                tt = ts * 4 + j
                env.tok_s = SS.emit(SS.activation(
                    out=probs_sb[:, tt * NEXP:(tt + 1) * NEXP],
                    in_=PB[1][:, (j % 2) * 512:(j % 2) * 512 + NEXP],
                    func=AF.Exp, bias=prow[:, 1:2]))
                SS.drain()
            sec("scalar", ssm)
            tok_ex3 = env.tok_s

            def vsm2(V, ts=ts, j=j, tok_ex3=tok_ex3):
                V.wait(tok_ex3)
                tt = ts * 4 + j
                pv = probs_sb[:, tt * NEXP:(tt + 1) * NEXP]
                V.emit(V.tensor_reduce(prow[:, 3:4], pv, AX.X, ALU.add))
                V.drain()
                V.emit(V.reciprocal(prow[:, 4:5], prow[:, 3:4]))
                V.drain()
                tok_sm[(ts, j)] = V.emit(V.tensor_scalar(pv, pv, prow[:, 4:5], None, ALU.mult))
                V.drain()
            sec("vector", vsm2)

            def yps(Y, ts=ts, j=j):
                tt = ts * 4 + j
                Y.wait(tok_sm[(ts, j)])
                tok_pst[tt] = Y.emit(Y.dma_start(
                    out=d.probs_d[tt * 128:(tt + 1) * 128, :],
                    in_=probs_sb[:, tt * NEXP:(tt + 1) * NEXP]), dma=True)
            sec("sync", yps)

    NITER = 50
    for it in range(NITER):
        def vbi(V, it=it):
            if it == 0:
                V.wait(tok_sm[(NT5 - 1, 3)])
            V.emit(V.tensor_add(mid_t[:], lo_t[:], hi_t[:]))
            V.drain()
            V.emit(V.tensor_scalar_mul(mid_t[:], mid_t[:], 0.5))
            V.drain()
            ins = None
            for ex in range(NEXP):
                pv = probs_sb[:].rearrange("p (t e) -> p e t", e=NEXP)[:, ex, :]
                V.emit(V.tensor_single_scalar(msk[:], pv, mid_t[:, ex:ex + 1], ALU.is_ge))
                V.drain()
                ins = V.emit(V.tensor_reduce(cnt_t[:, ex:ex + 1], msk[:], AX.X, ALU.add))
                V.drain()
            env.tok_v = ins
        sec("vector", vbi)
        tok_ct = env.tok_v

        def tbi(T, it=it, tok_ct=tok_ct):
            T.wait(tok_ct)
            env.tok_t = T.emit(T.matmul(PB[2][0:1, 0:NEXP], env.onescol[:], cnt_t[:],
                                        start=True, stop=True))
        sec("tensor", tbi)
        tok_cr = env.tok_t

        def vbi2(V, it=it, tok_cr=tok_cr):
            V.wait(tok_cr)
            env.tok_v = V.emit(V.tensor_copy(crow[:], PB[2][0:1, 0:NEXP]))
            V.drain()
        sec("vector", vbi2)
        tok_crv = env.tok_v

        def tbi2(T, it=it, tok_crv=tok_crv):
            T.wait(tok_crv)
            env.tok_t = T.emit(T.matmul(PB[2][:, 512:512 + NEXP], env.ones1[:], crow[:],
                                        start=True, stop=True))
        sec("tensor", tbi2)
        tok_cb2 = env.tok_t

        def vbi3(V, it=it, tok_cb2=tok_cb2):
            V.wait(tok_cb2)
            V.emit(V.tensor_single_scalar(cm_t[:], PB[2][:, 512:512 + NEXP], float(KE), ALU.is_ge))
            V.drain()
            V.emit(V.tensor_copy(cmi_t[:], cm_t[:]))
            V.drain()
            V.emit(V.select(lo_t[:], cmi_t[:], mid_t[:], lo_t[:]))
            V.drain()
            V.emit(V.select(cm_t[:], cmi_t[:], hi_t[:], mid_t[:]))
            V.drain()
            env.tok_v = V.emit(V.tensor_copy(hi_t[:], cm_t[:]))
            V.drain()
        sec("vector", vbi3)
    tok_thr = env.tok_v

    if cfg.dbg:
        def ythr(Y, tok_thr=tok_thr):
            Y.wait(tok_thr)
            env.tok_y = Y.emit(Y.dma_start(out=d.thr_d[:, :], in_=lo_t[:]), dma=True)
        sec("sync", ythr)

    for ex in range(NEXP):
        def vcp(V, ex=ex, tok_thr=tok_thr):
            V.wait(tok_thr)
            V.wait(tok_invsc.get(ex - 1), tok_gv.get((ex - 1, NJT - 1)))
            pv = probs_sb[:].rearrange("p (t e) -> p e t", e=NEXP)[:, ex, :]
            V.emit(V.tensor_single_scalar(msk[:], pv, lo_t[:, ex:ex + 1], ALU.is_ge))
            V.drain()
            V.emit(V.tensor_tensor_scan(pref[:], onesNT[:], msk[:], 0.0, ALU.mult, ALU.add))
            V.drain()
            env.tok_v = V.emit(V.tensor_copy(prow[:, 5:6], pref[:, NT - 1:NT]))
            V.drain()
        sec("vector", vcp)
        tok_pf = env.tok_v

        def tcp(T, ex=ex, tok_pf=tok_pf):
            T.wait(tok_pf)
            env.tok_t = T.emit(T.matmul(PB[2][:, 0:1], env.tri[:], prow[:, 5:6], start=True, stop=True))
        sec("tensor", tcp)
        tok_exm = env.tok_t

        def vcp2(V, ex=ex, tok_exm=tok_exm):
            V.wait(tok_exm)
            V.emit(V.tensor_copy(excl[:], PB[2][:, 0:1]))
            V.drain()
            V.emit(V.tensor_scalar(jf_t[:], pref[:], excl[:, 0:1], -1.0, ALU.add, ALU.add))
            V.drain()
            V.emit(V.tensor_copy(mski[:], msk[:]))
            V.drain()
            V.emit(V.select(jself[:], mski[:], jf_t[:], bigf[:]))
            V.drain()
            env.tok_v = V.emit(V.tensor_copy(jsel[:], jself[:]))
            V.drain()
        sec("vector", vcp2)
        tok_js = env.tok_v

        def gsc(G, ex=ex, tok_js=tok_js):
            G.wait(tok_js, tok_c3)
            if ex == 0:
                G.wait(tok_pst[NT - 1])
            for tt in range(NT):
                ins = G.indirect_dma_start(
                    out=d.m_ds[ex].rearrange("(a b) -> a b", b=1),
                    out_offset=IndirectOffsetOnAxis(jsel[:, tt:tt + 1], 0),
                    in_=tokid_sb[:, tt:tt + 1], in_offset=None,
                    bounds_check=env.breg_ke, oob_is_err=False)
                G.emit(ins, dma=True)
            env.tok_g = G.emit(G.dma_start(
                out=d.inv_ds[ex].rearrange("(p t) -> p t", t=NT), in_=bigi[:, 0:NT]), dma=True)
        sec("gpsimd", gsc)
        tok_msc = env.tok_g

        def ym(Y, ex=ex, tok_msc=tok_msc):
            Y.wait(tok_msc)
            tok_mld[ex] = Y.emit(Y.dma_start(
                out=m_sb[:, ex * NJT:(ex + 1) * NJT],
                in_=d.m_ds[ex].rearrange("(p j) -> p j", j=NJT)), dma=True)
        sec("sync", ym)

        def gsc2(G, ex=ex):
            G.wait(tok_mld[ex])
            ins = None
            for jt in range(NJT):
                ins = G.indirect_dma_start(
                    out=d.inv_ds[ex].rearrange("(a b) -> a b", b=1),
                    out_offset=IndirectOffsetOnAxis(m_sb[:, ex * NJT + jt:ex * NJT + jt + 1], 0),
                    in_=jcol_sb[:, jt:jt + 1], in_offset=None,
                    bounds_check=env.breg_seq, oob_is_err=False)
                G.emit(ins, dma=True)
            tok_invsc[ex] = (env.sems["gpsimd"], env.sems["gpsimd"].n)
        sec("gpsimd", gsc2)

        for jt in range(NJT):
            def gpv(G, ex=ex, jt=jt):
                if jt > 0:
                    G.wait(tok_gv[(ex, jt - 1)])
                ins = G.indirect_dma_start(
                    out=ptmp[:], out_offset=None,
                    in_=d.probs_d[:],
                    in_offset=IndirectOffsetOnAxis(m_sb[:, ex * NJT + jt:ex * NJT + jt + 1], 0))
                tok_xg[(ex, jt, "p")] = G.emit(ins, dma=True)
            sec("gpsimd", gpv)

            def vgv(V, ex=ex, jt=jt):
                V.wait(tok_xg[(ex, jt, "p")])
                tok_gv[(ex, jt)] = V.emit(V.tensor_copy(
                    g_sb[:, ex * NJT + jt:ex * NJT + jt + 1], ptmp[:, ex:ex + 1]))
            sec("vector", vgv)

    barrier(env, block)
    sta.close()

    stb = ExitStack()
    eb = stb.enter_context
    w1h = eb(nc.sbuf_tensor("w1h", [128, NDC * DFF], BF16))
    w1h3 = w1h[:].rearrange("p (dd f) -> p dd f", dd=NDC)
    w2h = eb(nc.sbuf_tensor("w2h", [128, NFC * 512], BF16))
    w2h3 = w2h[:].rearrange("p (kc f) -> p kc f", kc=NFC)
    xin_tm = eb(nc.sbuf_tensor("xin_tm", [128, 2 * 1024], BF16))
    xin_fm = eb(nc.sbuf_tensor("xin_fm", [128, NDC * KE], BF16))
    hgel = eb(nc.sbuf_tensor("hgel", [128, NFC * KE], BF16))
    eo_tm = eb(nc.sbuf_tensor("eo_tm", [128, 2 * 512], BF16))

    for ex in range(NEXP):
        for jt in range(NJT):
            def gx(G, ex=ex, jt=jt):
                G.wait(tok_xtr.get((ex, jt - 2, NDC - 1)) if jt >= 2 else
                       tok_xtr.get((ex - 1, NJT - 2 + jt, NDC - 1)))
                ins = G.indirect_dma_start(
                    out=xin_tm[:, (jt % 2) * 1024:(jt % 2 + 1) * 1024], out_offset=None,
                    in_=d.xn1tm_d[:],
                    in_offset=IndirectOffsetOnAxis(m_sb[:, ex * NJT + jt:ex * NJT + jt + 1], 0))
                tok_xg[(ex, jt)] = G.emit(ins, dma=True)
            sec("gpsimd", gx)

            for dc in range(NDC):
                def tx(T, ex=ex, jt=jt, dc=dc):
                    T.wait(tok_xg[(ex, jt)])
                    T.wait(tok_xtrv.get((ex, jt, dc - 2)) if dc >= 2 else
                           tok_xtrv.get((ex, jt - 1, NDC - 2 + dc),
                                        tok_xtrv.get((ex - 1, NJT - 1, NDC - 2 + dc)) if jt == 0 else None))
                    if ex == 0 and jt == 0 and dc == 0:
                        T.wait(tok_gv.get((NEXP - 1, NJT - 1)))
                    tok_xtr[(ex, jt, dc)] = T.emit(T.transpose(
                        PB[3].bitcast(BF16)[:, (dc % 2) * 1024:(dc % 2) * 1024 + 128],
                        xin_tm[:, (jt % 2) * 1024 + dc * 128:(jt % 2) * 1024 + (dc + 1) * 128],
                        env.ident_bf[:]))
                sec("tensor", tx)

                def vx(V, ex=ex, jt=jt, dc=dc):
                    V.wait(tok_xtr[(ex, jt, dc)])
                    tok_xtrv[(ex, jt, dc)] = V.emit(V.tensor_copy(
                        xin_fm[:, dc * KE + jt * 128:dc * KE + (jt + 1) * 128],
                        PB[3].bitcast(BF16)[:, (dc % 2) * 1024:(dc % 2) * 1024 + 128]))
                sec("vector", vx)

        for hh in range(2):
            def gw1(G, ex=ex, hh=hh):
                G.wait(tok_m1.get((ex - 1, NFC - 1)))
                G.wait(env.tok_pp[("ew1", ex)])
                tok_w1l[(ex, hh)] = G.emit(G.dma_start(
                    out=w1h3[:, :, hh * 2048:(hh + 1) * 2048],
                    in_=d.ew1b[ex].rearrange("p (dd f) -> p dd f", dd=NDC)[:, :, hh * 2048:(hh + 1) * 2048]), dma=True)
            sec("gpsimd", gw1)

        for oc in range(NFC):
            hh = oc // (NFC // 2)

            def tm1(T, ex=ex, oc=oc, hh=hh):
                T.wait(tok_w1l[(ex, hh)], tok_xtrv[(ex, NJT - 1, NDC - 1)])
                T.wait(tok_hg.get((ex, oc - 2)) if oc >= 2 else tok_hg.get((ex - 1, NFC - 2 + oc)))
                pb = PB[oc % 2]
                for jh in range(NJH):
                    for dd in range(NDC):
                        ins = T.matmul(pb[:, jh * JW:(jh + 1) * JW],
                                       w1h3[:, dd, oc * 128:(oc + 1) * 128],
                                       xin_fm[:, dd * KE + jh * JW:dd * KE + (jh + 1) * JW],
                                       start=(dd == 0), stop=(dd == NDC - 1))
                tok_m1[(ex, oc)] = T.emit(ins)
            sec("tensor", tm1)

            def sg1(SS, ex=ex, oc=oc):
                SS.wait(tok_m1[(ex, oc)])
                tok_hg[(ex, oc)] = SS.emit(SS.activation(
                    out=hgel[:, oc * KE:(oc + 1) * KE], in_=PB[oc % 2][:, 0:KE],
                    func=AF.Gelu_apprx_tanh))
            sec("scalar", sg1)

        if cfg.dbg and ex == 0:
            def ydbg4(Y, ex=ex):
                Y.wait(tok_hg[(0, NFC - 1)], tok_xtrv[(0, NJT - 1, NDC - 1)])
                Y.emit(Y.dma_start(out=d.xinfm_d[:, :], in_=xin_fm[:]), dma=True)
                env.tok_y = Y.emit(Y.dma_start(out=d.hgel_d[:, :], in_=hgel[:, 0:2 * KE]), dma=True)
            sec("sync", ydbg4)

            def vdbg4(V):
                V.wait(env.tok_y)
            sec("vector", vdbg4)

        for dh in range(2):
            def gw2(G, ex=ex, dh=dh):
                G.wait(tok_m2.get((ex, dh - 1, NJT - 1), tok_m2.get((ex - 1, 1, NJT - 1)) if dh == 0 else None))
                G.wait(env.tok_pp[("ew2", ex)])
                tok_w2l[(ex, dh)] = G.emit(G.dma_start(
                    out=w2h3,
                    in_=d.ew2b[ex].rearrange("p (kc f) -> p kc f", kc=NFC)[:, :, dh * 512:(dh + 1) * 512]), dma=True)
            sec("gpsimd", gw2)

            for jb in range(NJT):
                def tm2(T, ex=ex, dh=dh, jb=jb):
                    T.wait(tok_w2l[(ex, dh)], tok_hg[(ex, NFC - 1)])
                    T.wait(tok_eev.get((ex, dh, jb - 2)) if jb >= 2 else
                           (tok_eev.get((ex, dh - 1, NJT - 2 + jb)) if dh == 1 else
                            tok_eev.get((ex - 1, 1, NJT - 2 + jb))))
                    pb2 = PB[2][:, (jb % 2) * 512:((jb % 2) + 1) * 512]
                    for kc in range(NFC):
                        ins = T.matmul(pb2,
                                       hgel[:, kc * KE + jb * 128:kc * KE + (jb + 1) * 128],
                                       w2h3[:, kc, :],
                                       start=(kc == 0), stop=(kc == NFC - 1))
                    tok_m2[(ex, dh, jb)] = T.emit(ins)
                sec("tensor", tm2)

                def ve2(V, ex=ex, dh=dh, jb=jb):
                    V.wait(tok_m2[(ex, dh, jb)])
                    V.wait(tok_est.get((ex, dh, jb - 2)) if jb >= 2 else
                           (tok_est.get((ex, dh - 1, NJT - 2 + jb)) if dh == 1 else
                            tok_est.get((ex - 1, 1, NJT - 2 + jb))))
                    tok_eev[(ex, dh, jb)] = V.emit(V.tensor_scalar(
                        eo_tm[:, (jb % 2) * 512:((jb % 2) + 1) * 512],
                        PB[2][:, (jb % 2) * 512:((jb % 2) + 1) * 512],
                        g_sb[:, ex * NJT + jb:ex * NJT + jb + 1], None, ALU.mult))
                sec("vector", ve2)

                def ye2(G, ex=ex, dh=dh, jb=jb):
                    G.wait(tok_eev[(ex, dh, jb)])
                    tok_est[(ex, dh, jb)] = G.emit(G.dma_start(
                        out=d.expout_ds[ex][jb * 128:(jb + 1) * 128, dh * 512:(dh + 1) * 512],
                        in_=eo_tm[:, (jb % 2) * 512:((jb % 2) + 1) * 512]), dma=True)
                sec("gpsimd", ye2)

    barrier(env, block)
    stb.close()

    stc = ExitStack()
    ec3 = stc.enter_context
    gacc = ec3(nc.sbuf_tensor("gacc", [128, 2 * 1024], F32))
    gall = ec3(nc.sbuf_tensor("gall", [128, NEXP * 1024], F32))
    gsil = ec3(nc.sbuf_tensor("gsil", [128, 1024], F32))
    o1t = ec3(nc.sbuf_tensor("o1t", [128, 2 * 1024], F32))
    s1t = ec3(nc.sbuf_tensor("s1t", [128, 2 * 1024], F32))
    outt = ec3(nc.sbuf_tensor("outt", [128, 2 * 1024], F32))

    def yinv(Y):
        ins = None
        for ex in range(NEXP):
            ins = Y.dma_start(out=inv_sb[:, ex * NT:(ex + 1) * NT],
                              in_=d.inv_ds[ex].rearrange("(p t) -> p t", t=NT))
            env.tok_y = Y.emit(ins, dma=True)
    sec("sync", yinv)
    tok_invld = env.tok_y

    tok_ld3 = {}
    for tt in range(NT):
        def yl3(Y, tt=tt):
            Y.wait(tok_cmb3.get(tt - 2))
            tok_ld3[tt] = Y.emit(Y.dma_start(
                out=o1t[:, (tt % 2) * 1024:(tt % 2 + 1) * 1024],
                in_=d.out1_d.rearrange("(p t) dd -> p t dd", t=NT)[:, tt, :]), dma=True)
            tok_ld3[(tt, "s")] = Y.emit(Y.dma_start(
                out=s1t[:, (tt % 2) * 1024:(tt % 2 + 1) * 1024],
                in_=d.s1tm_d.rearrange("(p t) dd -> p t dd", t=NT)[:, tt, :]), dma=True)
        sec("sync", yl3)

        def gga(G, tt=tt):
            G.wait(tok_invld)
            if tt == 0:
                G.wait(tok_est[(NEXP - 1, 1, NJT - 1)])
            G.wait(tok_acc.get(tt - 1))
            for ex in range(NEXP):
                ins = G.indirect_dma_start(
                    out=gall[:, ex * 1024:(ex + 1) * 1024], out_offset=None,
                    in_=d.expout_ds[ex][:],
                    in_offset=IndirectOffsetOnAxis(inv_sb[:, ex * NT + tt:ex * NT + tt + 1], 0))
                tok_gat[tt] = G.emit(ins, dma=True)
        sec("gpsimd", gga)

        def vga(V, tt=tt):
            V.wait(tok_gat[tt])
            acc = gacc[:, (tt % 2) * 1024:(tt % 2 + 1) * 1024]
            V.emit(V.tensor_tensor(acc, gall[:, 0:1024], gall[:, 1024:2048], ALU.add))
            for ex in range(2, NEXP):
                env.tok_v = V.emit(V.tensor_tensor(
                    acc, acc, gall[:, ex * 1024:(ex + 1) * 1024], ALU.add))
            tok_acc[tt] = env.tok_v
        sec("vector", vga)
        tok_gsum = tok_acc[tt]

        if cfg.dbg and tt == 0:
            def ydbg3(Y, tok_gsum=tok_gsum):
                Y.wait(tok_gsum, tok_invld)
                Y.emit(Y.dma_start(out=d.gacc_d[:, :], in_=gacc[:, 0:1024]), dma=True)
                env.tok_y = Y.emit(Y.dma_start(out=d.inv0_d[:, :], in_=inv_sb[:]), dma=True)
            sec("sync", ydbg3)

            def vdbg3(V):
                V.wait(env.tok_y)
            sec("vector", vdbg3)

        def sga(SS, tt=tt, tok_gsum=tok_gsum):
            SS.wait(tok_gsum)
            env.tok_s = SS.emit(SS.activation(
                out=gsil[:], in_=gacc[:, (tt % 2) * 1024:(tt % 2 + 1) * 1024], func=AF.Silu))
        sec("scalar", sga)
        tok_sil = env.tok_s

        def vcb(V, tt=tt, tok_sil=tok_sil):
            V.wait(tok_sil, tok_ld3[(tt, "s")])
            V.wait(tok_ost.get(tt - 2))
            sl = slice((tt % 2) * 1024, (tt % 2 + 1) * 1024)
            V.emit(V.tensor_tensor(gsil[:], s1t[:, sl], gsil[:], ALU.mult))
            tok_cmb3[tt] = V.emit(V.tensor_add(outt[:, sl], o1t[:, sl], gsil[:]))
        sec("vector", vcb)

        def yo(Y, tt=tt):
            Y.wait(tok_cmb3[tt])
            tok_ost[tt] = Y.emit(Y.dma_start(
                out=d.out.rearrange("(p t) dd -> p t dd", t=NT)[:, tt, :],
                in_=outt[:, (tt % 2) * 1024:(tt % 2 + 1) * 1024]), dma=True)
        sec("sync", yo)

    env.tok_out_done = tok_ost[NT - 1]
    barrier(env, block)
    stc.close()
    st3.close()
    pstack3.close()

def barrier(env, block):
    toks = [(env.sems[k], env.sems[k].n) for k in ENGS]
    for k in ENGS:
        def body(E, toks=toks):
            E.wait(*toks)
        _mk_section(env, block, k, body)


class Cfg:
    def __init__(self, seq=SEQ, dbg=False, phases=(1, 2, 3)):
        self.seq = seq
        self.dbg = dbg
        self.phases = phases


def build_program(cfg):
    nc = bass.Bass("TRN2", target_bir_lowering=False, detect_race_conditions=False)
    with ExitStack() as ctx:
        env = mk_env(nc, ctx)
        d = declare_io(nc, cfg.seq, dbg=cfg.dbg)
        with nc.Block() as block:
            setup_stage(nc, env, d, block)
            if 1 in cfg.phases:
                phase1(nc, env, d, block, cfg)
            if 2 in cfg.phases:
                barrier(env, block)
                phase2(nc, env, d, block, cfg)
            if 3 in cfg.phases:
                barrier(env, block)
                phase3(nc, env, d, block, cfg)
            barrier(env, block)
    return nc


def kernel(**inputs):
    W = host_weights(inputs)
    x = np.asarray(inputs["L0_attn_x"])
    C = host_constants(x.shape[1])
    nb = x.shape[0]
    cfg = Cfg(seq=x.shape[1], dbg=False, phases=(1, 2, 3))
    nc = build_program(cfg)
    in_maps = [make_in_map(x[b], W, C) for b in range(nb)]
    res = run_bass_kernel_spmd(nc, in_maps, core_ids=list(range(nb)))
    return np.stack([r["out"] for r in res.results], axis=0).astype(np.float32)


# version 10
# speedup vs baseline: 1.0553x; 1.0155x over previous
import numpy as np
import ml_dtypes
from contextlib import ExitStack

import concourse.bass as bass
import concourse.mybir as mybir
from concourse.bass_utils import run_bass_kernel_spmd
from concourse.bass_core_rust import IndirectOffsetOnAxis

F32 = mybir.dt.float32
BF16 = mybir.dt.bfloat16
I32 = mybir.dt.int32
AF = mybir.ActivationFunctionType
ALU = mybir.AluOpType
AX = mybir.AxisListType

SEQ = 4096
D = 1024
NH = 16
DHD = 64
NDC = 8
DFF = 4096
NFC = 32
NEXP = 8
KEXP = 1024
NST = 16
LN_EPS = 1e-5
RMS_EPS = 1e-6
MAXLOG = float(np.log(1.0 / 0.01))
BIGIDX = 1 << 28

bf = ml_dtypes.bfloat16


class SemC:

    def __init__(self, ctx, nc, name):
        self.h = ctx.enter_context(nc.semaphore(name))
        self.n = 0

    def inc(self, instr, k=1):
        instr.then_inc(self.h, k)
        self.n += k
        return self.n


class Env:
    pass


def _wait(eng, sem, val=None):
    eng.wait_ge(sem.h, sem.n if val is None else val)



def host_constants(seq=SEQ):
    c = {}
    c["ident_bf"] = np.eye(128, dtype=bf)
    c["ident_f32"] = np.eye(128, dtype=np.float32)
    c["ones1"] = np.ones((1, 128), dtype=np.float32)
    sel2 = np.zeros((2, 128), dtype=np.float32)
    sel2[0, 0:64] = 1.0
    sel2[1, 64:128] = 1.0
    c["sel2"] = sel2
    s16 = np.zeros((NST, NST, 128), dtype=np.float32)
    for n in range(NST):
        s16[n, n, :] = 1.0
    c["sel16"] = s16
    hm = np.zeros((NDC, 128, NH), dtype=np.float32)
    for ch in range(NDC):
        hm[ch, 0:64, 2 * ch] = 1.0
        hm[ch, 64:128, 2 * ch + 1] = 1.0
    c["hmask"] = hm
    m2 = np.zeros((128, 2), dtype=np.float32)
    m2[0:64, 0] = 1.0
    m2[64:128, 1] = 1.0
    c["mask2"] = m2
    c["tri"] = np.triu(np.ones((128, 128), dtype=np.float32), 1)
    c["tokid"] = (np.arange(seq // 128)[None, :] * 128 + np.arange(128)[:, None]).astype(np.int32)
    c["jcol"] = (np.arange(8)[None, :] * 128 + np.arange(128)[:, None]).astype(np.int32)
    return c


def host_weights(I):
    w = {}
    w["w_qkv"] = I["L0_attn_w_qkv"]
    w["w_out"] = I["L0_attn_w_out"]
    w["b_out"] = I["L0_attn_b_out"].reshape(NDC, 128).T.copy()
    w["logit_scale"] = I["L0_attn_logit_scale"].reshape(NH)
    def och_pm(a, no):
        ic = a.shape[1] // 128
        return np.ascontiguousarray(
            a.T.reshape(ic, 128, no, 128).transpose(2, 1, 0, 3).reshape(no, 128, ic * 128))
    w["fw1"] = och_pm(I["L0_ffn_w1_w"], 32)
    w["fw2"] = och_pm(I["L0_ffn_w2_w"], 32)
    w["fw3"] = och_pm(I["L0_ffn_w3_w"], 8)
    w["fh1"] = och_pm(I["L0_ffn_h1_w"], 32)
    w["fh2"] = och_pm(I["L0_ffn_h2_w"], 32)
    w["s1"] = I["L0_ffn_w1_s"].reshape(NFC, 128).T.copy()
    w["s2"] = I["L0_ffn_w2_s"].reshape(NFC, 128).T.copy()
    w["h1s"] = I["L0_ffn_h1_s"].reshape(NFC, 128).T.copy()
    w["h2s"] = I["L0_ffn_h2_s"].reshape(NFC, 128).T.copy()
    w["s3"] = I["L0_ffn_w3_s"].reshape(NDC, 128).T.copy()
    w["norm1_w"] = I["L1_attn_norm1_w"].reshape(NDC, 128).T.copy()
    w["norm2_w"] = I["L1_attn_norm2_w"].reshape(NDC, 128).T.copy()
    w["A_log"] = I["L1_attn_A_log"]
    w["D_param"] = I["L1_attn_D_param"].reshape(NDC, 128).T.copy()
    w["mscale"] = I["L1_attn_scale"].reshape(NDC, 128).T.copy()
    w["Wdt"] = np.ascontiguousarray(I["L1_attn_Wdt"].T)
    w["bdt"] = I["L1_attn_bdt"].reshape(NDC, 128).T.copy()
    wbc = np.zeros((1024, 64), dtype=np.float32)
    wbc[:, 0:16] = I["L1_attn_WB"].T
    wbc[:, 32:48] = I["L1_attn_WC"].T
    w["WBC"] = wbc
    bbc = np.zeros((64, 1), dtype=np.float32)
    bbc[0:16, 0] = I["L1_attn_bB"]
    bbc[32:48, 0] = I["L1_attn_bC"]
    w["bBC"] = bbc
    w["W1"] = och_pm(I["L1_attn_W1"], 32)
    w["b1"] = I["L1_attn_b1"].reshape(NFC, 128).T.copy()
    w["W2"] = och_pm(I["L1_attn_W2"], 8)
    w["b2"] = I["L1_attn_b2"].reshape(NDC, 128).T.copy()
    w["gate_w"] = I["L1_ffn_gate_w"]
    w["ew1"] = np.ascontiguousarray(
        I["L1_ffn_w1"].reshape(NEXP, 8, 128, DFF).transpose(0, 2, 1, 3).reshape(NEXP, 128, 8 * DFF))
    w["ew2"] = np.ascontiguousarray(
        I["L1_ffn_w2"].reshape(NEXP, 32, 128, D).transpose(0, 2, 1, 3).reshape(NEXP, 128, 32 * D))
    return w



def declare_io(nc, seq, dbg=False):
    d = Env()
    T = lambda name, shape, dt, kind="ExternalInput": nc.dram_tensor(name, list(shape), dt, kind=kind).ap()
    scr = "ExternalOutput" if dbg else "Internal"
    d.x0 = T("x0", (seq, D), F32)
    d.w_qkv = T("w_qkv", (D, 3 * D), F32)
    d.w_out = T("w_out", (D, D), F32)
    d.b_out = T("b_out", (128, NDC), F32)
    d.logit_scale = T("logit_scale", (NH,), F32)
    d.fw1 = T("fw1", (NFC, 128, D), F32)
    d.fw2 = T("fw2", (NFC, 128, D), F32)
    d.fw3 = T("fw3", (NDC, 128, DFF), F32)
    d.fh1 = T("fh1", (NFC, 128, DFF), F32)
    d.fh2 = T("fh2", (NFC, 128, DFF), F32)
    d.fw1b = T("fw1b", (NFC, 128, D), BF16, "Internal")
    d.fw2b = T("fw2b", (NFC, 128, D), BF16, "Internal")
    d.fw3b = T("fw3b", (NDC, 128, DFF), BF16, "Internal")
    d.fh1b = T("fh1b", (NFC, 128, DFF), BF16, "Internal")
    d.fh2b = T("fh2b", (NFC, 128, DFF), BF16, "Internal")
    d.W1b = T("W1b", (NFC, 128, D), BF16, "Internal")
    d.W2b = T("W2b", (NDC, 128, DFF), BF16, "Internal")
    d.ew1b = T("ew1b", (NEXP, 128, 8 * DFF), BF16, "Internal")
    d.ew2b = T("ew2b", (NEXP, 128, 32 * D), BF16, "Internal")
    d.s1 = T("s1", (128, NFC), F32)
    d.s2 = T("s2", (128, NFC), F32)
    d.h1s = T("h1s", (128, NFC), F32)
    d.h2s = T("h2s", (128, NFC), F32)
    d.s3 = T("s3", (128, NDC), F32)
    d.norm1_w = T("norm1_w", (128, NDC), F32)
    d.norm2_w = T("norm2_w", (128, NDC), F32)
    d.A_log = T("A_log", (NST,), F32)
    d.D_param = T("D_param", (128, NDC), F32)
    d.mscale = T("mscale", (128, NDC), F32)
    d.Wdt = T("Wdt", (D, D), F32)
    d.bdt = T("bdt", (128, NDC), F32)
    d.WBC = T("WBC", (D, 64), F32)
    d.bBC = T("bBC", (64, 1), F32)
    d.W1 = T("W1", (NFC, 128, D), F32)
    d.b1 = T("b1", (128, NFC), F32)
    d.W2 = T("W2", (NDC, 128, DFF), F32)
    d.b2 = T("b2", (128, NDC), F32)
    d.gate_w = T("gate_w", (D, NEXP), F32)
    d.ew1 = T("ew1", (NEXP, 128, 8 * DFF), F32)
    d.ew2 = T("ew2", (NEXP, 128, 32 * D), F32)
    d.ident_bf = T("ident_bf", (128, 128), BF16)
    d.ident_f32 = T("ident_f32", (128, 128), F32)
    d.ones1 = T("ones1", (1, 128), F32)
    d.sel2 = T("sel2", (2, 128), F32)
    d.hmask = T("hmask", (NDC, 128, NH), F32)
    d.mask2 = T("mask2", (128, 2), F32)
    d.tri = T("tri", (128, 128), F32)
    d.tokid = T("tokid", (128, seq // 128), I32)
    d.sel16 = T("sel16", (NST, NST, 128), F32)
    d.jcol = T("jcol", (128, 8), I32)
    d.xn0_d = T("xn0_d", (D, seq), BF16, scr)
    d.vaug_d = T("vaug_d", (NDC, 128, (seq // 128) * 130), BF16, scr)
    d.s0_d = T("s0_d", (D, seq), F32, scr)
    d.out1_d = T("out1_d", (seq, D), F32, scr)
    d.xn1fm_d = T("xn1fm_d", (D, seq), BF16, scr)
    d.xn1tm_d = T("xn1tm_d", (seq, D), BF16, scr)
    d.s1tm_d = T("s1tm_d", (seq, D), F32, scr)
    d.x1_d = T("x1_d", (D, seq), BF16, scr)
    d.probs_d = T("probs_d", (seq, NEXP), F32, scr)
    d.expout_ds = [T(f"expout_d{ex}", (seq // 4 + 1, D), F32, scr) for ex in range(NEXP)]
    d.inv_ds = [T(f"inv_d{ex}", (seq,), I32, scr) for ex in range(NEXP)]
    d.m_ds = [T(f"m_d{ex}", (seq // 4,), I32, scr) for ex in range(NEXP)]
    if dbg:
        d.qn_d = T("qn_d", (128, NDC * 1024), BF16, "ExternalOutput")
        d.ps_d = T("ps_d", (128, 2 * 1024), F32, "ExternalOutput")
        d.pt_d = T("pt_d", (128, 2 * 1024), BF16, "ExternalOutput")
        d.rk_d = T("rk_d", (128, 64 * NH), F32, "ExternalOutput")
        d.rk1b_d = T("rk1b_d", (NST, 512), F32, "ExternalOutput")
        d.ss_d = T("ss_d", (NST, 512), F32, "ExternalOutput")
        d.sq_d = T("sq_d", (128, 512), F32, "ExternalOutput")
        d.k_d = T("k_d", (128, NDC * 512), BF16, "ExternalOutput")
        d.dt_d = T("dt_d", (128, NDC * 512), F32, "ExternalOutput")
        d.bvc_d = T("bvc_d", (NST, 2 * 512), F32, "ExternalOutput")
        d.yac_d = T("yac_d", (128, NDC * 512), F32, "ExternalOutput")
        d.xnn_d = T("xnn_d", (128, NDC * 512), BF16, "ExternalOutput")
        d.wc_d = T("wc_d", (128, NDC * 512), BF16, "ExternalOutput")
        d.thr_d = T("thr_d", (128, NEXP), F32, "ExternalOutput")
        d.gacc_d = T("gacc_d", (128, 1024), F32, "ExternalOutput")
        d.gtmp_d = T("gtmp_d", (128, 1024), BF16, "ExternalOutput")
        d.xinfm_d = T("xinfm_d", (128, NDC * (seq // 4)), BF16, "ExternalOutput")
        d.hgel_d = T("hgel_d", (128, 2 * (seq // 4)), BF16, "ExternalOutput")
        d.inv0_d = T("inv0_d", (128, NEXP * (seq // 128)), I32, "ExternalOutput")
    d.out = T("out", (seq, D), F32, "ExternalOutput")
    return d


def make_in_map(xb, W, C):
    m = dict(x0=np.ascontiguousarray(xb))
    m.update({k: np.ascontiguousarray(v) for k, v in W.items()})
    m.update({k: np.ascontiguousarray(v) for k, v in C.items()})
    return m




ACT_SET = {AF.Exp: 6, AF.Ln: 6, AF.Gelu: 10, AF.Gelu_apprx_tanh: 11, AF.Silu: 18}

ATTN_LA = 3
ATTN_PTM = 8


class Eng:
    def __init__(self, env, raw, sem, is_dma=False):
        self.env, self.raw, self.sem = env, raw, sem
        self.is_dma = is_dma

    def wait(self, *toks):
        for tok in toks:
            if tok is None:
                continue
            sem, val = tok
            if val <= 0:
                continue
            if sem is self.sem and not self.is_dma:
                continue
            assert val <= sem.n, f"wait on future token {val} > {sem.n}"
            key = (id(self.sem), id(sem))
            if self.env.waited.get(key, 0) >= val:
                continue
            self.raw.wait_ge(sem.h, val)
            self.env.waited[key] = val

    def emit(self, instr, dma=False):
        self.sem.inc(instr, 16 if dma else 1)
        return (self.sem, self.sem.n)

    def drain(self):
        self.raw.wait_ge(self.sem.h, self.sem.n)

    def activation(self, *a, **kw):
        sid = ACT_SET.get(kw.get("func"))
        if sid is None and self.env.cur_tset is None:
            sid = 6
        if sid is not None and self.env.cur_tset != sid:
            self.raw.add_instruction(mybir.InstLoadActFuncSet(
                name=self.raw.bass.get_next_instruction_name(),
                act_func_set_id=sid, ins=[], outs=[]))
            self.env.cur_tset = sid
        return self.raw.activation(*a, **kw)

    def __getattr__(self, name):
        return getattr(self.raw, name)


ENGS = ["tensor", "vector", "scalar", "gpsimd", "sync"]


def mk_env(nc, ctx):
    env = Env()
    env.nc = nc
    env.ctx = ctx
    env.waited = {}
    env.cur_tset = None
    env.sems = {k: SemC(ctx, nc, f"es_{k}") for k in ENGS}
    env.ppsem = SemC(ctx, nc, "es_pp")
    env.tok_pp = {}
    return env


def _mk_section(env, block, engname, body):
    getattr(block, engname)(lambda raw: body(
        Eng(env, raw, env.sems[engname], is_dma=engname in ("sync", "gpsimd"))))



CONSTS = ["r1024", "ln_eps", "rms_eps", "mhalf", "half", "mone", "one"]
CONST_VALS = {"r1024": 1.0 / 1024.0, "ln_eps": LN_EPS, "rms_eps": RMS_EPS,
              "mhalf": -0.5, "half": 0.5, "mone": -1.0, "one": 1.0}


def setup_stage(nc, env, d, block):
    ctx = env.ctx
    e = ctx.enter_context
    env.cpool = e(nc.sbuf_tensor("cpool", [128, len(CONSTS)], F32))
    env.cap = {}
    for i, k in enumerate(CONSTS):
        ap = env.cpool[:, i:i + 1]
        env.cap[k] = ap
        nc.const_aps.register(F32, float(CONST_VALS[k]), ap)
        if float(np.float32(CONST_VALS[k])) != float(CONST_VALS[k]):
            nc.const_aps.register(F32, float(np.float32(CONST_VALS[k])), ap)
    env.ident_bf = e(nc.sbuf_tensor("identbf", [128, 128], BF16))
    env.ident_f32 = e(nc.sbuf_tensor("identf", [128, 128], F32))
    env.ones1 = e(nc.sbuf_tensor("ones1s", [1, 128], F32))
    env.onescol = e(nc.sbuf_tensor("onescol", [128, 1], F32))
    env.sel2 = e(nc.sbuf_tensor("sel2s", [2, 128], F32))
    env.hmask = e(nc.sbuf_tensor("hmasks", [128, NDC * NH], F32))
    env.mask2 = e(nc.sbuf_tensor("mask2s", [128, 2], F32))
    env.tri = e(nc.sbuf_tensor("tris", [128, 128], F32))
    env.bout_sb = e(nc.sbuf_tensor("bouts", [128, NDC], F32))
    env.s1_sb = e(nc.sbuf_tensor("s1s", [128, NFC], F32))
    env.s2_sb = e(nc.sbuf_tensor("s2s", [128, NFC], F32))
    env.h1s_sb = e(nc.sbuf_tensor("h1ss", [128, NFC], F32))
    env.h2s_sb = e(nc.sbuf_tensor("h2ss", [128, NFC], F32))
    env.s3_sb = e(nc.sbuf_tensor("s3s", [128, NDC], F32))

    def ysec(Y):
        toks = []
        for src, dst in [(d.ident_bf, env.ident_bf), (d.ident_f32, env.ident_f32),
                         (d.ones1, env.ones1), (d.sel2, env.sel2),
                         (d.tri, env.tri), (d.b_out, env.bout_sb), (d.mask2, env.mask2),
                         (d.s1, env.s1_sb), (d.s2, env.s2_sb),
                         (d.h1s, env.h1s_sb), (d.h2s, env.h2s_sb), (d.s3, env.s3_sb)]:
            toks.append(Y.emit(Y.dma_start(out=dst[:], in_=src[:]), dma=True))
        toks.append(Y.emit(Y.dma_start(
            out=env.hmask[:].rearrange("p (c h) -> p c h", c=NDC),
            in_=d.hmask.transpose([1, 0, 2])), dma=True))
        env.tok_consts_loaded = toks[-1]
    _mk_section(env, block, "sync", ysec)

    def vsec(V):
        for i, k in enumerate(CONSTS):
            t = V.emit(V.memset(env.cpool[:, i:i + 1], float(np.float32(CONST_VALS[k]))))
        t = V.emit(V.memset(env.onescol[:], 1.0))
        env.tok_cpool = t
    _mk_section(env, block, "vector", vsec)


def prepass_weights(nc, env, d, block, group):
    pp = env.ppsem

    def ssec(SS):
        if group == "p1":
            pairs = [("fw1", d.fw1, d.fw1b, NFC), ("fw2", d.fw2, d.fw2b, NFC),
                     ("fh1", d.fh1, d.fh1b, NFC), ("fh2", d.fh2, d.fh2b, NFC),
                     ("fw3", d.fw3, d.fw3b, NDC), ("W1", d.W1, d.W1b, NFC),
                     ("W2", d.W2, d.W2b, NDC)]
        else:
            pairs = [("ew1", d.ew1, d.ew1b, NEXP), ("ew2", d.ew2, d.ew2b, NEXP)]
        for name, src, dst, n in pairs:
            for oc in range(n):
                ins = SS.raw.dma_start(out=dst[oc], in_=src[oc])
                pp.inc(ins, 16)
                env.tok_pp[(name, oc)] = (pp, pp.n)
    _mk_section(env, block, "gpsimd", ssec)



def phase1(nc, env, d, block, cfg):
    ctx = env.ctx
    e = ctx.enter_context
    seq = cfg.seq
    NT = seq // 128
    NT5 = seq // 512
    NG = seq // 1024
    sec = lambda name, body: _mk_section(env, block, name, body)

    pstack = ExitStack()
    PB = [pstack.enter_context(nc.psum_tensor(f"p1PB{i}", [128, 1024], F32)) for i in range(4)]

    kbstack = ExitStack()
    ekb = kbstack.enter_context
    K_sb = ekb(nc.sbuf_tensor("K_sb", [128, NDC * seq], BF16))
    rkT = ekb(nc.sbuf_tensor("rkT", [128, NT * NH], F32))
    negsh = ekb(nc.sbuf_tensor("negsh", [128, NH], F32))
    shv16 = ekb(nc.sbuf_tensor("shv16", [NST, 1], F32))
    lsrow = ekb(nc.sbuf_tensor("lsrow", [1, NH], F32))

    astack = ExitStack()
    ea = astack.enter_context
    kqw = ea(nc.sbuf_tensor("kqw", [128, NDC * 3 * D], BF16))
    kqw3 = kqw[:].rearrange("p (dd f) -> p dd f", dd=NDC)
    xin = ea(nc.sbuf_tensor("xin", [128, 2 * 1024], F32))
    junk = ea(nc.sbuf_tensor("junk1", [128, 1024], F32))
    mvr = ea(nc.sbuf_tensor("mvr", [128, 8], F32))
    xn_tm = ea(nc.sbuf_tensor("xn_tm", [128, 2 * 1024], BF16))
    xnfm = ea(nc.sbuf_tensor("xnfm", [128, 2 * NDC * 512], BF16))
    vtile = ea(nc.sbuf_tensor("vtile", [128, 2 * NH * 65], BF16))

    trps = [PB[0].bitcast(BF16)[:, 0:128], PB[0].bitcast(BF16)[:, 1024:1152]]
    kps = [PB[1][:, 0:512], PB[1][:, 512:1024]]
    vps = [PB[2][:, 0:512], PB[2][:, 512:1024]]

    def gsec_w(G):
        G.wait(env.tok_cpool)
        env.tok_kqw = G.emit(G.dma_start(
            out=kqw3, in_=d.w_qkv.rearrange("(dd p) f -> p dd f", p=128)), dma=True)
    sec("gpsimd", gsec_w)

    tok_ln = {}
    tok_xin_free = {}
    tok_tr = {}
    tok_trev = {}
    tok_kmm = {}
    tok_kev = {}
    tok_vmm = {}
    tok_vev = {}
    tok_vst = {}
    tok_xn0st = {}
    tok_slab_done = {}

    for ts in range(NT5):
        fmb = ts % 2
        xnv = xnfm[:, fmb * NDC * 512:(fmb + 1) * NDC * 512]
        for sub in range(4):
            tt = ts * 4 + sub
            bx = tt % 2

            def ysec(Y, tt=tt, bx=bx):
                Y.wait(tok_xin_free.get(tt - 2))
                env.tok_x0 = Y.emit(Y.dma_start(
                    out=xin[:, bx * 1024:(bx + 1) * 1024],
                    in_=d.x0[tt * 128:(tt + 1) * 128, :]), dma=True)
            sec("sync", ysec)
            tok_x0 = env.tok_x0

            def vsec(V, tt=tt, bx=bx, tok_x0=tok_x0):
                V.wait(tok_x0)
                xv = xin[:, bx * 1024:(bx + 1) * 1024]
                V.emit(V.tensor_reduce(mvr[:, 0:1], xv, AX.X, ALU.add))
                V.drain()
                V.emit(V.tensor_scalar_mul(mvr[:, 1:2], mvr[:, 0:1], 1.0 / 1024.0))
                V.drain()
                env.tok_v = V.emit(V.scalar_tensor_tensor(
                    junk[:], xv, mvr[:, 1:2], xv, ALU.subtract, ALU.mult,
                    accum_out=mvr[:, 2:3]))
            sec("vector", vsec)
            tok_stats = env.tok_v

            def ssec(SS, tok_stats=tok_stats):
                SS.wait(env.tok_cpool, tok_stats)
                SS.emit(SS.activation(out=mvr[:, 3:4], in_=mvr[:, 2:3], func=AF.Ln,
                                      scale=1.0 / 1024.0, bias=LN_EPS))
                SS.drain()
                env.tok_s = SS.emit(SS.activation(out=mvr[:, 4:5], in_=mvr[:, 3:4],
                                                  func=AF.Exp, scale=-0.5))
            sec("scalar", ssec)
            tok_rstd = env.tok_s

            def vsec2(V, tt=tt, bx=bx, tok_rstd=tok_rstd):
                V.wait(tok_rstd)
                V.wait(tok_tr.get((tt - 2, NDC - 1)))
                xv = xin[:, bx * 1024:(bx + 1) * 1024]
                env.tok_v = V.emit(V.tensor_scalar(
                    xn_tm[:, bx * 1024:(bx + 1) * 1024], xv, mvr[:, 1:2], mvr[:, 4:5],
                    ALU.subtract, ALU.mult))
                tok_xin_free[tt] = env.tok_v
            sec("vector", vsec2)
            tok_ln[tt] = env.tok_v

            for dc in range(NDC):
                def tsec(T, tt=tt, bx=bx, dc=dc, ts=ts):
                    T.wait(tok_ln[tt])
                    T.wait(tok_trev.get((tt, dc - 2), tok_trev.get((tt - 1, NDC - 2 + dc), None) if dc < 2 else None))
                    T.wait(tok_slab_done.get(ts - 2))
                    tok_tr[(tt, dc)] = T.emit(T.transpose(
                        trps[dc % 2], xn_tm[:, bx * 1024 + dc * 128: bx * 1024 + (dc + 1) * 128],
                        env.ident_bf[:]))
                sec("tensor", tsec)

                def vsec3(V, tt=tt, dc=dc, sub=sub, xnv=xnv):
                    V.wait(tok_tr[(tt, dc)])
                    dst = xnv[:, dc * 512 + sub * 128: dc * 512 + sub * 128 + 128]
                    tok_trev[(tt, dc)] = V.emit(V.tensor_copy(dst, trps[dc % 2]))
                sec("vector", vsec3)

        for c in range(NDC):
            def tsec(T, ts=ts, c=c, xnv=xnv):
                T.wait(env.tok_kqw)
                T.wait(tok_trev[(ts * 4 + 3, NDC - 1)])
                T.wait(tok_kev.get((ts, c - 2)) if c >= 2 else tok_kev.get((ts - 1, NDC - 2 + c)))
                for dd in range(NDC):
                    ins = T.matmul(kps[c % 2],
                                   kqw3[:, dd, D + c * 128: D + (c + 1) * 128],
                                   xnv[:, dd * 512:(dd + 1) * 512],
                                   start=(dd == 0), stop=(dd == NDC - 1))
                tok_kmm[(ts, c)] = T.emit(ins)
            sec("tensor", tsec)

            def vsec(V, ts=ts, c=c):
                V.wait(tok_kmm[(ts, c)])
                tok_kev[(ts, c)] = V.emit(V.tensor_copy(
                    K_sb[:, c * seq + ts * 512: c * seq + (ts + 1) * 512], kps[c % 2]))
            sec("vector", vsec)

        for sub in range(4):
            tt = ts * 4 + sub
            vb = tt % 2
            vt = vtile[:, vb * NH * 65:(vb + 1) * NH * 65]
            vt3 = vt.rearrange("p (h c) -> p h c", h=NH)
            for fc in range(2):
                def tsec(T, ts=ts, sub=sub, fc=fc, xnv=xnv):
                    T.wait(tok_vev.get((ts * 4 + sub - 1, fc)))
                    for dd in range(NDC):
                        ins = T.matmul(vps[fc],
                                       xnv[:, dd * 512 + sub * 128: dd * 512 + sub * 128 + 128],
                                       kqw3[:, dd, 2 * D + fc * 512: 2 * D + (fc + 1) * 512],
                                       start=(dd == 0), stop=(dd == NDC - 1))
                    tok_vmm[(tt, fc)] = T.emit(ins)
                sec("tensor", tsec)

            def vsec(V, tt=tt, vt3=vt3):
                V.wait(tok_vst.get(tt - 2))
                V.emit(V.memset(vt3[:, :, 64:65], 1.0))
                for fc in range(2):
                    V.wait(tok_vmm[(tt, fc)])
                    src = vps[fc].rearrange("p (h dh) -> p h dh", h=8)
                    tok_vev[(tt, fc)] = V.emit(V.tensor_copy(vt3[:, fc * 8:(fc + 1) * 8, 0:64], src))
            sec("vector", vsec)

            def ysec(Y, tt=tt, vt=vt):
                Y.wait(tok_vev[(tt, 1)])
                tok_vst[tt] = Y.emit(Y.dma_start(
                    out=d.vaug_d.transpose([1, 0, 2])[:, :, tt * 130:(tt + 1) * 130],
                    in_=vt.rearrange("p (pr c) -> p pr c", pr=NDC)), dma=True)
            sec("sync", ysec)

        def ysec2(Y, ts=ts, xnv=xnv):
            Y.wait(tok_kev[(ts, NDC - 1)])
            Y.wait(tok_vmm[(ts * 4 + 3, 1)])
            tok_xn0st[ts] = Y.emit(Y.dma_start(
                out=d.xn0_d.rearrange("(dc p) q -> p dc q", p=128)[:, :, ts * 512:(ts + 1) * 512],
                in_=xnv.rearrange("p (dc q) -> p dc q", dc=NDC)), dma=True)
            tok_slab_done[ts] = tok_xn0st[ts]
        sec("sync", ysec2)

    sq1b = ea(nc.sbuf_tensor("sq1b", [128, 2 * 512], F32))
    nrm1b = ea(nc.sbuf_tensor("nrm1b", [NST, 512], F32))
    rk1b = ea(nc.sbuf_tensor("rk1b", [NST, 512], F32))

    ssps = PB[3][0:16, 0:512]
    trp16 = PB[3][:, 512:640]
    nshps = PB[3][0:128, 512 + 128:512 + 144]

    def ysec(Y):
        Y.emit(Y.dma_start(out=shv16[:], in_=d.logit_scale.rearrange("(a b) -> a b", b=1)), dma=True)
        env.tok_ls = Y.emit(Y.dma_start(out=lsrow[:], in_=d.logit_scale.rearrange("(b a) -> b a", b=1)), dma=True)
    sec("sync", ysec)
    tok_ls = env.tok_ls

    def vsec(V, tok_ls=tok_ls):
        V.wait(tok_ls)
        V.emit(V.tensor_scalar_min(shv16[:], shv16[:], MAXLOG))
        env.tok_v = V.emit(V.tensor_scalar_min(lsrow[:], lsrow[:], MAXLOG))
        V.drain()
    sec("vector", vsec)
    tok_min = env.tok_v

    def ssec(SS, tok_min=tok_min):
        SS.wait(tok_min)
        SS.emit(SS.activation(out=shv16[:], in_=shv16[:], func=AF.Exp))
        env.tok_s = SS.emit(SS.activation(out=lsrow[:], in_=lsrow[:], func=AF.Exp))
        SS.drain()
    sec("scalar", ssec)
    tok_exp = env.tok_s

    def vsec2(V, tok_exp=tok_exp):
        V.wait(tok_exp)
        env.tok_v = V.emit(V.tensor_scalar_mul(lsrow[:], lsrow[:], -1.0))
        V.drain()
    sec("vector", vsec2)
    tok_neg = env.tok_v

    def tsec(T, tok_neg=tok_neg):
        T.wait(tok_neg, env.tok_consts_loaded)
        env.tok_t = T.emit(T.matmul(nshps, env.ones1[:], lsrow[:], start=True, stop=True))
    sec("tensor", tsec)
    tok_nshmm = env.tok_t

    def vsec3(V, tok_nshmm=tok_nshmm):
        V.wait(tok_nshmm)
        env.tok_v = V.emit(V.tensor_copy(negsh[:], nshps))
    sec("vector", vsec3)
    env.tok_negsh = env.tok_v

    hm3 = env.hmask[:].rearrange("p (c h) -> p c h", c=NDC)
    tok_sq = {}
    tok_ssmm = {}
    env.tok_trk_ev = {}
    env.tok_nrm_hist = {}
    for ts in range(NT5):
        for c in range(NDC):
            def ssec(SS, ts=ts, c=c):
                SS.wait(tok_kev[(ts, c)])
                SS.wait(tok_ssmm.get((ts, c - 2)) if c >= 2 else tok_ssmm.get((ts - 1, NDC - 2 + c)))
                tok_sq[(ts, c)] = SS.emit(SS.activation(
                    out=sq1b[:, (c % 2) * 512:(c % 2 + 1) * 512],
                    in_=K_sb[:, c * seq + ts * 512: c * seq + (ts + 1) * 512],
                    func=AF.Square))
            sec("scalar", ssec)

            def tsec(T, ts=ts, c=c):
                T.wait(tok_sq[(ts, c)])
                if c == 0 and ts > 0:
                    T.wait(env.tok_nrm_hist[ts - 1])
                tok_ssmm[(ts, c)] = T.emit(T.matmul(
                    ssps, hm3[:, c, :], sq1b[:, (c % 2) * 512:(c % 2 + 1) * 512],
                    start=(c == 0), stop=(c == NDC - 1)))
            sec("tensor", tsec)

        def ssec2(SS, ts=ts):
            SS.wait(tok_ssmm[(ts, NDC - 1)])
            if cfg.dbg and ts == 0:
                SS.emit(SS.activation(out=rk1b[:], in_=ssps, func=AF.Identity))
            SS.emit(SS.activation(out=nrm1b[:], in_=ssps, func=AF.Ln))
            SS.drain()
            env.tok_s = SS.emit(SS.activation(out=nrm1b[:], in_=nrm1b[:], func=AF.Exp, scale=0.5))
        sec("scalar", ssec2)
        if cfg.dbg and ts == 0:
            def ydbg1b(Y):
                Y.wait(env.tok_s)
                for cc in range(NDC):
                    Y.emit(Y.dma_start(out=d.k_d[:, cc * 512:(cc + 1) * 512],
                                       in_=K_sb[:, cc * seq + 0:cc * seq + 512]), dma=True)
                Y.emit(Y.dma_start(out=d.ss_d[:, :], in_=rk1b[:]), dma=True)
                env.tok_y1b = Y.emit(Y.dma_start(out=d.sq_d[:, :], in_=sq1b[:, 512:1024]), dma=True)
            sec("sync", ydbg1b)
        tok_nrm = env.tok_s
        env.tok_nrm_hist[ts] = tok_nrm

        def vsec4(V, ts=ts, tok_nrm=tok_nrm):
            V.wait(tok_nrm)
            if cfg.dbg and ts == 0:
                V.wait(getattr(env, "tok_y1b", None))
            V.emit(V.tensor_scalar_max(nrm1b[:], nrm1b[:], 1e-12))
            V.drain()
            V.emit(V.reciprocal(rk1b[:], nrm1b[:]))
            V.drain()
            env.tok_v = V.emit(V.tensor_scalar(rk1b[:], rk1b[:], shv16[:, 0:1], None, ALU.mult))
        sec("vector", vsec4)
        if cfg.dbg and ts == 0:
            def ydbg1c(Y):
                Y.wait(env.tok_v)
                Y.emit(Y.dma_start(out=d.rk1b_d[:, :], in_=rk1b[:]), dma=True)
            sec("sync", ydbg1c)
        tok_rk = env.tok_v

        for i in range(4):
            def tsec2(T, ts=ts, i=i, tok_rk=tok_rk):
                T.wait(tok_rk)
                T.wait(env.tok_trk_ev.get((ts, i - 1)) if i >= 1 else env.tok_trk_ev.get((ts - 1, 3)))
                if ts == 0 and i == 0:
                    T.wait(env.tok_negsh)
                env.tok_t = T.emit(T.transpose(trp16[0:128, 0:16],
                                               rk1b[0:16, i * 128:(i + 1) * 128], env.ident_f32[0:16, 0:16]))
            sec("tensor", tsec2)
            tok_trk = env.tok_t

            def vsec5(V, ts=ts, i=i, tok_trk=tok_trk):
                V.wait(tok_trk)
                kt = ts * 4 + i
                env.tok_v = V.emit(V.tensor_copy(rkT[:, kt * NH:(kt + 1) * NH], trp16[0:128, 0:16]))
                env.tok_trk_ev[(ts, i)] = env.tok_v
            sec("vector", vsec5)
    env.tok_rkT_done = env.tok_v


    barrier(env, block)
    astack.close()
    NKT = seq // 128
    cstack = ExitStack()
    ec = cstack.enter_context
    wq = ec(nc.sbuf_tensor("wq", [128, NDC * D], BF16))
    wq3 = wq[:].rearrange("p (dd f) -> p dd f", dd=NDC)
    wow = ec(nc.sbuf_tensor("wow", [128, NDC * D], BF16))
    wow3 = wow[:].rearrange("p (dd f) -> p dd f", dd=NDC)
    xnsl = ec(nc.sbuf_tensor("xnsl", [128, NDC * 1024], BF16))
    Qn = ec(nc.sbuf_tensor("Qn", [128, NDC * 1024], BF16))
    sqq = ec(nc.sbuf_tensor("sqq", [128, 2 * 1024], F32))
    rq2 = ec(nc.sbuf_tensor("rq2", [2, 2 * 1024], F32))
    Pt = ec(nc.sbuf_tensor("Pt", [128, 2 * 2 * 1024], BF16))
    vp = ec(nc.sbuf_tensor("vp", [128, 2 * NKT * 130], BF16))
    tmppv = ec(nc.sbuf_tensor("tmppv", [128, 1024], F32))
    rsumP = sqq[0:1, 0:2048]
    attnfm = ec(nc.sbuf_tensor("attnfm", [128, NDC * 1024], BF16))
    s0st = ec(nc.sbuf_tensor("s0st", [128, 2 * 1024], F32))

    def gsec_wq(G):
        G.emit(G.dma_start(out=wq3, in_=d.w_qkv.rearrange("(dd p) f -> p dd f", p=128)[:, :, 0:D]), dma=True)
        env.tok_wq = G.emit(G.dma_start(out=wow3, in_=d.w_out.rearrange("(dd p) f -> p dd f", p=128)), dma=True)
    sec("gpsimd", gsec_wq)
    prepass_weights(nc, env, d, block, "p1")

    tok_xnsl = {}
    tok_q = {}
    tok_qsq = {}
    tok_qev = {}
    tok_qssmm = {}
    tok_qnrm = {}
    tok_bc_hist = {}
    tok_qapply = {}
    tok_S = {}
    tok_E = {}
    tok_PV = {}
    tok_tail = {}
    tok_sums = {}
    tok_rln = {}
    tok_dmm = {}
    tok_vld = {}
    tok_proj = {}
    tok_projev = {}
    tok_projst = {}
    tok_div = {}

    psQs = {0: PB[0], 1: PB[2]}

    for g in range(NG):
        def ysec(Y, g=g):
            Y.wait(tok_xn0st[NT5 - 1])
            if g >= 1:
                Y.wait(tok_qapply.get((g - 1, NDC - 1)))
            tok_xnsl[g] = Y.emit(Y.dma_start(
                out=xnsl[:].rearrange("p (dc q) -> p dc q", dc=NDC),
                in_=d.xn0_d.rearrange("(dc p) q -> p dc q", p=128)[:, :, g * 1024:(g + 1) * 1024]), dma=True)
        sec("sync", ysec)

        for c in range(NDC):
            def tsec(T, g=g, c=c):
                T.wait(tok_xnsl[g], env.tok_wq)
                if c == 0:
                    if g == 0:
                        T.wait(env.tok_rkT_done)
                    else:
                        T.wait(tok_div[(g - 1, NH - 1)])
                        T.wait(tok_projev[(g - 1, NDC - 1)])
                T.wait(tok_qev.get((g, c - 2)) if c >= 2 else tok_qev.get((g - 1, NDC - 2 + c)))
                T.wait(tok_qsq.get((g, c - 2)) if c >= 2 else tok_qsq.get((g - 1, NDC - 2 + c)))
                for hf in range(2):
                    for dd in range(NDC):
                        ins = T.matmul(psQs[c % 2][:, hf * 512:(hf + 1) * 512],
                                       wq3[:, dd, c * 128:(c + 1) * 128],
                                       xnsl[:, dd * 1024 + hf * 512: dd * 1024 + (hf + 1) * 512],
                                       start=(dd == 0), stop=(dd == NDC - 1))
                tok_q[(g, c)] = T.emit(ins)
            sec("tensor", tsec)

            def ssec(SS, g=g, c=c):
                SS.wait(tok_q[(g, c)])
                tok_qsq[(g, c)] = SS.emit(SS.activation(
                    out=sqq[:, (c % 2) * 1024:(c % 2 + 1) * 1024], in_=psQs[c % 2][:], func=AF.Square))
            sec("scalar", ssec)

            def vsec(V, g=g, c=c):
                V.wait(tok_qsq[(g, c)])
                tok_qev[(g, c)] = V.emit(V.tensor_copy(Qn[:, c * 1024:(c + 1) * 1024], psQs[c % 2][:]))
            sec("vector", vsec)

            def tsec2(T, g=g, c=c):
                T.wait(tok_qsq[(g, c)])
                T.wait(tok_qnrm.get((g, c - 1), tok_qnrm.get((g - 1, NDC - 1)) if c == 0 else None))
                for hf in range(2):
                    ins = T.matmul(PB[3][0:2, hf * 512:(hf + 1) * 512], env.mask2[:],
                                   sqq[:, (c % 2) * 1024 + hf * 512:(c % 2) * 1024 + (hf + 1) * 512],
                                   start=True, stop=True)
                tok_qssmm[(g, c)] = T.emit(ins)
            sec("tensor", tsec2)

            def ssec2(SS, g=g, c=c):
                SS.wait(tok_qssmm[(g, c)])
                SS.wait(tok_bc_hist.get((g, c - 2)) if c >= 2 else tok_bc_hist.get((g - 1, NDC - 2 + c)))
                rv = rq2[0:2, (c % 2) * 1024:(c % 2 + 1) * 1024]
                SS.emit(SS.activation(out=rv, in_=PB[3][0:2, 0:1024], func=AF.Ln))
                SS.drain()
                env.tok_s = SS.emit(SS.activation(out=rv, in_=rv, func=AF.Exp, scale=0.5))
            sec("scalar", ssec2)
            tok_qn = env.tok_s

            def vsec2(V, g=g, c=c, tok_qn=tok_qn):
                V.wait(tok_qn)
                rv = rq2[0:2, (c % 2) * 1024:(c % 2 + 1) * 1024]
                V.emit(V.tensor_scalar_max(rv, rv, 1e-12))
                V.drain()
                env.tok_v = V.emit(V.reciprocal(rv, rv))
                V.drain()
            sec("vector", vsec2)
            tok_qnrm[(g, c)] = env.tok_v

            def tsec3(T, g=g, c=c):
                T.wait(tok_qnrm[(g, c)])
                T.wait(tok_qapply.get((g, c - 1), tok_qapply.get((g - 1, NDC - 1)) if c == 0 else None))
                for hf in range(2):
                    ins = T.matmul(PB[1][:, hf * 512:(hf + 1) * 512], env.sel2[:],
                                   rq2[0:2, (c % 2) * 1024 + hf * 512: (c % 2) * 1024 + (hf + 1) * 512],
                                   start=True, stop=True)
                env.tok_t = T.emit(ins)
            sec("tensor", tsec3)
            tok_bc = env.tok_t
            tok_bc_hist[(g, c)] = tok_bc

            def vsec3(V, g=g, c=c, tok_bc=tok_bc):
                V.wait(tok_bc)
                qv = Qn[:, c * 1024:(c + 1) * 1024]
                tok_qapply[(g, c)] = V.emit(V.tensor_tensor(qv, qv, PB[1][:], ALU.mult))
            sec("vector", vsec3)

        if cfg.dbg and g == 0:
            def ydbg(Y):
                Y.wait(tok_qapply[(0, NDC - 1)], env.tok_rkT_done)
                Y.emit(Y.dma_start(out=d.qn_d[:, :], in_=Qn[:]), dma=True)
                Y.emit(Y.dma_start(out=d.rk_d[:, 0:NT * NH], in_=rkT[:]), dma=True)
            sec("sync", ydbg)

        PS = [PB[0], PB[1], PB[2]]
        psPV = PB[3][0:65, :]
        LAGP = 3
        NPASS = NDC * 2

        def emit_divide(i, erd, g=g):
            ul = (i + 1) * NKT - 1
            t = PS[ul % 3] if i != NPASS - 1 else PS[(ul - 1) % 3]
            p_, qh_ = divmod(i, 2)

            def ssecR(SS, g=g, i=i):
                SS.wait(tok_sums[(g, i)])
                rs = rsumP[0:1, (i % 2) * 1024:(i % 2 + 1) * 1024]
                SS.emit(SS.activation(out=rs, in_=rs, func=AF.Ln))
                tok_rln[(g, i)] = SS.emit(SS.activation(out=rs, in_=rs, func=AF.Exp, scale=-1.0))
            sec("scalar", ssecR)

            def tsecD(T, g=g, i=i, t=t, erd=erd):
                T.wait(tok_rln[(g, i)])
                T.wait(tok_E[(g, erd)])
                for hh in range(2):
                    ins = T.matmul(t[:, hh * 512:(hh + 1) * 512], env.ones1[:],
                                   rsumP[0:1, (i % 2) * 1024 + hh * 512:
                                         (i % 2) * 1024 + (hh + 1) * 512],
                                   start=True, stop=True)
                tok_dmm[(g, i)] = T.emit(ins)
            sec("tensor", tsecD)

            def vsecD(V, g=g, i=i, t=t, p_=p_, qh_=qh_):
                V.wait(tok_dmm[(g, i)])
                for hh in range(2):
                    ins = V.tensor_tensor(
                        attnfm[hh * 64:(hh + 1) * 64,
                               p_ * 1024 + qh_ * 512: p_ * 1024 + qh_ * 512 + 512],
                        tmppv[0:64, hh * 512:(hh + 1) * 512],
                        t[hh * 64:(hh + 1) * 64, hh * 512:(hh + 1) * 512], ALU.mult)
                tok_div[(g, i)] = V.emit(ins)
            sec("vector", vsecD)

        for p in range(NDC):
            vslab = vp[:, (p % 2) * NKT * 130:(p % 2 + 1) * NKT * 130]

            def ysec2(Y, g=g, p=p, vslab=vslab):
                prev = tok_PV.get((g, (2 * (p - 2) + 2) * NKT - 1),
                                  tok_PV.get((g - 1, (2 * (NDC - 2 + p) + 2) * NKT - 1)))
                Y.wait(prev)
                if g == 0:
                    Y.wait(tok_vst[NT - 1])
                tok_vld[(g, p)] = Y.emit(Y.dma_start(out=vslab, in_=d.vaug_d[p, :, :]), dma=True)
            sec("sync", ysec2)

            for qh in range(2):
                i = 2 * p + qh
                ub = i * NKT

                def emit_pv(u, g=g, p=p, i=i, vslab=vslab, ub=ub):
                    kt = u - ub

                    def tsecPV(T, g=g, p=p, i=i, u=u, kt=kt, vslab=vslab):
                        T.wait(tok_E[(g, u)])
                        if kt == 0:
                            T.wait(tok_vld[(g, p)])
                            T.wait(tok_tail.get((g, i - 1),
                                                tok_tail.get((g - 1, NPASS - 1)) if i == 0 else None))
                        for hh in range(2):
                            ins = T.matmul(
                                psPV[:, hh * 512:(hh + 1) * 512],
                                vslab[:, kt * 130 + hh * 65: kt * 130 + (hh + 1) * 65],
                                Pt[:, ((u % 4) * 2 + hh) * 512:((u % 4) * 2 + hh + 1) * 512],
                                start=(kt == 0), stop=(kt == NKT - 1))
                        tok_PV[(g, u)] = T.emit(ins)
                    sec("tensor", tsecPV)

                for kt in range(NKT):
                    u = ub + kt
                    if kt == 1 and i >= 1:
                        emit_divide(i - 1, ub)

                    def tsecS(T, g=g, p=p, qh=qh, i=i, u=u, kt=kt):
                        if u == 0:
                            T.wait(tok_qapply[(g, NDC - 1)])
                        if u >= 2:
                            T.wait(tok_E[(g, u - 2)])
                        if kt == 2 and i >= 1:
                            T.wait(tok_div[(g, i - 1)])
                        for hh in range(2):
                            ins = T.matmul(
                                PS[u % 3][:, hh * 512:(hh + 1) * 512],
                                K_sb[hh * 64:(hh + 1) * 64, p * seq + kt * 128:p * seq + (kt + 1) * 128],
                                Qn[hh * 64:(hh + 1) * 64, p * 1024 + qh * 512: p * 1024 + qh * 512 + 512],
                                start=True, stop=True)
                        tok_S[(g, u)] = T.emit(ins)
                    sec("tensor", tsecS)

                    if kt >= LAGP:
                        emit_pv(u - LAGP)

                    def ssecE(SS, g=g, p=p, u=u, kt=kt):
                        SS.wait(tok_S[(g, u)])
                        if u >= 4:
                            SS.wait(tok_PV[(g, u - 4)])
                        for hh in range(2):
                            hx = 2 * p + hh
                            ins = SS.activation(
                                out=Pt[:, ((u % 4) * 2 + hh) * 512:((u % 4) * 2 + hh + 1) * 512],
                                in_=PS[u % 3][:, hh * 512:(hh + 1) * 512],
                                func=AF.Exp,
                                scale=rkT[:, kt * NH + hx: kt * NH + hx + 1],
                                bias=negsh[:, hx:hx + 1])
                        tok_E[(g, u)] = SS.emit(ins)
                    sec("scalar", ssecE)

                for u in range(ub + NKT - LAGP, ub + NKT):
                    emit_pv(u)

                def vsecT(V, g=g, i=i, ub=ub):
                    V.wait(tok_PV[(g, ub + NKT - 1)])
                    tok_sums[(g, i)] = V.emit(V.tensor_copy(
                        rsumP[0:1, (i % 2) * 1024:(i % 2 + 1) * 1024], psPV[64:65, :]))
                    tok_tail[(g, i)] = V.emit(V.tensor_copy(
                        tmppv[0:64, :], psPV[0:64, :]))
                sec("vector", vsecT)

        emit_divide(NPASS - 1, NPASS * NKT - 1)

        for oc in range(NDC):
            def tsecP(T, g=g, oc=oc):
                T.wait(tok_div[(g, NH - 1)], env.tok_wq)
                T.wait(tok_projev.get((g, oc - 2)))
                for hf in range(2):
                    for dd in range(NDC):
                        ins = T.matmul(psQs[oc % 2][:, hf * 512:(hf + 1) * 512],
                                       wow3[:, dd, oc * 128:(oc + 1) * 128],
                                       attnfm[:, dd * 1024 + hf * 512: dd * 1024 + (hf + 1) * 512],
                                       start=(dd == 0), stop=(dd == NDC - 1))
                tok_proj[(g, oc)] = T.emit(ins)
            sec("tensor", tsecP)

            def ssecP(SS, g=g, oc=oc):
                SS.wait(tok_proj[(g, oc)])
                SS.wait(tok_projst.get((g, oc - 2)))
                tok_projev[(g, oc)] = SS.emit(SS.activation(
                    out=s0st[:, (oc % 2) * 1024:(oc % 2 + 1) * 1024], in_=psQs[oc % 2][:],
                    func=AF.Identity, bias=env.bout_sb[:, oc:oc + 1]))
            sec("scalar", ssecP)

            def ysecP(Y, g=g, oc=oc):
                Y.wait(tok_projev[(g, oc)])
                tok_projst[(g, oc)] = Y.emit(Y.dma_start(
                    out=d.s0_d[oc * 128:(oc + 1) * 128, g * 1024:(g + 1) * 1024],
                    in_=s0st[:, (oc % 2) * 1024:(oc % 2 + 1) * 1024]), dma=True)
            sec("sync", ysecP)
    env.tok_attn_done = tok_projst[(NG - 1, NDC - 1)]

    barrier(env, block)
    cstack.close()
    kbstack.close()
    dstack = ExitStack()
    e = dstack.enter_context
    fw12st = e(nc.sbuf_tensor("fw12st", [128, 2 * 2 * NDC * 128], BF16))
    fw3st = e(nc.sbuf_tensor("fw3st", [128, 2 * NFC * 128], BF16))
    fhst = e(nc.sbuf_tensor("fhst", [128, 2 * NFC * 128], BF16))
    xnsl2 = e(nc.sbuf_tensor("xnsl2", [128, 2 * NDC * 512], BF16))
    ab_sb = e(nc.sbuf_tensor("ab_sb", [128, 4 * 512], BF16))
    h_sb = e(nc.sbuf_tensor("h_sb", [128, NFC * 512], BF16))
    h1o_sb = e(nc.sbuf_tensor("h1o_sb", [128, NFC * 512], BF16))
    g0_sb = e(nc.sbuf_tensor("g0_sb", [128, 2 * 512], F32))
    s0sl = e(nc.sbuf_tensor("s0sl", [128, 2 * 512], F32))
    x0tm = e(nc.sbuf_tensor("x0tm", [128, 2 * 512], F32))
    x0fm = e(nc.sbuf_tensor("x0fm", [128, 2 * 512], F32))
    out1_sb = e(nc.sbuf_tensor("out1_sb", [128, NDC * 512], F32))
    sqd = e(nc.sbuf_tensor("sqd", [128, 2 * 512], F32))
    lnrow = e(nc.sbuf_tensor("lnrow", [1, 6 * 512], F32))
    xn1c = e(nc.sbuf_tensor("xn1c", [128, 2 * 512], BF16))
    xn1tm_st = e(nc.sbuf_tensor("xn1tm_st", [128, 4 * 1024], BF16))
    out1tm_st = e(nc.sbuf_tensor("out1tm_st", [128, 4 * 1024], F32))

    tok_fw12 = {}
    tok_fw3 = {}
    tok_xn2 = {}
    tok_ab = {}
    tok_abev = {}
    tok_h = {}
    tok_fh = {}
    tok_h1mm = {}
    tok_h1ev = {}
    tok_h2mm = {}
    tok_h2ev = {}
    tok_w3 = {}
    tok_g0 = {}
    tok_s0l = {}
    tok_x0l = {}
    tok_x0tr = {}
    tok_x0ev = {}
    tok_cmb = {}
    tok_lnm = {}
    tok_sqf = {}
    tok_lns = {}
    tok_xn1 = {}
    tok_tr1 = {}
    tok_trev1 = {}
    tok_st1d = {}
    env.tok_lbc_hist = {}
    env.tok_ffn_last = None

    for ft in range(NT5):
        fb = ft % 2
        xnv = xnsl2[:, fb * NDC * 512:(fb + 1) * NDC * 512]

        def ysec(Y, ft=ft, xnv=xnv):
            Y.wait(tok_xn0st[NT5 - 1])
            Y.wait(tok_ab.get((ft - 2, NFC - 1)))
            tok_xn2[ft] = Y.emit(Y.dma_start(
                out=xnv.rearrange("p (dc q) -> p dc q", dc=NDC),
                in_=d.xn0_d.rearrange("(dc p) q -> p dc q", p=128)[:, :, ft * 512:(ft + 1) * 512]), dma=True)
        sec("sync", ysec)

        for oc in range(NFC):
            pb = PB[oc % 2]

            def gsecA(G, ft=ft, oc=oc):
                G.wait(tok_ab.get((ft, oc - 2)) if oc >= 2 else tok_ab.get((ft - 1, NFC - 2 + oc)))
                G.wait(env.tok_pp[("fw1", oc)], env.tok_pp[("fw2", oc)])
                sl = fw12st[:, (oc % 2) * 2 * NDC * 128:((oc % 2) + 1) * 2 * NDC * 128]
                G.emit(G.dma_start(out=sl[:, 0:NDC * 128], in_=d.fw1b[oc]), dma=True)
                tok_fw12[(ft, oc)] = G.emit(G.dma_start(
                    out=sl[:, NDC * 128:2 * NDC * 128], in_=d.fw2b[oc]), dma=True)
            sec("gpsimd", gsecA)

            def tsec(T, ft=ft, oc=oc, pb=pb, xnv=xnv):
                T.wait(tok_xn2[ft], tok_fw12[(ft, oc)])
                if ft == 0 and oc == 0:
                    T.wait(tok_div[(NG - 1, NH - 1)], tok_projev[(NG - 1, NDC - 1)])
                if oc == 0 and ft >= 1:
                    T.wait(tok_xn1[(ft - 1, NDC - 1)])
                T.wait(tok_abev.get((ft, oc - 2)) if oc >= 2 else tok_abev.get((ft - 1, NFC - 2 + oc)))
                sl = fw12st[:, (oc % 2) * 2 * NDC * 128:((oc % 2) + 1) * 2 * NDC * 128]
                for dd in range(NDC):
                    T.matmul(pb[:, 0:512], sl[:, dd * 128:(dd + 1) * 128],
                             xnv[:, dd * 512:(dd + 1) * 512],
                             start=(dd == 0), stop=(dd == NDC - 1))
                for dd in range(NDC):
                    ins = T.matmul(pb[:, 512:1024], sl[:, NDC * 128 + dd * 128:NDC * 128 + (dd + 1) * 128],
                                   xnv[:, dd * 512:(dd + 1) * 512],
                                   start=(dd == 0), stop=(dd == NDC - 1))
                tok_ab[(ft, oc)] = T.emit(ins)
            sec("tensor", tsec)

            def ssec(SS, ft=ft, oc=oc, pb=pb):
                SS.wait(tok_ab[(ft, oc)])
                SS.wait(tok_h.get((ft, oc - 2)) if oc >= 2 else tok_h.get((ft - 1, NFC - 2 + oc)))
                SS.emit(SS.activation(out=ab_sb[:, (oc % 2) * 1024:(oc % 2) * 1024 + 512],
                                      in_=pb[:, 0:512], func=AF.Silu,
                                      scale=env.s1_sb[:, oc:oc + 1]))
                tok_abev[(ft, oc)] = SS.emit(SS.activation(
                    out=ab_sb[:, (oc % 2) * 1024 + 512:(oc % 2) * 1024 + 1024],
                    in_=pb[:, 512:1024], func=AF.Identity, scale=env.s2_sb[:, oc:oc + 1]))
            sec("scalar", ssec)

            def vsec(V, ft=ft, oc=oc):
                V.wait(tok_abev[(ft, oc)])
                tok_h[(ft, oc)] = V.emit(V.tensor_tensor(
                    h_sb[:, oc * 512:(oc + 1) * 512],
                    ab_sb[:, (oc % 2) * 1024:(oc % 2) * 1024 + 512],
                    ab_sb[:, (oc % 2) * 1024 + 512:(oc % 2) * 1024 + 1024], ALU.mult))
            sec("vector", vsec)

        for oc in range(NFC):
            def gsec(G, ft=ft, oc=oc):
                G.wait(tok_h1mm.get((ft, oc - 2)) if oc >= 2 else tok_h2mm.get((ft - 1, NFC - 2 + oc)))
                G.wait(env.tok_pp[("fh1", oc)])
                tok_fh[(ft, 1, oc)] = G.emit(G.dma_start(
                    out=fhst[:, (oc % 2) * NFC * 128:((oc % 2) + 1) * NFC * 128],
                    in_=d.fh1b[oc]), dma=True)
            sec("gpsimd", gsec)

            def tsec(T, ft=ft, oc=oc):
                T.wait(tok_fh[(ft, 1, oc)])
                T.wait(tok_h[(ft, NFC - 1)])
                T.wait(tok_h1ev.get((ft, oc - 2)))
                fv = fhst[:, (oc % 2) * NFC * 128:((oc % 2) + 1) * NFC * 128]
                for kc in range(NFC):
                    ins = T.matmul(PB[2][:, (oc % 2) * 512:((oc % 2) + 1) * 512],
                                   fv[:, kc * 128:(kc + 1) * 128],
                                   h_sb[:, kc * 512:(kc + 1) * 512],
                                   start=(kc == 0), stop=(kc == NFC - 1))
                tok_h1mm[(ft, oc)] = T.emit(ins)
            sec("tensor", tsec)

            def ssec(SS, ft=ft, oc=oc):
                SS.wait(tok_h1mm[(ft, oc)])
                tok_h1ev[(ft, oc)] = SS.emit(SS.activation(
                    out=h1o_sb[:, oc * 512:(oc + 1) * 512],
                    in_=PB[2][:, (oc % 2) * 512:((oc % 2) + 1) * 512],
                    func=AF.Gelu, scale=env.h1s_sb[:, oc:oc + 1]))
            sec("scalar", ssec)

        for oc in range(NFC):
            def gsec(G, ft=ft, oc=oc):
                G.wait(tok_h1mm.get((ft, NFC - 2 + oc)) if oc < 2 else tok_h2mm.get((ft, oc - 2)))
                G.wait(env.tok_pp[("fh2", oc)])
                tok_fh[(ft, 2, oc)] = G.emit(G.dma_start(
                    out=fhst[:, (oc % 2) * NFC * 128:((oc % 2) + 1) * NFC * 128],
                    in_=d.fh2b[oc]), dma=True)
            sec("gpsimd", gsec)

            def tsec(T, ft=ft, oc=oc):
                T.wait(tok_fh[(ft, 2, oc)])
                T.wait(tok_h1ev[(ft, NFC - 1)])
                T.wait(tok_h2ev.get((ft, oc - 2)))
                fv = fhst[:, (oc % 2) * NFC * 128:((oc % 2) + 1) * NFC * 128]
                for kc in range(NFC):
                    ins = T.matmul(PB[2][:, (oc % 2) * 512:((oc % 2) + 1) * 512],
                                   fv[:, kc * 128:(kc + 1) * 128],
                                   h1o_sb[:, kc * 512:(kc + 1) * 512],
                                   start=(kc == 0), stop=(kc == NFC - 1))
                tok_h2mm[(ft, oc)] = T.emit(ins)
            sec("tensor", tsec)

            def ssec(SS, ft=ft, oc=oc):
                SS.wait(tok_h2mm[(ft, oc)])
                tok_h2ev[(ft, oc)] = SS.emit(SS.activation(
                    out=h_sb[:, oc * 512:(oc + 1) * 512],
                    in_=PB[2][:, (oc % 2) * 512:((oc % 2) + 1) * 512],
                    func=AF.Gelu, scale=env.h2s_sb[:, oc:oc + 1]))
            sec("scalar", ssec)

        for oc in range(NDC):
            def gsecD(G, ft=ft, oc=oc):
                G.wait(tok_w3.get((ft, oc - 2)) if oc >= 2 else tok_w3.get((ft - 1, NDC - 2 + oc)))
                G.wait(env.tok_pp[("fw3", oc)])
                tok_fw3[(ft, oc)] = G.emit(G.dma_start(
                    out=fw3st[:, (oc % 2) * NFC * 128:((oc % 2) + 1) * NFC * 128],
                    in_=d.fw3b[oc]), dma=True)
            sec("gpsimd", gsecD)

            def tsec(T, ft=ft, oc=oc):
                T.wait(tok_h2ev[(ft, NFC - 1)], tok_fw3[(ft, oc)])
                T.wait(tok_g0.get((ft, oc - 2)) if oc >= 2 else tok_g0.get((ft - 1, NDC - 2 + oc)))
                fv = fw3st[:, (oc % 2) * NFC * 128:((oc % 2) + 1) * NFC * 128]
                for kc in range(NFC):
                    ins = T.matmul(PB[3][:, (oc % 2) * 512:((oc % 2) + 1) * 512],
                                   fv[:, kc * 128:(kc + 1) * 128],
                                   h_sb[:, kc * 512:(kc + 1) * 512],
                                   start=(kc == 0), stop=(kc == NFC - 1))
                tok_w3[(ft, oc)] = T.emit(ins)
            sec("tensor", tsec)

            def ssec(SS, ft=ft, oc=oc):
                SS.wait(tok_w3[(ft, oc)])
                SS.wait(tok_cmb.get((ft, oc - 2)))
                tok_g0[(ft, oc)] = SS.emit(SS.activation(
                    out=g0_sb[:, (oc % 2) * 512:((oc % 2) + 1) * 512],
                    in_=PB[3][:, (oc % 2) * 512:((oc % 2) + 1) * 512],
                    func=AF.Silu, scale=env.s3_sb[:, oc:oc + 1]))
            sec("scalar", ssec)

            def ysec2(Y, ft=ft, oc=oc):
                Y.wait(tok_cmb.get((ft, oc - 2)) if oc >= 2 else tok_cmb.get((ft - 1, NDC - 2 + oc)))
                Y.wait(tok_projst[(ft // 2, oc)])
                tok_s0l[(ft, oc)] = Y.emit(Y.dma_start(
                    out=s0sl[:, (oc % 2) * 512:((oc % 2) + 1) * 512],
                    in_=d.s0_d[oc * 128:(oc + 1) * 128, ft * 512:(ft + 1) * 512]), dma=True)
                tok_x0l[(ft, oc)] = Y.emit(Y.dma_start(
                    out=x0tm[:, (oc % 2) * 512:((oc % 2) + 1) * 512].rearrange("p (s q) -> p s q", s=4),
                    in_=d.x0.rearrange("(s p) dd -> p s dd", p=128)[:, 4 * ft:4 * ft + 4,
                                                                   oc * 128:(oc + 1) * 128]), dma=True)
            sec("sync", ysec2)

            for j in range(4):
                def tsec2(T, ft=ft, oc=oc, j=j):
                    T.wait(tok_x0l[(ft, oc)])
                    T.wait(tok_x0ev.get((ft, oc, j - 2)) if j >= 2 else
                           (tok_x0ev.get((ft, oc - 1, 2 + j)) if oc >= 1 else tok_x0ev.get((ft - 1, NDC - 1, 2 + j))))
                    tok_x0tr[(ft, oc, j)] = T.emit(T.transpose(
                        PB[0].bitcast(F32)[:, (j % 2) * 512:(j % 2) * 512 + 128],
                        x0tm[:, (oc % 2) * 512 + j * 128:(oc % 2) * 512 + (j + 1) * 128],
                        env.ident_f32[:]))
                sec("tensor", tsec2)

                def vsec2(V, ft=ft, oc=oc, j=j):
                    V.wait(tok_x0tr[(ft, oc, j)])
                    tok_x0ev[(ft, oc, j)] = V.emit(V.tensor_copy(
                        x0fm[:, (oc % 2) * 512 + j * 128:(oc % 2) * 512 + (j + 1) * 128],
                        PB[0].bitcast(F32)[:, (j % 2) * 512:(j % 2) * 512 + 128]))
                sec("vector", vsec2)

            def vsec3(V, ft=ft, oc=oc):
                V.wait(tok_g0[(ft, oc)], tok_s0l[(ft, oc)], tok_x0ev[(ft, oc, 3)])
                sl = slice((oc % 2) * 512, ((oc % 2) + 1) * 512)
                V.emit(V.tensor_tensor(sqd[:, 0:512], s0sl[:, sl], g0_sb[:, sl], ALU.mult))
                tok_cmb[(ft, oc)] = V.emit(V.tensor_tensor(
                    out1_sb[:, oc * 512:(oc + 1) * 512], sqd[:, 0:512], x0fm[:, sl], ALU.add))
            sec("vector", vsec3)

            def tsec3(T, ft=ft, oc=oc):
                T.wait(tok_cmb[(ft, oc)])
                tok_lnm[(ft, oc)] = T.emit(T.matmul(
                    PB[1][0:1, 0:512], env.onescol[:], out1_sb[:, oc * 512:(oc + 1) * 512],
                    start=(oc == 0), stop=(oc == NDC - 1)))
            sec("tensor", tsec3)

            def ssec2(SS, ft=ft, oc=oc):
                SS.wait(tok_cmb[(ft, oc)])
                SS.wait(tok_lnm.get((ft, oc - 1, "ss")) if oc >= 1 else tok_lnm.get((ft - 1, NDC - 1, "ss")))
                tok_sqf[(ft, oc)] = SS.emit(SS.activation(
                    out=sqd[:, 512:1024], in_=out1_sb[:, oc * 512:(oc + 1) * 512], func=AF.Square))
            sec("scalar", ssec2)

            def tsec4(T, ft=ft, oc=oc):
                T.wait(tok_sqf[(ft, oc)])
                tok_lnm[(ft, oc, "ss")] = T.emit(T.matmul(
                    PB[1][0:1, 512:1024], env.onescol[:], sqd[:, 512:1024],
                    start=(oc == 0), stop=(oc == NDC - 1)))
            sec("tensor", tsec4)

        def ssec3(SS, ft=ft):
            SS.wait(tok_lnm[(ft, NDC - 1)], tok_lnm[(ft, NDC - 1, "ss")])
            SS.wait(env.tok_lbc_hist.get(ft - 1))
            SS.emit(SS.activation(out=lnrow[0:1, 0:512], in_=PB[1][0:1, 0:512],
                                  func=AF.Identity, scale=1.0 / 1024.0))
            SS.drain()
            SS.emit(SS.activation(out=lnrow[0:1, 512:1024], in_=lnrow[0:1, 0:512], func=AF.Square))
            env.tok_s = SS.emit(SS.activation(out=lnrow[0:1, 1024:1536], in_=PB[1][0:1, 512:1024],
                                              func=AF.Identity, scale=1.0 / 1024.0))
            SS.drain()
        sec("scalar", ssec3)
        tok_l1 = env.tok_s

        def vsec4(V, ft=ft, tok_l1=tok_l1):
            V.wait(tok_l1)
            env.tok_v = V.emit(V.tensor_sub(lnrow[0:1, 1024:1536], lnrow[0:1, 1024:1536],
                                            lnrow[0:1, 512:1024]))
            V.drain()
        sec("vector", vsec4)
        tok_l2 = env.tok_v

        def ssec4(SS, ft=ft, tok_l2=tok_l2):
            SS.wait(tok_l2)
            SS.emit(SS.activation(out=lnrow[0:1, 1536:2048], in_=lnrow[0:1, 1024:1536],
                                  func=AF.Ln, bias=LN_EPS))
            SS.drain()
            env.tok_s = SS.emit(SS.activation(out=lnrow[0:1, 1536:2048], in_=lnrow[0:1, 1536:2048],
                                              func=AF.Exp, scale=-0.5))
            SS.drain()
        sec("scalar", ssec4)
        tok_l3 = env.tok_s

        def vsec5(V, ft=ft, tok_l3=tok_l3):
            V.wait(tok_l3)
            V.emit(V.tensor_mul(lnrow[0:1, 2048:2560], lnrow[0:1, 0:512], lnrow[0:1, 1536:2048]))
            V.drain()
            env.tok_v = V.emit(V.tensor_scalar_mul(lnrow[0:1, 2048:2560], lnrow[0:1, 2048:2560], -1.0))
            V.drain()
        sec("vector", vsec5)
        tok_l4 = env.tok_v

        def tsec5(T, ft=ft, tok_l4=tok_l4):
            T.wait(tok_l4)
            T.emit(T.matmul(PB[0][:, 0:512], env.ones1[:], lnrow[0:1, 1536:2048], start=True, stop=True))
            env.tok_t = T.emit(T.matmul(PB[0][:, 512:1024], env.ones1[:], lnrow[0:1, 2048:2560],
                                        start=True, stop=True))
        sec("tensor", tsec5)
        tok_lbc = env.tok_t
        env.tok_lbc_hist[ft] = tok_lbc

        for oc in range(NDC):
            def vsec6(V, ft=ft, oc=oc, tok_lbc=tok_lbc):
                V.wait(tok_lbc)
                if oc >= 2:
                    V.wait(tok_st1d.get((ft, oc - 2, "fm")), tok_tr1.get((ft, oc - 2, "x", 3)))
                else:
                    V.wait(tok_st1d.get((ft - 1, NDC - 2 + oc, "fm")), tok_tr1.get((ft - 1, NDC - 2 + oc, "x", 3)))
                sl = slice((oc % 2) * 512, ((oc % 2) + 1) * 512)
                V.emit(V.tensor_tensor(sqd[:, 0:512], out1_sb[:, oc * 512:(oc + 1) * 512],
                                       PB[0][:, 0:512], ALU.mult))
                tok_xn1[(ft, oc)] = V.emit(V.tensor_tensor(
                    xn1c[:, sl], sqd[:, 0:512], PB[0][:, 512:1024], ALU.add))
            sec("vector", vsec6)

            def ysec3(Y, ft=ft, oc=oc):
                Y.wait(tok_xn1[(ft, oc)])
                tok_st1d[(ft, oc, "fm")] = Y.emit(Y.dma_start(
                    out=d.xn1fm_d[oc * 128:(oc + 1) * 128, ft * 512:(ft + 1) * 512],
                    in_=xn1c[:, (oc % 2) * 512:((oc % 2) + 1) * 512]), dma=True)
            sec("sync", ysec3)

            for j in range(4):
                def tsec6(T, ft=ft, oc=oc, j=j):
                    T.wait(tok_xn1[(ft, oc)])
                    T.wait(tok_trev1.get((ft, oc, "x", j - 1)) if j >= 1 else
                           tok_trev1.get((ft, oc - 1, "x", 3), tok_trev1.get((ft - 1, NDC - 1, "x", 3)) if oc == 0 else None))
                    tok_tr1[(ft, oc, "x", j)] = T.emit(T.transpose(
                        PB[2].bitcast(BF16)[:, 0:128],
                        xn1c[:, (oc % 2) * 512 + j * 128:(oc % 2) * 512 + (j + 1) * 128],
                        env.ident_bf[:]))
                sec("tensor", tsec6)

                def vsec7(V, ft=ft, oc=oc, j=j):
                    V.wait(tok_tr1[(ft, oc, "x", j)])
                    V.wait(tok_st1d.get((ft - 1, "xn1tm")))
                    tok_trev1[(ft, oc, "x", j)] = V.emit(V.tensor_copy(
                        xn1tm_st[:, j * 1024 + oc * 128: j * 1024 + (oc + 1) * 128],
                        PB[2].bitcast(BF16)[:, 0:128]))
                sec("vector", vsec7)

                def tsec7(T, ft=ft, oc=oc, j=j):
                    T.wait(tok_trev1.get((ft, oc, "o", j - 1)) if j >= 1 else
                           tok_trev1.get((ft, oc - 1, "o", 3), tok_trev1.get((ft - 1, NDC - 1, "o", 3)) if oc == 0 else None))
                    tok_tr1[(ft, oc, "o", j)] = T.emit(T.transpose(
                        PB[2].bitcast(F32)[:, 512:640],
                        out1_sb[:, oc * 512 + j * 128: oc * 512 + (j + 1) * 128],
                        env.ident_f32[:]))
                sec("tensor", tsec7)

                def vsec8(V, ft=ft, oc=oc, j=j):
                    V.wait(tok_tr1[(ft, oc, "o", j)])
                    V.wait(tok_st1d.get((ft - 1, "tmstores")))
                    tok_trev1[(ft, oc, "o", j)] = V.emit(V.tensor_copy(
                        out1tm_st[:, j * 1024 + oc * 128: j * 1024 + (oc + 1) * 128],
                        PB[2].bitcast(F32)[:, 512:640]))
                sec("vector", vsec8)

        def ysec4(Y, ft=ft):
            Y.wait(tok_trev1[(ft, NDC - 1, "x", 3)], tok_trev1[(ft, NDC - 1, "o", 3)])
            tok_st1d[(ft, "xn1tm")] = Y.emit(Y.dma_start(
                out=d.xn1tm_d.rearrange("(s p) dd -> p s dd", p=128)[:, 4 * ft:4 * ft + 4, :],
                in_=xn1tm_st[:].rearrange("p (s dd) -> p s dd", s=4)), dma=True)
            tok_st1d[(ft, "tmstores")] = Y.emit(Y.dma_start(
                out=d.out1_d.rearrange("(s p) dd -> p s dd", p=128)[:, 4 * ft:4 * ft + 4, :],
                in_=out1tm_st[:].rearrange("p (s dd) -> p s dd", s=4)), dma=True)
        sec("sync", ysec4)
        env.tok_ffn_last = tok_st1d[(ft, "tmstores")]
    env.tok_phase1_done = env.tok_ffn_last
    dstack.close()
    pstack.close()




def phase2(nc, env, d, block, cfg):
    ctx = env.ctx
    e = ctx.enter_context
    seq = cfg.seq
    NT5 = seq // 512
    sec = lambda name, body: _mk_section(env, block, name, body)

    pstack2 = ExitStack()
    PB = [pstack2.enter_context(nc.psum_tensor(f"p2PB{i}", [128, 1024], F32)) for i in range(4)]

    n2w = e(nc.sbuf_tensor("n2w", [128, NDC], F32))

    sstack = ExitStack()
    es = sstack.enter_context
    xsl = es(nc.sbuf_tensor("xsl", [128, 2 * NDC * 512], BF16))
    xnn = es(nc.sbuf_tensor("xnn", [128, 2 * NDC * 512], BF16))
    wdt = es(nc.sbuf_tensor("wdt", [128, NDC * D], BF16))
    wdt3 = wdt[:].rearrange("p (dd f) -> p dd f", dd=NDC)
    wbc_sb = es(nc.sbuf_tensor("wbc_sb", [128, NDC * 64], BF16))
    wbc3 = wbc_sb[:].rearrange("p (dd f) -> p dd f", dd=NDC)
    bbc_sb = es(nc.sbuf_tensor("bbc_sb", [64, 1], F32))
    n1w = es(nc.sbuf_tensor("n1w", [128, NDC], F32))
    env.bdt_sb = es(nc.sbuf_tensor("bdt_sb", [128, NDC], F32))
    dpar = es(nc.sbuf_tensor("dpar", [128, NDC], F32))
    mscl = es(nc.sbuf_tensor("mscl", [128, NDC], F32))
    sel16 = es(nc.sbuf_tensor("sel16s", [NST, NST * 128], F32))
    sel163 = sel16[:].rearrange("p (n m) -> p n m", n=NST)
    alog = es(nc.sbuf_tensor("alog", [NST, 2], F32))
    alogr = es(nc.sbuf_tensor("alogr", [1, NST], F32))
    abc = es(nc.sbuf_tensor("abc", [128, NST], F32))
    dtc = es(nc.sbuf_tensor("dtc", [128, 2 * NDC * 512], F32))
    wc_ = es(nc.sbuf_tensor("wc_", [128, NDC * 512], BF16))
    rdt = es(nc.sbuf_tensor("rdt", [128, 512], F32))
    sp1 = es(nc.sbuf_tensor("sp1", [128, 2 * 512], F32))
    spc = es(nc.sbuf_tensor("spc", [128, NDC * 512], F32))
    bvc = es(nc.sbuf_tensor("bvc", [NST, 2 * 512], F32))
    dA_s = es(nc.sbuf_tensor("dA_s", [128, 2 * 512], F32))
    b3_s = es(nc.sbuf_tensor("b3_s", [128, 2 * 512], F32))
    h_s = es(nc.sbuf_tensor("h_s", [128, 6 * 512], F32))
    cv_sb = es(nc.sbuf_tensor("cv_sb", [128, 2 * 512], F32))
    ybp = es(nc.sbuf_tensor("ybp", [128, 512], F32))
    yeps = es(nc.sbuf_tensor("yeps", [128, 512], F32))
    yac = es(nc.sbuf_tensor("yac", [128, NDC * 512], F32))
    stt_s = es(nc.sbuf_tensor("stt_s", [128, NST * NDC], F32))
    lrow = es(nc.sbuf_tensor("lrow", [1, 2 * 512], F32))
    x1c = es(nc.sbuf_tensor("x1c", [128, 2 * 512], BF16))

    tok_ld2 = {}
    tok_rs1 = {}
    tok_sq2 = {}
    tok_dt = {}
    tok_dtev = {}
    tok_w = {}
    tok_bvmm = {}
    tok_bv = {}
    tok_bc2 = {}
    tok_scan = {}
    tok_y = {}
    tok_cv = {}
    tok_yep = {}
    tok_x1st = {}
    tok_exp2 = {}
    tok_bvevh = {}
    env.tok_rs1row = {}

    def gsec0(G):
        G.emit(G.dma_start(out=wdt3, in_=d.Wdt.rearrange("(dd p) f -> p dd f", p=128)), dma=True)
        env.tok_g = G.emit(G.dma_start(out=wbc3, in_=d.WBC.rearrange("(dd p) f -> p dd f", p=128)), dma=True)
    sec("gpsimd", gsec0)
    tok_wdt = env.tok_g
    if 3 in cfg.phases:
        prepass_weights(nc, env, d, block, "p3")

    def ysec0(Y):
        for src_, dst in [(d.norm1_w, n1w), (d.norm2_w, n2w), (d.D_param, dpar),
                          (d.mscale, mscl), (d.bBC, bbc_sb), (d.bdt, env.bdt_sb)]:
            Y.emit(Y.dma_start(out=dst[:], in_=src_[:]), dma=True)
        Y.emit(Y.dma_start(out=sel163, in_=d.sel16[:]), dma=True)
        Y.emit(Y.dma_start(out=alog[:, 0:1], in_=d.A_log.rearrange("(a b) -> a b", b=1)), dma=True)
        env.tok_y = Y.emit(Y.dma_start(out=alogr[:], in_=d.A_log.rearrange("(b a) -> b a", b=1)), dma=True)
    sec("sync", ysec0)
    tok_cst2 = env.tok_y

    def ssec0(SS, tok_cst2=tok_cst2):
        SS.wait(tok_cst2)
        SS.emit(SS.activation(out=alog[:, 0:1], in_=alog[:, 0:1], func=AF.Exp))
        env.tok_s = SS.emit(SS.activation(out=alogr[:], in_=alogr[:], func=AF.Exp))
        SS.drain()
    sec("scalar", ssec0)
    tok_aexp = env.tok_s

    def vsec0(V, tok_aexp=tok_aexp):
        V.wait(tok_aexp)
        V.emit(V.tensor_scalar_mul(alog[:, 0:1], alog[:, 0:1], -1.0))
        V.drain()
        V.emit(V.reciprocal(alog[:, 1:2], alog[:, 0:1]))
        env.tok_v = V.emit(V.tensor_scalar_mul(alogr[:], alogr[:], -1.0))
        V.drain()
    sec("vector", vsec0)
    tok_arow = env.tok_v

    def tsec0(T, tok_arow=tok_arow):
        T.wait(tok_arow, env.tok_consts_loaded)
        env.tok_t = T.emit(T.matmul(PB[3][:, 0:NST], env.ones1[:], alogr[:], start=True, stop=True))
    sec("tensor", tsec0)
    tok_abcmm = env.tok_t

    def vsec0b(V, tok_abcmm=tok_abcmm):
        V.wait(tok_abcmm)
        env.tok_v = V.emit(V.tensor_copy(abc[:], PB[3][:, 0:NST]))
        V.drain()
    sec("vector", vsec0b)
    tok_abc = env.tok_v

    tok_rbcp = {}

    def prol_load(ct):
        xv = xsl[:, (ct % 2) * NDC * 512:((ct % 2) + 1) * NDC * 512]

        def yld(Y, ct=ct, xv=xv):
            Y.wait(env.tok_phase1_done)
            Y.wait(tok_yep.get((ct - 2, NDC - 1)))
            tok_ld2[ct] = Y.emit(Y.dma_start(
                out=xv.rearrange("p (dc q) -> p dc q", dc=NDC),
                in_=d.xn1fm_d.rearrange("(dc p) q -> p dc q", p=128)[:, :, ct * 512:(ct + 1) * 512]), dma=True)
        sec("sync", yld)

    def prol_rms(ct):
        xv = xsl[:, (ct % 2) * NDC * 512:((ct % 2) + 1) * NDC * 512]
        for dc in range(NDC):
            def ssq(SS, ct=ct, dc=dc, xv=xv):
                SS.wait(tok_ld2[ct])
                SS.wait(tok_rs1.get((ct, dc - 2)) if dc >= 2 else tok_rs1.get((ct - 1, NDC - 2 + dc)))
                tok_sq2[(ct, dc)] = SS.emit(SS.activation(
                    out=sp1[:, (dc % 2) * 512:(dc % 2 + 1) * 512],
                    in_=xv[:, dc * 512:(dc + 1) * 512], func=AF.Square))
            sec("scalar", ssq)

            def tsq(T, ct=ct, dc=dc):
                T.wait(tok_sq2[(ct, dc)])
                if dc == 0:
                    T.wait(env.tok_rs1row.get(ct - 1))
                tok_rs1[(ct, dc)] = T.emit(T.matmul(
                    PB[2][64:65, 0:512], env.onescol[:], sp1[:, (dc % 2) * 512:(dc % 2 + 1) * 512],
                    start=(dc == 0), stop=(dc == NDC - 1)))
            sec("tensor", tsq)

        def srow(SS, ct=ct):
            SS.wait(tok_rs1[(ct, NDC - 1)])
            SS.emit(SS.activation(out=lrow[0:1, 0:512], in_=PB[2][64:65, 0:512],
                                  func=AF.Ln, scale=1.0 / 1024.0, bias=RMS_EPS))
            SS.drain()
            env.tok_s = SS.emit(SS.activation(out=lrow[0:1, 0:512], in_=lrow[0:1, 0:512],
                                              func=AF.Exp, scale=-0.5))
            SS.drain()
        sec("scalar", srow)
        env.tok_rs1row[ct] = env.tok_s
        tok_rrow = env.tok_s

        def tbc(T, ct=ct, tok_rrow=tok_rrow):
            T.wait(tok_rrow)
            T.wait(tok_w.get((ct - 1, NDC - 1, "nn")))
            env.tok_t = T.emit(T.matmul(PB[3][:, 512:1024], env.ones1[:], lrow[0:1, 0:512],
                                        start=True, stop=True))
        sec("tensor", tbc)
        tok_rbcp[ct] = env.tok_t

    def prol_nn(ct):
        xv = xsl[:, (ct % 2) * NDC * 512:((ct % 2) + 1) * NDC * 512]
        xnv = xnn[:, (ct % 2) * NDC * 512:((ct % 2) + 1) * NDC * 512]
        for dc in range(NDC):
            def vnn(V, ct=ct, dc=dc, xv=xv, xnv=xnv):
                V.wait(tok_rbcp[ct])
                V.wait(tok_dt.get((ct - 2, NDC - 1)))
                V.wait(tok_bvmm.get(ct - 2))
                tok_w[(ct, dc, "nn")] = V.emit(V.scalar_tensor_tensor(
                    xnv[:, dc * 512:(dc + 1) * 512], xv[:, dc * 512:(dc + 1) * 512],
                    n1w[:, dc:dc + 1], PB[3][:, 512:1024], ALU.mult, ALU.mult))
            sec("vector", vnn)

    def prol_dt(ct):
        xnv = xnn[:, (ct % 2) * NDC * 512:((ct % 2) + 1) * NDC * 512]
        dslab = dtc[:, (ct % 2) * NDC * 512:((ct % 2) + 1) * NDC * 512]
        for oc in range(NDC):
            def tdt(T, ct=ct, oc=oc, xnv=xnv):
                T.wait(tok_w[(ct, NDC - 1, "nn")], tok_wdt)
                if oc >= 2:
                    T.wait(tok_dtev.get((ct, oc - 1, "a")))
                else:
                    T.wait(tok_dtev.get((ct - 1, NDC - 1, "a")))
                    T.wait(tok_bvevh.get(ct - 1))
                pb = PB[oc % 2]
                for dd in range(NDC):
                    ins = T.matmul(pb[:, 0:512], wdt3[:, dd, oc * 128:(oc + 1) * 128],
                                   xnv[:, dd * 512:(dd + 1) * 512],
                                   start=(dd == 0), stop=(dd == NDC - 1))
                tok_dt[(ct, oc)] = T.emit(ins)
            sec("tensor", tdt)

            def sdt(SS, ct=ct, oc=oc, dslab=dslab):
                SS.wait(tok_dt[(ct, oc)])
                SS.wait(tok_bv.get(ct - 1))
                pb = PB[oc % 2]
                sc = spc[:, oc * 512:(oc + 1) * 512]
                SS.emit(SS.activation(out=sc, in_=pb[:, 0:512], func=AF.Abs,
                                      bias=d_bdt_col(env, oc)))
                SS.drain()
                SS.emit(SS.activation(out=sc, in_=sc, func=AF.Exp, scale=-1.0))
                SS.drain()
                SS.emit(SS.activation(out=sc, in_=sc, func=AF.Ln, bias=1.0))
                SS.drain()
                tok_dtev[(ct, oc, "a")] = SS.emit(SS.activation(
                    out=dslab[:, oc * 512:(oc + 1) * 512],
                    in_=pb[:, 0:512], func=AF.Relu, bias=d_bdt_col(env, oc)))
            sec("scalar", sdt)

    def prol_bv(ct):
        xnv = xnn[:, (ct % 2) * NDC * 512:((ct % 2) + 1) * NDC * 512]

        def tbv(T, ct=ct, xnv=xnv):
            T.wait(tok_w[(ct, NDC - 1, "nn")], tok_wdt)
            T.wait(tok_bvevh.get(ct - 1))
            T.wait(tok_cv.get((ct - 1, 0)))
            for dd in range(NDC):
                ins = T.matmul(PB[2][0:64, 0:512], wbc3[:, dd, :],
                               xnv[:, dd * 512:(dd + 1) * 512],
                               start=(dd == 0), stop=(dd == NDC - 1))
            tok_bvmm[ct] = T.emit(ins)
        sec("tensor", tbv)

    def prol_tail(ct):
        xnv = xnn[:, (ct % 2) * NDC * 512:((ct % 2) + 1) * NDC * 512]
        dslab = dtc[:, (ct % 2) * NDC * 512:((ct % 2) + 1) * NDC * 512]

        def sbv(SS, ct=ct):
            SS.wait(tok_bvmm[ct])
            SS.wait(tok_bc2.get((ct - 1, NST - 1)))
            SS.emit(SS.activation(out=bvc[:, 0:512], in_=PB[2][0:16, 0:512],
                                  func=AF.Identity, bias=bbc_sb[0:16, 0:1]))
            env.tok_s = SS.emit(SS.activation(out=bvc[:, 512:1024], in_=PB[2][32:48, 0:512],
                                              func=AF.Identity, bias=bbc_sb[32:48, 0:1]))
            SS.drain()
        sec("scalar", sbv)
        tok_bvev = env.tok_s
        tok_bvevh[ct] = tok_bvev

        for oc in range(NDC):
            def vdt(V, ct=ct, oc=oc, xnv=xnv, dslab=dslab):
                V.wait(tok_dtev[(ct, oc, "a")])
                dv = dslab[:, oc * 512:(oc + 1) * 512]
                tok_dtev[(ct, oc)] = V.emit(V.tensor_add(
                    dv, dv, spc[:, oc * 512:(oc + 1) * 512]))
                V.drain()
                V.emit(V.reciprocal(rdt[:], dv))
                V.drain()
                tok_w[(ct, oc)] = V.emit(V.tensor_tensor(
                    wc_[:, oc * 512:(oc + 1) * 512], xnv[:, oc * 512:(oc + 1) * 512],
                    rdt[:], ALU.mult))
            sec("vector", vdt)

        def vbv(V, ct=ct, tok_bvev=tok_bvev):
            V.wait(tok_bvev, tok_abc)
            tok_bv[ct] = V.emit(V.tensor_scalar(bvc[:, 0:512], bvc[:, 0:512],
                                                alog[:, 1:2], None, ALU.mult))
            V.drain()
        sec("vector", vbv)

    def scan_group(sct, n):
        dslab = dtc[:, (sct % 2) * NDC * 512:((sct % 2) + 1) * NDC * 512]
        if n % 2 == 0:
            Bslot, Cslot = PB[2][:, 512:1024], PB[3][:, 0:512]
        else:
            Bslot, Cslot = PB[0][:, 512:1024], PB[1][:, 512:1024]
        pn = (sct, n - 2) if n >= 2 else (sct - 1, NST - 2 + n)

        def tbc2(T, sct=sct, n=n, Bslot=Bslot, Cslot=Cslot, pn=pn):
            T.wait(tok_bv[sct])
            T.wait(tok_scan.get((pn[0] * NST + pn[1]) * NDC + NDC - 1))
            T.wait(tok_exp2.get((pn[0], pn[1], 0)))
            T.matmul(Bslot, sel163[:, n, :], bvc[:, 0:512], start=True, stop=True)
            env.tok_t = T.emit(T.matmul(Cslot, sel163[:, n, :], bvc[:, 512:1024],
                                        start=True, stop=True))
        sec("tensor", tbc2)
        tok_bc2[(sct, n)] = env.tok_t

        def sevac(SS, sct=sct, n=n, Cslot=Cslot):
            SS.wait(tok_bc2[(sct, n)])
            SS.wait(tok_y.get((sct * NST + n - 1) * NDC + 0))
            tok_cv[(sct, n)] = SS.emit(SS.activation(
                out=cv_sb[:, (n % 2) * 512:(n % 2 + 1) * 512], in_=Cslot,
                func=AF.Identity))
        sec("scalar", sevac)

        for dc in range(NDC):
            sl = ((n * NDC + dc) % 2)
            G = (sct * NST + n) * NDC + dc

            def sexp(SS, sct=sct, n=n, dc=dc, sl=sl, G=G, dslab=dslab):
                SS.wait(tok_dtev[(sct, NDC - 1)])
                SS.wait(tok_scan.get(G - 2))
                tok_exp2[(sct, n, dc)] = SS.emit(SS.activation(
                    out=dA_s[:, sl * 512:(sl + 1) * 512], in_=dslab[:, dc * 512:(dc + 1) * 512],
                    func=AF.Exp, scale=abc[:, n:n + 1]))
            sec("scalar", sexp)

            def vscan(V, sct=sct, n=n, dc=dc, sl=sl, G=G, Bslot=Bslot):
                V.wait(tok_exp2[(sct, n, dc)], tok_bc2[(sct, n)])
                V.wait(tok_y.get(G - 5))
                dav = dA_s[:, sl * 512:(sl + 1) * 512]
                b3v = b3_s[:, sl * 512:(sl + 1) * 512]
                hv = h_s[:, (G % 6) * 512:(G % 6 + 1) * 512]
                V.emit(V.scalar_tensor_tensor(b3v, dav, -1.0, wc_[:, dc * 512:(dc + 1) * 512],
                                              ALU.add, ALU.mult))
                V.emit(V.tensor_tensor(b3v, b3v, Bslot, ALU.mult))
                stcol = stt_s[:, (n * NDC + dc):(n * NDC + dc) + 1]
                init = stcol if sct > 0 else 0.0
                V.emit(V.tensor_tensor_scan(hv, dav, b3v, init, ALU.mult, ALU.add))
                tok_scan[G] = V.emit(V.tensor_copy(stcol, hv[:, 511:512]))
            sec("vector", vscan)

            def gacc(G_, sct=sct, n=n, dc=dc, G=G):
                G_.wait(tok_scan[G], tok_cv[(sct, n)])
                hv = h_s[:, (G % 6) * 512:(G % 6 + 1) * 512]
                cvv = cv_sb[:, (n % 2) * 512:(n % 2 + 1) * 512]
                acc = yac[:, dc * 512:(dc + 1) * 512]
                if n == 0:
                    if sct >= 1:
                        G_.wait(tok_yep[(sct - 1, dc)])
                    tok_y[G] = G_.emit(G_.tensor_tensor(acc, hv, cvv, ALU.mult))
                else:
                    G_.emit(G_.tensor_tensor(ybp[:], hv, cvv, ALU.mult))
                    tok_y[G] = G_.emit(G_.tensor_add(acc, acc, ybp[:]))
            sec("gpsimd", gacc)

    def epi(sct):
        xv = xsl[:, (sct % 2) * NDC * 512:((sct % 2) + 1) * NDC * 512]
        xnv = xnn[:, (sct % 2) * NDC * 512:((sct % 2) + 1) * NDC * 512]
        for dc in range(NDC):
            def vyep(V, sct=sct, dc=dc, xv=xv, xnv=xnv):
                V.wait(tok_y[(sct * NST + NST - 1) * NDC + dc])
                V.wait(tok_x1st.get((sct, dc - 2)) if dc >= 2 else tok_x1st.get((sct - 1, NDC - 2 + dc)))
                V.emit(V.scalar_tensor_tensor(
                    yeps[:], xnv[:, dc * 512:(dc + 1) * 512], dpar[:, dc:dc + 1],
                    yac[:, dc * 512:(dc + 1) * 512], ALU.mult, ALU.add))
                tok_yep[(sct, dc)] = V.emit(V.scalar_tensor_tensor(
                    x1c[:, (dc % 2) * 512:(dc % 2 + 1) * 512], yeps[:],
                    mscl[:, dc:dc + 1], xv[:, dc * 512:(dc + 1) * 512], ALU.mult, ALU.add))
            sec("vector", vyep)

            def yx1(Y, sct=sct, dc=dc):
                Y.wait(tok_yep[(sct, dc)])
                tok_x1st[(sct, dc)] = Y.emit(Y.dma_start(
                    out=d.x1_d[dc * 128:(dc + 1) * 128, sct * 512:(sct + 1) * 512],
                    in_=x1c[:, (dc % 2) * 512:(dc % 2 + 1) * 512]), dma=True)
            sec("sync", yx1)

    for ct in range(NT5 + 1):
        live = ct < NT5
        sct = ct - 1
        if live:
            prol_load(ct)
        if sct >= 0:
            scan_group(sct, 0)
        if live:
            prol_rms(ct)
        if sct >= 0:
            scan_group(sct, 1)
        if live:
            prol_nn(ct)
        if sct >= 0:
            scan_group(sct, 2)
            scan_group(sct, 3)
        if live:
            prol_dt(ct)
        if sct >= 0:
            scan_group(sct, 4)
        if live:
            prol_bv(ct)
        if sct >= 0:
            for n in range(5, NST):
                scan_group(sct, n)
            epi(sct)
        if live:
            prol_tail(ct)

    barrier(env, block)
    sstack.close()

    fstack = ExitStack()
    ef = fstack.enter_context
    x1sl = ef(nc.sbuf_tensor("x1sl", [128, 2 * NDC * 512], BF16))
    xn2c = ef(nc.sbuf_tensor("xn2c", [128, NDC * 512], BF16))
    sqf2 = ef(nc.sbuf_tensor("sqf2", [128, 2 * 512], F32))
    hf2 = ef(nc.sbuf_tensor("hf2", [128, NFC * 512], BF16))
    w1st = ef(nc.sbuf_tensor("w1st", [128, 2 * NDC * 128], BF16))
    w2st = ef(nc.sbuf_tensor("w2st", [128, 2 * NFC * 128], BF16))
    b1sb = ef(nc.sbuf_tensor("b1sb", [128, NFC], F32))
    b2sb = ef(nc.sbuf_tensor("b2sb", [128, NDC], F32))
    sch = ef(nc.sbuf_tensor("sch", [128, 2 * 512], F32))
    stm = ef(nc.sbuf_tensor("stm", [128, 4 * 1024], F32))
    lrow2 = ef(nc.sbuf_tensor("lrow2", [1, 512], F32))

    def yldb(Y):
        Y.emit(Y.dma_start(out=b1sb[:], in_=d.b1[:]), dma=True)
        env.tok_y = Y.emit(Y.dma_start(out=b2sb[:], in_=d.b2[:]), dma=True)
    sec("sync", yldb)
    tok_b12 = env.tok_y

    tok_lx1 = {}
    tok_r2 = {}
    tok_sq3 = {}
    tok_n2 = {}
    tok_w1l = {}
    tok_w1m = {}
    tok_hf = {}
    tok_w2l = {}
    tok_w2m = {}
    tok_sev = {}
    tok_str = {}
    tok_strv = {}
    tok_stst = {}
    env.tok_r2row = {}

    for ft in range(NT5):
        fb = ft % 2
        xv = x1sl[:, fb * NDC * 512:(fb + 1) * NDC * 512]

        def yld2(Y, ft=ft, xv=xv):
            Y.wait(tok_x1st.get((ft, NDC - 1)))
            Y.wait(tok_w1m.get((ft - 2, NFC - 1)))
            tok_lx1[ft] = Y.emit(Y.dma_start(
                out=xv.rearrange("p (dc q) -> p dc q", dc=NDC),
                in_=d.x1_d.rearrange("(dc p) q -> p dc q", p=128)[:, :, ft * 512:(ft + 1) * 512]), dma=True)
        sec("sync", yld2)

        for dc in range(NDC):
            def ssq2(SS, ft=ft, dc=dc, xv=xv):
                SS.wait(tok_lx1[ft])
                SS.wait(tok_r2.get((ft, dc - 2)) if dc >= 2 else tok_r2.get((ft - 1, NDC - 2 + dc)))
                tok_sq3[(ft, dc)] = SS.emit(SS.activation(
                    out=sqf2[:, (dc % 2) * 512:(dc % 2 + 1) * 512],
                    in_=xv[:, dc * 512:(dc + 1) * 512], func=AF.Square))
            sec("scalar", ssq2)

            def tsq2(T, ft=ft, dc=dc):
                T.wait(tok_sq3[(ft, dc)])
                if dc == 0:
                    T.wait(env.tok_r2row.get(ft - 1))
                    T.wait(tok_strv.get((ft - 1, NDC - 1, 3)))
                tok_r2[(ft, dc)] = T.emit(T.matmul(
                    PB[3][0:1, 0:512], env.onescol[:], sqf2[:, (dc % 2) * 512:(dc % 2 + 1) * 512],
                    start=(dc == 0), stop=(dc == NDC - 1)))
            sec("tensor", tsq2)

        def srow2(SS, ft=ft):
            SS.wait(tok_r2[(ft, NDC - 1)])
            SS.emit(SS.activation(out=lrow2[0:1, 0:512], in_=PB[3][0:1, 0:512],
                                  func=AF.Ln, scale=1.0 / 1024.0, bias=RMS_EPS))
            SS.drain()
            env.tok_s = SS.emit(SS.activation(out=lrow2[0:1, 0:512], in_=lrow2[0:1, 0:512],
                                              func=AF.Exp, scale=-0.5))
            SS.drain()
        sec("scalar", srow2)
        env.tok_r2row[ft] = env.tok_s
        tok_rr2 = env.tok_s

        def tbc3(T, ft=ft, tok_rr2=tok_rr2):
            T.wait(tok_rr2)
            T.wait(tok_n2.get((ft - 1, NDC - 1)))
            env.tok_t = T.emit(T.matmul(PB[3][:, 512:1024], env.ones1[:], lrow2[0:1, 0:512],
                                        start=True, stop=True))
        sec("tensor", tbc3)
        tok_rbc2 = env.tok_t

        for dc in range(NDC):
            def vn2(V, ft=ft, dc=dc, xv=xv, tok_rbc2=tok_rbc2):
                V.wait(tok_rbc2)
                V.wait(tok_w1m.get((ft - 1, NFC - 1)))
                tok_n2[(ft, dc)] = V.emit(V.scalar_tensor_tensor(
                    xn2c[:, dc * 512:(dc + 1) * 512], xv[:, dc * 512:(dc + 1) * 512],
                    n2w[:, dc:dc + 1], PB[3][:, 512:1024], ALU.mult, ALU.mult))
            sec("vector", vn2)

        for oc in range(NFC):
            def gw1(G, ft=ft, oc=oc):
                G.wait(tok_w1m.get((ft, oc - 2)) if oc >= 2 else tok_w1m.get((ft - 1, NFC - 2 + oc)))
                G.wait(env.tok_pp[("W1", oc)])
                tok_w1l[(ft, oc)] = G.emit(G.dma_start(
                    out=w1st[:, (oc % 2) * NDC * 128:((oc % 2) + 1) * NDC * 128],
                    in_=d.W1b[oc]), dma=True)
            sec("gpsimd", gw1)

            def tw1(T, ft=ft, oc=oc):
                T.wait(tok_n2[(ft, NDC - 1)], tok_w1l[(ft, oc)])
                T.wait(tok_hf.get((ft, oc - 2)) if oc >= 2 else tok_hf.get((ft - 1, NFC - 2 + oc)))
                sl = w1st[:, (oc % 2) * NDC * 128:((oc % 2) + 1) * NDC * 128]
                pb = PB[oc % 2]
                for dd in range(NDC):
                    ins = T.matmul(pb[:, 0:512], sl[:, dd * 128:(dd + 1) * 128],
                                   xn2c[:, dd * 512:(dd + 1) * 512],
                                   start=(dd == 0), stop=(dd == NDC - 1))
                tok_w1m[(ft, oc)] = T.emit(ins)
            sec("tensor", tw1)

            def sg1(SS, ft=ft, oc=oc):
                SS.wait(tok_w1m[(ft, oc)])
                tok_hf[(ft, oc)] = SS.emit(SS.activation(
                    out=hf2[:, oc * 512:(oc + 1) * 512], in_=PB[oc % 2][:, 0:512],
                    func=AF.Gelu_apprx_tanh, bias=b1sb[:, oc:oc + 1]))
            sec("scalar", sg1)

        for oc in range(NDC):
            def gw2(G, ft=ft, oc=oc):
                G.wait(tok_w2m.get((ft, oc - 2)) if oc >= 2 else tok_w2m.get((ft - 1, NDC - 2 + oc)))
                G.wait(env.tok_pp[("W2", oc)])
                tok_w2l[(ft, oc)] = G.emit(G.dma_start(
                    out=w2st[:, (oc % 2) * NFC * 128:((oc % 2) + 1) * NFC * 128],
                    in_=d.W2b[oc]), dma=True)
            sec("gpsimd", gw2)

            def tw2(T, ft=ft, oc=oc):
                T.wait(tok_hf[(ft, NFC - 1)], tok_w2l[(ft, oc)])
                T.wait(tok_sev.get((ft, oc - 2)) if oc >= 2 else tok_sev.get((ft - 1, NDC - 2 + oc)))
                sl = w2st[:, (oc % 2) * NFC * 128:((oc % 2) + 1) * NFC * 128]
                pb = PB[2]
                for kc in range(NFC):
                    ins = T.matmul(pb[:, (oc % 2) * 512:((oc % 2) + 1) * 512],
                                   sl[:, kc * 128:(kc + 1) * 128],
                                   hf2[:, kc * 512:(kc + 1) * 512],
                                   start=(kc == 0), stop=(kc == NFC - 1))
                tok_w2m[(ft, oc)] = T.emit(ins)
            sec("tensor", tw2)

            def vs2(V, ft=ft, oc=oc, xv=xv):
                V.wait(tok_w2m[(ft, oc)])
                V.wait(tok_str.get((ft, oc - 2, 3)) if oc >= 2 else tok_str.get((ft - 1, NDC - 2 + oc, 3)))
                tok_sev[(ft, oc)] = V.emit(V.scalar_tensor_tensor(
                    sch[:, (oc % 2) * 512:((oc % 2) + 1) * 512],
                    PB[2][:, (oc % 2) * 512:((oc % 2) + 1) * 512],
                    b2sb[:, oc:oc + 1], xv[:, oc * 512:(oc + 1) * 512], ALU.add, ALU.add))
            sec("vector", vs2)

            for j in range(4):
                def ts3(T, ft=ft, oc=oc, j=j):
                    T.wait(tok_sev[(ft, oc)])
                    T.wait(tok_strv.get((ft, oc, j - 1)) if j >= 1 else
                           tok_strv.get((ft, oc - 1, 3), tok_strv.get((ft - 1, NDC - 1, 3)) if oc == 0 else None))
                    tok_str[(ft, oc, j)] = T.emit(T.transpose(
                        PB[3].bitcast(F32)[:, 0:128],
                        sch[:, (oc % 2) * 512 + j * 128:(oc % 2) * 512 + (j + 1) * 128],
                        env.ident_f32[:]))
                sec("tensor", ts3)

                def vs3(V, ft=ft, oc=oc, j=j):
                    V.wait(tok_str[(ft, oc, j)])
                    V.wait(tok_stst.get(ft - 1))
                    tok_strv[(ft, oc, j)] = V.emit(V.tensor_copy(
                        stm[:, j * 1024 + oc * 128: j * 1024 + (oc + 1) * 128],
                        PB[3].bitcast(F32)[:, 0:128]))
                sec("vector", vs3)

        def yst(Y, ft=ft):
            Y.wait(tok_strv[(ft, NDC - 1, 3)])
            tok_stst[ft] = Y.emit(Y.dma_start(
                out=d.s1tm_d.rearrange("(s p) dd -> p s dd", p=128)[:, 4 * ft:4 * ft + 4, :],
                in_=stm[:].rearrange("p (s dd) -> p s dd", s=4)), dma=True)
        sec("sync", yst)
    env.tok_phase2_done = tok_stst[NT5 - 1]
    barrier(env, block)
    fstack.close()
    pstack2.close()


def d_bdt_col(env, oc):
    return env.bdt_sb[:, oc:oc + 1]



def phase3(nc, env, d, block, cfg):
    ctx = env.ctx
    seq = cfg.seq
    NT = seq // 128
    NT5 = seq // 512
    KE = seq // 4
    NJT = KE // 128
    JW = min(512, KE)
    NJH = KE // JW
    sec = lambda name, body: _mk_section(env, block, name, body)

    pstack3 = ExitStack()
    PB = [pstack3.enter_context(nc.psum_tensor(f"p3PB{i}", [128, 1024], F32)) for i in range(4)]
    st3 = ExitStack()
    e3 = st3.enter_context
    probs_sb = e3(nc.sbuf_tensor("probs_sb", [128, NT * NEXP], F32))
    prow = e3(nc.sbuf_tensor("prow", [128, 8], F32))
    tokid_sb = e3(nc.sbuf_tensor("tokid_sb", [128, NT], I32))
    jcol_sb = e3(nc.sbuf_tensor("jcol_sb", [128, NJT], I32))
    m_sb = e3(nc.sbuf_tensor("m_sb", [128, NEXP * NJT], I32))
    g_sb = e3(nc.sbuf_tensor("g_sb", [128, NEXP * NJT], F32))
    ptmp = e3(nc.sbuf_tensor("ptmp", [128, NEXP], F32))
    inv_sb = e3(nc.sbuf_tensor("inv_sb", [128, NEXP * NT], I32))
    bigi = e3(nc.sbuf_tensor("bigi", [128, NT], I32))

    tok_gld = {}
    tok_gmm = {}
    tok_gtr = {}
    tok_sm = {}
    tok_pst = {}
    tok_invsc = {}
    tok_gv = {}
    tok_mld = {}
    tok_w1l = {}
    tok_m1 = {}
    tok_hg = {}
    tok_w2l = {}
    tok_m2 = {}
    tok_eev = {}
    tok_est = {}
    tok_xg = {}
    tok_xtr = {}
    tok_xtrv = {}
    tok_gat = {}
    tok_acc = {}
    tok_cmb3 = {}
    tok_ost = {}

    def yc3(Y):
        Y.emit(Y.dma_start(out=tokid_sb[:], in_=d.tokid[:]), dma=True)
        env.tok_y = Y.emit(Y.dma_start(out=jcol_sb[:], in_=d.jcol[:, 0:NJT]), dma=True)
    sec("sync", yc3)
    tok_c3 = env.tok_y

    sta = ExitStack()
    ea = sta.enter_context
    gate_sb = ea(nc.sbuf_tensor("gate_sb", [128, NDC * NEXP], BF16))
    gate3 = gate_sb[:].rearrange("p (dd f) -> p dd f", dd=NDC)
    xsl3 = ea(nc.sbuf_tensor("xsl3", [128, 2 * NDC * 512], BF16))
    sc_st = ea(nc.sbuf_tensor("sc_st", [NEXP, 512], F32))
    lo_t = ea(nc.sbuf_tensor("lo_t", [128, NEXP], F32))
    hi_t = ea(nc.sbuf_tensor("hi_t", [128, NEXP], F32))
    mid_t = ea(nc.sbuf_tensor("mid_t", [128, NEXP], F32))
    cm_t = ea(nc.sbuf_tensor("cm_t", [128, NEXP], F32))
    cmi_t = ea(nc.sbuf_tensor("cmi_t", [128, NEXP], I32))
    cnt_t = ea(nc.sbuf_tensor("cnt_t", [128, NEXP], F32))
    crow = ea(nc.sbuf_tensor("crow", [1, NEXP], F32))
    msk = ea(nc.sbuf_tensor("msk", [128, NT], F32))
    onesNT = ea(nc.sbuf_tensor("onesNT", [128, NT], F32))
    pref = ea(nc.sbuf_tensor("pref", [128, NT], F32))
    excl = ea(nc.sbuf_tensor("excl", [128, 1], F32))
    jf_t = ea(nc.sbuf_tensor("jf_t", [128, NT], F32))
    jself = ea(nc.sbuf_tensor("jself", [128, NT], F32))
    bigf = ea(nc.sbuf_tensor("bigf", [128, NT], F32))
    jsel = ea(nc.sbuf_tensor("jsel", [128, NT], I32))
    mski = ea(nc.sbuf_tensor("mski", [128, NT], I32))

    def gg3(G):
        env.breg_ke = G.raw.alloc_register("bke")
        G.raw.reg_mov(env.breg_ke, KE - 1)
        env.breg_seq = G.raw.alloc_register("bseq")
        G.raw.reg_mov(env.breg_seq, seq - 1)
        env.tok_g = G.emit(G.dma_start(out=gate3, in_=d.gate_w.rearrange("(dd p) f -> p dd f", p=128)), dma=True)
    sec("gpsimd", gg3)
    tok_gw = env.tok_g

    zrow = ea(nc.sbuf_tensor("zrow", [1, 1024], F32))

    def vc3(V):
        env.tok_zr = V.emit(V.memset(zrow[:], 0.0))
        V.emit(V.memset(onesNT[:], 1.0))
        V.emit(V.memset(bigf[:], float(BIGIDX)))
        V.emit(V.memset(bigi[:], KE))
        V.emit(V.memset(lo_t[:], 0.0))
        env.tok_v = V.emit(V.memset(hi_t[:], 1.0))
        V.drain()
    sec("vector", vc3)

    def yzr(Y):
        Y.wait(env.tok_zr)
        for ex in range(NEXP):
            env.tok_y = Y.emit(Y.dma_start(out=d.expout_ds[ex][KE:KE + 1, :], in_=zrow[:]), dma=True)
    sec("sync", yzr)

    for ts in range(NT5):
        sb = ts % 2
        xv = xsl3[:, sb * NDC * 512:(sb + 1) * NDC * 512]

        def yg(Y, ts=ts, xv=xv):
            Y.wait(env.tok_phase2_done)
            Y.wait(tok_gmm.get(ts - 2))
            tok_gld[ts] = Y.emit(Y.dma_start(
                out=xv.rearrange("p (dc q) -> p dc q", dc=NDC),
                in_=d.xn1fm_d.rearrange("(dc p) q -> p dc q", p=128)[:, :, ts * 512:(ts + 1) * 512]), dma=True)
        sec("sync", yg)

        def tg(T, ts=ts, xv=xv):
            T.wait(tok_gld[ts], tok_gw)
            T.wait(tok_sm.get((ts - 1, 3)))
            for dd in range(NDC):
                ins = T.matmul(PB[0][0:NEXP, 0:512], gate3[:, dd, :],
                               xv[:, dd * 512:(dd + 1) * 512],
                               start=(dd == 0), stop=(dd == NDC - 1))
            tok_gmm[ts] = T.emit(ins)
        sec("tensor", tg)

        def vgev(V, ts=ts):
            V.wait(tok_gmm[ts])
            env.tok_v = V.emit(V.tensor_copy(sc_st[:], PB[0][0:NEXP, 0:512]))
            V.drain()
        sec("vector", vgev)
        tok_scst = env.tok_v

        for j in range(4):
            def tgt2(T, ts=ts, j=j, tok_scst=tok_scst):
                T.wait(tok_scst)
                T.wait(tok_sm.get((ts, j - 2)) if j >= 2 else tok_sm.get((ts - 1, 2 + j)))
                tok_gtr[(ts, j)] = T.emit(T.transpose(
                    PB[1][:, (j % 2) * 512:(j % 2) * 512 + NEXP],
                    sc_st[:, j * 128:(j + 1) * 128], env.ident_f32[0:NEXP, 0:NEXP]))
            sec("tensor", tgt2)

            def vsm(V, ts=ts, j=j):
                V.wait(tok_gtr[(ts, j)])
                ps = PB[1][:, (j % 2) * 512:(j % 2) * 512 + NEXP]
                V.emit(V.tensor_reduce(prow[:, 0:1], ps, AX.X, ALU.max))
                V.drain()
                env.tok_v = V.emit(V.tensor_scalar_mul(prow[:, 1:2], prow[:, 0:1], -1.0))
                V.drain()
            sec("vector", vsm)
            tok_nm = env.tok_v

            def ssm(SS, ts=ts, j=j, tok_nm=tok_nm):
                SS.wait(tok_nm)
                tt = ts * 4 + j
                env.tok_s = SS.emit(SS.activation(
                    out=probs_sb[:, tt * NEXP:(tt + 1) * NEXP],
                    in_=PB[1][:, (j % 2) * 512:(j % 2) * 512 + NEXP],
                    func=AF.Exp, bias=prow[:, 1:2]))
                SS.drain()
            sec("scalar", ssm)
            tok_ex3 = env.tok_s

            def vsm2(V, ts=ts, j=j, tok_ex3=tok_ex3):
                V.wait(tok_ex3)
                tt = ts * 4 + j
                pv = probs_sb[:, tt * NEXP:(tt + 1) * NEXP]
                V.emit(V.tensor_reduce(prow[:, 3:4], pv, AX.X, ALU.add))
                V.drain()
                V.emit(V.reciprocal(prow[:, 4:5], prow[:, 3:4]))
                V.drain()
                tok_sm[(ts, j)] = V.emit(V.tensor_scalar(pv, pv, prow[:, 4:5], None, ALU.mult))
                V.drain()
            sec("vector", vsm2)

            def yps(Y, ts=ts, j=j):
                tt = ts * 4 + j
                Y.wait(tok_sm[(ts, j)])
                tok_pst[tt] = Y.emit(Y.dma_start(
                    out=d.probs_d[tt * 128:(tt + 1) * 128, :],
                    in_=probs_sb[:, tt * NEXP:(tt + 1) * NEXP]), dma=True)
            sec("sync", yps)

    NITER = 50
    for it in range(NITER):
        def vbi(V, it=it):
            if it == 0:
                V.wait(tok_sm[(NT5 - 1, 3)])
            V.emit(V.tensor_add(mid_t[:], lo_t[:], hi_t[:]))
            V.drain()
            V.emit(V.tensor_scalar_mul(mid_t[:], mid_t[:], 0.5))
            V.drain()
            ins = None
            for ex in range(NEXP):
                pv = probs_sb[:].rearrange("p (t e) -> p e t", e=NEXP)[:, ex, :]
                V.emit(V.tensor_single_scalar(msk[:], pv, mid_t[:, ex:ex + 1], ALU.is_ge))
                V.drain()
                ins = V.emit(V.tensor_reduce(cnt_t[:, ex:ex + 1], msk[:], AX.X, ALU.add))
                V.drain()
            env.tok_v = ins
        sec("vector", vbi)
        tok_ct = env.tok_v

        def tbi(T, it=it, tok_ct=tok_ct):
            T.wait(tok_ct)
            env.tok_t = T.emit(T.matmul(PB[2][0:1, 0:NEXP], env.onescol[:], cnt_t[:],
                                        start=True, stop=True))
        sec("tensor", tbi)
        tok_cr = env.tok_t

        def vbi2(V, it=it, tok_cr=tok_cr):
            V.wait(tok_cr)
            env.tok_v = V.emit(V.tensor_copy(crow[:], PB[2][0:1, 0:NEXP]))
            V.drain()
        sec("vector", vbi2)
        tok_crv = env.tok_v

        def tbi2(T, it=it, tok_crv=tok_crv):
            T.wait(tok_crv)
            env.tok_t = T.emit(T.matmul(PB[2][:, 512:512 + NEXP], env.ones1[:], crow[:],
                                        start=True, stop=True))
        sec("tensor", tbi2)
        tok_cb2 = env.tok_t

        def vbi3(V, it=it, tok_cb2=tok_cb2):
            V.wait(tok_cb2)
            V.emit(V.tensor_single_scalar(cm_t[:], PB[2][:, 512:512 + NEXP], float(KE), ALU.is_ge))
            V.drain()
            V.emit(V.tensor_copy(cmi_t[:], cm_t[:]))
            V.drain()
            V.emit(V.select(lo_t[:], cmi_t[:], mid_t[:], lo_t[:]))
            V.drain()
            V.emit(V.select(cm_t[:], cmi_t[:], hi_t[:], mid_t[:]))
            V.drain()
            env.tok_v = V.emit(V.tensor_copy(hi_t[:], cm_t[:]))
            V.drain()
        sec("vector", vbi3)
    tok_thr = env.tok_v

    if cfg.dbg:
        def ythr(Y, tok_thr=tok_thr):
            Y.wait(tok_thr)
            env.tok_y = Y.emit(Y.dma_start(out=d.thr_d[:, :], in_=lo_t[:]), dma=True)
        sec("sync", ythr)

    for ex in range(NEXP):
        def vcp(V, ex=ex, tok_thr=tok_thr):
            V.wait(tok_thr)
            V.wait(tok_invsc.get(ex - 1), tok_gv.get((ex - 1, NJT - 1)))
            pv = probs_sb[:].rearrange("p (t e) -> p e t", e=NEXP)[:, ex, :]
            V.emit(V.tensor_single_scalar(msk[:], pv, lo_t[:, ex:ex + 1], ALU.is_ge))
            V.drain()
            V.emit(V.tensor_tensor_scan(pref[:], onesNT[:], msk[:], 0.0, ALU.mult, ALU.add))
            V.drain()
            env.tok_v = V.emit(V.tensor_copy(prow[:, 5:6], pref[:, NT - 1:NT]))
            V.drain()
        sec("vector", vcp)
        tok_pf = env.tok_v

        def tcp(T, ex=ex, tok_pf=tok_pf):
            T.wait(tok_pf)
            env.tok_t = T.emit(T.matmul(PB[2][:, 0:1], env.tri[:], prow[:, 5:6], start=True, stop=True))
        sec("tensor", tcp)
        tok_exm = env.tok_t

        def vcp2(V, ex=ex, tok_exm=tok_exm):
            V.wait(tok_exm)
            V.emit(V.tensor_copy(excl[:], PB[2][:, 0:1]))
            V.drain()
            V.emit(V.tensor_scalar(jf_t[:], pref[:], excl[:, 0:1], -1.0, ALU.add, ALU.add))
            V.drain()
            V.emit(V.tensor_copy(mski[:], msk[:]))
            V.drain()
            V.emit(V.select(jself[:], mski[:], jf_t[:], bigf[:]))
            V.drain()
            env.tok_v = V.emit(V.tensor_copy(jsel[:], jself[:]))
            V.drain()
        sec("vector", vcp2)
        tok_js = env.tok_v

        def gsc(G, ex=ex, tok_js=tok_js):
            G.wait(tok_js, tok_c3)
            if ex == 0:
                G.wait(tok_pst[NT - 1])
            for tt in range(NT):
                ins = G.indirect_dma_start(
                    out=d.m_ds[ex].rearrange("(a b) -> a b", b=1),
                    out_offset=IndirectOffsetOnAxis(jsel[:, tt:tt + 1], 0),
                    in_=tokid_sb[:, tt:tt + 1], in_offset=None,
                    bounds_check=env.breg_ke, oob_is_err=False)
                G.emit(ins, dma=True)
            env.tok_g = G.emit(G.dma_start(
                out=d.inv_ds[ex].rearrange("(p t) -> p t", t=NT), in_=bigi[:, 0:NT]), dma=True)
        sec("gpsimd", gsc)
        tok_msc = env.tok_g

        def ym(Y, ex=ex, tok_msc=tok_msc):
            Y.wait(tok_msc)
            tok_mld[ex] = Y.emit(Y.dma_start(
                out=m_sb[:, ex * NJT:(ex + 1) * NJT],
                in_=d.m_ds[ex].rearrange("(p j) -> p j", j=NJT)), dma=True)
        sec("sync", ym)

        def gsc2(G, ex=ex):
            G.wait(tok_mld[ex])
            ins = None
            for jt in range(NJT):
                ins = G.indirect_dma_start(
                    out=d.inv_ds[ex].rearrange("(a b) -> a b", b=1),
                    out_offset=IndirectOffsetOnAxis(m_sb[:, ex * NJT + jt:ex * NJT + jt + 1], 0),
                    in_=jcol_sb[:, jt:jt + 1], in_offset=None,
                    bounds_check=env.breg_seq, oob_is_err=False)
                G.emit(ins, dma=True)
            tok_invsc[ex] = (env.sems["gpsimd"], env.sems["gpsimd"].n)
        sec("gpsimd", gsc2)

        for jt in range(NJT):
            def gpv(G, ex=ex, jt=jt):
                if jt > 0:
                    G.wait(tok_gv[(ex, jt - 1)])
                ins = G.indirect_dma_start(
                    out=ptmp[:], out_offset=None,
                    in_=d.probs_d[:],
                    in_offset=IndirectOffsetOnAxis(m_sb[:, ex * NJT + jt:ex * NJT + jt + 1], 0))
                tok_xg[(ex, jt, "p")] = G.emit(ins, dma=True)
            sec("gpsimd", gpv)

            def vgv(V, ex=ex, jt=jt):
                V.wait(tok_xg[(ex, jt, "p")])
                tok_gv[(ex, jt)] = V.emit(V.tensor_copy(
                    g_sb[:, ex * NJT + jt:ex * NJT + jt + 1], ptmp[:, ex:ex + 1]))
            sec("vector", vgv)

    barrier(env, block)
    sta.close()

    stb = ExitStack()
    eb = stb.enter_context
    w1h = eb(nc.sbuf_tensor("w1h", [128, NDC * DFF], BF16))
    w1h3 = w1h[:].rearrange("p (dd f) -> p dd f", dd=NDC)
    w2h = eb(nc.sbuf_tensor("w2h", [128, NFC * 512], BF16))
    w2h3 = w2h[:].rearrange("p (kc f) -> p kc f", kc=NFC)
    xin_tm = eb(nc.sbuf_tensor("xin_tm", [128, 2 * 1024], BF16))
    xin_fm = eb(nc.sbuf_tensor("xin_fm", [128, NDC * KE], BF16))
    hgel = eb(nc.sbuf_tensor("hgel", [128, NFC * KE], BF16))
    eo_tm = eb(nc.sbuf_tensor("eo_tm", [128, 2 * 512], BF16))

    for ex in range(NEXP):
        for jt in range(NJT):
            def gx(G, ex=ex, jt=jt):
                G.wait(tok_xtr.get((ex, jt - 2, NDC - 1)) if jt >= 2 else
                       tok_xtr.get((ex - 1, NJT - 2 + jt, NDC - 1)))
                ins = G.indirect_dma_start(
                    out=xin_tm[:, (jt % 2) * 1024:(jt % 2 + 1) * 1024], out_offset=None,
                    in_=d.xn1tm_d[:],
                    in_offset=IndirectOffsetOnAxis(m_sb[:, ex * NJT + jt:ex * NJT + jt + 1], 0))
                tok_xg[(ex, jt)] = G.emit(ins, dma=True)
            sec("gpsimd", gx)

            for dc in range(NDC):
                def tx(T, ex=ex, jt=jt, dc=dc):
                    T.wait(tok_xg[(ex, jt)])
                    T.wait(tok_xtrv.get((ex, jt, dc - 2)) if dc >= 2 else
                           tok_xtrv.get((ex, jt - 1, NDC - 2 + dc),
                                        tok_xtrv.get((ex - 1, NJT - 1, NDC - 2 + dc)) if jt == 0 else None))
                    if ex == 0 and jt == 0 and dc == 0:
                        T.wait(tok_gv.get((NEXP - 1, NJT - 1)))
                    tok_xtr[(ex, jt, dc)] = T.emit(T.transpose(
                        PB[3].bitcast(BF16)[:, (dc % 2) * 1024:(dc % 2) * 1024 + 128],
                        xin_tm[:, (jt % 2) * 1024 + dc * 128:(jt % 2) * 1024 + (dc + 1) * 128],
                        env.ident_bf[:]))
                sec("tensor", tx)

                def vx(V, ex=ex, jt=jt, dc=dc):
                    V.wait(tok_xtr[(ex, jt, dc)])
                    tok_xtrv[(ex, jt, dc)] = V.emit(V.tensor_copy(
                        xin_fm[:, dc * KE + jt * 128:dc * KE + (jt + 1) * 128],
                        PB[3].bitcast(BF16)[:, (dc % 2) * 1024:(dc % 2) * 1024 + 128]))
                sec("vector", vx)

        for hh in range(2):
            def gw1(G, ex=ex, hh=hh):
                G.wait(tok_m1.get((ex - 1, NFC - 1)))
                G.wait(env.tok_pp[("ew1", ex)])
                tok_w1l[(ex, hh)] = G.emit(G.dma_start(
                    out=w1h3[:, :, hh * 2048:(hh + 1) * 2048],
                    in_=d.ew1b[ex].rearrange("p (dd f) -> p dd f", dd=NDC)[:, :, hh * 2048:(hh + 1) * 2048]), dma=True)
            sec("gpsimd", gw1)

        for oc in range(NFC):
            hh = oc // (NFC // 2)

            def tm1(T, ex=ex, oc=oc, hh=hh):
                T.wait(tok_w1l[(ex, hh)], tok_xtrv[(ex, NJT - 1, NDC - 1)])
                T.wait(tok_hg.get((ex, oc - 2)) if oc >= 2 else tok_hg.get((ex - 1, NFC - 2 + oc)))
                pb = PB[oc % 2]
                for jh in range(NJH):
                    for dd in range(NDC):
                        ins = T.matmul(pb[:, jh * JW:(jh + 1) * JW],
                                       w1h3[:, dd, oc * 128:(oc + 1) * 128],
                                       xin_fm[:, dd * KE + jh * JW:dd * KE + (jh + 1) * JW],
                                       start=(dd == 0), stop=(dd == NDC - 1))
                tok_m1[(ex, oc)] = T.emit(ins)
            sec("tensor", tm1)

            def sg1(SS, ex=ex, oc=oc):
                SS.wait(tok_m1[(ex, oc)])
                tok_hg[(ex, oc)] = SS.emit(SS.activation(
                    out=hgel[:, oc * KE:(oc + 1) * KE], in_=PB[oc % 2][:, 0:KE],
                    func=AF.Gelu_apprx_tanh))
            sec("scalar", sg1)

        if cfg.dbg and ex == 0:
            def ydbg4(Y, ex=ex):
                Y.wait(tok_hg[(0, NFC - 1)], tok_xtrv[(0, NJT - 1, NDC - 1)])
                Y.emit(Y.dma_start(out=d.xinfm_d[:, :], in_=xin_fm[:]), dma=True)
                env.tok_y = Y.emit(Y.dma_start(out=d.hgel_d[:, :], in_=hgel[:, 0:2 * KE]), dma=True)
            sec("sync", ydbg4)

            def vdbg4(V):
                V.wait(env.tok_y)
            sec("vector", vdbg4)

        for dh in range(2):
            def gw2(G, ex=ex, dh=dh):
                G.wait(tok_m2.get((ex, dh - 1, NJT - 1), tok_m2.get((ex - 1, 1, NJT - 1)) if dh == 0 else None))
                G.wait(env.tok_pp[("ew2", ex)])
                tok_w2l[(ex, dh)] = G.emit(G.dma_start(
                    out=w2h3,
                    in_=d.ew2b[ex].rearrange("p (kc f) -> p kc f", kc=NFC)[:, :, dh * 512:(dh + 1) * 512]), dma=True)
            sec("gpsimd", gw2)

            for jb in range(NJT):
                def tm2(T, ex=ex, dh=dh, jb=jb):
                    T.wait(tok_w2l[(ex, dh)], tok_hg[(ex, NFC - 1)])
                    T.wait(tok_eev.get((ex, dh, jb - 2)) if jb >= 2 else
                           (tok_eev.get((ex, dh - 1, NJT - 2 + jb)) if dh == 1 else
                            tok_eev.get((ex - 1, 1, NJT - 2 + jb))))
                    pb2 = PB[2][:, (jb % 2) * 512:((jb % 2) + 1) * 512]
                    for kc in range(NFC):
                        ins = T.matmul(pb2,
                                       hgel[:, kc * KE + jb * 128:kc * KE + (jb + 1) * 128],
                                       w2h3[:, kc, :],
                                       start=(kc == 0), stop=(kc == NFC - 1))
                    tok_m2[(ex, dh, jb)] = T.emit(ins)
                sec("tensor", tm2)

                def ve2(V, ex=ex, dh=dh, jb=jb):
                    V.wait(tok_m2[(ex, dh, jb)])
                    V.wait(tok_est.get((ex, dh, jb - 2)) if jb >= 2 else
                           (tok_est.get((ex, dh - 1, NJT - 2 + jb)) if dh == 1 else
                            tok_est.get((ex - 1, 1, NJT - 2 + jb))))
                    tok_eev[(ex, dh, jb)] = V.emit(V.tensor_scalar(
                        eo_tm[:, (jb % 2) * 512:((jb % 2) + 1) * 512],
                        PB[2][:, (jb % 2) * 512:((jb % 2) + 1) * 512],
                        g_sb[:, ex * NJT + jb:ex * NJT + jb + 1], None, ALU.mult))
                sec("vector", ve2)

                def ye2(G, ex=ex, dh=dh, jb=jb):
                    G.wait(tok_eev[(ex, dh, jb)])
                    tok_est[(ex, dh, jb)] = G.emit(G.dma_start(
                        out=d.expout_ds[ex][jb * 128:(jb + 1) * 128, dh * 512:(dh + 1) * 512],
                        in_=eo_tm[:, (jb % 2) * 512:((jb % 2) + 1) * 512]), dma=True)
                sec("gpsimd", ye2)

    barrier(env, block)
    stb.close()

    stc = ExitStack()
    ec3 = stc.enter_context
    gacc = ec3(nc.sbuf_tensor("gacc", [128, 2 * 1024], F32))
    gall = ec3(nc.sbuf_tensor("gall", [128, NEXP * 1024], F32))
    gsil = ec3(nc.sbuf_tensor("gsil", [128, 1024], F32))
    o1t = ec3(nc.sbuf_tensor("o1t", [128, 2 * 1024], F32))
    s1t = ec3(nc.sbuf_tensor("s1t", [128, 2 * 1024], F32))
    outt = ec3(nc.sbuf_tensor("outt", [128, 2 * 1024], F32))

    def yinv(Y):
        ins = None
        for ex in range(NEXP):
            ins = Y.dma_start(out=inv_sb[:, ex * NT:(ex + 1) * NT],
                              in_=d.inv_ds[ex].rearrange("(p t) -> p t", t=NT))
            env.tok_y = Y.emit(ins, dma=True)
    sec("sync", yinv)
    tok_invld = env.tok_y

    tok_ld3 = {}
    for tt in range(NT):
        def yl3(Y, tt=tt):
            Y.wait(tok_cmb3.get(tt - 2))
            tok_ld3[tt] = Y.emit(Y.dma_start(
                out=o1t[:, (tt % 2) * 1024:(tt % 2 + 1) * 1024],
                in_=d.out1_d.rearrange("(p t) dd -> p t dd", t=NT)[:, tt, :]), dma=True)
            tok_ld3[(tt, "s")] = Y.emit(Y.dma_start(
                out=s1t[:, (tt % 2) * 1024:(tt % 2 + 1) * 1024],
                in_=d.s1tm_d.rearrange("(p t) dd -> p t dd", t=NT)[:, tt, :]), dma=True)
        sec("sync", yl3)

        def gga(G, tt=tt):
            G.wait(tok_invld)
            if tt == 0:
                G.wait(tok_est[(NEXP - 1, 1, NJT - 1)])
            G.wait(tok_acc.get(tt - 1))
            for ex in range(NEXP):
                ins = G.indirect_dma_start(
                    out=gall[:, ex * 1024:(ex + 1) * 1024], out_offset=None,
                    in_=d.expout_ds[ex][:],
                    in_offset=IndirectOffsetOnAxis(inv_sb[:, ex * NT + tt:ex * NT + tt + 1], 0))
                tok_gat[tt] = G.emit(ins, dma=True)
        sec("gpsimd", gga)

        def vga(V, tt=tt):
            V.wait(tok_gat[tt])
            acc = gacc[:, (tt % 2) * 1024:(tt % 2 + 1) * 1024]
            V.emit(V.tensor_tensor(acc, gall[:, 0:1024], gall[:, 1024:2048], ALU.add))
            for ex in range(2, NEXP):
                env.tok_v = V.emit(V.tensor_tensor(
                    acc, acc, gall[:, ex * 1024:(ex + 1) * 1024], ALU.add))
            tok_acc[tt] = env.tok_v
        sec("vector", vga)
        tok_gsum = tok_acc[tt]

        if cfg.dbg and tt == 0:
            def ydbg3(Y, tok_gsum=tok_gsum):
                Y.wait(tok_gsum, tok_invld)
                Y.emit(Y.dma_start(out=d.gacc_d[:, :], in_=gacc[:, 0:1024]), dma=True)
                env.tok_y = Y.emit(Y.dma_start(out=d.inv0_d[:, :], in_=inv_sb[:]), dma=True)
            sec("sync", ydbg3)

            def vdbg3(V):
                V.wait(env.tok_y)
            sec("vector", vdbg3)

        def sga(SS, tt=tt, tok_gsum=tok_gsum):
            SS.wait(tok_gsum)
            env.tok_s = SS.emit(SS.activation(
                out=gsil[:], in_=gacc[:, (tt % 2) * 1024:(tt % 2 + 1) * 1024], func=AF.Silu))
        sec("scalar", sga)
        tok_sil = env.tok_s

        def vcb(V, tt=tt, tok_sil=tok_sil):
            V.wait(tok_sil, tok_ld3[(tt, "s")])
            V.wait(tok_ost.get(tt - 2))
            sl = slice((tt % 2) * 1024, (tt % 2 + 1) * 1024)
            V.emit(V.tensor_tensor(gsil[:], s1t[:, sl], gsil[:], ALU.mult))
            tok_cmb3[tt] = V.emit(V.tensor_add(outt[:, sl], o1t[:, sl], gsil[:]))
        sec("vector", vcb)

        def yo(Y, tt=tt):
            Y.wait(tok_cmb3[tt])
            tok_ost[tt] = Y.emit(Y.dma_start(
                out=d.out.rearrange("(p t) dd -> p t dd", t=NT)[:, tt, :],
                in_=outt[:, (tt % 2) * 1024:(tt % 2 + 1) * 1024]), dma=True)
        sec("sync", yo)

    env.tok_out_done = tok_ost[NT - 1]
    barrier(env, block)
    stc.close()
    st3.close()
    pstack3.close()

def barrier(env, block):
    toks = [(env.sems[k], env.sems[k].n) for k in ENGS]
    for k in ENGS:
        def body(E, toks=toks):
            E.wait(*toks)
        _mk_section(env, block, k, body)


class Cfg:
    def __init__(self, seq=SEQ, dbg=False, phases=(1, 2, 3)):
        self.seq = seq
        self.dbg = dbg
        self.phases = phases


def build_program(cfg):
    nc = bass.Bass("TRN2", target_bir_lowering=False, detect_race_conditions=False)
    with ExitStack() as ctx:
        env = mk_env(nc, ctx)
        d = declare_io(nc, cfg.seq, dbg=cfg.dbg)
        with nc.Block() as block:
            setup_stage(nc, env, d, block)
            if 1 in cfg.phases:
                phase1(nc, env, d, block, cfg)
            if 2 in cfg.phases:
                barrier(env, block)
                phase2(nc, env, d, block, cfg)
            if 3 in cfg.phases:
                barrier(env, block)
                phase3(nc, env, d, block, cfg)
            barrier(env, block)
    return nc


def kernel(**inputs):
    W = host_weights(inputs)
    x = np.asarray(inputs["L0_attn_x"])
    C = host_constants(x.shape[1])
    nb = x.shape[0]
    cfg = Cfg(seq=x.shape[1], dbg=False, phases=(1, 2, 3))
    nc = build_program(cfg)
    in_maps = [make_in_map(x[b], W, C) for b in range(nb)]
    res = run_bass_kernel_spmd(nc, in_maps, core_ids=list(range(nb)))
    return np.stack([r["out"] for r in res.results], axis=0).astype(np.float32)
